# version 01
# baseline (speedup 1.0000x reference)
import contextlib
import math
import numpy as np
import ml_dtypes
import concourse.bass as bass
import concourse.mybir as mybir
from concourse.bass_utils import run_bass_kernel_spmd

F32 = mybir.dt.float32
BF16 = mybir.dt.bfloat16
AF = mybir.ActivationFunctionType
ALU = mybir.AluOpType
AXL = mybir.AxisListType

NEG = -1.0e30
NEGSEED = -3.0e38
DBG = False
D = 1024
T_OWN = 2048
NT_OWN = 16
NCD = 8

BF = np.dtype(ml_dtypes.bfloat16)


def _span(ap: bass.AP):
    t = ap.tensor
    if ap.space == "SBUF" or ap.space == "PSUM":
        dims = ap.ap
        p0 = ap.base_partition
        pn = dims[0][1]
        esz = 2 if ap.dtype == BF16 else 4
        off = ap.offset * esz if isinstance(ap.offset, int) else 0
        lo = hi = 0
        for step, cnt in dims[1:]:
            d = step * (cnt - 1) * esz
            if d >= 0:
                hi += d
            else:
                lo += d
        return (p0, p0 + pn, off + lo, off + hi + esz)
    else:
        return (0, 128, 0, 1 << 62)


class Prog:
    ENGS = ("tensor", "vector", "scalar", "gpsimd", "sync")

    def __init__(self, arena_kb=192):
        self.nc = bass.Bass("TRN2", target_bir_lowering=False)
        self.es = contextlib.ExitStack()
        self.ops = []
        self.acc = {}
        self.psum_names = set()
        self.dram_names = set()
        self._fin = False
        self.arena = self.es.enter_context(
            self.nc.sbuf_tensor("ARENA", [128, arena_kb * 256], F32))
        self.arena_bf = self.arena[:].bitcast(BF16)
        self._freelist = [(0, arena_kb * 1024)]
        self._allocs = {}

    def alloc(self, name, shape, dtype, align=64):
        esz = 2 if dtype == BF16 else 4
        n = 1
        for s in shape[1:]:
            n *= s
        nbytes = n * esz
        nbytes_al = (nbytes + align - 1) // align * align
        for i, (off, sz) in enumerate(self._freelist):
            offa = (off + align - 1) // align * align
            if offa + nbytes_al <= off + sz:
                newfree = []
                if offa > off:
                    newfree.append((off, offa - off))
                rest = (off + sz) - (offa + nbytes_al)
                if rest > 0:
                    newfree.append((offa + nbytes_al, rest))
                self._freelist[i:i + 1] = newfree
                self._allocs[name] = (offa, nbytes_al, shape, dtype)
                return self._view(offa, shape, dtype)
        allocs = sorted((v[0], v[1], k) for k, v in self._allocs.items())
        raise MemoryError(f"arena OOM for {name}: {nbytes_al}B/partition; "
                          f"free={self._freelist}; allocs={allocs}")

    def _view(self, off_bytes, shape, dtype):
        esz = 2 if dtype == BF16 else 4
        base = self.arena_bf if dtype == BF16 else self.arena[:]
        n = 1
        for s in shape[1:]:
            n *= s
        v = base[0:shape[0], off_bytes // esz: off_bytes // esz + n]
        if len(shape) > 2:
            pat = "p (" + " ".join(f"a{i}" for i in range(len(shape) - 1)) + ") -> p " + \
                  " ".join(f"a{i}" for i in range(len(shape) - 1))
            kw = {f"a{i}": shape[i + 1] for i in range(1, len(shape) - 1)}
            v = v.rearrange(pat, **kw)
        return v

    def free(self, *names):
        for name in names:
            off, sz, _, _ = self._allocs.pop(name)
            self._freelist.append((off, sz))
        self._freelist.sort()
        merged = []
        for off, sz in self._freelist:
            if merged and merged[-1][0] + merged[-1][1] == off:
                merged[-1] = (merged[-1][0], merged[-1][1] + sz)
            else:
                merged.append((off, sz))
        self._freelist = [tuple(x) for x in merged]

    def dram(self, name, shape, dtype, kind):
        ap = self.nc.dram_tensor(name, list(shape), dtype, kind=kind).ap()
        self.dram_names.add(name)
        return ap

    def sbuf(self, name, shape, dtype):
        return self.es.enter_context(self.nc.sbuf_tensor(name, list(shape), dtype))[:]

    def psum(self, name, shape, dtype):
        ap = self.es.enter_context(self.nc.psum_tensor(name, list(shape), dtype))[:]
        self.psum_names.add(name)
        return ap

    def op(self, eng, meth, *, deps=(), **kwargs):
        reads, writes = [], []
        for k, v in kwargs.items():
            aps = []
            _collect_aps(v, aps)
            for ap in aps:
                nm = ap.tensor.name
                sp = _span(ap)
                if nm in self.psum_names:
                    sp = (0, 128, 0, 1 << 30)
                is_w = k in ("out", "out_ap", "accum_out", "ap")
                (writes if is_w else reads).append((nm, sp))
        idx = len(self.ops)
        depset = set(deps)
        for nm, sp in reads:
            for rec in self.acc.get(nm, ()):
                if rec[5] and _ovl(rec, sp):
                    depset.add(rec[4])
        for nm, sp in writes:
            for rec in self.acc.get(nm, ()):
                if _ovl(rec, sp):
                    depset.add(rec[4])
        is_dma = meth.startswith("dma") or meth == "indirect_dma_start"
        deps2 = []
        for d in sorted(depset):
            dop = self.ops[d]
            if eng == "tensor" and dop["eng"] == "tensor" and not dop["dma"]:
                continue
            deps2.append(d)
            dop["inc"] = True
        semkey = None
        if is_dma:
            semkey = writes[0][0] if writes else "_misc"
            if semkey == "ARENA":
                b0 = writes[0][1][2]
                for anm, (aoff, asz, _, _) in self._allocs.items():
                    if aoff <= b0 < aoff + asz:
                        semkey = "A_" + anm
                        break
        rec = dict(eng=eng, meth=meth, kwargs=kwargs, deps=deps2, dma=is_dma,
                   inc=False, w=writes, sem=None, val=None, semkey=semkey)
        self.ops.append(rec)
        for nm, sp in writes:
            lst = self.acc.setdefault(nm, [])
            lst[:] = [r for r in lst if not _cover(sp, r)]
            lst.append((sp[0], sp[1], sp[2], sp[3], idx, True))
        for nm, sp in reads:
            self.acc.setdefault(nm, []).append((sp[0], sp[1], sp[2], sp[3], idx, False))
        return idx

    def dbg(self, name, ap):
        shp = [d[1] for d in ap.ap]
        d = self.dram("dbg_" + name, shp, ap.dtype, "ExternalOutput")
        self.dma("sync", out=d[tuple(slice(None) for _ in shp)], in_=ap)

    def mm(self, out, lhsT, rhs, start, stop, **kw):
        return self.op("tensor", "matmul", out=out, lhsT=lhsT, rhs=rhs,
                       start=start, stop=stop, **kw)

    def dma(self, eng, out, in_, **kw):
        return self.op(eng, "dma_start", out=out, in_=in_, **kw)

    def xpose(self, out_psum, in_sbuf, ident):
        return self.op("tensor", "transpose", out=out_psum, in_=in_sbuf,
                       identity=ident)

    def build(self, verbose=False):
        assert not self._fin
        self._fin = True
        nc = self.nc
        eng_sems, dma_sems = {}, {}
        for o in self.ops:
            if o["dma"]:
                key = o["semkey"]
                if key not in dma_sems:
                    dma_sems[key] = self.es.enter_context(
                        nc.semaphore(f"d{len(dma_sems)}_{key}"[:28]))
                o["sem"] = ("d", key)
            elif o["inc"]:
                e = o["eng"]
                if e not in eng_sems:
                    eng_sems[e] = self.es.enter_context(nc.semaphore(f"p_{e}"))
                o["sem"] = ("e", e)
        nsem = len(eng_sems) + len(dma_sems)
        if nsem > 250:
            raise RuntimeError(f"{nsem} semaphores")
        vals = {}
        for o in self.ops:
            if o["sem"]:
                k = o["sem"]
                vals[k] = vals.get(k, 0) + (16 if o["dma"] else 1)
                o["val"] = vals[k]
        self.es.enter_context(nc.allow_low_precision(
            reason="bf16 operands/outputs are deliberate (fp32 accumulation in PSUM)"))
        blk = self.es.enter_context(nc.Block())
        per = {e: [] for e in self.ENGS}
        for o in self.ops:
            per[o["eng"]].append(o)
        allops = self.ops

        def sem_of(o):
            kind, key = o["sem"]
            return (dma_sems if kind == "d" else eng_sems)[key]

        def replayer(oplist):
            def f(h):
                waited = {}
                for o in oplist:
                    if o["dma"] and o["val"] > 16:
                        s = sem_of(o)
                        if waited.get(id(s), -1) < o["val"] - 16:
                            h.wait_ge(s, o["val"] - 16)
                            waited[id(s)] = o["val"] - 16
                    for d in o["deps"]:
                        dop = allops[d]
                        s, v = sem_of(dop), dop["val"]
                        if waited.get(id(s), -1) < v:
                            h.wait_ge(s, v)
                            waited[id(s)] = v
                    try:
                        ins = getattr(h, o["meth"])(**o["kwargs"])
                    except Exception:
                        import sys
                        print("FAILING OP:", o["eng"], o["meth"],
                              {k: (tuple(v.shape) if isinstance(v, bass.AP) else v)
                               for k, v in o["kwargs"].items()}, file=sys.stderr)
                        raise
                    if o["dma"]:
                        ins.then_inc(sem_of(o), 16)
                    elif o["inc"]:
                        ins.then_inc(sem_of(o), 1)
            return f

        for e in self.ENGS:
            if per[e]:
                getattr(blk, e)(replayer(per[e]))
        fin = {}
        for o in self.ops:
            if o["dma"] and o["w"] and o["w"][0][0] in self.dram_names:
                fin[o["sem"][1]] = o["val"]

        def terminal(h):
            for key, val in fin.items():
                h.wait_ge(dma_sems[key], val)
        blk.sync(terminal)
        self.es.close()
        if verbose:
            n_inc = sum(1 for o in self.ops if o["inc"] or o["dma"])
            print(f"[prog] {len(self.ops)} ops, {nsem} sems, {n_inc} incs")
        return nc


def _collect_aps(v, acc):
    if isinstance(v, bass.AP):
        acc.append(v)
    elif isinstance(v, (list, tuple)):
        for e in v:
            _collect_aps(e, acc)


def _ovl(rec, sp):
    return rec[0] < sp[1] and sp[0] < rec[1] and rec[2] < sp[3] and sp[2] < rec[3]


def _cover(sp, rec):
    return sp[0] <= rec[0] and rec[1] <= sp[1] and sp[2] <= rec[2] and rec[3] <= sp[3]


class Common:
    def __init__(self, p: Prog):
        self.p = p
        self.ident = p.sbuf("c_ident", [128, 128], BF16)
        self.identf = p.sbuf("c_identf", [128, 128], F32)
        iot_row = p.sbuf("c_iotrow", [128, 128], F32)
        iot_col = p.sbuf("c_iotcol", [128, 1], F32)
        self.iot_row = iot_row
        self.iot_col = iot_col
        self.ones_row = p.sbuf("c_onesrow", [1, 128], BF16)
        self.ones_rowf = p.sbuf("c_onesrowf", [1, 128], F32)
        self.ones_colf = p.sbuf("c_onescolf", [128, 1], F32)
        p.op("gpsimd", "iota", out=iot_row, pattern=[[1, 128]], base=0,
             channel_multiplier=0, allow_small_or_imprecise_dtypes=True)
        p.op("gpsimd", "iota", out=iot_col, pattern=[[0, 1]], base=0,
             channel_multiplier=1, allow_small_or_imprecise_dtypes=True)
        p.op("vector", "tensor_scalar", out=self.ident, in0=iot_row,
             scalar1=iot_col, scalar2=None, op0=ALU.is_equal)
        p.op("vector", "tensor_scalar", out=self.identf, in0=iot_row,
             scalar1=iot_col, scalar2=None, op0=ALU.is_equal)
        p.op("vector", "memset", ap=self.ones_row, constant=1.0)
        p.op("vector", "memset", ap=self.ones_rowf, constant=1.0)
        p.op("vector", "memset", ap=self.ones_colf, constant=1.0)
        self._consts = {}

    def c(self, val):
        if val not in self._consts:
            t = self.p.sbuf(f"c_k{len(self._consts)}", [128, 1], F32)
            self.p.op("vector", "memset", ap=t, constant=float(val))
            self._consts[val] = t
        return self._consts[val]


def emit_ln(p, cm, x_f32, xn_out, s1, s2, mean, var, rstd, nmr, xsq, n=1024):
    p.op("scalar", "activation", out=xsq, in_=x_f32, func=AF.Square)
    p.op("vector", "reduce_sum", out=s1, in_=x_f32, axis=AXL.X)
    p.op("vector", "reduce_sum", out=s2, in_=xsq, axis=AXL.X)
    p.op("vector", "tensor_scalar", out=mean, in0=s1, scalar1=1.0 / n,
         scalar2=None, op0=ALU.mult)
    p.op("vector", "tensor_scalar", out=s2, in0=s2, scalar1=1.0 / n,
         scalar2=None, op0=ALU.mult)
    p.op("vector", "scalar_tensor_tensor", out=var, in0=mean, scalar=mean,
         in1=s2, op0=ALU.mult, op1=ALU.subtract)
    p.op("vector", "tensor_scalar", out=var, in0=var, scalar1=-1.0,
         scalar2=None, op0=ALU.mult)
    p.op("scalar", "activation", out=rstd, in_=var, func=AF.Sqrt, bias=cm.c(1e-5))
    p.op("vector", "reciprocal", out=rstd, in_=rstd)
    p.op("vector", "tensor_scalar", out=nmr, in0=mean, scalar1=rstd,
         scalar2=-1.0, op0=ALU.mult, op1=ALU.mult)
    p.op("vector", "tensor_scalar", out=xn_out, in0=x_f32,
         scalar1=rstd, scalar2=nmr, op0=ALU.mult, op1=ALU.add)



def build_block0(p: Prog, cm: Common, h, x_own_dram, x_other_dram, W,
                 x_out_dram):
    nc_kv = 4 * (h + 1)
    kv_len = 512 * nc_kv
    nt_kv = kv_len // 128

    psS = p.psum("psS", [128, 512], F32)
    psA = p.psum("psA", [128, 512], F32)
    psB = p.psum("psB", [128, 512], F32)
    psC = p.psum("psC", [128, 512], F32)
    psD = p.psum("psD", [128, 512], F32)
    psE = p.psum("psE", [128, 512], F32)
    psT0 = p.psum("psT0", [128, 128], BF16)
    psT1 = p.psum("psT1", [128, 128], BF16)

    w1t_d = p.dram("b0_w1t", [8, 4096, 128], BF16, "Internal")
    w2t_d = p.dram("b0_w2t", [32, 1024, 128], BF16, "Internal")

    xn_fm = p.alloc("xn_fm", [128, NCD, T_OWN], BF16)
    lnb = [p.alloc(f"ln{i}", [128, 1], F32) for i in range(6)]
    xsq = p.alloc("xsq", [128, 1024], F32)

    wtile = [p.alloc(f"wtile{i}", [128, 4096], F32) for i in range(2)]
    ttile = [p.alloc(f"ttile{i}", [128, 4096], BF16) for i in range(2)]
    tmpt = p.alloc("tmpt", [128, 4096], F32)
    absa = p.alloc("absa", [128, 2], F32)
    scl = p.alloc("scl", [128, 2], F32)

    for mi, (src, nt, wd, scratch) in enumerate(
            ((W["fw1"], 32, 1024, w1t_d), (W["fw2"], 8, 4096, w2t_d))):
        for i in range(nt):
            b = i % 2
            p.dma("gpsimd", out=wtile[b][:, 0:wd], in_=src[i * 128:(i + 1) * 128, :])
            p.op("scalar", "activation", out=tmpt[:, 0:wd], in_=wtile[b][:, 0:wd],
                 func=AF.Abs, accum_out=absa[:, 1:2])
            if i == 0:
                p.op("vector", "tensor_copy", out=absa[:, 0:1], in_=absa[:, 1:2])
            else:
                p.op("vector", "tensor_tensor", out=absa[:, 0:1],
                     in0=absa[:, 0:1], in1=absa[:, 1:2], op=ALU.add)
        p.mm(psS[0:1, 0:1], cm.ones_colf, absa[:, 0:1], True, True)
        p.op("scalar", "activation", out=scl[0:1, 1:2], in_=psS[0:1, 0:1],
             func=AF.Copy, scale=cm.c(1.0 / (4096.0 * 1024.0))[0:1, :])
        p.op("vector", "tensor_scalar", out=scl[0:1, 1:2], in0=scl[0:1, 1:2],
             scalar1=1e-8, scalar2=None, op0=ALU.max)
        p.op("vector", "reciprocal", out=scl[0:1, 1:2], in_=scl[0:1, 1:2])
        p.mm(psA[:, 0:1], cm.ones_rowf, scl[0:1, 1:2], True, True)
        p.op("vector", "tensor_copy", out=scl[:, 0:1], in_=psA[:, 0:1])
        for i in range(nt):
            b = i % 2
            p.dma("gpsimd", out=wtile[b][:, 0:wd], in_=src[i * 128:(i + 1) * 128, :])
            p.op("vector", "tensor_scalar", out=tmpt[:, 0:wd], in0=wtile[b][:, 0:wd],
                 scalar1=scl[:, 0:1], scalar2=0.5, op0=ALU.mult, op1=ALU.is_gt)
            p.op("gpsimd", "tensor_scalar", out=wtile[b][:, 0:wd], in0=wtile[b][:, 0:wd],
                 scalar1=scl[:, 0:1], scalar2=-0.5, op0=ALU.mult, op1=ALU.is_lt)
            p.op("vector", "tensor_tensor", out=ttile[b][:, 0:wd], in0=tmpt[:, 0:wd],
                 in1=wtile[b][:, 0:wd], op=ALU.subtract)
            p.dma("gpsimd",
                  out=scratch[:, i * 128:(i + 1) * 128, :].rearrange("c p q -> p c q"),
                  in_=ttile[b][:, 0:wd].rearrange("p (c q) -> p c q", q=128))
    p.free("wtile0", "wtile1", "ttile0", "ttile1", "tmpt", "absa", "scl")

    wq = p.alloc("wq", [128, NCD, 64], BF16)
    wk = p.alloc("wk", [128, NCD, 64], BF16)
    wv = p.alloc("wv", [128, NCD, 1024], BF16)
    q_fm = p.alloc("q_fm", [64, T_OWN], BF16)
    k_fm = p.alloc("k_fm", [64, kv_len], BF16)
    v_tm = p.alloc("v_tm", [128, nt_kv, 1024], BF16)
    xg_fm = p.alloc("xg_fm", [128, NCD, 512], BF16)
    xraw = [p.alloc(f"xraw{i}", [128, 1024], F32) for i in range(2)]
    xn_t = [p.alloc(f"xn_t{i}", [128, 1024], BF16) for i in range(2)]

    p.dma("gpsimd", out=wq, in_=W["q"].rearrange("(c p) f -> p c f", p=128))
    p.dma("gpsimd", out=wk, in_=W["k"].rearrange("(c p) f -> p c f", p=128))
    p.dma("gpsimd", out=wv, in_=W["v"].rearrange("(c p) f -> p c f", p=128))

    for g in range(nc_kv):
        own = g >= nc_kv - 4
        og = g - (nc_kv - 4)
        dst = xn_fm[:, :, og * 512:(og + 1) * 512] if own else xg_fm
        for j in range(4):
            t = g * 4 + j
            buf = xraw[j % 2]
            src = x_own_dram if own else x_other_dram
            row = (og * 4 + j) if own else t
            p.dma("sync", out=buf, in_=src[row * 128:(row + 1) * 128, :])
            xnt = xn_t[j % 2]
            emit_ln(p, cm, buf, xnt, *lnb, xsq)
            for c in range(NCD):
                pb = psT0 if c % 2 == 0 else psT1
                p.xpose(pb, xnt[:, c * 128:(c + 1) * 128], cm.ident)
                p.op("scalar", "copy", out=dst[:, c, j * 128:(j + 1) * 128], in_=pb)
        for c in range(NCD):
            p.mm(psS[0:64, :], wk[:, c, :], dst[:, c, :], c == 0, c == NCD - 1)
        p.op("scalar", "copy", out=k_fm[:, g * 512:(g + 1) * 512], in_=psS[0:64, :])
        if own:
            for c in range(NCD):
                p.mm(psA[0:64, :], wq[:, c, :], dst[:, c, :], c == 0, c == NCD - 1)
            p.op("scalar", "copy", out=q_fm[:, og * 512:(og + 1) * 512], in_=psA[0:64, :])
        for j in range(4):
            for fh in range(2):
                po = psB if fh == 0 else psC
                for c in range(NCD):
                    p.mm(po, dst[:, c, j * 128:(j + 1) * 128],
                         wv[:, c, fh * 512:(fh + 1) * 512], c == 0, c == NCD - 1)
                p.op("vector", "tensor_copy",
                     out=v_tm[:, g * 4 + j, fh * 512:(fh + 1) * 512], in_=po)
    p.free("wq", "wk", "wv", "xg_fm", "xraw0", "xraw1", "xn_t0", "xn_t1")

    s_row = p.alloc("s_row", [128, kv_len], F32)
    p_row = p.alloc("p_row", [128, kv_len], BF16)
    pt_buf = p.alloc("pt_buf", [128, nt_kv * 128], BF16)
    mr = p.alloc("mr", [128, 4], F32)
    rinv = p.alloc("rinv", [128, 1], F32)
    xres = [p.alloc(f"xres{i}", [128, 1024], F32) for i in range(2)]

    for t in range(NT_OWN):
        q0g = h * T_OWN + t * 128
        n_ch = q0g // 512 + 1
        n_kt = q0g // 128 + 1
        p.dma("sync", out=xres[t % 2], in_=x_own_dram[t * 128:(t + 1) * 128, :])
        for kc in range(n_ch):
            p.mm(psS, q_fm[:, t * 128:(t + 1) * 128],
                 k_fm[:, kc * 512:(kc + 1) * 512], True, True)
            sl = s_row[:, kc * 512:(kc + 1) * 512]
            if kc == n_ch - 1:
                p.op("scalar", "copy", out=sl, in_=psS)
                p.op("gpsimd", "affine_select", out=sl, in_=sl,
                     pattern=[[-1, 512]], base=q0g - kc * 512,
                     channel_multiplier=1, compare_op=ALU.is_ge, fill=NEG)
                src_ts = sl
            else:
                src_ts = psS
            seed = NEGSEED if kc == 0 else mr[:, (kc - 1) % 2:(kc - 1) % 2 + 1]
            p.op("vector", "tensor_scalar", out=sl, in0=src_ts, scalar1=0.0,
                 scalar2=seed, op0=ALU.add, op1=ALU.max,
                 accum_out=mr[:, kc % 2:kc % 2 + 1])
        mfin = mr[:, (n_ch - 1) % 2:(n_ch - 1) % 2 + 1]
        p.op("vector", "tensor_scalar", out=mr[:, 2:3], in0=mfin,
             scalar1=-0.125, scalar2=None, op0=ALU.mult)
        p.op("scalar", "activation", out=p_row[:, 0:n_ch * 512],
             in_=s_row[:, 0:n_ch * 512], func=AF.Exp, scale=cm.c(0.125),
             bias=mr[:, 2:3], accum_out=mr[:, 3:4])
        p.op("vector", "reciprocal", out=rinv, in_=mr[:, 3:4])
        for kt in range(n_kt):
            pb = psT0 if kt % 2 == 0 else psT1
            p.xpose(pb, p_row[:, kt * 128:(kt + 1) * 128], cm.ident)
            p.op("scalar", "copy", out=pt_buf[:, kt * 128:(kt + 1) * 128], in_=pb)
        for fh in range(2):
            po = psA if fh == 0 else psB
            for kt in range(n_kt):
                p.mm(po, pt_buf[:, kt * 128:(kt + 1) * 128],
                     v_tm[:, kt, fh * 512:(fh + 1) * 512], kt == 0, kt == n_kt - 1)
            p.op("vector", "scalar_tensor_tensor",
                 out=xres[t % 2][:, fh * 512:(fh + 1) * 512], in0=po,
                 scalar=rinv, in1=xres[t % 2][:, fh * 512:(fh + 1) * 512],
                 op0=ALU.mult, op1=ALU.add)
        p.dma("sync", out=x_out_dram[t * 128:(t + 1) * 128, :], in_=xres[t % 2])
    p.free("s_row", "p_row", "pt_buf", "mr", "rinv", "xres0", "xres1",
           "q_fm", "k_fm", "v_tm")

    b1s = p.alloc("b1s", [32, 128], F32)
    b1t = p.alloc("b1t", [128, 32], F32)
    b2r = p.alloc("b2r", [1, 1024], BF16)
    b2b = p.alloc("b2b", [128, 1024], F32)
    p.dma("sync", out=b1s, in_=W["b1"].rearrange("(j p) -> j p", p=128))
    p.xpose(psS[:, 0:32], b1s, cm.identf[0:32, 0:32])
    p.op("vector", "tensor_copy", out=b1t, in_=psS[:, 0:32])
    p.dma("gpsimd", out=b2r, in_=W["b2"].unsqueeze(0))
    for fh in range(2):
        p.mm(psA, cm.ones_row, b2r[:, fh * 512:(fh + 1) * 512], True, True)
        p.op("vector", "tensor_copy", out=b2b[:, fh * 512:(fh + 1) * 512], in_=psA)

    w1ts = [p.alloc(f"w1ts{i}", [128, NCD, 512], BF16) for i in range(2)]
    w2ts = [p.alloc(f"w2ts{i}", [128, 1024], BF16) for i in range(2)]
    h_fm = p.alloc("h_fm", [128, 32, 256], BF16)
    xres2 = [p.alloc(f"xres2{i}", [128, 1024], F32) for i in range(2)]

    for g in range(8):
        tok0 = g * 256
        for jg in range(8):
            wb = w1ts[jg % 2]
            for c in range(NCD):
                p.dma("sync", out=wb[:, c, :],
                      in_=w1t_d[c, jg * 512:(jg + 1) * 512, :], transpose=True)
            for j4 in range(4):
                j = jg * 4 + j4
                ph = psS if j % 2 == 0 else psA
                for c in range(NCD):
                    p.mm(ph[:, 0:256], wb[:, c, j4 * 128:(j4 + 1) * 128],
                         xn_fm[:, c, tok0:tok0 + 256], c == 0, c == NCD - 1)
                p.op("scalar", "activation", out=h_fm[:, j, :], in_=ph[:, 0:256],
                     func=AF.Gelu, bias=b1t[:, j:j + 1])
        banks = (psB, psC, psD, psE)
        for j in range(32):
            wb2 = w2ts[j % 2]
            p.dma("sync", out=wb2, in_=w2t_d[j, :, :], transpose=True)
            for tt in range(2):
                for fh in range(2):
                    p.mm(banks[tt * 2 + fh],
                         h_fm[:, j, tt * 128:(tt + 1) * 128],
                         wb2[:, fh * 512:(fh + 1) * 512], j == 0, j == 31)
        for tt in range(2):
            t = g * 2 + tt
            xr = xres2[tt]
            p.dma("sync", out=xr, in_=x_out_dram[t * 128:(t + 1) * 128, :])
            p.op("vector", "tensor_tensor", out=xr, in0=xr, in1=b2b, op=ALU.add)
            for fh in range(2):
                p.op("vector", "tensor_tensor",
                     out=xr[:, fh * 512:(fh + 1) * 512],
                     in0=xr[:, fh * 512:(fh + 1) * 512],
                     in1=banks[tt * 2 + fh], op=ALU.add)
            p.dma("sync", out=x_out_dram[t * 128:(t + 1) * 128, :], in_=xr)
    p.free("b1s", "b1t", "b2r", "b2b", "w1ts0", "w1ts1", "w2ts0", "w2ts1",
           "h_fm", "xres20", "xres21", "xn_fm")
    for nm in ("ln0", "ln1", "ln2", "ln3", "ln4", "ln5", "xsq"):
        p.free(nm)



def make_wt_scratch(p: Prog, name, w_dram, O, I, row_step=512):
    C = (I + 127) // 128
    O_pad = (O + 15) // 16 * 16
    scr = p.dram(name, [C, O_pad, 128], BF16, "Internal")
    i_full = (I // 128) * 128
    for r0 in range(0, O, row_step):
        nr = min(row_step, O - r0)
        if i_full > 0:
            p.dma("gpsimd",
                  out=scr[0:i_full // 128, r0:r0 + nr, :].rearrange("c p q -> p c q"),
                  in_=w_dram[r0:r0 + nr, 0:i_full].rearrange("p (c q) -> p c q", q=128))
        if I > i_full:
            p.dma("gpsimd",
                  out=scr[C - 1, r0:r0 + nr, 0:I - i_full],
                  in_=w_dram[r0:r0 + nr, i_full:I])
    return scr


def bcast_row(p, cm, psum, out_sb, row_ap, width):
    for o in range(0, width, 512):
        w = min(512, width - o)
        p.mm(psum[:, 0:w], cm.ones_rowf, row_ap[:, o:o + w], True, True)
        p.op("vector", "tensor_copy", out=out_sb[:, o:o + w], in_=psum[:, 0:w])


def emit_softplus(p, cm, x, tmp):
    p.op("scalar", "activation", out=tmp, in_=x, func=AF.Abs)
    p.op("scalar", "activation", out=tmp, in_=tmp, func=AF.Exp, scale=cm.c(-1.0))
    p.op("scalar", "activation", out=tmp, in_=tmp, func=AF.Ln, bias=cm.c(1.0))
    p.op("vector", "tensor_scalar", out=x, in0=x, scalar1=0.0, scalar2=None,
         op0=ALU.max)
    p.op("vector", "tensor_tensor", out=x, in0=x, in1=tmp, op=ALU.add)



NH = 16
HP = 128
NST = 64
CH = 256
DIN = 2048
DPROJ = 4240
FFM = 2730


def build_block1(p: Prog, cm: Common, h, x_own_dram, x_other_dram, W, x_out_dram):
    n_chunks = 8 * (h + 1)
    own0 = n_chunks - 8

    psP = [p.psum(f"ps{i}", [128, 512], F32) for i in range(5)]
    psT0 = p.psum("psT0", [128, 128], BF16)
    psT1 = p.psum("psT1", [128, 128], BF16)
    psF0 = p.psum("psF0", [128, 128], F32)

    win_t = make_wt_scratch(p, "b1_wint", W["in_proj"], DPROJ, 1024)
    wout_t = make_wt_scratch(p, "b1_woutt", W["out_proj"], 1024, DIN)
    w1_t = [make_wt_scratch(p, f"b1_w1t{e}", W["W1"][e], FFM, 1024) for e in range(8)]
    w2_t = [make_wt_scratch(p, f"b1_w2t{e}", W["W2"][e], FFM, 1024) for e in range(8)]
    w3_t = [make_wt_scratch(p, f"b1_w3t{e}", W["W3"][e], 1024, FFM) for e in range(8)]

    small = p.alloc("small", [1, 2096], F32)
    dtb_b = p.alloc("dtb_b", [128, 16], F32)
    a_b = p.alloc("a_b", [128, 16], F32)
    dp_b = p.alloc("dp_b", [128, 16], F32)
    nw_b = p.alloc("nw_b", [128, 2048], BF16)
    cw_x = p.alloc("cw_x", [128, 16, 4], F32)
    cb_x = p.alloc("cb_x", [128, 16], F32)
    cw_B = p.alloc("cw_B", [64, 4], F32)
    cb_B = p.alloc("cb_B", [64, 1], F32)
    cw_C = p.alloc("cw_C", [64, 4], F32)
    cb_C = p.alloc("cb_C", [64, 1], F32)
    p.dma("sync", out=small[:, 0:16], in_=W["dt_bias"].unsqueeze(0))
    p.dma("sync", out=small[:, 16:32], in_=W["A_log"].unsqueeze(0))
    p.dma("sync", out=small[:, 32:48], in_=W["Dp"].unsqueeze(0))
    p.dma("sync", out=small[:, 48:2096], in_=W["norm_w"].unsqueeze(0))
    p.dma("sync", out=cw_x,
          in_=W["conv_w"][0:2048, :].rearrange("(c p) w -> p c w", p=128))
    p.dma("sync", out=cb_x, in_=W["conv_b"][0:2048].rearrange("(c p) -> p c", p=128),
          allow_slow_non_contiguous=True)
    p.dma("sync", out=cw_B, in_=W["conv_w"][2048:2112, :])
    p.dma("sync", out=cb_B, in_=W["conv_b"][2048:2112].unsqueeze(1))
    p.dma("sync", out=cw_C, in_=W["conv_w"][2112:2176, :])
    p.dma("sync", out=cb_C, in_=W["conv_b"][2112:2176].unsqueeze(1))
    p.op("scalar", "activation", out=small[:, 16:32], in_=small[:, 16:32], func=AF.Exp)
    p.op("vector", "tensor_scalar", out=small[:, 16:32], in0=small[:, 16:32],
         scalar1=-1.0, scalar2=None, op0=ALU.mult)
    bcast_row(p, cm, psP[0], dtb_b, small[:, 0:16], 16)
    bcast_row(p, cm, psP[0], a_b, small[:, 16:32], 16)
    bcast_row(p, cm, psP[0], dp_b, small[:, 32:48], 16)
    nw_f = p.alloc("nw_f", [128, 2048], F32)
    bcast_row(p, cm, psP[0], nw_f, small[:, 48:2096], 2048)
    p.op("vector", "tensor_copy", out=nw_b, in_=nw_f)
    p.free("nw_f")

    maskc = p.alloc("maskc", [128, 2, 256], F32)
    p.op("vector", "memset", ap=maskc, constant=0.0)
    for i in range(2):
        p.op("gpsimd", "affine_select", out=maskc[:, i, :], in_=maskc[:, i, :],
             pattern=[[1, 256]], base=-i * 128, channel_multiplier=-1,
             compare_op=ALU.is_ge, fill=NEG)
    triu = p.alloc("triu", [128, 128], F32)
    p.op("vector", "tensor_scalar", out=triu, in0=cm.iot_row,
         scalar1=cm.iot_col, scalar2=None, op0=ALU.is_ge)

    xn_own = p.alloc("xn_own", [128, NCD, T_OWN], BF16)
    prev = p.alloc("prev", [128, NH, NST], F32)
    prevT = p.alloc("prevT", [64, NH, HP], BF16)
    p.op("vector", "memset", ap=prev, constant=0.0)
    win_s = p.alloc("win_s", [128, NCD, 2192], BF16)
    for c in range(NCD):
        p.dma("sync", out=win_s[:, c, 0:2048], in_=win_t[c, 2048:4096, :],
              transpose=True)
        p.dma("sync", out=win_s[:, c, 2048:2192], in_=win_t[c, 4096:4240, :],
              transpose=True)
    wz_s = [p.alloc("wz_s0", [128, NCD, 512], BF16)] * 2
    woc = [p.alloc(f"woc{i}", [128, 1024], BF16) for i in range(2)]

    xn_g = p.alloc("xn_g", [128, NCD, CH], BF16)
    xraw = [p.alloc(f"xraw{i}", [128, 1024], F32) for i in range(2)]
    xnt = [p.alloc(f"xnt{i}", [128, 1024], BF16) for i in range(2)]
    lnb = [p.alloc(f"ln{i}", [128, 1], F32) for i in range(6)]
    scr16 = p.alloc("scr16", [128, 4096], F32)
    xsq = scr16[:, 0:2048]
    zbuf = p.alloc("zbuf", [128, 2, DIN], BF16)
    xbc = p.alloc("xbc", [128, 16, CH + 4], BF16)
    bpre = p.alloc("bpre", [64, CH + 4], BF16)
    cpre = p.alloc("cpre", [64, CH + 4], BF16)
    xbs = p.alloc("xbs", [128, 16, CH], BF16)
    b_fm = p.alloc("b_fm", [64, CH], BF16)
    c_fm = p.alloc("c_fm", [64, CH], BF16)
    dts = p.alloc("dts", [128, 2, NH], F32)
    acum = p.alloc("acum", [128, 2, NH], F32)
    nac = p.alloc("nac", [128, 2, NH], F32)
    ea = p.alloc("ea", [128, 2, NH], F32)
    acr = p.alloc("acr", [1, CH], F32)
    awk = p.alloc("awk", [128, 2, NH], F32)
    alb = p.alloc("alb", [128, 16], F32)
    cdb = p.alloc("cdb", [128, 16], F32)
    s_sb = p.alloc("s_sb", [128, 2, CH], F32)
    g_sb = p.alloc("g_sb", [128, 2, CH], BF16)
    xtm = p.alloc("xtm", [128, 2, HP], BF16)
    btm = p.alloc("btm", [128, 2, NST], BF16)
    xw1 = p.alloc("xw1", [128, 2, HP], BF16)
    ysb = p.alloc("ysb", [128, 2, DIN], F32)
    tmp32 = p.alloc("tmp32", [128, 2, NH], F32)
    carry = None
    yfm = p.alloc("yfm", [128, 16, CH], BF16)
    ynb = None
    xres = [scr16[:, 0:1024]] * 2
    ynb = scr16[:, 2048:3072].bitcast(BF16)
    e_sb = scr16[:, 3072:3328]
    ge = scr16[:, 3328:3456].bitcast(BF16)
    carry = scr16[0:1, 3968:4000]

    p.op("vector", "memset", ap=xbc[:, :, 0:4], constant=0.0)
    p.op("vector", "memset", ap=bpre[:, 0:4], constant=0.0)
    p.op("vector", "memset", ap=cpre[:, 0:4], constant=0.0)

    XOF = 2048

    for ck in range(n_chunks):
        own = ck >= own0
        oc = ck - own0
        xn_fm = xn_own[:, :, oc * CH:(oc + 1) * CH] if own else xn_g[:, :, :]
        for j in range(2):
            tglob = ck * 2 + j
            buf = xraw[j % 2]
            if own:
                p.dma("sync", out=buf,
                      in_=x_own_dram[(oc * 2 + j) * 128:(oc * 2 + j + 1) * 128, :])
            else:
                p.dma("sync", out=buf, in_=x_other_dram[tglob * 128:(tglob + 1) * 128, :])
            emit_ln(p, cm, buf, xnt[j % 2], *lnb, xsq[:, 0:1024])
            for c in range(NCD):
                pb = psT0 if c % 2 == 0 else psT1
                p.xpose(pb, xnt[j % 2][:, c * 128:(c + 1) * 128], cm.ident)
                p.op("scalar", "copy", out=xn_fm[:, c, j * 128:(j + 1) * 128], in_=pb)
        if ck > 0:
            p.op("vector", "tensor_copy", out=xbc[:, :, 0:4], in_=xbc[:, :, CH:CH + 4])
            p.op("vector", "tensor_copy", out=bpre[:, 0:4], in_=bpre[:, CH:CH + 4])
            p.op("vector", "tensor_copy", out=cpre[:, 0:4], in_=cpre[:, CH:CH + 4])
        for fo in range(16):
            ph = psP[fo % 2]
            for c in range(NCD):
                p.mm(ph[:, 0:CH], win_s[:, c, fo * 128:(fo + 1) * 128],
                     xn_fm[:, c, :], c == 0, c == NCD - 1)
            p.op("vector", "tensor_copy", out=xbc[:, fo, 4:CH + 4], in_=ph[:, 0:CH])
        for dst, col0 in ((bpre, 2048), (cpre, 2112)):
            ph = psP[2]
            for c in range(NCD):
                p.mm(ph[0:64, 0:CH], win_s[:, c, col0:col0 + 64],
                     xn_fm[:, c, :], c == 0, c == NCD - 1)
            p.op("vector", "tensor_copy", out=dst[:, 4:CH + 4], in_=ph[0:64, 0:CH])
        if own:
            for f4 in range(4):
                wz = wz_s[f4 % 2]
                for c in range(NCD):
                    p.dma("sync", out=wz[:, c, :],
                          in_=win_t[c, f4 * 512:(f4 + 1) * 512, :], transpose=True)
                for j in range(2):
                    ph = psP[3 + j]
                    for c in range(NCD):
                        p.mm(ph, xn_fm[:, c, j * 128:(j + 1) * 128],
                             wz[:, c, :], c == 0, c == NCD - 1)
                    p.op("vector", "tensor_copy", out=zbuf[:, j, f4 * 512:(f4 + 1) * 512],
                         in_=ph)
        for j in range(2):
            ph = psP[2]
            for c in range(NCD):
                p.mm(ph[:, 0:16], xn_fm[:, c, j * 128:(j + 1) * 128],
                     win_s[:, c, 2176:2192], c == 0, c == NCD - 1)
            p.op("vector", "tensor_copy", out=dts[:, j, :], in_=ph[:, 0:16])
        conv_out = scr16.rearrange("p (c t) -> p c t", t=CH)
        for fo in range(16):
            co = conv_out[:, fo, :]
            p.op("vector", "tensor_scalar", out=co, in0=xbc[:, fo, 1:CH + 1],
                 scalar1=cw_x[:, fo, 0:1], scalar2=cb_x[:, fo:fo + 1],
                 op0=ALU.mult, op1=ALU.add)
            for w in range(1, 4):
                p.op("vector", "scalar_tensor_tensor", out=co,
                     in0=xbc[:, fo, 1 + w:CH + 1 + w], scalar=cw_x[:, fo, w:w + 1],
                     in1=co, op0=ALU.mult, op1=ALU.add)
            p.op("scalar", "activation", out=xbs[:, fo, :], in_=co, func=AF.Silu)
        for pre, cw, cb, dstf in ((bpre, cw_B, cb_B, b_fm), (cpre, cw_C, cb_C, c_fm)):
            co = conv_out[0:64, 0, :]
            p.op("vector", "tensor_scalar", out=co, in0=pre[:, 1:CH + 1],
                 scalar1=cw[:, 0:1], scalar2=cb[:, 0:1], op0=ALU.mult, op1=ALU.add)
            for w in range(1, 4):
                p.op("vector", "scalar_tensor_tensor", out=co, in0=pre[:, 1 + w:CH + 1 + w],
                     scalar=cw[:, w:w + 1], in1=co, op0=ALU.mult, op1=ALU.add)
            p.op("scalar", "activation", out=dstf, in_=co, func=AF.Silu)
        dflat = dts.rearrange("p a b -> p (a b)")
        p.op("vector", "tensor_tensor", out=dts[:, 0, :], in0=dts[:, 0, :],
             in1=dtb_b, op=ALU.add)
        p.op("vector", "tensor_tensor", out=dts[:, 1, :], in0=dts[:, 1, :],
             in1=dtb_b, op=ALU.add)
        emit_softplus(p, cm, dflat, tmp32.rearrange("p a b -> p (a b)"))
        for j in range(2):
            p.op("vector", "tensor_tensor", out=tmp32[:, j, :], in0=dts[:, j, :],
                 in1=a_b, op=ALU.mult)
        p.mm(psF0[:, 0:16], triu, tmp32[:, 0, :], True, True)
        p.op("vector", "tensor_copy", out=acum[:, 0, :], in_=psF0[:, 0:16])
        p.mm(psP[2][0:1, 0:16], cm.ones_colf, tmp32[:, 0, :], True, True)
        p.op("scalar", "copy", out=carry[:, 0:16], in_=psP[2][0:1, 0:16])
        p.mm(psF0[:, 0:16], triu, tmp32[:, 1, :], True, False)
        p.mm(psF0[:, 0:16], cm.ones_rowf, carry[:, 0:16], False, True)
        p.op("vector", "tensor_copy", out=acum[:, 1, :], in_=psF0[:, 0:16])
        p.mm(psP[2][0:1, 0:16], cm.ones_colf, tmp32[:, 0, :], True, False)
        p.mm(psP[2][0:1, 0:16], cm.ones_colf, tmp32[:, 1, :], False, True)
        p.op("scalar", "copy", out=carry[:, 16:32], in_=psP[2][0:1, 0:16])
        p.mm(psP[2][:, 0:16], cm.ones_rowf, carry[:, 16:32], True, True)
        p.op("vector", "tensor_copy", out=alb, in_=psP[2][:, 0:16])
        p.op("scalar", "activation", out=cdb, in_=alb, func=AF.Exp)
        p.op("vector", "tensor_scalar", out=nac.rearrange("p a b -> p (a b)"),
             in0=acum.rearrange("p a b -> p (a b)"), scalar1=-1.0, scalar2=None,
             op0=ALU.mult)
        p.op("scalar", "activation", out=ea.rearrange("p a b -> p (a b)"),
             in_=acum.rearrange("p a b -> p (a b)"), func=AF.Exp)
        for j in range(2):
            p.op("vector", "tensor_tensor", out=awk[:, j, :], in0=alb,
                 in1=acum[:, j, :], op=ALU.subtract)
        p.op("scalar", "activation", out=awk.rearrange("p a b -> p (a b)"),
             in_=awk.rearrange("p a b -> p (a b)"), func=AF.Exp)
        for j in range(2):
            p.op("vector", "tensor_tensor", out=awk[:, j, :], in0=awk[:, j, :],
                 in1=dts[:, j, :], op=ALU.mult)
        for i in range(2):
            pb = psT0 if i == 0 else psT1
            p.xpose(pb[:, 0:64], b_fm[:, i * 128:(i + 1) * 128], cm.ident[0:64, 0:64])
            p.op("scalar", "copy", out=btm[:, i, :], in_=pb[:, 0:64])
        if own:
            for i in range(2):
                p.mm(psP[2][:, 0:CH], b_fm[:, i * 128:(i + 1) * 128], c_fm,
                     True, True)
                p.op("vector", "tensor_copy", out=s_sb[:, i, :], in_=psP[2][:, 0:CH])
            for hh in range(NH):
                p.op("tensor", "transpose", out=psF0[0:64, 0:128], in_=prev[:, hh, :],
                     identity=cm.identf)
                p.op("vector", "tensor_copy", out=prevT[:, hh, :], in_=psF0[0:64, 0:128])
        for hh in range(NH):
            for j in range(2):
                pb = psT0 if j == 0 else psT1
                p.xpose(pb, xbs[:, hh, j * 128:(j + 1) * 128], cm.ident)
                p.op("scalar", "copy", out=xtm[:, j, :], in_=pb)
            if own:
                for j in range(2):
                    p.op("tensor", "transpose", out=psF0[0:1, 0:128],
                         in_=acum[:, j, hh:hh + 1], identity=cm.identf)
                    p.op("scalar", "copy", out=acr[:, j * 128:(j + 1) * 128],
                         in_=psF0[0:1, 0:128])
                for i in range(2):
                    p.mm(psP[3][:, 0:CH], cm.ones_rowf, acr, True, True)
                    p.op("vector", "tensor_tensor", out=e_sb, in0=psP[3][:, 0:CH],
                         in1=maskc[:, i, :], op=ALU.add)
                    p.op("scalar", "activation", out=ge, in_=e_sb, func=AF.Exp,
                         bias=nac[:, i, hh:hh + 1])
                    p.op("vector", "tensor_tensor", out=g_sb[:, i, :], in0=s_sb[:, i, :],
                         in1=ge, op=ALU.mult)
                for j in range(2):
                    p.op("vector", "tensor_scalar", out=xw1[:, j, :], in0=xtm[:, j, :],
                         scalar1=dts[:, j, hh:hh + 1], scalar2=None, op0=ALU.mult)
                for qs in range(2):
                    ysl = ysb[:, qs, hh * 128:(hh + 1) * 128]
                    for i in range(2):
                        p.mm(psP[4][:, 0:128], g_sb[:, i, qs * 128:(qs + 1) * 128],
                             xw1[:, i, :], i == 0, i == 1)
                    p.mm(psP[2][:, 0:128], c_fm[:, qs * 128:(qs + 1) * 128],
                         prevT[:, hh, :], True, True)
                    p.op("vector", "tensor_copy", out=ysl, in_=psP[4][:, 0:128])
                    p.op("vector", "scalar_tensor_tensor", out=ysl,
                         in0=psP[2][:, 0:128], scalar=ea[:, qs, hh:hh + 1], in1=ysl,
                         op0=ALU.mult, op1=ALU.add)
                    p.op("vector", "scalar_tensor_tensor", out=ysl,
                         in0=xtm[:, qs, :], scalar=dp_b[:, hh:hh + 1], in1=ysl,
                         op0=ALU.mult, op1=ALU.add)
            for j in range(2):
                p.op("vector", "tensor_scalar", out=xw1[:, j, :], in0=xtm[:, j, :],
                     scalar1=awk[:, j, hh:hh + 1], scalar2=None, op0=ALU.mult)
            for i in range(2):
                p.mm(psP[3][:, 0:NST], xw1[:, i, :], btm[:, i, :], i == 0, i == 1)
            p.op("vector", "scalar_tensor_tensor", out=prev[:, hh, :],
                 in0=prev[:, hh, :], scalar=cdb[:, hh:hh + 1], in1=psP[3][:, 0:NST],
                 op0=ALU.mult, op1=ALU.add)
        if DBG and ck == own0:
            p.dbg("xnfm", xn_fm)
            p.dbg("xbs", xbs)
            p.dbg("bfm", b_fm)
            p.dbg("cfm", c_fm)
            p.dbg("dts", dts)
            p.dbg("acum", acum)
            p.dbg("awk", awk)
            p.dbg("cdb", cdb)
            p.dbg("prev", prev)
            if own:
                p.dbg("ssb", s_sb)
                p.dbg("ysb", ysb)
                p.dbg("zbuf", zbuf)
        if not own:
            continue
        zf = zbuf.rearrange("p a b -> p (a b)")
        p.op("scalar", "activation", out=zf, in_=zf, func=AF.Silu)
        p.op("vector", "tensor_tensor", out=ysb.rearrange("p a b -> p (a b)"),
             in0=ysb.rearrange("p a b -> p (a b)"), in1=zf, op=ALU.mult)
        for j in range(2):
            p.op("scalar", "activation", out=xsq, in_=ysb[:, j, :], func=AF.Square)
            p.op("vector", "reduce_sum", out=lnb[0], in_=xsq, axis=AXL.X)
            p.op("vector", "tensor_scalar", out=lnb[1], in0=lnb[0],
                 scalar1=1.0 / DIN, scalar2=None, op0=ALU.mult)
            p.op("scalar", "activation", out=lnb[2], in_=lnb[1], func=AF.Sqrt,
                 bias=cm.c(1e-5))
            p.op("vector", "reciprocal", out=lnb[2], in_=lnb[2])
            p.op("vector", "tensor_scalar", out=ynb, in0=ysb[:, j, :],
                 scalar1=lnb[2], scalar2=None, op0=ALU.mult)
            p.op("vector", "tensor_tensor", out=ynb, in0=ynb,
                 in1=nw_b, op=ALU.mult)
            for c in range(16):
                pb = psT0 if c % 2 == 0 else psT1
                p.xpose(pb, ynb[:, c * 128:(c + 1) * 128], cm.ident)
                p.op("scalar", "copy", out=yfm[:, c, j * 128:(j + 1) * 128], in_=pb)
        for c in range(16):
            p.dma("sync", out=woc[c % 2], in_=wout_t[c, :, :], transpose=True)
            for j in range(2):
                for fh in range(2):
                    p.mm(psP[j * 2 + fh], yfm[:, c, j * 128:(j + 1) * 128],
                         woc[c % 2][:, fh * 512:(fh + 1) * 512], c == 0, c == 15)
        for j in range(2):
            trow = oc * 2 + j
            p.dma("sync", out=xres[j], in_=x_own_dram[trow * 128:(trow + 1) * 128, :])
            for fh in range(2):
                p.op("vector", "tensor_tensor", out=xres[j][:, fh * 512:(fh + 1) * 512],
                     in0=xres[j][:, fh * 512:(fh + 1) * 512], in1=psP[j * 2 + fh],
                     op=ALU.add)
            p.dma("sync", out=x_out_dram[trow * 128:(trow + 1) * 128, :], in_=xres[j])

    p.free("xn_g", "zbuf", "xbc", "bpre", "cpre", "xbs", "b_fm", "c_fm",
           "dts", "acum", "nac", "ea", "acr", "awk", "alb", "cdb", "s_sb",
           "g_sb", "xtm", "btm", "xw1", "ysb", "tmp32", "yfm",
           "prev", "prevT", "win_s", "wz_s0", "woc0", "woc1",
           "maskc", "triu",
           "dtb_b", "a_b", "dp_b", "nw_b", "cw_x", "cb_x", "cw_B", "cb_B",
           "cw_C", "cb_C", "small")

    if DBG:
        dbg_xa = p.dram("dbg_xattn", [2048, 1024], F32, "ExternalOutput")
        p.dma("sync", out=dbg_xa[:, :], in_=x_out_dram[:, :])
    xres2 = [p.alloc("xres2", [128, 1024], F32)] * 2
    build_moe(p, cm, xn_own, x_own_dram, x_out_dram, W["Wg"], w1_t, w2_t, w3_t,
              psP, psT0, psT1, psF0, xres2, lnb, xsq)
    p.free("xn_own", "xraw0", "xraw1", "xnt0", "xnt1", "xres2", "scr16")
    for i in range(6):
        p.free(f"ln{i}")


def build_moe(p, cm, xn_own, x_own_dram, x_out_dram, wg_raw, w1_t, w2_t, w3_t,
              psP, psT0, psT1, psF0, xres, lnb, xsq):
    NJ = 22
    jsz = [128] * 21 + [42]
    wg_s = p.alloc("wg_s", [128, NCD, 8], BF16)
    wg_n = p.alloc("wg_n", [8, 1024], BF16)
    glog = p.alloc("glog", [128, NT_OWN, 8], F32)
    gat = p.alloc("gat", [128, NT_OWN, 8], F32)
    p.dma("gpsimd", out=wg_n, in_=wg_raw)
    for c in range(NCD):
        p.xpose(psT0[:, 0:8], wg_n[:, c * 128:(c + 1) * 128], cm.ident[0:8, 0:8])
        p.op("scalar", "copy", out=wg_s[:, c, :], in_=psT0[:, 0:8])
    for t in range(NT_OWN):
        ph = psP[4]
        for c in range(NCD):
            p.mm(ph[:, 0:8], xn_own[:, c, t * 128:(t + 1) * 128], wg_s[:, c, :],
                 c == 0, c == NCD - 1)
        p.op("vector", "tensor_copy", out=glog[:, t, :], in_=ph[:, 0:8])
    m1 = lnb[0]
    m2 = lnb[1]
    rs = lnb[2]
    for t in range(NT_OWN):
        g = glog[:, t, :]
        sel = gat[:, t, :]
        p.op("vector", "reduce_max", out=m1, in_=g, axis=AXL.X)
        p.op("vector", "tensor_scalar", out=sel, in0=g, scalar1=m1, scalar2=None,
             op0=ALU.is_ge)
        p.op("vector", "scalar_tensor_tensor", out=xsq[:, 0:8], in0=sel,
             scalar=NEG, in1=g, op0=ALU.mult, op1=ALU.add)
        p.op("vector", "reduce_max", out=m2, in_=xsq[:, 0:8], axis=AXL.X)
        p.op("vector", "tensor_scalar", out=sel, in0=g, scalar1=m2, scalar2=None,
             op0=ALU.is_ge)
        p.op("vector", "tensor_scalar", out=m1, in0=m1, scalar1=-1.0, scalar2=None,
             op0=ALU.mult)
        p.op("scalar", "activation", out=xsq[:, 0:8], in_=g, func=AF.Exp, bias=m1)
        p.op("vector", "tensor_tensor", out=sel, in0=sel, in1=xsq[:, 0:8],
             op=ALU.mult)
        p.op("vector", "reduce_sum", out=rs, in_=sel, axis=AXL.X)
        p.op("vector", "reciprocal", out=rs, in_=rs)
        p.op("vector", "tensor_scalar", out=sel, in0=sel, scalar1=rs, scalar2=None,
             op0=ALU.mult)
    if 'DBG' in globals() and DBG:
        pass
    try:
        _dbg = DBG
    except NameError:
        _dbg = False
    if _dbg:
        p.dbg("glog", glog)
        p.dbg("gat", gat)
    hid = p.alloc("hid", [128, NJ, 1024], BF16)
    w3s = p.alloc("w3s", [128, NJ, 1024], BF16)
    w1s = [p.alloc(f"w1s{i}", [128, NCD, 256], BF16) for i in range(2)]
    w2s = [p.alloc(f"w2s{i}", [128, NCD, 256], BF16) for i in range(2)]
    sil = p.alloc("sil", [128, 1024], F32)
    for e in range(8):
        for j in range(NJ):
            p.dma("sync", out=w3s[:, j, :], in_=w3_t[e][j, :, :], transpose=True)
        for half in range(2):
            tok0 = half * 1024
            for jg in range((NJ + 1) // 2):
                wb1, wb2 = w1s[jg % 2], w2s[jg % 2]
                j0 = jg * 2
                nj = min(2, NJ - j0)
                nr = sum(jsz[j0:j0 + nj])
                nrl = (nr + 15) // 16 * 16
                for c in range(NCD):
                    p.dma("sync", out=wb1[:, c, 0:nrl],
                          in_=w1_t[e][c, j0 * 128:j0 * 128 + nrl, :], transpose=True)
                    p.dma("sync", out=wb2[:, c, 0:nrl],
                          in_=w2_t[e][c, j0 * 128:j0 * 128 + nrl, :], transpose=True)
                for ji in range(nj):
                    j = j0 + ji
                    n = jsz[j]
                    for tg in range(2):
                        ph1, ph2 = psP[tg], psP[2 + tg]
                        for c in range(NCD):
                            p.mm(ph1[0:n, :], wb1[:, c, ji * 128:ji * 128 + n],
                                 xn_own[:, c, tok0 + tg * 512:tok0 + (tg + 1) * 512],
                                 c == 0, c == NCD - 1)
                        for c in range(NCD):
                            p.mm(ph2[0:n, :], wb2[:, c, ji * 128:ji * 128 + n],
                                 xn_own[:, c, tok0 + tg * 512:tok0 + (tg + 1) * 512],
                                 c == 0, c == NCD - 1)
                        p.op("scalar", "activation", out=sil[0:n, tg * 512:(tg + 1) * 512],
                             in_=ph1[0:n, :], func=AF.Silu)
                        p.op("vector", "tensor_tensor",
                             out=hid[0:n, j, tg * 512:(tg + 1) * 512],
                             in0=sil[0:n, tg * 512:(tg + 1) * 512], in1=ph2[0:n, :],
                             op=ALU.mult)
            for grp in range(4):
                banks = (psP[0], psP[1], psP[2], psP[3])
                for j in range(NJ):
                    n = jsz[j]
                    for tt in range(2):
                        t = half * 8 + grp * 2 + tt
                        lt = (grp * 2 + tt) * 128
                        for fh in range(2):
                            p.mm(banks[tt * 2 + fh],
                                 hid[0:n, j, lt:lt + 128],
                                 w3s[0:n, j, fh * 512:(fh + 1) * 512],
                                 j == 0, j == NJ - 1)
                for tt in range(2):
                    t = half * 8 + grp * 2 + tt
                    xr = xres[tt]
                    p.dma("sync", out=xr, in_=x_out_dram[t * 128:(t + 1) * 128, :])
                    for fh in range(2):
                        p.op("vector", "scalar_tensor_tensor",
                             out=xr[:, fh * 512:(fh + 1) * 512],
                             in0=banks[tt * 2 + fh], scalar=gat[:, t, e:e + 1],
                             in1=xr[:, fh * 512:(fh + 1) * 512],
                             op0=ALU.mult, op1=ALU.add)
                    p.dma("sync", out=x_out_dram[t * 128:(t + 1) * 128, :], in_=xr)
    p.free("wg_s", "wg_n", "glog", "gat", "hid", "w3s", "w1s0", "w1s1", "w2s0", "w2s1", "sil")



def emit_fp4(p, cm, v_f32, out_bf16, Wd, qsc, eng="vector"):
    G = Wd // 16
    bmax, rec, an, acc1, accB = qsc
    bm, rc = bmax[:, 0:G], rec[:, 0:G]
    a1, aB, anv = acc1[:, 0:Wd], accB[:, 0:Wd], an[:, 0:Wd]
    vg = v_f32.rearrange("p (g i) -> p g i", i=16)
    p.op("vector", "tensor_reduce", out=bm, in_=vg, axis=AXL.X, op=ALU.max,
         apply_absolute_value=True)
    p.op("vector", "tensor_scalar", out=bm, in0=bm, scalar1=1.0 / 6.0,
         scalar2=1e-12, op0=ALU.mult, op1=ALU.max)
    p.op("vector", "reciprocal", out=rc, in_=bm)
    bcr = rc.unsqueeze(2).broadcast_to([128, G, 16])
    p.op(eng, "tensor_tensor", out=anv.rearrange("p (g i) -> p g i", i=16),
         in0=vg, in1=bcr, op=ALU.mult)
    p.op("scalar", "activation", out=a1, in_=anv, func=AF.Abs)
    p.op(eng, "tensor_scalar", out=aB, in0=a1, scalar1=2.5, scalar2=None,
         op0=ALU.is_gt)
    p.op("vector", "scalar_tensor_tensor", out=aB, in0=a1, scalar=3.5, in1=aB,
         op0=ALU.is_gt, op1=ALU.add)
    p.op(eng, "tensor_scalar", out=anv, in0=a1, scalar1=5.0, scalar2=2.0,
         op0=ALU.is_gt, op1=ALU.mult)
    p.op(eng, "tensor_tensor", out=aB, in0=aB, in1=anv, op=ALU.add)
    p.op(eng, "tensor_scalar", out=anv, in0=a1, scalar1=0.25, scalar2=None,
         op0=ALU.is_gt)
    p.op("vector", "scalar_tensor_tensor", out=anv, in0=a1, scalar=0.75, in1=anv,
         op0=ALU.is_gt, op1=ALU.add)
    p.op("vector", "scalar_tensor_tensor", out=anv, in0=a1, scalar=1.25, in1=anv,
         op0=ALU.is_gt, op1=ALU.add)
    p.op("vector", "scalar_tensor_tensor", out=a1, in0=a1, scalar=1.75, in1=anv,
         op0=ALU.is_gt, op1=ALU.add)
    p.op("vector", "scalar_tensor_tensor", out=a1, in0=a1, scalar=0.5, in1=aB,
         op0=ALU.mult, op1=ALU.add)
    p.op("scalar", "activation", out=aB, in_=v_f32, func=AF.Sign)
    p.op(eng, "tensor_tensor", out=a1, in0=a1, in1=aB, op=ALU.mult)
    bcm = bm.unsqueeze(2).broadcast_to([128, G, 16])
    p.op(eng, "tensor_tensor", out=out_bf16.rearrange("p (g i) -> p g i", i=16),
         in0=a1.rearrange("p (g i) -> p g i", i=16), in1=bcm, op=ALU.mult)


def build_block2(p: Prog, cm: Common, h, x_own_dram, x_other_dram, W, x_out_dram):
    own0 = h * T_OWN
    NLAT = 32 * (h + 1)
    KVT = 17

    psP = [p.psum(f"ps{i}", [128, 512], F32) for i in range(5)]
    psT0 = p.psum("psT0", [128, 128], BF16)
    psT1 = p.psum("psT1", [128, 128], BF16)
    psF0 = p.psum("psF0", [128, 128], F32)

    wgq_d = p.dram("b2_wgq", [8, 2816, 128], BF16, "Internal")
    wuq_d = p.dram("b2_wuq", [8, 2816, 128], BF16, "Internal")
    wdq_d = p.dram("b2_wdq", [22, 1024, 128], BF16, "Internal")
    xq_d = p.dram("b2_xq", [128, NCD, T_OWN], BF16, "Internal")

    xn_fm = p.alloc("xn_fm", [128, NCD, T_OWN], BF16)
    k_fm2 = p.alloc("k_fm2", [128, 8, KVT * 128], BF16)
    v_tm = p.alloc("v_tm", [128, KVT, 1024], BF16)
    xg_fm = p.alloc("xg_fm", [128, NCD, 128], BF16)
    lnb = [p.alloc(f"ln{i}", [128, 1], F32) for i in range(6)]
    scrA = p.alloc("scrA", [128, 4096], F32)
    xsq = scrA[:, 0:1024]
    xraw = [p.alloc(f"xraw{i}", [128, 1024], F32) for i in range(2)]
    xnf = p.alloc("xnf", [128, 1024], F32)
    xnb = p.alloc("xnb", [128, 1024], BF16)
    xqt = p.alloc("xqt", [128, 1024], BF16)
    xqf = p.alloc("xqf", [128, NCD, 128], BF16)
    poolA = p.alloc("poolA", [128, 16, 32], BF16)
    pooled = [p.alloc(f"pooled{i}", [32, 1024], BF16) for i in range(2)]
    pool_f = p.alloc("pool_f", [128, NCD, 64], BF16)
    qsc = (p.alloc("qsc0", [128, 256], F32), p.alloc("qsc1", [128, 256], F32),
           p.alloc("qsc2", [128, 1024], F32), p.alloc("qsc3", [128, 1024], F32),
           p.alloc("qsc4", [128, 1024], F32))

    p.op("vector", "memset", ap=poolA, constant=1.0 / 64.0)
    for j in range(16):
        for half in range(2):
            p.op("gpsimd", "affine_select", out=poolA[half * 64:(half + 1) * 64, j, :],
                 in_=poolA[half * 64:(half + 1) * 64, j, :],
                 pattern=[[1, 32]], base=-(2 * j + half), channel_multiplier=0,
                 compare_op=ALU.is_equal, fill=0.0)
    lmask = p.alloc("lmask", [128, 2, 256], BF16)
    p.op("vector", "memset", ap=lmask, constant=1.0)
    for v in range(2):
        p.op("gpsimd", "affine_select", out=lmask[:, v, :], in_=lmask[:, v, :],
             pattern=[[1, 256]], base=0, channel_multiplier=-1,
             compare_op=ALU.is_ge, fill=0.0)
        p.op("gpsimd", "affine_select", out=lmask[:, v, :], in_=lmask[:, v, :],
             pattern=[[-1, 256]], base=127, channel_multiplier=1,
             compare_op=ALU.is_ge, fill=0.0)
    p.op("gpsimd", "affine_select", out=lmask[:, 1, :], in_=lmask[:, 1, :],
         pattern=[[1, 256]], base=-128, channel_multiplier=0,
         compare_op=ALU.is_ge, fill=0.0)

    brow = p.alloc("brow", [64, 1024], F32)
    kqb = p.alloc("kqb", [128, 8], F32)
    qqb = p.alloc("qqb", [128, 8], F32)
    rqb = p.alloc("rqb", [128, 2], F32)
    vb_b = p.alloc("vb_b", [128, 1024], F32)
    ob_b = p.alloc("ob_b", [128, 1024], F32)
    rob_b = p.alloc("rob_b", [128, 1024], F32)
    rkvb = p.alloc("rkvb", [1, 512], BF16)
    glw_b = p.alloc("glw_b", [128, 1024], F32)
    grw_b = p.alloc("grw_b", [128, 1024], F32)
    p.dma("sync", out=brow[0:8, 0:128], in_=W["qkv_b"][0:1024].rearrange("(a b) -> a b", b=128))
    p.xpose(psF0[:, 0:8], brow[0:8, 0:128], cm.identf[0:8, 0:8])
    p.op("vector", "tensor_copy", out=qqb, in_=psF0[:, 0:8])
    p.dma("sync", out=brow[0:8, 0:128], in_=W["qkv_b"][1024:2048].rearrange("(a b) -> a b", b=128))
    p.xpose(psF0[:, 0:8], brow[0:8, 0:128], cm.identf[0:8, 0:8])
    p.op("vector", "tensor_copy", out=kqb, in_=psF0[:, 0:8])
    p.dma("sync", out=brow[0:1, :], in_=W["qkv_b"][2048:3072].unsqueeze(0))
    bcast_row(p, cm, psP[0], vb_b, brow[0:1, :], 1024)
    p.dma("sync", out=brow[0:1, :], in_=W["out_b"].unsqueeze(0))
    bcast_row(p, cm, psP[0], ob_b, brow[0:1, :], 1024)
    p.dma("sync", out=brow[0:1, :], in_=W["rout_b"].unsqueeze(0))
    bcast_row(p, cm, psP[0], rob_b, brow[0:1, :], 1024)
    p.dma("sync", out=brow[0:2, 0:128], in_=W["rq_b"].rearrange("(a b) -> a b", b=128))
    p.xpose(psF0[:, 0:2], brow[0:2, 0:128], cm.identf[0:2, 0:2])
    p.op("vector", "tensor_copy", out=rqb, in_=psF0[:, 0:2])
    p.dma("gpsimd", out=rkvb, in_=W["rkv_b"].unsqueeze(0))
    for wsrc, dstb in ((W["gl_w"], glw_b), (W["gr_w"], grw_b)):
        p.dma("sync", out=brow[0:64, 0:1], in_=wsrc)
        p.xpose(psF0[0:1, 0:64], brow[0:64, 0:1], cm.identf[0:64, 0:64])
        p.op("vector", "tensor_copy", out=brow[0:1, 0:64], in_=psF0[0:1, 0:64])
        p.mm(psP[0][:, 0:64], cm.ones_rowf, brow[0:1, 0:64], True, True)
        p.op("vector", "tensor_copy", out=scrA[:, 0:64], in_=psP[0][:, 0:64])
        p.op("vector", "tensor_copy",
             out=dstb.rearrange("p (r d) -> p r d", d=64),
             in_=scrA[:, 0:64].unsqueeze(1).broadcast_to([128, 16, 64]))

    n_ln = 16 * (h + 1)
    for j in range(n_ln):
        own = j >= n_ln - 16
        oj = j - (n_ln - 16)
        buf = xraw[j % 2]
        if own:
            p.dma("sync", out=buf, in_=x_own_dram[oj * 128:(oj + 1) * 128, :])
        else:
            p.dma("sync", out=buf, in_=x_other_dram[j * 128:(j + 1) * 128, :])
        emit_ln(p, cm, buf, xnf, *lnb, xsq)
        p.op("vector", "tensor_copy", out=xnb, in_=xnf)
        bb = 0 if own else 2
        jj = oj if own else j
        for fh in range(2):
            p.mm(psP[bb + fh][0:32, :], poolA[:, jj, :],
                 xnb[:, fh * 512:(fh + 1) * 512], jj == 0, jj == 15)
        if own:
            for c in range(NCD):
                pb = psT0 if c % 2 == 0 else psT1
                p.xpose(pb, xnb[:, c * 128:(c + 1) * 128], cm.ident)
                p.op("scalar", "copy", out=xn_fm[:, c, oj * 128:(oj + 1) * 128], in_=pb)
            emit_fp4(p, cm, xnf, xqt, 1024, qsc, eng="gpsimd" if oj % 2 else "vector")
            for c in range(NCD):
                pb = psT0 if c % 2 == 0 else psT1
                p.xpose(pb, xqt[:, c * 128:(c + 1) * 128], cm.ident)
                p.op("scalar", "copy", out=xqf[:, c, :], in_=pb)
            p.dma("sync", out=xq_d[:, :, oj * 128:(oj + 1) * 128], in_=xqf)
        elif j == 15:
            for c in range(NCD):
                pb = psT0 if c % 2 == 0 else psT1
                p.xpose(pb, xnb[:, c * 128:(c + 1) * 128], cm.ident)
                p.op("scalar", "copy", out=xg_fm[:, c, :], in_=pb)
    npre = NLAT - 32
    for fh in range(2):
        p.op("vector", "tensor_copy", out=pooled[0][:, fh * 512:(fh + 1) * 512],
             in_=psP[0 + fh][0:32, :])
        if h == 1:
            p.op("vector", "tensor_copy", out=pooled[1][:, fh * 512:(fh + 1) * 512],
                 in_=psP[2 + fh][0:32, :])
    for c in range(NCD):
        if h == 1:
            p.xpose(psT0[:, 0:32], pooled[1][:, c * 128:(c + 1) * 128],
                    cm.ident[0:32, 0:32])
            p.op("scalar", "copy", out=pool_f[:, c, 0:32], in_=psT0[:, 0:32])
        p.xpose(psT1[:, 0:32], pooled[0][:, c * 128:(c + 1) * 128],
                cm.ident[0:32, 0:32])
        p.op("scalar", "copy", out=pool_f[:, c, npre:npre + 32], in_=psT1[:, 0:32])
    wrkv = p.alloc("wrkv", [128, NCD, 512], BF16)
    rkv_tm = p.alloc("rkv_tm", [64, 512], BF16)
    rk_fm2 = p.alloc("rk_fm2", [128, 2, 64], BF16)
    p.dma("gpsimd", out=wrkv, in_=W["rkv_w"].rearrange("(c p) f -> p c f", p=128))
    for c in range(NCD):
        p.mm(psP[4][0:NLAT, :], pool_f[:, c, 0:NLAT], wrkv[:, c, :], c == 0, False)
    p.mm(psP[4][0:NLAT, :], cm.ones_row[:, 0:NLAT], rkvb, False, True)
    p.op("vector", "tensor_copy", out=rkv_tm[0:NLAT, :], in_=psP[4][0:NLAT, :])
    for hp in range(2):
        p.xpose(psT0[:, 0:NLAT], rkv_tm[0:NLAT, hp * 128:(hp + 1) * 128],
                cm.ident[0:NLAT, 0:NLAT])
        p.op("scalar", "copy", out=rk_fm2[:, hp, 0:NLAT], in_=psT0[:, 0:NLAT])
    p.free("wrkv", "poolA", "pooled0", "pooled1", "pool_f", "brow",
           "xraw0", "xraw1", "xnf", "xnb", "xqt", "xqf")

    wkv = p.alloc("wkv", [128, NCD, 1024], BF16)
    p.dma("gpsimd", out=wkv, in_=W["qkv_w"][:, 1024:2048].rearrange("(c p) f -> p c f", p=128))
    if h == 0:
        p.op("vector", "memset", ap=k_fm2[:, :, 0:128], constant=0.0)
        p.op("vector", "memset", ap=v_tm[:, 0, :], constant=0.0)
    for hp in range(8):
        if h == 1:
            p.mm(psP[0][:, 0:128], wkv[:, 0, hp * 128:(hp + 1) * 128], xg_fm[:, 0, :],
                 True, False)
            for c in range(1, NCD):
                p.mm(psP[0][:, 0:128], wkv[:, c, hp * 128:(hp + 1) * 128],
                     xg_fm[:, c, :], False, c == NCD - 1)
            p.op("vector", "tensor_scalar", out=k_fm2[:, hp, 0:128],
                 in0=psP[0][:, 0:128], scalar1=kqb[:, hp:hp + 1], scalar2=None,
                 op0=ALU.add)
        for g in range(4):
            ph = psP[g % 2]
            for c in range(NCD):
                p.mm(ph, wkv[:, c, hp * 128:(hp + 1) * 128],
                     xn_fm[:, c, g * 512:(g + 1) * 512], c == 0, c == NCD - 1)
            p.op("vector", "tensor_scalar", out=k_fm2[:, hp, 128 + g * 512:128 + (g + 1) * 512],
                 in0=ph, scalar1=kqb[:, hp:hp + 1], scalar2=None, op0=ALU.add)
    p.dma("gpsimd", out=wkv, in_=W["qkv_w"][:, 2048:3072].rearrange("(c p) f -> p c f", p=128))
    if h == 1:
        for fh in range(2):
            ph = psP[fh]
            for c in range(NCD):
                p.mm(ph, xg_fm[:, c, :], wkv[:, c, fh * 512:(fh + 1) * 512],
                     c == 0, c == NCD - 1)
            p.op("vector", "tensor_tensor", out=v_tm[:, 0, fh * 512:(fh + 1) * 512],
                 in0=ph, in1=vb_b[:, fh * 512:(fh + 1) * 512], op=ALU.add)
    for oj in range(16):
        for fh in range(2):
            ph = psP[fh]
            for c in range(NCD):
                p.mm(ph, xn_fm[:, c, oj * 128:(oj + 1) * 128],
                     wkv[:, c, fh * 512:(fh + 1) * 512], c == 0, c == NCD - 1)
            p.op("vector", "tensor_tensor", out=v_tm[:, 1 + oj, fh * 512:(fh + 1) * 512],
                 in0=ph, in1=vb_b[:, fh * 512:(fh + 1) * 512], op=ALU.add)
    p.free("wkv", "xg_fm")

    wqp = p.alloc("wqp", [128, NCD, 1024], BF16)
    wrq = p.alloc("wrq", [128, NCD, 256], BF16)
    woc = [p.alloc(f"woc{i}", [128, 1024], BF16) for i in range(2)]
    wro = p.alloc("wro", [128, 2, 1024], BF16)
    p.dma("gpsimd", out=wqp, in_=W["qkv_w"][:, 0:1024].rearrange("(c p) f -> p c f", p=128))
    p.dma("gpsimd", out=wrq, in_=W["rq_w"].rearrange("(c p) f -> p c f", p=128))
    p.dma("gpsimd", out=wro, in_=W["rout_w"].rearrange("(c p) f -> p c f", p=128))
    qf = p.alloc("qf", [128, 128], BF16)
    s_sb = p.alloc("s_sb", [128, 256], F32)
    p_sb = p.alloc("p_sb", [128, 256], BF16)
    pt_sb = p.alloc("pt_sb", [128, 256], BF16)
    loc = p.alloc("loc", [128, 1024], BF16)
    rem = p.alloc("rem", [128, 1024], F32)
    lat = p.alloc("lat", [128, 256], BF16)
    latf = p.alloc("latf", [128, 2, 128], BF16)
    gsc = p.alloc("gsc", [128, 16], F32)
    mixb = scrA[:, 2048:2560].bitcast(BF16)
    mixf = scrA[:, 2560:3072].bitcast(BF16).rearrange("p (c q) -> p c q", q=128)
    lm01 = p.alloc("lm01", [128, 64], BF16)
    xres = scrA[:, 3072:4096]
    mr = lnb

    for j in range(NT_OWN):
        for hp in range(8):
            ph = psP[0]
            for c in range(NCD):
                p.mm(ph[:, 0:128], wqp[:, c, hp * 128:(hp + 1) * 128],
                     xn_fm[:, c, j * 128:(j + 1) * 128], c == 0, c == NCD - 1)
            p.op("vector", "tensor_scalar", out=qf, in0=ph[:, 0:128],
                 scalar1=qqb[:, hp:hp + 1], scalar2=None, op0=ALU.add)
            for lh in range(2):
                hh = hp * 2 + lh
                base = lh * 64
                p.mm(psP[1][:, 0:256], qf[base:base + 64, :],
                     k_fm2[base:base + 64, hp, j * 128:j * 128 + 256], True, True)
                p.op("vector", "tensor_scalar", out=s_sb, in0=psP[1][:, 0:256],
                     scalar1=0.0, scalar2=NEGSEED, op0=ALU.add, op1=ALU.max,
                     accum_out=mr[0])
                p.op("vector", "tensor_scalar", out=mr[1], in0=mr[0],
                     scalar1=-0.125, scalar2=None, op0=ALU.mult)
                p.op("scalar", "activation", out=p_sb, in_=s_sb, func=AF.Exp,
                     scale=cm.c(0.125), bias=mr[1])
                mv = 1 if (h == 0 and j == 0) else 0
                p.op("vector", "scalar_tensor_tensor", out=p_sb, in0=p_sb,
                     scalar=1.0, in1=lmask[:, mv, :], op0=ALU.mult, op1=ALU.mult,
                     accum_out=mr[2])
                p.op("vector", "tensor_scalar", out=mr[2], in0=mr[2],
                     scalar1=1e-30, scalar2=None, op0=ALU.max)
                p.op("vector", "reciprocal", out=mr[2], in_=mr[2])
                for kt in range(2):
                    pb = psT0 if kt == 0 else psT1
                    p.xpose(pb, p_sb[:, kt * 128:(kt + 1) * 128], cm.ident)
                    p.op("scalar", "copy", out=pt_sb[:, kt * 128:(kt + 1) * 128], in_=pb)
                for kt in range(2):
                    p.mm(psP[2][:, 0:64], pt_sb[:, kt * 128:(kt + 1) * 128],
                         v_tm[:, j + kt, hh * 64:(hh + 1) * 64], kt == 0, kt == 1)
                p.op("vector", "tensor_scalar", out=loc[:, hh * 64:(hh + 1) * 64],
                     in0=psP[2][:, 0:64], scalar1=mr[2], scalar2=None, op0=ALU.mult)
        C0 = own0 // 64 + 2 * j
        p.op("vector", "memset", ap=lm01, constant=1.0)
        for half in range(2):
            p.op("gpsimd", "affine_select", out=lm01[half * 64:(half + 1) * 64, :],
                 in_=lm01[half * 64:(half + 1) * 64, :], pattern=[[-1, 64]],
                 base=C0 + half - 1, channel_multiplier=0, compare_op=ALU.is_ge,
                 fill=0.0)
        for hp in range(2):
            ph = psP[0]
            for c in range(NCD):
                p.mm(ph[:, 0:128], wrq[:, c, hp * 128:(hp + 1) * 128],
                     xn_fm[:, c, j * 128:(j + 1) * 128], c == 0, c == NCD - 1)
            p.op("vector", "tensor_scalar", out=qf, in0=ph[:, 0:128],
                 scalar1=rqb[:, hp:hp + 1], scalar2=None, op0=ALU.add)
            for lh in range(2):
                hh = hp * 2 + lh
                base = lh * 64
                p.mm(psP[1][:, 0:NLAT], qf[base:base + 64, :],
                     rk_fm2[base:base + 64, hp, 0:NLAT], True, True)
                p.op("vector", "tensor_scalar", out=s_sb[:, 0:NLAT],
                     in0=psP[1][:, 0:NLAT], scalar1=0.0, scalar2=NEGSEED,
                     op0=ALU.add, op1=ALU.max, accum_out=mr[0])
                p.op("vector", "tensor_scalar", out=mr[1], in0=mr[0],
                     scalar1=-0.125, scalar2=None, op0=ALU.mult)
                p.op("scalar", "activation", out=p_sb[:, 0:NLAT], in_=s_sb[:, 0:NLAT],
                     func=AF.Exp, scale=cm.c(0.125), bias=mr[1])
                p.op("vector", "scalar_tensor_tensor", out=p_sb[:, 0:NLAT],
                     in0=p_sb[:, 0:NLAT], scalar=1.0, in1=lm01[:, 0:NLAT],
                     op0=ALU.mult, op1=ALU.mult, accum_out=mr[2])
                p.op("vector", "tensor_scalar", out=mr[2], in0=mr[2],
                     scalar1=1e-30, scalar2=None, op0=ALU.max)
                p.op("vector", "reciprocal", out=mr[2], in_=mr[2])
                p.xpose(psT0[0:NLAT, 0:128], p_sb[:, 0:NLAT], cm.ident)
                p.op("scalar", "copy", out=pt_sb[0:NLAT, 0:128], in_=psT0[0:NLAT, 0:128])
                p.mm(psP[2][:, 0:64], pt_sb[0:NLAT, 0:128],
                     rkv_tm[0:NLAT, 256 + hh * 64:256 + (hh + 1) * 64], True, True)
                p.op("vector", "tensor_scalar", out=lat[:, hh * 64:(hh + 1) * 64],
                     in0=psP[2][:, 0:64], scalar1=mr[2], scalar2=None, op0=ALU.mult)
        for lc in range(2):
            pb = psT0 if lc == 0 else psT1
            p.xpose(pb, lat[:, lc * 128:(lc + 1) * 128], cm.ident)
            p.op("scalar", "copy", out=latf[:, lc, :], in_=pb)
        for fh in range(2):
            ph = psP[3 + fh]
            for lc in range(2):
                p.mm(ph, latf[:, lc, :], wro[:, lc, fh * 512:(fh + 1) * 512],
                     lc == 0, lc == 1)
            p.op("vector", "tensor_tensor", out=rem[:, fh * 512:(fh + 1) * 512],
                 in0=ph, in1=rob_b[:, fh * 512:(fh + 1) * 512], op=ALU.add)
        p.op("vector", "tensor_tensor", out=scrA[:, 0:1024], in0=loc, in1=glw_b,
             op=ALU.mult)
        p.op("vector", "tensor_reduce", out=gsc,
             in_=scrA[:, 0:1024].rearrange("p (r d) -> p r d", d=64),
             axis=AXL.X, op=ALU.add)
        p.op("vector", "tensor_tensor", out=scrA[:, 0:1024], in0=rem, in1=grw_b,
             op=ALU.mult)
        p.op("vector", "tensor_reduce", out=scrA[:, 1024:1040],
             in_=scrA[:, 0:1024].rearrange("p (r d) -> p r d", d=64),
             axis=AXL.X, op=ALU.add)
        p.op("vector", "tensor_tensor", out=gsc, in0=gsc, in1=scrA[:, 1024:1040],
             op=ALU.add)
        p.op("scalar", "activation", out=gsc, in_=gsc, func=AF.Sigmoid)
        p.op("vector", "tensor_tensor", out=scrA[:, 0:1024], in0=loc, in1=rem,
             op=ALU.subtract)
        for hh in range(16):
            p.op("vector", "scalar_tensor_tensor", out=mixb[:, hh * 64:(hh + 1) * 64],
                 in0=scrA[:, hh * 64:(hh + 1) * 64], scalar=gsc[:, hh:hh + 1],
                 in1=rem[:, hh * 64:(hh + 1) * 64], op0=ALU.mult, op1=ALU.add)
        for c in range(NCD):
            pb = psT0 if c % 2 == 0 else psT1
            p.xpose(pb, mixb[:, c * 128:(c + 1) * 128], cm.ident)
            p.op("scalar", "copy", out=mixf[:, c, :], in_=pb)
        p.dma("sync", out=xres, in_=x_own_dram[j * 128:(j + 1) * 128, :])
        p.op("vector", "tensor_tensor", out=xres, in0=xres, in1=ob_b, op=ALU.add)
        for c in range(NCD):
            p.dma("gpsimd", out=woc[c % 2],
                  in_=W["out_w"][c * 128:(c + 1) * 128, :])
            for fh in range(2):
                p.mm(psP[3 + fh], mixf[:, c, :], woc[c % 2][:, fh * 512:(fh + 1) * 512],
                     c == 0, c == NCD - 1)
        for fh in range(2):
            p.op("vector", "tensor_tensor", out=xres[:, fh * 512:(fh + 1) * 512],
                 in0=xres[:, fh * 512:(fh + 1) * 512], in1=psP[3 + fh], op=ALU.add)
        p.dma("sync", out=x_out_dram[j * 128:(j + 1) * 128, :], in_=xres)
    p.free("wqp", "wrq", "woc0", "woc1", "wro", "qf", "s_sb", "p_sb", "pt_sb", "loc",
           "rem", "lat", "latf", "gsc", "lm01",
           "k_fm2", "v_tm", "rkv_tm", "rk_fm2", "lmask",
           "kqb", "qqb", "rqb", "vb_b", "rob_b", "rkvb", "glw_b", "grw_b")

    for i in range(5):
        p.free(f"qsc{i}")
    qsc = (p.alloc("qsc0", [128, 256], F32), p.alloc("qsc1", [128, 256], F32),
           scrA[:, 0:2816], p.alloc("qsc3", [128, 2816], F32),
           p.alloc("qsc4", [128, 2816], F32))
    wtile = [p.alloc(f"wtile{i}", [128, 2816], F32) for i in range(2)]
    wqt = p.alloc("wqt", [128, 2816], BF16)
    for src_w, nt, wd, scratch in ((W["w_gate"], 22, 1024, wgq_d),
                                   (W["w_up"], 22, 1024, wuq_d),
                                   (W["w_down"], 8, 2816, wdq_d)):
        for i in range(nt):
            b = i % 2
            p.dma("gpsimd", out=wtile[b][:, 0:wd], in_=src_w[i * 128:(i + 1) * 128, :])
            emit_fp4(p, cm, wtile[b][:, 0:wd], wqt[:, 0:wd], wd, qsc,
                     eng="gpsimd" if i % 2 else "vector")
            p.dma("gpsimd",
                  out=scratch[:, i * 128:(i + 1) * 128, :].rearrange("c p q -> p c q"),
                  in_=wqt[:, 0:wd].rearrange("p (c q) -> p c q", q=128))
    p.free("wtile0", "wtile1", "wqt", "xn_fm")

    xq_fm = p.alloc("xq_fm", [128, NCD, T_OWN], BF16)
    p.dma("sync", out=xq_fm, in_=xq_d)
    wdq_s = p.alloc("wdq_s", [128, 22, 1024], BF16)
    for jc in range(22):
        p.dma("sync", out=wdq_s[:, jc, :], in_=wdq_d[jc, :, :], transpose=True)
    wgs = [p.alloc(f"wgs{i}", [128, NCD, 512], BF16) for i in range(2)]
    wus = [p.alloc(f"wus{i}", [128, NCD, 512], BF16) for i in range(2)]
    h2 = p.alloc("h2", [128, 2, 2816], F32)
    xrE = scrA[:, 2816:3840]
    hq_tm = p.alloc("hq_tm", [128, 2816], BF16)
    hq_fm = p.alloc("hq_fm", [128, 22, 128], BF16)
    sil = p.alloc("sil", [128, 512], F32)
    fcs = [512, 512, 512, 512, 512, 256]
    for jg in range(8):
        for fc in range(6):
            f0 = fc * 512
            n = fcs[fc]
            wg, wu = wgs[fc % 2], wus[fc % 2]
            for c in range(NCD):
                p.dma("sync", out=wg[:, c, 0:n], in_=wgq_d[c, f0:f0 + n, :],
                      transpose=True)
                p.dma("sync", out=wu[:, c, 0:n], in_=wuq_d[c, f0:f0 + n, :],
                      transpose=True)
            for jj in range(2):
                j = jg * 2 + jj
                ph1, ph2 = psP[jj * 2], psP[jj * 2 + 1]
                for c in range(NCD):
                    p.mm(ph1[:, 0:n], xq_fm[:, c, j * 128:(j + 1) * 128],
                         wg[:, c, 0:n], c == 0, c == NCD - 1)
                for c in range(NCD):
                    p.mm(ph2[:, 0:n], xq_fm[:, c, j * 128:(j + 1) * 128],
                         wu[:, c, 0:n], c == 0, c == NCD - 1)
                p.op("scalar", "activation", out=sil[:, 0:n], in_=ph1[:, 0:n],
                     func=AF.Silu)
                p.op("vector", "tensor_tensor", out=h2[:, jj, f0:f0 + n],
                     in0=sil[:, 0:n], in1=ph2[:, 0:n], op=ALU.mult)
        for jj in range(2):
            j = jg * 2 + jj
            emit_fp4(p, cm, h2[:, jj, :], hq_tm, 2816, qsc,
                     eng="gpsimd" if jj else "vector")
            for jc in range(22):
                pb = psT0 if jc % 2 == 0 else psT1
                p.xpose(pb, hq_tm[:, jc * 128:(jc + 1) * 128], cm.ident)
                p.op("scalar", "copy", out=hq_fm[:, jc, :], in_=pb)
            for fh in range(2):
                ph = psP[4] if fh == 0 else psP[jj * 2]
                for jc in range(22):
                    p.mm(ph, hq_fm[:, jc, :], wdq_s[:, jc, fh * 512:(fh + 1) * 512],
                         jc == 0, jc == 21)
            xr = xrE
            p.dma("sync", out=xr, in_=x_out_dram[j * 128:(j + 1) * 128, :])
            for fh in range(2):
                ph = psP[4] if fh == 0 else psP[jj * 2]
                p.op("vector", "tensor_tensor", out=xr[:, fh * 512:(fh + 1) * 512],
                     in0=xr[:, fh * 512:(fh + 1) * 512], in1=ph, op=ALU.add)
            p.dma("sync", out=x_out_dram[j * 128:(j + 1) * 128, :], in_=xr)



def build_block3(p: Prog, cm: Common, h, x_own_dram, x_other_dram, W, x_out_dram):
    kv_len = 2048 * (h + 1)
    own0 = h * 2048
    NKV = kv_len // 128

    psP = [p.psum(f"ps{i}", [128, 512], F32) for i in range(5)]
    psT0 = p.psum("psT0", [128, 128], BF16)
    psT1 = p.psum("psT1", [128, 128], BF16)
    psF0 = p.psum("psF0", [128, 128], F32)

    xnpre_d = p.dram("b3_xnpre", [128, NCD, 2048], BF16, "Internal")
    p_d = p.dram("b3_p", [128, 16, 256], BF16, "Internal")

    xn_own = p.alloc("xn_own", [128, NCD, T_OWN], BF16)
    cosb = p.alloc("cosb", [128, 512], BF16)
    sinb = p.alloc("sinb", [128, 512], BF16)
    lnb = [p.alloc(f"ln{i}", [128, 1], F32) for i in range(6)]
    scrA = p.alloc("scrA", [128, 4096], F32)
    xsq = scrA[:, 0:1024]
    xraw = [p.alloc(f"xraw{i}", [128, 1024], F32) for i in range(2)]
    xnf = p.alloc("xnf", [128, 1024], F32)
    xnb = p.alloc("xnb", [128, 1024], BF16)
    xn_g = p.alloc("xn_g", [128, NCD, 512], BF16)
    p_tm = p.alloc("p_tm", [128, 256], BF16)
    skb = p.alloc("skb", [128, 256], F32)
    svb = p.alloc("svb", [128, 256], F32)
    wsk1 = p.alloc("wsk1", [128, NCD, 16], BF16)
    wsk2 = p.alloc("wsk2", [16, 256], BF16)
    t1f = p.alloc("t1f", [16, 512], BF16)
    wkt = p.alloc("wkt", [128, 256], F32)
    carry = p.alloc("carry", [1, 256], F32)
    mask01d = p.alloc("mask01d", [128, 128], BF16)
    brow = p.alloc("brow", [32, 1024], F32)

    p.op("vector", "memset", ap=mask01d, constant=1.0)
    p.op("gpsimd", "affine_select", out=mask01d, in_=mask01d, pattern=[[1, 128]],
         base=0, channel_multiplier=-1, compare_op=ALU.is_ge, fill=0.0)
    p.dma("sync", out=brow[0:1, 0:256], in_=W["sk_b2"].unsqueeze(0))
    bcast_row(p, cm, psP[0], skb, brow[0:1, 0:256], 256)
    p.dma("sync", out=brow[0:1, 0:256], in_=W["sv_b2"].unsqueeze(0))
    bcast_row(p, cm, psP[0], svb, brow[0:1, 0:256], 256)


    def ln_tile(src_dram, row, out_fm_slice, stash=None):
        buf = xraw[row % 2]
        p.dma("sync", out=buf, in_=src_dram[row * 128:(row + 1) * 128, :])
        emit_ln(p, cm, buf, xnf, *lnb, xsq)
        p.op("vector", "tensor_copy", out=xnb, in_=xnf)
        for c in range(NCD):
            pb = psT0 if c % 2 == 0 else psT1
            p.xpose(pb, xnb[:, c * 128:(c + 1) * 128], cm.ident)
            p.op("scalar", "copy", out=out_fm_slice[:, c, :], in_=pb)
        if stash is not None:
            p.dma("sync", out=stash, in_=out_fm_slice)

    def rope_fm(dst_fm, src_fm, h4, wdt):
        c0, c1 = 2 * h4, 2 * h4 + 1
        cs = cosb[:, 0:wdt]
        sn = sinb[:, 0:wdt]
        t1 = scrA[:, 0:wdt]
        t2 = scrA[:, 1024:1024 + wdt]
        p.op("vector", "tensor_tensor", out=t1, in0=src_fm[:, c0, :], in1=cs, op=ALU.mult)
        p.op("vector", "tensor_tensor", out=t2, in0=src_fm[:, c1, :], in1=sn, op=ALU.mult)
        p.op("vector", "tensor_tensor", out=dst_fm[:, c0, :], in0=t1, in1=t2,
             op=ALU.subtract)
        p.op("vector", "tensor_tensor", out=t1, in0=src_fm[:, c1, :], in1=cs, op=ALU.mult)
        p.op("vector", "tensor_tensor", out=t2, in0=src_fm[:, c0, :], in1=sn, op=ALU.mult)
        p.op("vector", "tensor_tensor", out=dst_fm[:, c1, :], in0=t1, in1=t2, op=ALU.add)

    def skv_chain(w1ap, w2ap, bias_b, wout_tm, zout, g, xn_src, first_carry,
                  mod4=False):
        ph = psP[2]
        for c in range(NCD):
            p.mm(ph[0:16, :], wsk1[:, c, :], xn_src[:, c, :], c == 0, c == NCD - 1)
        p.op("vector", "tensor_copy", out=t1f, in_=ph[0:16, :])
        for jj in range(4):
            tg = g * 4 + jj
            ph2 = psP[3]
            p.mm(ph2[:, 0:256], t1f[:, jj * 128:(jj + 1) * 128], wsk2, True, True)
            p.op("vector", "tensor_tensor", out=wkt, in0=ph2[:, 0:256], in1=bias_b,
                 op=ALU.add)
            p.op("vector", "tensor_scalar", out=wkt, in0=wkt, scalar1=32.0,
                 scalar2=-32.0, op0=ALU.min, op1=ALU.max)
            p.op("scalar", "activation", out=wkt, in_=wkt, func=AF.Exp)
            p.op("vector", "tensor_copy", out=wout_tm[:, tg % 4 if mod4 else tg, :],
                 in_=wkt)
            p.mm(psP[4][:, 0:256], triu3, wkt, True, False)
            p.mm(psP[4][:, 0:256], cm.ones_rowf, carry, False, True)
            if zout is not None:
                p.op("vector", "reciprocal", out=zout[:, tg - (NKV - 16), :],
                     in_=psP[4][:, 0:256])
            p.mm(psP[2][0:1, 0:256], cm.ones_colf, wkt, True, False)
            p.mm(psP[2][0:1, 0:256], cm.ones_rowf[0:1, 0:1], carry, False, True)
            p.op("scalar", "copy", out=carry, in_=psP[2][0:1, 0:256])

    triu3 = p.alloc("triu3", [128, 128], F32)
    p.op("vector", "tensor_scalar", out=triu3, in0=cm.iot_row,
         scalar1=cm.iot_col, scalar2=None, op0=ALU.is_ge)
    kraw = scrA[:, 2048:4096].bitcast(BF16).rearrange("p (c q) -> p c q", q=512)

    k_fm = p.alloc("k_fm", [128, NCD, kv_len], BF16)
    wk_tm = p.alloc("wk_tm", [128, NKV, 256], BF16)
    zk_own = p.alloc("zk_own", [128, 16, 256], BF16)
    wkv_a = p.alloc("wkv_a", [128, 4, 1024], BF16)
    wkv_b = p.alloc("wkv_b", [128, 4, 1024], BF16)
    p.dma("gpsimd", out=wkv_a, in_=W["Wk"][0:512, :].rearrange("(c p) f -> p c f", p=128))
    p.dma("gpsimd", out=wkv_b, in_=W["Wk"][512:1024, :].rearrange("(c p) f -> p c f", p=128))
    p.dma("gpsimd", out=wsk1, in_=W["sk_w1"].rearrange("(c p) f -> p c f", p=128))
    p.dma("gpsimd", out=wsk2, in_=W["sk_w2"])
    p.op("vector", "memset", ap=carry, constant=0.0)
    for g in range(kv_len // 512):
        own = g >= (kv_len - 2048) // 512
        og = g - (kv_len - 2048) // 512
        if own:
            src = xn_own[:, :, og * 512:(og + 1) * 512]
            for jj in range(4):
                ln_tile(x_own_dram, og * 4 + jj, _fm_slice(xn_own, og * 4 + jj))
        else:
            src = xn_g
            for jj in range(4):
                ln_tile(x_other_dram, g * 4 + jj, _fm_slice(xn_g, jj))
            p.dma("sync", out=xnpre_d[:, :, g * 512:(g + 1) * 512], in_=xn_g)
        for hc in range(NCD):
            ph = psP[hc % 2]
            for c in range(NCD):
                wslc = wkv_a[:, c, :] if c < 4 else wkv_b[:, c - 4, :]
                p.mm(ph, wslc[:, hc * 128:(hc + 1) * 128], src[:, c, :],
                     c == 0, c == NCD - 1)
            p.op("scalar", "copy", out=kraw[:, hc, :], in_=ph)
        p.dma("gpsimd", out=cosb, in_=W["cos_t"][:, g * 512:(g + 1) * 512])
        p.dma("gpsimd", out=sinb, in_=W["sin_t"][:, g * 512:(g + 1) * 512])
        kdst = k_fm[:, :, g * 512:(g + 1) * 512]
        for h4 in range(4):
            rope_fm(kdst, kraw, h4, 512)
        skv_chain(None, None, skb, wk_tm, zk_own if own else None, g, src, g == 0)
    p.free("wkv_a", "wkv_b")

    p.free("brow")
    qraw = p.alloc("qraw", [128, NCD, 128], BF16)
    qfm = p.alloc("qfm", [128, NCD, 128], BF16)
    s_bf = p.alloc("s_bf", [128, 128], BF16)
    lgt = p.alloc("lgt", [128, 256], F32)
    mxg = p.alloc("mxg", [128, 4], F32)
    wq_w = p.alloc("wq_w", [128, NCD, 1024], BF16)
    p.dma("gpsimd", out=wq_w, in_=W["Wq"].rearrange("(c p) f -> p c f", p=128))
    for t in range(NT_OWN):
        n_st = own0 // 128 + t + 1
        for hc in range(NCD):
            ph = psP[0]
            for c in range(NCD):
                p.mm(ph[:, 0:128], wq_w[:, c, hc * 128:(hc + 1) * 128],
                     xn_own[:, c, t * 128:(t + 1) * 128], c == 0, c == NCD - 1)
            p.op("scalar", "copy", out=qraw[:, hc, :], in_=ph[:, 0:128])
        q0g = own0 + t * 128
        p.dma("gpsimd", out=cosb[:, 0:128], in_=W["cos_t"][:, q0g:q0g + 128])
        p.dma("gpsimd", out=sinb[:, 0:128], in_=W["sin_t"][:, q0g:q0g + 128])
        for h4 in range(4):
            rope_fm(qfm, qraw, h4, 128)
        for h4 in range(4):
            for st in range(n_st):
                for ci in range(2):
                    p.mm(psP[1][:, 0:128], k_fm[:, 2 * h4 + ci, st * 128:(st + 1) * 128],
                         qfm[:, 2 * h4 + ci, :], ci == 0, ci == 1)
                if st == n_st - 1:
                    p.op("vector", "scalar_tensor_tensor", out=s_bf,
                         in0=psP[1][:, 0:128], scalar=1.0, in1=mask01d,
                         op0=ALU.mult, op1=ALU.mult)
                else:
                    p.op("scalar", "copy", out=s_bf, in_=psP[1][:, 0:128])
                p.mm(psP[2][:, h4 * 64:(h4 + 1) * 64], s_bf,
                     wk_tm[:, st, h4 * 64:(h4 + 1) * 64], st == 0, st == n_st - 1)
        p.op("vector", "scalar_tensor_tensor", out=lgt, in0=psP[2][:, 0:256],
             scalar=1.0 / 16.0, in1=zk_own[:, t, :], op0=ALU.mult, op1=ALU.mult)
        p.op("vector", "tensor_reduce", out=mxg,
             in_=lgt.rearrange("p (r d) -> p r d", d=64), axis=AXL.X, op=ALU.max)
        p.op("vector", "tensor_tensor", out=lgt.rearrange("p (r d) -> p r d", d=64),
             in0=lgt.rearrange("p (r d) -> p r d", d=64),
             in1=mxg.unsqueeze(2).broadcast_to([128, 4, 64]), op=ALU.subtract)
        p.op("scalar", "activation", out=lgt, in_=lgt, func=AF.Exp)
        p.op("vector", "tensor_reduce", out=mxg,
             in_=lgt.rearrange("p (r d) -> p r d", d=64), axis=AXL.X, op=ALU.add)
        p.op("vector", "reciprocal", out=mxg, in_=mxg)
        p.op("vector", "tensor_tensor", out=p_tm.rearrange("p (r d) -> p r d", d=64),
             in0=lgt.rearrange("p (r d) -> p r d", d=64),
             in1=mxg.unsqueeze(2).broadcast_to([128, 4, 64]), op=ALU.mult)
        p.dma("sync", out=p_d[:, t, :], in_=p_tm)
    p.free("k_fm", "wk_tm", "zk_own", "wq_w", "qraw", "qfm", "cosb", "sinb")

    v_tm = p.alloc("v_tm", [128, NKV, 1024], BF16)
    wv_fm2 = p.alloc("wv_fm2", [128, 2, kv_len], BF16)
    zv_own = p.alloc("zv_own", [128, 16, 256], BF16)
    wv_stage = p.alloc("wv_stage", [128, 4, 256], BF16)
    wkv_a = p.alloc("wkv_a2", [128, 4, 1024], BF16)
    wkv_b = p.alloc("wkv_b2", [128, 4, 1024], BF16)
    p.dma("gpsimd", out=wkv_a, in_=W["Wv"][0:512, :].rearrange("(c p) f -> p c f", p=128))
    p.dma("gpsimd", out=wkv_b, in_=W["Wv"][512:1024, :].rearrange("(c p) f -> p c f", p=128))
    p.dma("gpsimd", out=wsk1, in_=W["sv_w1"].rearrange("(c p) f -> p c f", p=128))
    p.dma("gpsimd", out=wsk2, in_=W["sv_w2"])
    p.op("vector", "memset", ap=carry, constant=0.0)
    for g in range(kv_len // 512):
        own = g >= (kv_len - 2048) // 512
        og = g - (kv_len - 2048) // 512
        if own:
            src = xn_own[:, :, og * 512:(og + 1) * 512]
        else:
            src = xn_g
            p.dma("sync", out=xn_g, in_=xnpre_d[:, :, g * 512:(g + 1) * 512])
        for jj in range(4):
            tg = g * 4 + jj
            for fh in range(2):
                ph = psP[fh]
                for c in range(NCD):
                    wslc = wkv_a[:, c, :] if c < 4 else wkv_b[:, c - 4, :]
                    p.mm(ph, src[:, c, jj * 128:(jj + 1) * 128],
                         wslc[:, fh * 512:(fh + 1) * 512], c == 0, c == NCD - 1)
                p.op("vector", "tensor_copy", out=v_tm[:, tg, fh * 512:(fh + 1) * 512],
                     in_=ph)
        skv_chain(None, None, svb, wv_stage, zv_own if own else None, g, src,
                  g == 0, mod4=True)
        for jj in range(4):
            tg = g * 4 + jj
            for hp in range(2):
                pb = psT0 if hp == 0 else psT1
                p.xpose(pb, wv_stage[:, jj, hp * 128:(hp + 1) * 128], cm.ident)
                p.op("scalar", "copy", out=wv_fm2[:, hp, tg * 128:(tg + 1) * 128], in_=pb)
    p.free("wkv_a2", "wkv_b2")

    p.free("xraw0", "xraw1", "xnf", "xnb", "xn_g", "skb", "svb", "wsk1",
           "wsk2", "t1f", "wkt", "wv_stage")
    gnw = p.alloc("gnw", [128, 1024], F32)
    p.dma("sync", out=carry, in_=W["gnorm_w"].unsqueeze(0))
    bcast_row(p, cm, psP[0], gnw[:, 0:256], carry, 256)
    p.op("vector", "tensor_copy",
         out=gnw.rearrange("p (r d) -> p r d", d=256),
         in_=gnw[:, 0:256].unsqueeze(1).broadcast_to([128, 4, 256]))
    wg_w = p.alloc("wg_w", [128, NCD, 1024], BF16)
    wo_wA = p.alloc("wo_wA", [128, 4, 1024], BF16)
    wo_wB = p.alloc("wo_wB", [128, 4, 1024], BF16)
    p.dma("gpsimd", out=wg_w, in_=W["Wg"].rearrange("(c p) f -> p c f", p=128))
    p.dma("gpsimd", out=wo_wA, in_=W["Wo"][0:512, :].rearrange("(c p) f -> p c f", p=128))
    p.dma("gpsimd", out=wo_wB, in_=W["Wo"][512:1024, :].rearrange("(c p) f -> p c f", p=128))
    pt_fm2 = p.alloc("pt_fm2", [128, 2, 128], BF16)
    a_bf = p.alloc("a_bf", [128, 128], BF16)
    osb = p.alloc("osb", [128, 1024], F32)
    gsb = p.alloc("gsb", [128, 1024], F32)
    ptt = p.alloc("ptt", [128, 256], BF16)
    ofin = scrA[:, 1024:1536].bitcast(BF16)
    ofm = scrA[:, 1536:2048].bitcast(BF16).rearrange("p (c q) -> p c q", q=128)
    xres = scrA[:, 2048:3072]
    for t in range(NT_OWN):
        n_st = own0 // 128 + t + 1
        p.dma("sync", out=p_tm, in_=p_d[:, t, :])
        p.op("vector", "tensor_tensor", out=ptt, in0=p_tm,
             in1=zv_own[:, t, :], op=ALU.mult)
        for hp in range(2):
            pb = psT0 if hp == 0 else psT1
            p.xpose(pb, ptt[:, hp * 128:(hp + 1) * 128], cm.ident)
            p.op("scalar", "copy", out=pt_fm2[:, hp, :], in_=pb)
        for h4 in range(4):
            hp, lh = h4 // 2, h4 % 2
            base = lh * 64
            for st in range(n_st):
                p.mm(psP[1][:, 0:128], wv_fm2[base:base + 64, hp, st * 128:(st + 1) * 128],
                     pt_fm2[base:base + 64, hp, :], True, True)
                if st == n_st - 1:
                    p.op("vector", "scalar_tensor_tensor", out=a_bf,
                         in0=psP[1][:, 0:128], scalar=1.0, in1=mask01d,
                         op0=ALU.mult, op1=ALU.mult)
                else:
                    p.op("scalar", "copy", out=a_bf, in_=psP[1][:, 0:128])
                p.mm(psP[2][:, 0:256], a_bf, v_tm[:, st, h4 * 256:(h4 + 1) * 256],
                     st == 0, st == n_st - 1)
            p.op("vector", "tensor_copy", out=osb[:, h4 * 256:(h4 + 1) * 256],
                 in_=psP[2][:, 0:256])
        for fh in range(2):
            ph = psP[3 + fh]
            for c in range(NCD):
                p.mm(ph, xn_own[:, c, t * 128:(t + 1) * 128],
                     wg_w[:, c, fh * 512:(fh + 1) * 512], c == 0, c == NCD - 1)
            p.op("vector", "tensor_copy", out=gsb[:, fh * 512:(fh + 1) * 512], in_=ph)
        p.op("scalar", "activation", out=scrA[:, 0:1024], in_=osb, func=AF.Square)
        p.op("vector", "tensor_reduce", out=mxg,
             in_=scrA[:, 0:1024].rearrange("p (r d) -> p r d", d=256),
             axis=AXL.X, op=ALU.add)
        p.op("vector", "tensor_scalar", out=mxg, in0=mxg, scalar1=1.0 / 256.0,
             scalar2=None, op0=ALU.mult)
        p.op("scalar", "activation", out=mxg, in_=mxg, func=AF.Sqrt, bias=cm.c(1e-5))
        p.op("vector", "reciprocal", out=mxg, in_=mxg)
        p.op("vector", "tensor_tensor", out=osb.rearrange("p (r d) -> p r d", d=256),
             in0=osb.rearrange("p (r d) -> p r d", d=256),
             in1=mxg.unsqueeze(2).broadcast_to([128, 4, 256]), op=ALU.mult)
        p.op("vector", "tensor_tensor", out=osb, in0=osb, in1=gnw, op=ALU.mult)
        p.op("scalar", "activation", out=scrA[:, 0:1024], in_=gsb, func=AF.Sigmoid)
        p.op("vector", "tensor_tensor", out=gsb, in0=gsb, in1=scrA[:, 0:1024],
             op=ALU.mult)
        p.op("vector", "tensor_tensor", out=ofin, in0=osb, in1=gsb, op=ALU.mult)
        for c in range(NCD):
            pb = psT0 if c % 2 == 0 else psT1
            p.xpose(pb, ofin[:, c * 128:(c + 1) * 128], cm.ident)
            p.op("scalar", "copy", out=ofm[:, c, :], in_=pb)
        p.dma("sync", out=xres, in_=x_own_dram[t * 128:(t + 1) * 128, :])
        for fh in range(2):
            ph = psP[3 + fh]
            for c in range(NCD):
                wo_c = wo_wA[:, c, :] if c < 4 else wo_wB[:, c - 4, :]
                p.mm(ph, ofm[:, c, :], wo_c[:, fh * 512:(fh + 1) * 512],
                     c == 0, c == NCD - 1)
            p.op("vector", "tensor_tensor", out=xres[:, fh * 512:(fh + 1) * 512],
                 in0=xres[:, fh * 512:(fh + 1) * 512], in1=ph, op=ALU.add)
        p.dma("sync", out=x_out_dram[t * 128:(t + 1) * 128, :], in_=xres)
    p.free("v_tm", "wv_fm2", "zv_own", "wg_w", "wo_wA", "wo_wB", "pt_fm2",
           "a_bf", "osb", "gsb", "ptt", "p_tm", "mxg",
           "s_bf", "lgt", "gnw", "carry", "mask01d", "triu3")

    brow = p.alloc("brow", [32, 1024], F32)
    build_b3_ffn(p, cm, xn_own, x_own_dram, x_out_dram, W, psP, psT0, psT1,
                 psF0, lnb, xraw, brow, scrA)
    p.free("xn_own", "scrA", "brow")
    for i in range(6):
        p.free(f"ln{i}")


def _fm_slice(fm, tile):
    return fm[:, :, tile * 128:(tile + 1) * 128]


def build_b3_ffn(p, cm, xn_own, x_own_dram, x_out_dram, W, psP, psT0, psT1,
                 psF0, lnb, xraw, brow, scrA):
    wgt = p.alloc("wgt", [128, NCD, 2], BF16)
    sel = p.alloc("sel", [128, 16, 2], F32)
    p.dma("sync", out=brow[0:2, :], in_=W["gate_w"])
    for c in range(NCD):
        p.xpose(psF0[:, 0:2], brow[0:2, c * 128:(c + 1) * 128], cm.identf[0:2, 0:2])
        p.op("vector", "tensor_copy", out=wgt[:, c, :], in_=psF0[:, 0:2])
    for t in range(NT_OWN):
        ph = psP[4]
        for c in range(NCD):
            p.mm(ph[:, 0:2], xn_own[:, c, t * 128:(t + 1) * 128], wgt[:, c, :],
                 c == 0, c == NCD - 1)
        p.op("vector", "tensor_copy", out=sel[:, t, :], in_=ph[:, 0:2])
        p.op("vector", "tensor_tensor", out=lnb[0], in0=sel[:, t, 0:1],
             in1=sel[:, t, 1:2], op=ALU.subtract)
        p.op("vector", "tensor_scalar", out=sel[:, t, 0:1], in0=lnb[0], scalar1=0.0,
             scalar2=None, op0=ALU.is_ge)
        p.op("vector", "tensor_scalar", out=sel[:, t, 1:2], in0=sel[:, t, 0:1],
             scalar1=-1.0, scalar2=1.0, op0=ALU.mult, op1=ALU.add)
    cst = [p.alloc(f"cst{e}", [128, 1024], F32) for e in range(2)]
    rb1 = p.alloc("rb1", [128, 32], F32)
    w2s = [p.alloc(f"w2s{i}", [128, 1024], BF16) for i in range(2)]
    for e in range(2):
        p.dma("sync", out=brow[0:4, :], in_=W["b1"][e].rearrange("(a b) -> a b", b=1024))
        p.op("vector", "tensor_scalar", out=brow[0:4, :], in0=brow[0:4, :],
             scalar1=0.0, scalar2=None, op0=ALU.max)
        for cc in range(8):
            p.xpose(psF0[:, 0:4], brow[0:4, cc * 128:(cc + 1) * 128],
                    cm.identf[0:4, 0:4])
            p.op("vector", "tensor_copy", out=rb1[:, cc * 4:(cc + 1) * 4],
                 in_=psF0[:, 0:4])
        rb1b = p.alloc("rb1b", [128, 32], BF16)
        p.op("vector", "tensor_copy", out=rb1b, in_=rb1)
        for fh in range(2):
            ph = psP[fh]
            first = True
            for r in range(4):
                for cc in range(8):
                    jf = r * 8 + cc
                    p.dma("gpsimd", out=w2s[jf % 2],
                          in_=W["w2"][e, jf * 128:(jf + 1) * 128, :])
                    p.mm(ph[0:1, :], rb1b[:, cc * 4 + r:cc * 4 + r + 1],
                         w2s[jf % 2][:, fh * 512:(fh + 1) * 512], first, jf == 31)
                    first = False
            p.op("vector", "tensor_copy", out=cst[e][0:1, fh * 512:(fh + 1) * 512],
                 in_=ph[0:1, :])
        p.dma("sync", out=brow[0:1, :], in_=W["b2"][e].unsqueeze(0))
        p.op("vector", "tensor_tensor", out=cst[e][0:1, :], in0=cst[e][0:1, :],
             in1=brow[0:1, :], op=ALU.add)
        p.op("vector", "tensor_copy", out=brow.bitcast(BF16)[0:1, 0:1024],
             in_=cst[e][0:1, :])
        for fh in range(2):
            p.mm(psP[0], cm.ones_row,
                 brow.bitcast(BF16)[0:1, fh * 512:(fh + 1) * 512], True, True)
            p.op("vector", "tensor_copy", out=cst[e][:, fh * 512:(fh + 1) * 512],
                 in_=psP[0])
    p.free("rb1", "rb1b")

    hid = p.alloc("hid", [128, 32, 1024], BF16)
    w1s = [p.alloc(f"w1s{i}", [128, NCD, 512], BF16) for i in range(2)]
    b1t = p.alloc("b1t", [128, 32], F32)
    b1r = p.alloc("b1r", [32, 128], F32)

    def mlp_pass(w1ap, b1ap, w2ap, b2row_or_none, epilogue):
        p.dma("sync", out=b1r, in_=b1ap.rearrange("(j p) -> j p", p=128))
        p.xpose(psF0[:, 0:32], b1r, cm.identf[0:32, 0:32])
        p.op("vector", "tensor_copy", out=b1t, in_=psF0[:, 0:32])
        for half in range(2):
            tok0 = half * 1024
            for jg in range(8):
                wb = w1s[jg % 2]
                for c in range(NCD):
                    p.dma("gpsimd", out=wb[:, c, :],
                          in_=w1ap[c * 128:(c + 1) * 128, jg * 512:(jg + 1) * 512])
                for j4 in range(4):
                    j = jg * 4 + j4
                    for tg in range(2):
                        ph2 = psP[tg]
                        for c in range(NCD):
                            p.mm(ph2, wb[:, c, j4 * 128:(j4 + 1) * 128],
                                 xn_own[:, c, tok0 + tg * 512:tok0 + (tg + 1) * 512],
                                 c == 0, c == NCD - 1)
                        p.op("scalar", "activation", out=hid[:, j, tg * 512:(tg + 1) * 512],
                             in_=ph2, func=AF.Relu, bias=b1t[:, j:j + 1])
            for grp in range(4):
                for j in range(32):
                    p.dma("gpsimd", out=w2s[j % 2],
                          in_=w2ap[j * 128:(j + 1) * 128, :])
                    for tt in range(2):
                        lt = (grp * 2 + tt) * 128
                        for fh in range(2):
                            p.mm(psP[tt * 2 + fh], hid[:, j, lt:lt + 128],
                                 w2s[j % 2][:, fh * 512:(fh + 1) * 512], j == 0, j == 31)
                for tt in range(2):
                    t = half * 8 + grp * 2 + tt
                    epilogue(t, (psP[tt * 2], psP[tt * 2 + 1]))

    xr = scrA[:, 0:1024]

    def epi_shared(t, banks):
        p.dma("sync", out=xr, in_=x_out_dram[t * 128:(t + 1) * 128, :])
        p.op("vector", "tensor_tensor", out=xr, in0=xr, in1=sb2b, op=ALU.add)
        for fh in range(2):
            p.op("vector", "tensor_tensor", out=xr[:, fh * 512:(fh + 1) * 512],
                 in0=xr[:, fh * 512:(fh + 1) * 512], in1=banks[fh], op=ALU.add)
        p.dma("sync", out=x_out_dram[t * 128:(t + 1) * 128, :], in_=xr)

    def mk_epi_expert(e):
        def epi(t, banks):
            p.dma("sync", out=xr, in_=x_out_dram[t * 128:(t + 1) * 128, :])
            s_ap = sel[:, t, e:e + 1]
            o_ap = sel[:, t, 1 - e:2 - e]
            for fh in range(2):
                p.op("vector", "scalar_tensor_tensor",
                     out=xr[:, fh * 512:(fh + 1) * 512], in0=banks[fh], scalar=s_ap,
                     in1=xr[:, fh * 512:(fh + 1) * 512], op0=ALU.mult, op1=ALU.add)
                p.op("vector", "scalar_tensor_tensor",
                     out=xr[:, fh * 512:(fh + 1) * 512],
                     in0=cst[e][:, fh * 512:(fh + 1) * 512], scalar=o_ap,
                     in1=xr[:, fh * 512:(fh + 1) * 512], op0=ALU.mult, op1=ALU.add)
            p.dma("sync", out=x_out_dram[t * 128:(t + 1) * 128, :], in_=xr)
        return epi

    sb2b = p.alloc("sb2b", [128, 1024], F32)
    p.dma("sync", out=brow[0:1, :], in_=W["sb2"].unsqueeze(0))
    bcast_row(p, cm, psP[4], sb2b, brow[0:1, :], 1024)
    mlp_pass(W["sw1"], W["sb1"], W["sw2"], None, epi_shared)
    for e in range(2):
        mlp_pass(W["w1"][e], W["b1"][e], W["w2"][e], None, mk_epi_expert(e))
    p.free("wgt", "sel", "cst0", "cst1", "w2s0", "w2s1", "hid", "w1s0", "w1s1",
           "b1t", "b1r", "sb2b")



def _shard_x(x_full):
    shards = []
    for core in range(8):
        b, h = core // 2, core % 2
        own = np.ascontiguousarray(x_full[b, h * 2048:(h + 1) * 2048, :])
        other = np.ascontiguousarray(x_full[b, 0:2048, :])
        shards.append((own, other))
    return shards


def _run_block0(x_full, inputs):
    outs = []
    progs = {}
    for h in (0, 1):
        p = Prog()
        cm = Common(p)
        x_own = p.dram("x_own", [2048, 1024], F32, "ExternalInput")
        x_other = p.dram("x_other", [2048, 1024], F32, "ExternalInput")
        W = {
            "q": p.dram("wq_d", [1024, 64], F32, "ExternalInput"),
            "k": p.dram("wk_d", [1024, 64], F32, "ExternalInput"),
            "v": p.dram("wv_d", [1024, 1024], F32, "ExternalInput"),
            "fw1": p.dram("fw1_d", [4096, 1024], F32, "ExternalInput"),
            "b1": p.dram("fb1_d", [4096], F32, "ExternalInput"),
            "fw2": p.dram("fw2_d", [1024, 4096], F32, "ExternalInput"),
            "b2": p.dram("fb2_d", [1024], F32, "ExternalInput"),
        }
        x_out = p.dram("x_new", [2048, 1024], F32, "ExternalOutput")
        build_block0(p, cm, h, x_own, x_other, W, x_out)
        progs[h] = p.build()
    shards = _shard_x(x_full)
    res_by_core = {}
    for h in (0, 1):
        cores = [c for c in range(8) if c % 2 == h]
        in_maps = []
        for c in cores:
            own, other = shards[c]
            in_maps.append({
                "x_own": own, "x_other": other,
                "wq_d": inputs["L0_attn_Wq"], "wk_d": inputs["L0_attn_Wk"],
                "wv_d": inputs["L0_attn_Wv"],
                "fw1_d": inputs["L0_ffn_W1"], "fb1_d": inputs["L0_ffn_b1"],
                "fw2_d": inputs["L0_ffn_W2"], "fb2_d": inputs["L0_ffn_b2"],
            })
        res = run_bass_kernel_spmd(progs[h], in_maps, core_ids=cores)
        for c, r in zip(cores, res.results):
            res_by_core[c] = r["x_new"]
    x_new = np.empty_like(x_full)
    for core in range(8):
        b, hh = core // 2, core % 2
        x_new[b, hh * 2048:(hh + 1) * 2048, :] = res_by_core[core]
    return x_new


def _run_block1(x_full, inputs):
    progs = {}
    for h in (0, 1):
        p = Prog()
        cm = Common(p)
        x_own = p.dram("x_own", [2048, 1024], F32, "ExternalInput")
        x_other = p.dram("x_other", [2048, 1024], F32, "ExternalInput")
        W = {
            "in_proj": p.dram("w_inp", [4240, 1024], F32, "ExternalInput"),
            "conv_w": p.dram("w_cw", [2176, 4], F32, "ExternalInput"),
            "conv_b": p.dram("w_cb", [2176], F32, "ExternalInput"),
            "dt_bias": p.dram("w_dtb", [16], F32, "ExternalInput"),
            "A_log": p.dram("w_al", [16], F32, "ExternalInput"),
            "Dp": p.dram("w_dp", [16], F32, "ExternalInput"),
            "norm_w": p.dram("w_nw", [2048], F32, "ExternalInput"),
            "out_proj": p.dram("w_outp", [1024, 2048], F32, "ExternalInput"),
            "Wg": p.dram("w_wg", [8, 1024], F32, "ExternalInput"),
            "W1": p.dram("w_w1", [8, 2730, 1024], F32, "ExternalInput"),
            "W2": p.dram("w_w2", [8, 2730, 1024], F32, "ExternalInput"),
            "W3": p.dram("w_w3", [8, 1024, 2730], F32, "ExternalInput"),
        }
        x_out = p.dram("x_new", [2048, 1024], F32, "ExternalOutput")
        build_block1(p, cm, h, x_own, x_other, W, x_out)
        progs[h] = p.build()
    shards = _shard_x(x_full)
    res_by_core = {}
    for h in (0, 1):
        cores = [c for c in range(8) if c % 2 == h]
        in_maps = []
        for c in cores:
            own, other = shards[c]
            in_maps.append({
                "x_own": own, "x_other": other,
                "w_inp": inputs["L1_attn_in_proj_w"],
                "w_cw": inputs["L1_attn_conv_w"], "w_cb": inputs["L1_attn_conv_b"],
                "w_dtb": inputs["L1_attn_dt_bias"], "w_al": inputs["L1_attn_A_log"],
                "w_dp": inputs["L1_attn_Dp"], "w_nw": inputs["L1_attn_norm_w"],
                "w_outp": inputs["L1_attn_out_proj_w"],
                "w_wg": inputs["L1_ffn_Wg"], "w_w1": inputs["L1_ffn_W1"],
                "w_w2": inputs["L1_ffn_W2"], "w_w3": inputs["L1_ffn_W3"],
            })
        res = run_bass_kernel_spmd(progs[h], in_maps, core_ids=cores)
        for c, r in zip(cores, res.results):
            res_by_core[c] = r["x_new"]
    x_new = np.empty_like(x_full)
    for core in range(8):
        b, hh = core // 2, core % 2
        x_new[b, hh * 2048:(hh + 1) * 2048, :] = res_by_core[core]
    return x_new


def _run_block2(x_full, inputs):
    progs = {}
    for h in (0, 1):
        p = Prog()
        cm = Common(p)
        x_own = p.dram("x_own", [2048, 1024], F32, "ExternalInput")
        x_other = p.dram("x_other", [2048, 1024], F32, "ExternalInput")
        W = {
            "qkv_w": p.dram("w_qkv", [1024, 3072], F32, "ExternalInput"),
            "qkv_b": p.dram("w_qkvb", [3072], F32, "ExternalInput"),
            "rq_w": p.dram("w_rq", [1024, 256], F32, "ExternalInput"),
            "rq_b": p.dram("w_rqb", [256], F32, "ExternalInput"),
            "rkv_w": p.dram("w_rkv", [1024, 512], F32, "ExternalInput"),
            "rkv_b": p.dram("w_rkvb", [512], F32, "ExternalInput"),
            "rout_w": p.dram("w_ro", [256, 1024], F32, "ExternalInput"),
            "rout_b": p.dram("w_rob", [1024], F32, "ExternalInput"),
            "out_w": p.dram("w_ow", [1024, 1024], F32, "ExternalInput"),
            "out_b": p.dram("w_ob", [1024], F32, "ExternalInput"),
            "gl_w": p.dram("w_gl", [64, 1], F32, "ExternalInput"),
            "gr_w": p.dram("w_gr", [64, 1], F32, "ExternalInput"),
            "w_gate": p.dram("w_wg", [2816, 1024], F32, "ExternalInput"),
            "w_up": p.dram("w_wu", [2816, 1024], F32, "ExternalInput"),
            "w_down": p.dram("w_wd", [1024, 2816], F32, "ExternalInput"),
        }
        x_out = p.dram("x_new", [2048, 1024], F32, "ExternalOutput")
        build_block2(p, cm, h, x_own, x_other, W, x_out)
        progs[h] = p.build()
    shards = _shard_x(x_full)
    res_by_core = {}
    for h in (0, 1):
        cores = [c for c in range(8) if c % 2 == h]
        in_maps = []
        for c in cores:
            own, other = shards[c]
            in_maps.append({
                "x_own": own, "x_other": other,
                "w_qkv": inputs["L2_attn_qkv_w"], "w_qkvb": inputs["L2_attn_qkv_b"],
                "w_rq": inputs["L2_attn_rq_w"], "w_rqb": inputs["L2_attn_rq_b"],
                "w_rkv": inputs["L2_attn_rkv_w"], "w_rkvb": inputs["L2_attn_rkv_b"],
                "w_ro": inputs["L2_attn_rout_w"], "w_rob": inputs["L2_attn_rout_b"],
                "w_ow": inputs["L2_attn_out_w"], "w_ob": inputs["L2_attn_out_b"],
                "w_gl": inputs["L2_attn_gl_w"], "w_gr": inputs["L2_attn_gr_w"],
                "w_wg": inputs["L2_ffn_w_gate"], "w_wu": inputs["L2_ffn_w_up"],
                "w_wd": inputs["L2_ffn_w_down"],
            })
        res = run_bass_kernel_spmd(progs[h], in_maps, core_ids=cores)
        for c, r in zip(cores, res.results):
            res_by_core[c] = r["x_new"]
    x_new = np.empty_like(x_full)
    for core in range(8):
        b, hh = core // 2, core % 2
        x_new[b, hh * 2048:(hh + 1) * 2048, :] = res_by_core[core]
    return x_new


def _rope_tables():
    half = 128
    inv_freq = (1.0 / (10000.0 ** (np.arange(half, dtype=np.float32) / half))).astype(np.float32)
    ang = (np.arange(4096, dtype=np.float32)[:, None] * inv_freq[None, :]).astype(np.float32)
    cos_t = np.ascontiguousarray(np.cos(ang).T.astype(np.float32))
    sin_t = np.ascontiguousarray(np.sin(ang).T.astype(np.float32))
    return cos_t, sin_t


def _run_block3(x_full, inputs):
    cos_t, sin_t = _rope_tables()
    progs = {}
    for h in (0, 1):
        p = Prog()
        cm = Common(p)
        x_own = p.dram("x_own", [2048, 1024], F32, "ExternalInput")
        x_other = p.dram("x_other", [2048, 1024], F32, "ExternalInput")
        W = {
            "Wq": p.dram("w_q", [1024, 1024], F32, "ExternalInput"),
            "Wk": p.dram("w_k", [1024, 1024], F32, "ExternalInput"),
            "Wv": p.dram("w_v", [1024, 1024], F32, "ExternalInput"),
            "Wg": p.dram("w_g", [1024, 1024], F32, "ExternalInput"),
            "Wo": p.dram("w_o", [1024, 1024], F32, "ExternalInput"),
            "sk_w1": p.dram("w_sk1", [1024, 16], F32, "ExternalInput"),
            "sk_w2": p.dram("w_sk2", [16, 256], F32, "ExternalInput"),
            "sk_b2": p.dram("w_skb", [256], F32, "ExternalInput"),
            "sv_w1": p.dram("w_sv1", [1024, 16], F32, "ExternalInput"),
            "sv_w2": p.dram("w_sv2", [16, 256], F32, "ExternalInput"),
            "sv_b2": p.dram("w_svb", [256], F32, "ExternalInput"),
            "gnorm_w": p.dram("w_gn", [256], F32, "ExternalInput"),
            "cos_t": p.dram("w_cos", [128, 4096], F32, "ExternalInput"),
            "sin_t": p.dram("w_sin", [128, 4096], F32, "ExternalInput"),
            "gate_w": p.dram("w_gw", [2, 1024], F32, "ExternalInput"),
            "w1": p.dram("w_w1", [2, 1024, 4096], F32, "ExternalInput"),
            "b1": p.dram("w_b1", [2, 4096], F32, "ExternalInput"),
            "w2": p.dram("w_w2", [2, 4096, 1024], F32, "ExternalInput"),
            "b2": p.dram("w_b2", [2, 1024], F32, "ExternalInput"),
            "sw1": p.dram("w_sw1", [1024, 4096], F32, "ExternalInput"),
            "sb1": p.dram("w_sb1", [4096], F32, "ExternalInput"),
            "sw2": p.dram("w_sw2", [4096, 1024], F32, "ExternalInput"),
            "sb2": p.dram("w_sb2", [1024], F32, "ExternalInput"),
        }
        x_out = p.dram("x_new", [2048, 1024], F32, "ExternalOutput")
        build_block3(p, cm, h, x_own, x_other, W, x_out)
        progs[h] = p.build()
    shards = _shard_x(x_full)
    res_by_core = {}
    for h in (0, 1):
        cores = [c for c in range(8) if c % 2 == h]
        in_maps = []
        for c in cores:
            own, other = shards[c]
            in_maps.append({
                "x_own": own, "x_other": other,
                "w_q": inputs["L3_attn_Wq"], "w_k": inputs["L3_attn_Wk"],
                "w_v": inputs["L3_attn_Wv"], "w_g": inputs["L3_attn_Wg"],
                "w_o": inputs["L3_attn_Wo"],
                "w_sk1": inputs["L3_attn_sk_w1"], "w_sk2": inputs["L3_attn_sk_w2"],
                "w_skb": inputs["L3_attn_sk_b2"],
                "w_sv1": inputs["L3_attn_sv_w1"], "w_sv2": inputs["L3_attn_sv_w2"],
                "w_svb": inputs["L3_attn_sv_b2"],
                "w_gn": inputs["L3_attn_gnorm_w"],
                "w_cos": cos_t, "w_sin": sin_t,
                "w_gw": inputs["L3_ffn_gate_w"], "w_w1": inputs["L3_ffn_w1"],
                "w_b1": inputs["L3_ffn_b1"], "w_w2": inputs["L3_ffn_w2"],
                "w_b2": inputs["L3_ffn_b2"],
                "w_sw1": inputs["L3_ffn_sw1"], "w_sb1": inputs["L3_ffn_sb1"],
                "w_sw2": inputs["L3_ffn_sw2"], "w_sb2": inputs["L3_ffn_sb2"],
            })
        res = run_bass_kernel_spmd(progs[h], in_maps, core_ids=cores)
        for c, r in zip(cores, res.results):
            res_by_core[c] = r["x_new"]
    x_new = np.empty_like(x_full)
    for core in range(8):
        b, hh = core // 2, core % 2
        x_new[b, hh * 2048:(hh + 1) * 2048, :] = res_by_core[core]
    return x_new


def kernel(**inputs):
    x = inputs["L0_attn_X"].astype(np.float32)
    x = _run_block0(x, inputs)
    x = _run_block1(x, inputs)
    x = _run_block2(x, inputs)
    x = _run_block3(x, inputs)
    return x


# version 03
# speedup vs baseline: 2.0705x; 2.0705x over previous
import contextlib
import math
import numpy as np
import ml_dtypes
import concourse.bass as bass
import concourse.mybir as mybir
from concourse.bass_utils import run_bass_kernel_spmd

F32 = mybir.dt.float32
BF16 = mybir.dt.bfloat16
AF = mybir.ActivationFunctionType
ALU = mybir.AluOpType
AXL = mybir.AxisListType

NEG = -1.0e30
NEGSEED = -3.0e38
DBG = False
D = 1024
T_OWN = 2048
NT_OWN = 16
NCD = 8

BF = np.dtype(ml_dtypes.bfloat16)


def _span(ap: bass.AP):
    t = ap.tensor
    if ap.space == "SBUF" or ap.space == "PSUM":
        dims = ap.ap
        p0 = ap.base_partition
        pn = dims[0][1]
        esz = 2 if ap.dtype == BF16 else 4
        off = ap.offset * esz if isinstance(ap.offset, int) else 0
        lo = hi = 0
        for step, cnt in dims[1:]:
            d = step * (cnt - 1) * esz
            if d >= 0:
                hi += d
            else:
                lo += d
        return (p0, p0 + pn, off + lo, off + hi + esz)
    else:
        return (0, 128, 0, 1 << 62)


class Prog:
    ENGS = ("tensor", "vector", "scalar", "gpsimd", "sync")

    def __init__(self, arena_kb=192):
        self.nc = bass.Bass("TRN2", target_bir_lowering=False)
        self.es = contextlib.ExitStack()
        self.ops = []
        self.acc = {}
        self.psum_names = set()
        self.dram_names = set()
        self.dram_aps = {}
        self.psum_aps = {}
        self.ext_sems = {}
        self.var = None
        self.seg = 0
        self.common = []
        self._fin = False
        self.arena = self.es.enter_context(
            self.nc.sbuf_tensor("ARENA", [128, arena_kb * 256], F32))
        self.arena_bf = self.arena[:].bitcast(BF16)
        self._arena_bytes = arena_kb * 1024
        self._freelist = [(0, arena_kb * 1024)]
        self._allocs = {}

    def alloc(self, name, shape, dtype, align=64):
        esz = 2 if dtype == BF16 else 4
        n = 1
        for s in shape[1:]:
            n *= s
        nbytes = n * esz
        nbytes_al = (nbytes + align - 1) // align * align
        for i, (off, sz) in enumerate(self._freelist):
            offa = (off + align - 1) // align * align
            if offa + nbytes_al <= off + sz:
                newfree = []
                if offa > off:
                    newfree.append((off, offa - off))
                rest = (off + sz) - (offa + nbytes_al)
                if rest > 0:
                    newfree.append((offa + nbytes_al, rest))
                self._freelist[i:i + 1] = newfree
                self._allocs[name] = (offa, nbytes_al, shape, dtype)
                return self._view(offa, shape, dtype)
        allocs = sorted((v[0], v[1], k) for k, v in self._allocs.items())
        raise MemoryError(f"arena OOM for {name}: {nbytes_al}B/partition; "
                          f"free={self._freelist}; allocs={allocs}")

    def _view(self, off_bytes, shape, dtype):
        esz = 2 if dtype == BF16 else 4
        base = self.arena_bf if dtype == BF16 else self.arena[:]
        n = 1
        for s in shape[1:]:
            n *= s
        v = base[0:shape[0], off_bytes // esz: off_bytes // esz + n]
        if len(shape) > 2:
            pat = "p (" + " ".join(f"a{i}" for i in range(len(shape) - 1)) + ") -> p " + \
                  " ".join(f"a{i}" for i in range(len(shape) - 1))
            kw = {f"a{i}": shape[i + 1] for i in range(1, len(shape) - 1)}
            v = v.rearrange(pat, **kw)
        return v

    def free(self, *names):
        for name in names:
            off, sz, _, _ = self._allocs.pop(name)
            self._freelist.append((off, sz))
        self._freelist.sort()
        merged = []
        for off, sz in self._freelist:
            if merged and merged[-1][0] + merged[-1][1] == off:
                merged[-1] = (merged[-1][0], merged[-1][1] + sz)
            else:
                merged.append((off, sz))
        self._freelist = [tuple(x) for x in merged]

    def dram(self, name, shape, dtype, kind):
        if name in self.dram_aps:
            return self.dram_aps[name]
        ap = self.nc.dram_tensor(name, list(shape), dtype, kind=kind).ap()
        self.dram_names.add(name)
        self.dram_aps[name] = ap
        return ap

    def sbuf(self, name, shape, dtype):
        if not hasattr(self, "_sbufs"):
            self._sbufs = {}
        if name not in self._sbufs:
            self._sbufs[name] = self.es.enter_context(
                self.nc.sbuf_tensor(name, list(shape), dtype))[:]
        return self._sbufs[name]

    def psum(self, name, shape, dtype):
        if name in self.psum_aps:
            return self.psum_aps[name]
        ap = self.es.enter_context(self.nc.psum_tensor(name, list(shape), dtype))[:]
        self.psum_names.add(name)
        self.psum_aps[name] = ap
        return ap

    def ext_sem(self, name):
        if name not in self.ext_sems:
            self.ext_sems[name] = self.es.enter_context(self.nc.semaphore("x_" + name))
        return self.ext_sems[name]

    def begin_variant(self, v):
        self.var = v
        self.acc = {}
        self._freelist = [(0, len(self._allocs) * 0 + self._arena_bytes)]
        self._allocs = {}

    def mark_sem_write(self, tname, semname, value):
        self.ext_sem(semname)
        lst = self.acc.setdefault(tname, [])
        lst[:] = []
        lst.append((0, 128, 0, 1 << 62, ("sem", semname, value), True))

    def bump(self):
        self.seg += 1

    def add_common(self, seg, fn):
        self.common.append((seg, fn))

    def op(self, eng, meth, *, deps=(), xreads=(), **kwargs):
        reads, writes = [], []
        for k, v in kwargs.items():
            aps = []
            _collect_aps(v, aps)
            for ap in aps:
                nm = ap.tensor.name
                sp = _span(ap)
                if nm in self.psum_names:
                    sp = (0, 128, 0, 1 << 30)
                is_w = k in ("out", "out_ap", "accum_out", "ap", "outs")
                (writes if is_w else reads).append((nm, sp))
        for nm in xreads:
            reads.append((nm, (0, 128, 0, 1 << 62)))
        idx = len(self.ops)
        depset = set(deps)
        semdeps = set()
        for nm, sp in reads:
            for rec in self.acc.get(nm, ()):
                if rec[5] and _ovl(rec, sp):
                    (semdeps if isinstance(rec[4], tuple) else depset).add(rec[4])
        for nm, sp in writes:
            for rec in self.acc.get(nm, ()):
                if _ovl(rec, sp):
                    (semdeps if isinstance(rec[4], tuple) else depset).add(rec[4])
        is_dma = (meth.startswith("dma") or meth == "indirect_dma_start"
                  or meth == "collective_compute")
        deps2 = []
        for d in sorted(depset):
            dop = self.ops[d]
            if eng == "tensor" and dop["eng"] == "tensor" and not dop["dma"]:
                continue
            deps2.append(d)
            dop["inc"] = True
        semkey = None
        if is_dma:
            semkey = writes[0][0] if writes else "_misc"
            if semkey == "ARENA":
                b0 = writes[0][1][2]
                for anm, (aoff, asz, _, _) in self._allocs.items():
                    if aoff <= b0 < aoff + asz:
                        semkey = "A_" + anm
                        break
        rec = dict(eng=eng, meth=meth, kwargs=kwargs, deps=deps2, dma=is_dma,
                   inc=False, w=writes, sem=None, val=None, semkey=semkey,
                   var=self.var, seg=self.seg, semdeps=sorted(semdeps))
        self.ops.append(rec)
        for nm, sp in writes:
            lst = self.acc.setdefault(nm, [])
            lst[:] = [r for r in lst if not _cover(sp, r)]
            lst.append((sp[0], sp[1], sp[2], sp[3], idx, True))
        for nm, sp in reads:
            self.acc.setdefault(nm, []).append((sp[0], sp[1], sp[2], sp[3], idx, False))
        return idx

    def dbg(self, name, ap):
        shp = [d[1] for d in ap.ap]
        d = self.dram("dbg_" + name, shp, ap.dtype, "ExternalOutput")
        self.dma("sync", out=d[tuple(slice(None) for _ in shp)], in_=ap)

    def mm(self, out, lhsT, rhs, start, stop, **kw):
        return self.op("tensor", "matmul", out=out, lhsT=lhsT, rhs=rhs,
                       start=start, stop=stop, **kw)

    def dma(self, eng, out, in_, **kw):
        return self.op(eng, "dma_start", out=out, in_=in_, **kw)

    def xpose(self, out_psum, in_sbuf, ident):
        return self.op("tensor", "transpose", out=out_psum, in_=in_sbuf,
                       identity=ident)

    def build(self, verbose=False):
        assert not self._fin
        self._fin = True
        nc = self.nc
        eng_sems, dma_sems = {}, {}
        for o in self.ops:
            if o["dma"]:
                key = o["semkey"]
                if key not in dma_sems:
                    dma_sems[key] = self.es.enter_context(
                        nc.semaphore(f"d{len(dma_sems)}_{key}"[:28]))
                o["sem"] = ("d", key)
            elif o["inc"]:
                e = (o["eng"], o["seg"] // 1000)
                if e not in eng_sems:
                    eng_sems[e] = self.es.enter_context(
                        nc.semaphore(f"p_{e[0]}_{e[1]}"))
                o["sem"] = ("e", e)
        nsem = len(eng_sems) + len(dma_sems) + len(self.ext_sems)
        if nsem > 250:
            raise RuntimeError(f"{nsem} semaphores")
        vals = {}
        for o in self.ops:
            if o["sem"]:
                k = (o["var"], o["sem"])
                vals[k] = vals.get(k, 0) + (16 if o["dma"] else 1)
                o["val"] = vals[k]
        if vals:
            mx = max(vals.items(), key=lambda kv: kv[1])
            print(f"[prog] max sem count: {mx[0]} = {mx[1]}")
            for kk, vv in vals.items():
                if vv > 60000:
                    raise RuntimeError(f"sem overflow risk: {kk} = {vv}")
        self.es.enter_context(nc.allow_low_precision(
            reason="bf16 operands/outputs are deliberate (fp32 accumulation in PSUM)"))
        blk = self.es.enter_context(nc.Block())
        allops = self.ops
        variants = sorted({o["var"] for o in self.ops})
        multivar = variants != [None]
        import os as _os
        coal = int(_os.environ.get("SEGCOAL", "1000000"))
        for o in self.ops:
            o["seg"] = o["seg"] // coal
        self.common = [(s // coal, f) for s, f in self.common]
        segs = sorted({o["seg"] for o in self.ops})

        def sem_of(o):
            kind, key = o["sem"]
            return (dma_sems if kind == "d" else eng_sems)[key]

        by = {}
        for o in self.ops:
            by.setdefault((o["seg"], o["var"], o["eng"]), []).append(o)
        fin = {}
        for o in self.ops:
            if o["dma"] and o["w"] and o["w"][0][0] in self.dram_names:
                fin.setdefault(o["var"], {})[o["sem"][1]] = o["val"]

        def replay(h, oplist, waited):
            for o in oplist:
                if o["dma"] and o["val"] > 16:
                    s = sem_of(o)
                    if waited.get(id(s), -1) < o["val"] - 16:
                        h.wait_ge(s, o["val"] - 16)
                        waited[id(s)] = o["val"] - 16
                for sd in o["semdeps"]:
                    _, semname, v = sd
                    s = self.ext_sems[semname]
                    if waited.get(id(s), -1) < v:
                        h.wait_ge(s, v)
                        waited[id(s)] = v
                for d in o["deps"]:
                    dop = allops[d]
                    s, v = sem_of(dop), dop["val"]
                    if waited.get(id(s), -1) < v:
                        h.wait_ge(s, v)
                        waited[id(s)] = v
                try:
                    ins = getattr(h, o["meth"])(**o["kwargs"])
                except Exception:
                    import sys
                    print("FAILING OP:", o["eng"], o["meth"],
                          {k: (tuple(v.shape) if isinstance(v, bass.AP) else v)
                           for k, v in o["kwargs"].items()}, file=sys.stderr)
                    raise
                if o["dma"]:
                    ins.then_inc(sem_of(o), 16)
                elif o["inc"]:
                    ins.then_inc(sem_of(o), 1)

        common_by_seg = {}
        for seg, fn in self.common:
            common_by_seg.setdefault(seg, []).append(fn)

        def make_fn(eng):
            def f(h):
                waited = {v: {} for v in variants}
                if multivar:
                    r = h.alloc_register("hvar")
                    h.reg_mov(r, h.partition_id())
                    h.reg_mod(r, r, 2)
                for seg in segs:
                    if multivar:
                        o0 = by.get((seg, 0, eng), [])
                        o1 = by.get((seg, 1, eng), [])
                        if o0 or o1:
                            with h.If_cmp(r, 0, "IS_EQ"):
                                replay(h, o0, waited[0])
                            with h.Else():
                                replay(h, o1, waited[1])
                            h.end_ifs()
                    else:
                        replay(h, by.get((seg, None, eng), []), waited[None])
                    if eng == "gpsimd":
                        for fn2 in common_by_seg.get(seg, []):
                            fn2(h)
                if eng == "sync":
                    if multivar:
                        with h.If_cmp(r, 0, "IS_EQ"):
                            for key, val in fin.get(0, {}).items():
                                h.wait_ge(dma_sems[key], val)
                        with h.Else():
                            for key, val in fin.get(1, {}).items():
                                h.wait_ge(dma_sems[key], val)
                        h.end_ifs()
                    else:
                        for key, val in fin.get(None, {}).items():
                            h.wait_ge(dma_sems[key], val)
            return f

        engs_used = sorted({o["eng"] for o in self.ops})
        if "gpsimd" not in engs_used and self.common:
            engs_used.append("gpsimd")
        if "sync" not in engs_used:
            engs_used.append("sync")
        for e in self.ENGS:
            if e in engs_used:
                getattr(blk, e)(make_fn(e))
        self.es.close()
        if verbose:
            print(f"[prog] {len(self.ops)} ops, {nsem} sems")
        return nc


def _collect_aps(v, acc):
    if isinstance(v, bass.AP):
        acc.append(v)
    elif isinstance(v, (list, tuple)):
        for e in v:
            _collect_aps(e, acc)


def _ovl(rec, sp):
    return rec[0] < sp[1] and sp[0] < rec[1] and rec[2] < sp[3] and sp[2] < rec[3]


def _cover(sp, rec):
    return sp[0] <= rec[0] and rec[1] <= sp[1] and sp[2] <= rec[2] and rec[3] <= sp[3]



def std_psums(p: Prog):
    psP = [p.psum(f"ps{i}", [128, 512], F32) for i in range(6)]
    psT0 = p.psum("psT0", [128, 128], BF16)
    psT1 = p.psum("psT1", [128, 128], BF16)
    psF0 = psP[5][:, 0:128]
    return psP, psT0, psT1, psF0


class Common:
    def __init__(self, p: Prog):
        self.p = p
        self.ident = p.sbuf("c_ident", [128, 128], BF16)
        self.identf = p.sbuf("c_identf", [128, 128], F32)
        iot_row = p.sbuf("c_iotrow", [128, 128], F32)
        iot_col = p.sbuf("c_iotcol", [128, 1], F32)
        self.iot_row = iot_row
        self.iot_col = iot_col
        self.ones_row = p.sbuf("c_onesrow", [1, 128], BF16)
        self.ones_rowf = p.sbuf("c_onesrowf", [1, 128], F32)
        self.ones_colf = p.sbuf("c_onescolf", [128, 1], F32)
        p.op("gpsimd", "iota", out=iot_row, pattern=[[1, 128]], base=0,
             channel_multiplier=0, allow_small_or_imprecise_dtypes=True)
        p.op("gpsimd", "iota", out=iot_col, pattern=[[0, 1]], base=0,
             channel_multiplier=1, allow_small_or_imprecise_dtypes=True)
        p.op("vector", "tensor_scalar", out=self.ident, in0=iot_row,
             scalar1=iot_col, scalar2=None, op0=ALU.is_equal)
        p.op("vector", "tensor_scalar", out=self.identf, in0=iot_row,
             scalar1=iot_col, scalar2=None, op0=ALU.is_equal)
        p.op("vector", "memset", ap=self.ones_row, constant=1.0)
        p.op("vector", "memset", ap=self.ones_rowf, constant=1.0)
        p.op("vector", "memset", ap=self.ones_colf, constant=1.0)
        self._consts = {}

    def c(self, val):
        if val not in self._consts:
            t = self.p.sbuf(f"c_k{len(self._consts)}", [128, 1], F32)
            self.p.op("vector", "memset", ap=t, constant=float(val))
            self._consts[val] = t
        return self._consts[val]


def emit_ln(p, cm, x_f32, xn_out, s1, s2, mean, var, rstd, nmr, xsq, n=1024):
    p.op("scalar", "activation", out=xsq, in_=x_f32, func=AF.Square)
    p.op("vector", "reduce_sum", out=s1, in_=x_f32, axis=AXL.X)
    p.op("vector", "reduce_sum", out=s2, in_=xsq, axis=AXL.X)
    p.op("vector", "tensor_scalar", out=mean, in0=s1, scalar1=1.0 / n,
         scalar2=None, op0=ALU.mult)
    p.op("vector", "tensor_scalar", out=s2, in0=s2, scalar1=1.0 / n,
         scalar2=None, op0=ALU.mult)
    p.op("vector", "scalar_tensor_tensor", out=var, in0=mean, scalar=mean,
         in1=s2, op0=ALU.mult, op1=ALU.subtract)
    p.op("vector", "tensor_scalar", out=var, in0=var, scalar1=-1.0,
         scalar2=None, op0=ALU.mult)
    p.op("scalar", "activation", out=rstd, in_=var, func=AF.Sqrt, bias=cm.c(1e-5))
    p.op("vector", "reciprocal", out=rstd, in_=rstd)
    p.op("vector", "tensor_scalar", out=nmr, in0=mean, scalar1=rstd,
         scalar2=-1.0, op0=ALU.mult, op1=ALU.mult)
    p.op("vector", "tensor_scalar", out=xn_out, in0=x_f32,
         scalar1=rstd, scalar2=nmr, op0=ALU.mult, op1=ALU.add)



def build_block0(p: Prog, cm: Common, h, x_own_dram, x_other_dram, W,
                 x_out_dram):
    nc_kv = 4 * (h + 1)
    kv_len = 512 * nc_kv
    nt_kv = kv_len // 128

    psP, psT0, psT1, _psF0 = std_psums(p)
    psS, psA, psB, psC, psD, psE = psP

    w1t_d = p.dram("b0_w1t", [8, 4096, 128], BF16, "Internal")
    w2t_d = p.dram("b0_w2t", [32, 1024, 128], BF16, "Internal")

    xn_fm = p.alloc("xn_fm", [128, NCD, T_OWN], BF16)
    lnb = [p.alloc(f"ln{i}", [128, 1], F32) for i in range(6)]
    xsq = p.alloc("xsq", [128, 1024], F32)

    wtile = [p.alloc(f"wtile{i}", [128, 4096], F32) for i in range(2)]
    ttile = [p.alloc(f"ttile{i}", [128, 4096], BF16) for i in range(2)]
    tmpt = p.alloc("tmpt", [128, 4096], F32)
    absa = p.alloc("absa", [128, 2], F32)
    scl = p.alloc("scl", [128, 2], F32)

    for mi, (src, nt, wd, scratch) in enumerate(
            ((W["fw1"], 32, 1024, w1t_d), (W["fw2"], 8, 4096, w2t_d))):
        for i in range(nt):
            b = i % 2
            p.dma("gpsimd", out=wtile[b][:, 0:wd], in_=src[i * 128:(i + 1) * 128, :])
            p.op("scalar", "activation", out=tmpt[:, 0:wd], in_=wtile[b][:, 0:wd],
                 func=AF.Abs, accum_out=absa[:, 1:2])
            if i == 0:
                p.op("vector", "tensor_copy", out=absa[:, 0:1], in_=absa[:, 1:2])
            else:
                p.op("vector", "tensor_tensor", out=absa[:, 0:1],
                     in0=absa[:, 0:1], in1=absa[:, 1:2], op=ALU.add)
        p.mm(psS[0:1, 0:1], cm.ones_colf, absa[:, 0:1], True, True)
        p.op("scalar", "activation", out=scl[0:1, 1:2], in_=psS[0:1, 0:1],
             func=AF.Copy, scale=cm.c(1.0 / (4096.0 * 1024.0))[0:1, :])
        p.op("vector", "tensor_scalar", out=scl[0:1, 1:2], in0=scl[0:1, 1:2],
             scalar1=1e-8, scalar2=None, op0=ALU.max)
        p.op("vector", "reciprocal", out=scl[0:1, 1:2], in_=scl[0:1, 1:2])
        p.mm(psA[:, 0:1], cm.ones_rowf, scl[0:1, 1:2], True, True)
        p.op("vector", "tensor_copy", out=scl[:, 0:1], in_=psA[:, 0:1])
        for i in range(nt):
            b = i % 2
            p.dma("gpsimd", out=wtile[b][:, 0:wd], in_=src[i * 128:(i + 1) * 128, :])
            p.op("vector", "tensor_scalar", out=tmpt[:, 0:wd], in0=wtile[b][:, 0:wd],
                 scalar1=scl[:, 0:1], scalar2=0.5, op0=ALU.mult, op1=ALU.is_gt)
            p.op("gpsimd", "tensor_scalar", out=wtile[b][:, 0:wd], in0=wtile[b][:, 0:wd],
                 scalar1=scl[:, 0:1], scalar2=-0.5, op0=ALU.mult, op1=ALU.is_lt)
            p.op("vector", "tensor_tensor", out=ttile[b][:, 0:wd], in0=tmpt[:, 0:wd],
                 in1=wtile[b][:, 0:wd], op=ALU.subtract)
            p.dma("gpsimd",
                  out=scratch[:, i * 128:(i + 1) * 128, :].rearrange("c p q -> p c q"),
                  in_=ttile[b][:, 0:wd].rearrange("p (c q) -> p c q", q=128))
    p.free("wtile0", "wtile1", "ttile0", "ttile1", "tmpt", "absa", "scl")

    p.bump()
    wq = p.alloc("wq", [128, NCD, 64], BF16)
    wk = p.alloc("wk", [128, NCD, 64], BF16)
    wv = p.alloc("wv", [128, NCD, 1024], BF16)
    q_fm = p.alloc("q_fm", [64, T_OWN], BF16)
    k_fm = p.alloc("k_fm", [64, kv_len], BF16)
    v_tm = p.alloc("v_tm", [128, nt_kv, 1024], BF16)
    xg_fm = p.alloc("xg_fm", [128, NCD, 512], BF16)
    xraw = [p.alloc(f"xraw{i}", [128, 1024], F32) for i in range(2)]
    xn_t = [p.alloc(f"xn_t{i}", [128, 1024], BF16) for i in range(2)]

    p.dma("gpsimd", out=wq, in_=W["q"].rearrange("(c p) f -> p c f", p=128))
    p.dma("gpsimd", out=wk, in_=W["k"].rearrange("(c p) f -> p c f", p=128))
    p.dma("gpsimd", out=wv, in_=W["v"].rearrange("(c p) f -> p c f", p=128))

    for g in range(nc_kv):
        own = g >= nc_kv - 4
        og = g - (nc_kv - 4)
        dst = xn_fm[:, :, og * 512:(og + 1) * 512] if own else xg_fm
        for j in range(4):
            t = g * 4 + j
            buf = xraw[j % 2]
            src = x_own_dram if own else x_other_dram
            row = (og * 4 + j) if own else t
            p.dma("sync", out=buf, in_=src[row * 128:(row + 1) * 128, :])
            xnt = xn_t[j % 2]
            emit_ln(p, cm, buf, xnt, *lnb, xsq)
            for c in range(NCD):
                pb = psT0 if c % 2 == 0 else psT1
                p.xpose(pb, xnt[:, c * 128:(c + 1) * 128], cm.ident)
                p.op("scalar", "copy", out=dst[:, c, j * 128:(j + 1) * 128], in_=pb)
        for c in range(NCD):
            p.mm(psS[0:64, :], wk[:, c, :], dst[:, c, :], c == 0, c == NCD - 1)
        p.op("scalar", "copy", out=k_fm[:, g * 512:(g + 1) * 512], in_=psS[0:64, :])
        if own:
            for c in range(NCD):
                p.mm(psA[0:64, :], wq[:, c, :], dst[:, c, :], c == 0, c == NCD - 1)
            p.op("scalar", "copy", out=q_fm[:, og * 512:(og + 1) * 512], in_=psA[0:64, :])
        for j in range(4):
            for fh in range(2):
                po = psB if fh == 0 else psC
                for c in range(NCD):
                    p.mm(po, dst[:, c, j * 128:(j + 1) * 128],
                         wv[:, c, fh * 512:(fh + 1) * 512], c == 0, c == NCD - 1)
                p.op("vector", "tensor_copy",
                     out=v_tm[:, g * 4 + j, fh * 512:(fh + 1) * 512], in_=po)
    p.free("wq", "wk", "wv", "xg_fm", "xraw0", "xraw1", "xn_t0", "xn_t1")

    s_row = p.alloc("s_row", [128, kv_len], F32)
    p_row = p.alloc("p_row", [128, kv_len], BF16)
    pt_buf = p.alloc("pt_buf", [128, nt_kv * 128], BF16)
    mr = p.alloc("mr", [128, 4], F32)
    rinv = p.alloc("rinv", [128, 1], F32)
    xres = [p.alloc(f"xres{i}", [128, 1024], F32) for i in range(2)]

    for t in range(NT_OWN):
        p.bump()
        q0g = h * T_OWN + t * 128
        n_ch = q0g // 512 + 1
        n_kt = q0g // 128 + 1
        p.dma("sync", out=xres[t % 2], in_=x_own_dram[t * 128:(t + 1) * 128, :])
        for kc in range(n_ch):
            p.mm(psS, q_fm[:, t * 128:(t + 1) * 128],
                 k_fm[:, kc * 512:(kc + 1) * 512], True, True)
            sl = s_row[:, kc * 512:(kc + 1) * 512]
            if kc == n_ch - 1:
                p.op("scalar", "copy", out=sl, in_=psS)
                p.op("gpsimd", "affine_select", out=sl, in_=sl,
                     pattern=[[-1, 512]], base=q0g - kc * 512,
                     channel_multiplier=1, compare_op=ALU.is_ge, fill=NEG)
                src_ts = sl
            else:
                src_ts = psS
            seed = NEGSEED if kc == 0 else mr[:, (kc - 1) % 2:(kc - 1) % 2 + 1]
            p.op("vector", "tensor_scalar", out=sl, in0=src_ts, scalar1=0.0,
                 scalar2=seed, op0=ALU.add, op1=ALU.max,
                 accum_out=mr[:, kc % 2:kc % 2 + 1])
        mfin = mr[:, (n_ch - 1) % 2:(n_ch - 1) % 2 + 1]
        p.op("vector", "tensor_scalar", out=mr[:, 2:3], in0=mfin,
             scalar1=-0.125, scalar2=None, op0=ALU.mult)
        p.op("scalar", "activation", out=p_row[:, 0:n_ch * 512],
             in_=s_row[:, 0:n_ch * 512], func=AF.Exp, scale=cm.c(0.125),
             bias=mr[:, 2:3], accum_out=mr[:, 3:4])
        p.op("vector", "reciprocal", out=rinv, in_=mr[:, 3:4])
        for kt in range(n_kt):
            pb = psT0 if kt % 2 == 0 else psT1
            p.xpose(pb, p_row[:, kt * 128:(kt + 1) * 128], cm.ident)
            p.op("scalar", "copy", out=pt_buf[:, kt * 128:(kt + 1) * 128], in_=pb)
        for fh in range(2):
            po = psA if fh == 0 else psB
            for kt in range(n_kt):
                p.mm(po, pt_buf[:, kt * 128:(kt + 1) * 128],
                     v_tm[:, kt, fh * 512:(fh + 1) * 512], kt == 0, kt == n_kt - 1)
            p.op("vector", "scalar_tensor_tensor",
                 out=xres[t % 2][:, fh * 512:(fh + 1) * 512], in0=po,
                 scalar=rinv, in1=xres[t % 2][:, fh * 512:(fh + 1) * 512],
                 op0=ALU.mult, op1=ALU.add)
        p.dma("sync", out=x_out_dram[t * 128:(t + 1) * 128, :], in_=xres[t % 2])
    p.free("s_row", "p_row", "pt_buf", "mr", "rinv", "xres0", "xres1",
           "q_fm", "k_fm", "v_tm")

    b1s = p.alloc("b1s", [32, 128], F32)
    b1t = p.alloc("b1t", [128, 32], F32)
    b2r = p.alloc("b2r", [1, 1024], BF16)
    b2b = p.alloc("b2b", [128, 1024], F32)
    p.dma("sync", out=b1s, in_=W["b1"].rearrange("(j p) -> j p", p=128))
    p.xpose(psS[:, 0:32], b1s, cm.identf[0:32, 0:32])
    p.op("vector", "tensor_copy", out=b1t, in_=psS[:, 0:32])
    p.dma("gpsimd", out=b2r, in_=W["b2"].unsqueeze(0))
    for fh in range(2):
        p.mm(psA, cm.ones_row, b2r[:, fh * 512:(fh + 1) * 512], True, True)
        p.op("vector", "tensor_copy", out=b2b[:, fh * 512:(fh + 1) * 512], in_=psA)

    w1ts = [p.alloc(f"w1ts{i}", [128, NCD, 512], BF16) for i in range(2)]
    w2ts = [p.alloc(f"w2ts{i}", [128, 1024], BF16) for i in range(2)]
    h_fm = p.alloc("h_fm", [128, 32, 256], BF16)
    xres2 = [p.alloc(f"xres2{i}", [128, 1024], F32) for i in range(2)]

    for g in range(8):
        p.bump()
        tok0 = g * 256
        for jg in range(8):
            wb = w1ts[jg % 2]
            for c in range(NCD):
                p.dma("sync", out=wb[:, c, :],
                      in_=w1t_d[c, jg * 512:(jg + 1) * 512, :], transpose=True)
            for j4 in range(4):
                j = jg * 4 + j4
                ph = psS if j % 2 == 0 else psA
                for c in range(NCD):
                    p.mm(ph[:, 0:256], wb[:, c, j4 * 128:(j4 + 1) * 128],
                         xn_fm[:, c, tok0:tok0 + 256], c == 0, c == NCD - 1)
                p.op("scalar", "activation", out=h_fm[:, j, :], in_=ph[:, 0:256],
                     func=AF.Gelu, bias=b1t[:, j:j + 1])
        banks = (psB, psC, psD, psE)
        for j in range(32):
            wb2 = w2ts[j % 2]
            p.dma("sync", out=wb2, in_=w2t_d[j, :, :], transpose=True)
            for tt in range(2):
                for fh in range(2):
                    p.mm(banks[tt * 2 + fh],
                         h_fm[:, j, tt * 128:(tt + 1) * 128],
                         wb2[:, fh * 512:(fh + 1) * 512], j == 0, j == 31)
        for tt in range(2):
            t = g * 2 + tt
            xr = xres2[tt]
            p.dma("sync", out=xr, in_=x_out_dram[t * 128:(t + 1) * 128, :])
            p.op("vector", "tensor_tensor", out=xr, in0=xr, in1=b2b, op=ALU.add)
            for fh in range(2):
                p.op("vector", "tensor_tensor",
                     out=xr[:, fh * 512:(fh + 1) * 512],
                     in0=xr[:, fh * 512:(fh + 1) * 512],
                     in1=banks[tt * 2 + fh], op=ALU.add)
            p.dma("sync", out=x_out_dram[t * 128:(t + 1) * 128, :], in_=xr)
    p.free("b1s", "b1t", "b2r", "b2b", "w1ts0", "w1ts1", "w2ts0", "w2ts1",
           "h_fm", "xres20", "xres21", "xn_fm")
    for nm in ("ln0", "ln1", "ln2", "ln3", "ln4", "ln5", "xsq"):
        p.free(nm)



def make_wt_scratch(p: Prog, name, w_dram, O, I, row_step=512):
    C = (I + 127) // 128
    O_pad = (O + 15) // 16 * 16
    scr = p.dram(name, [C, O_pad, 128], BF16, "Internal")
    i_full = (I // 128) * 128
    for r0 in range(0, O, row_step):
        nr = min(row_step, O - r0)
        if i_full > 0:
            p.dma("gpsimd",
                  out=scr[0:i_full // 128, r0:r0 + nr, :].rearrange("c p q -> p c q"),
                  in_=w_dram[r0:r0 + nr, 0:i_full].rearrange("p (c q) -> p c q", q=128))
        if I > i_full:
            p.dma("gpsimd",
                  out=scr[C - 1, r0:r0 + nr, 0:I - i_full],
                  in_=w_dram[r0:r0 + nr, i_full:I])
    return scr


def bcast_row(p, cm, psum, out_sb, row_ap, width):
    for o in range(0, width, 512):
        w = min(512, width - o)
        p.mm(psum[:, 0:w], cm.ones_rowf, row_ap[:, o:o + w], True, True)
        p.op("vector", "tensor_copy", out=out_sb[:, o:o + w], in_=psum[:, 0:w])


def emit_softplus(p, cm, x, tmp):
    p.op("scalar", "activation", out=tmp, in_=x, func=AF.Abs)
    p.op("scalar", "activation", out=tmp, in_=tmp, func=AF.Exp, scale=cm.c(-1.0))
    p.op("scalar", "activation", out=tmp, in_=tmp, func=AF.Ln, bias=cm.c(1.0))
    p.op("vector", "tensor_scalar", out=x, in0=x, scalar1=0.0, scalar2=None,
         op0=ALU.max)
    p.op("vector", "tensor_tensor", out=x, in0=x, in1=tmp, op=ALU.add)



NH = 16
HP = 128
NST = 64
CH = 256
DIN = 2048
DPROJ = 4240
FFM = 2730


def build_block1(p: Prog, cm: Common, h, x_own_dram, x_other_dram, W, x_out_dram):
    n_chunks = 8 * (h + 1)
    own0 = n_chunks - 8

    psP, psT0, psT1, psF0 = std_psums(p)

    win_t = make_wt_scratch(p, "b1_wint", W["in_proj"], DPROJ, 1024)
    wout_t = make_wt_scratch(p, "b1_woutt", W["out_proj"], 1024, DIN)
    w1_t = [make_wt_scratch(p, f"b1_w1t{e}", W["W1"][e], FFM, 1024) for e in range(8)]
    w2_t = [make_wt_scratch(p, f"b1_w2t{e}", W["W2"][e], FFM, 1024) for e in range(8)]
    w3_t = [make_wt_scratch(p, f"b1_w3t{e}", W["W3"][e], 1024, FFM) for e in range(8)]

    small = p.alloc("small", [1, 2096], F32)
    dtb_b = p.alloc("dtb_b", [128, 16], F32)
    a_b = p.alloc("a_b", [128, 16], F32)
    dp_b = p.alloc("dp_b", [128, 16], F32)
    nw_b = p.alloc("nw_b", [128, 2048], BF16)
    cw_x = p.alloc("cw_x", [128, 16, 4], F32)
    cb_x = p.alloc("cb_x", [128, 16], F32)
    cw_B = p.alloc("cw_B", [64, 4], F32)
    cb_B = p.alloc("cb_B", [64, 1], F32)
    cw_C = p.alloc("cw_C", [64, 4], F32)
    cb_C = p.alloc("cb_C", [64, 1], F32)
    p.dma("sync", out=small[:, 0:16], in_=W["dt_bias"].unsqueeze(0))
    p.dma("sync", out=small[:, 16:32], in_=W["A_log"].unsqueeze(0))
    p.dma("sync", out=small[:, 32:48], in_=W["Dp"].unsqueeze(0))
    p.dma("sync", out=small[:, 48:2096], in_=W["norm_w"].unsqueeze(0))
    p.dma("sync", out=cw_x,
          in_=W["conv_w"][0:2048, :].rearrange("(c p) w -> p c w", p=128))
    p.dma("sync", out=cb_x, in_=W["conv_b"][0:2048].rearrange("(c p) -> p c", p=128),
          allow_slow_non_contiguous=True)
    p.dma("sync", out=cw_B, in_=W["conv_w"][2048:2112, :])
    p.dma("sync", out=cb_B, in_=W["conv_b"][2048:2112].unsqueeze(1))
    p.dma("sync", out=cw_C, in_=W["conv_w"][2112:2176, :])
    p.dma("sync", out=cb_C, in_=W["conv_b"][2112:2176].unsqueeze(1))
    p.op("scalar", "activation", out=small[:, 16:32], in_=small[:, 16:32], func=AF.Exp)
    p.op("vector", "tensor_scalar", out=small[:, 16:32], in0=small[:, 16:32],
         scalar1=-1.0, scalar2=None, op0=ALU.mult)
    bcast_row(p, cm, psP[0], dtb_b, small[:, 0:16], 16)
    bcast_row(p, cm, psP[0], a_b, small[:, 16:32], 16)
    bcast_row(p, cm, psP[0], dp_b, small[:, 32:48], 16)
    nw_f = p.alloc("nw_f", [128, 2048], F32)
    bcast_row(p, cm, psP[0], nw_f, small[:, 48:2096], 2048)
    p.op("vector", "tensor_copy", out=nw_b, in_=nw_f)
    p.free("nw_f")

    maskc = p.alloc("maskc", [128, 2, 256], F32)
    p.op("vector", "memset", ap=maskc, constant=0.0)
    for i in range(2):
        p.op("gpsimd", "affine_select", out=maskc[:, i, :], in_=maskc[:, i, :],
             pattern=[[1, 256]], base=-i * 128, channel_multiplier=-1,
             compare_op=ALU.is_ge, fill=NEG)
    triu = p.alloc("triu", [128, 128], F32)
    p.op("vector", "tensor_scalar", out=triu, in0=cm.iot_row,
         scalar1=cm.iot_col, scalar2=None, op0=ALU.is_ge)

    xn_own = p.alloc("xn_own", [128, NCD, T_OWN], BF16)
    prev = p.alloc("prev", [128, NH, NST], F32)
    prevT = p.alloc("prevT", [64, NH, HP], BF16)
    p.op("vector", "memset", ap=prev, constant=0.0)
    win_s = p.alloc("win_s", [128, NCD, 2192], BF16)
    for c in range(NCD):
        p.dma("sync", out=win_s[:, c, 0:2048], in_=win_t[c, 2048:4096, :],
              transpose=True)
        p.dma("sync", out=win_s[:, c, 2048:2192], in_=win_t[c, 4096:4240, :],
              transpose=True)
    wz_s = [p.alloc("wz_s0", [128, NCD, 512], BF16)] * 2
    woc = [p.alloc(f"woc{i}", [128, 1024], BF16) for i in range(2)]

    xn_g = p.alloc("xn_g", [128, NCD, CH], BF16)
    xraw = [p.alloc(f"xraw{i}", [128, 1024], F32) for i in range(2)]
    xnt = [p.alloc(f"xnt{i}", [128, 1024], BF16) for i in range(2)]
    lnb = [p.alloc(f"ln{i}", [128, 1], F32) for i in range(6)]
    scr16 = p.alloc("scr16", [128, 4096], F32)
    xsq = scr16[:, 0:2048]
    zbuf = p.alloc("zbuf", [128, 2, DIN], BF16)
    xbc = p.alloc("xbc", [128, 16, CH + 4], BF16)
    bpre = p.alloc("bpre", [64, CH + 4], BF16)
    cpre = p.alloc("cpre", [64, CH + 4], BF16)
    xbs = p.alloc("xbs", [128, 16, CH], BF16)
    b_fm = p.alloc("b_fm", [64, CH], BF16)
    c_fm = p.alloc("c_fm", [64, CH], BF16)
    dts = p.alloc("dts", [128, 2, NH], F32)
    acum = p.alloc("acum", [128, 2, NH], F32)
    nac = p.alloc("nac", [128, 2, NH], F32)
    ea = p.alloc("ea", [128, 2, NH], F32)
    acr = p.alloc("acr", [1, CH], F32)
    awk = p.alloc("awk", [128, 2, NH], F32)
    alb = p.alloc("alb", [128, 16], F32)
    cdb = p.alloc("cdb", [128, 16], F32)
    s_sb = p.alloc("s_sb", [128, 2, CH], F32)
    g_sb = p.alloc("g_sb", [128, 2, CH], BF16)
    xtm = p.alloc("xtm", [128, 2, HP], BF16)
    btm = p.alloc("btm", [128, 2, NST], BF16)
    xw1 = p.alloc("xw1", [128, 2, HP], BF16)
    ysb = p.alloc("ysb", [128, 2, DIN], F32)
    tmp32 = p.alloc("tmp32", [128, 2, NH], F32)
    carry = None
    yfm = p.alloc("yfm", [128, 16, CH], BF16)
    ynb = None
    xres = [scr16[:, 0:1024]] * 2
    ynb = scr16[:, 2048:3072].bitcast(BF16)
    e_sb = scr16[:, 3072:3328]
    ge = scr16[:, 3328:3456].bitcast(BF16)
    carry = scr16[0:1, 3968:4000]

    p.op("vector", "memset", ap=xbc[:, :, 0:4], constant=0.0)
    p.op("vector", "memset", ap=bpre[:, 0:4], constant=0.0)
    p.op("vector", "memset", ap=cpre[:, 0:4], constant=0.0)

    XOF = 2048

    for ck in range(n_chunks):
        own = ck >= own0
        oc = ck - own0
        if own:
            p.bump()
        xn_fm = xn_own[:, :, oc * CH:(oc + 1) * CH] if own else xn_g[:, :, :]
        for j in range(2):
            tglob = ck * 2 + j
            buf = xraw[j % 2]
            if own:
                p.dma("sync", out=buf,
                      in_=x_own_dram[(oc * 2 + j) * 128:(oc * 2 + j + 1) * 128, :])
            else:
                p.dma("sync", out=buf, in_=x_other_dram[tglob * 128:(tglob + 1) * 128, :])
            emit_ln(p, cm, buf, xnt[j % 2], *lnb, xsq[:, 0:1024])
            for c in range(NCD):
                pb = psT0 if c % 2 == 0 else psT1
                p.xpose(pb, xnt[j % 2][:, c * 128:(c + 1) * 128], cm.ident)
                p.op("scalar", "copy", out=xn_fm[:, c, j * 128:(j + 1) * 128], in_=pb)
        if ck > 0:
            p.op("vector", "tensor_copy", out=xbc[:, :, 0:4], in_=xbc[:, :, CH:CH + 4])
            p.op("vector", "tensor_copy", out=bpre[:, 0:4], in_=bpre[:, CH:CH + 4])
            p.op("vector", "tensor_copy", out=cpre[:, 0:4], in_=cpre[:, CH:CH + 4])
        for fo in range(16):
            ph = psP[fo % 2]
            for c in range(NCD):
                p.mm(ph[:, 0:CH], win_s[:, c, fo * 128:(fo + 1) * 128],
                     xn_fm[:, c, :], c == 0, c == NCD - 1)
            p.op("vector", "tensor_copy", out=xbc[:, fo, 4:CH + 4], in_=ph[:, 0:CH])
        for dst, col0 in ((bpre, 2048), (cpre, 2112)):
            ph = psP[2]
            for c in range(NCD):
                p.mm(ph[0:64, 0:CH], win_s[:, c, col0:col0 + 64],
                     xn_fm[:, c, :], c == 0, c == NCD - 1)
            p.op("vector", "tensor_copy", out=dst[:, 4:CH + 4], in_=ph[0:64, 0:CH])
        if own:
            for f4 in range(4):
                wz = wz_s[f4 % 2]
                for c in range(NCD):
                    p.dma("sync", out=wz[:, c, :],
                          in_=win_t[c, f4 * 512:(f4 + 1) * 512, :], transpose=True)
                for j in range(2):
                    ph = psP[3 + j]
                    for c in range(NCD):
                        p.mm(ph, xn_fm[:, c, j * 128:(j + 1) * 128],
                             wz[:, c, :], c == 0, c == NCD - 1)
                    p.op("vector", "tensor_copy", out=zbuf[:, j, f4 * 512:(f4 + 1) * 512],
                         in_=ph)
        for j in range(2):
            ph = psP[2]
            for c in range(NCD):
                p.mm(ph[:, 0:16], xn_fm[:, c, j * 128:(j + 1) * 128],
                     win_s[:, c, 2176:2192], c == 0, c == NCD - 1)
            p.op("vector", "tensor_copy", out=dts[:, j, :], in_=ph[:, 0:16])
        conv_out = scr16.rearrange("p (c t) -> p c t", t=CH)
        for fo in range(16):
            co = conv_out[:, fo, :]
            p.op("vector", "tensor_scalar", out=co, in0=xbc[:, fo, 1:CH + 1],
                 scalar1=cw_x[:, fo, 0:1], scalar2=cb_x[:, fo:fo + 1],
                 op0=ALU.mult, op1=ALU.add)
            for w in range(1, 4):
                p.op("vector", "scalar_tensor_tensor", out=co,
                     in0=xbc[:, fo, 1 + w:CH + 1 + w], scalar=cw_x[:, fo, w:w + 1],
                     in1=co, op0=ALU.mult, op1=ALU.add)
            p.op("scalar", "activation", out=xbs[:, fo, :], in_=co, func=AF.Silu)
        for pre, cw, cb, dstf in ((bpre, cw_B, cb_B, b_fm), (cpre, cw_C, cb_C, c_fm)):
            co = conv_out[0:64, 0, :]
            p.op("vector", "tensor_scalar", out=co, in0=pre[:, 1:CH + 1],
                 scalar1=cw[:, 0:1], scalar2=cb[:, 0:1], op0=ALU.mult, op1=ALU.add)
            for w in range(1, 4):
                p.op("vector", "scalar_tensor_tensor", out=co, in0=pre[:, 1 + w:CH + 1 + w],
                     scalar=cw[:, w:w + 1], in1=co, op0=ALU.mult, op1=ALU.add)
            p.op("scalar", "activation", out=dstf, in_=co, func=AF.Silu)
        dflat = dts.rearrange("p a b -> p (a b)")
        p.op("vector", "tensor_tensor", out=dts[:, 0, :], in0=dts[:, 0, :],
             in1=dtb_b, op=ALU.add)
        p.op("vector", "tensor_tensor", out=dts[:, 1, :], in0=dts[:, 1, :],
             in1=dtb_b, op=ALU.add)
        emit_softplus(p, cm, dflat, tmp32.rearrange("p a b -> p (a b)"))
        for j in range(2):
            p.op("vector", "tensor_tensor", out=tmp32[:, j, :], in0=dts[:, j, :],
                 in1=a_b, op=ALU.mult)
        p.mm(psF0[:, 0:16], triu, tmp32[:, 0, :], True, True)
        p.op("vector", "tensor_copy", out=acum[:, 0, :], in_=psF0[:, 0:16])
        p.mm(psP[2][0:1, 0:16], cm.ones_colf, tmp32[:, 0, :], True, True)
        p.op("scalar", "copy", out=carry[:, 0:16], in_=psP[2][0:1, 0:16])
        p.mm(psF0[:, 0:16], triu, tmp32[:, 1, :], True, False)
        p.mm(psF0[:, 0:16], cm.ones_rowf, carry[:, 0:16], False, True)
        p.op("vector", "tensor_copy", out=acum[:, 1, :], in_=psF0[:, 0:16])
        p.mm(psP[2][0:1, 0:16], cm.ones_colf, tmp32[:, 0, :], True, False)
        p.mm(psP[2][0:1, 0:16], cm.ones_colf, tmp32[:, 1, :], False, True)
        p.op("scalar", "copy", out=carry[:, 16:32], in_=psP[2][0:1, 0:16])
        p.mm(psP[2][:, 0:16], cm.ones_rowf, carry[:, 16:32], True, True)
        p.op("vector", "tensor_copy", out=alb, in_=psP[2][:, 0:16])
        p.op("scalar", "activation", out=cdb, in_=alb, func=AF.Exp)
        p.op("vector", "tensor_scalar", out=nac.rearrange("p a b -> p (a b)"),
             in0=acum.rearrange("p a b -> p (a b)"), scalar1=-1.0, scalar2=None,
             op0=ALU.mult)
        p.op("scalar", "activation", out=ea.rearrange("p a b -> p (a b)"),
             in_=acum.rearrange("p a b -> p (a b)"), func=AF.Exp)
        for j in range(2):
            p.op("vector", "tensor_tensor", out=awk[:, j, :], in0=alb,
                 in1=acum[:, j, :], op=ALU.subtract)
        p.op("scalar", "activation", out=awk.rearrange("p a b -> p (a b)"),
             in_=awk.rearrange("p a b -> p (a b)"), func=AF.Exp)
        for j in range(2):
            p.op("vector", "tensor_tensor", out=awk[:, j, :], in0=awk[:, j, :],
                 in1=dts[:, j, :], op=ALU.mult)
        for i in range(2):
            pb = psT0 if i == 0 else psT1
            p.xpose(pb[:, 0:64], b_fm[:, i * 128:(i + 1) * 128], cm.ident[0:64, 0:64])
            p.op("scalar", "copy", out=btm[:, i, :], in_=pb[:, 0:64])
        if own:
            for i in range(2):
                p.mm(psP[2][:, 0:CH], b_fm[:, i * 128:(i + 1) * 128], c_fm,
                     True, True)
                p.op("vector", "tensor_copy", out=s_sb[:, i, :], in_=psP[2][:, 0:CH])
            for hh in range(NH):
                p.op("tensor", "transpose", out=psF0[0:64, 0:128], in_=prev[:, hh, :],
                     identity=cm.identf)
                p.op("vector", "tensor_copy", out=prevT[:, hh, :], in_=psF0[0:64, 0:128])
        for hh in range(NH):
            for j in range(2):
                pb = psT0 if j == 0 else psT1
                p.xpose(pb, xbs[:, hh, j * 128:(j + 1) * 128], cm.ident)
                p.op("scalar", "copy", out=xtm[:, j, :], in_=pb)
            if own:
                for j in range(2):
                    p.op("tensor", "transpose", out=psF0[0:1, 0:128],
                         in_=acum[:, j, hh:hh + 1], identity=cm.identf)
                    p.op("scalar", "copy", out=acr[:, j * 128:(j + 1) * 128],
                         in_=psF0[0:1, 0:128])
                for i in range(2):
                    p.mm(psP[3][:, 0:CH], cm.ones_rowf, acr, True, True)
                    p.op("vector", "tensor_tensor", out=e_sb, in0=psP[3][:, 0:CH],
                         in1=maskc[:, i, :], op=ALU.add)
                    p.op("scalar", "activation", out=ge, in_=e_sb, func=AF.Exp,
                         bias=nac[:, i, hh:hh + 1])
                    p.op("vector", "tensor_tensor", out=g_sb[:, i, :], in0=s_sb[:, i, :],
                         in1=ge, op=ALU.mult)
                for j in range(2):
                    p.op("vector", "tensor_scalar", out=xw1[:, j, :], in0=xtm[:, j, :],
                         scalar1=dts[:, j, hh:hh + 1], scalar2=None, op0=ALU.mult)
                for qs in range(2):
                    ysl = ysb[:, qs, hh * 128:(hh + 1) * 128]
                    for i in range(2):
                        p.mm(psP[4][:, 0:128], g_sb[:, i, qs * 128:(qs + 1) * 128],
                             xw1[:, i, :], i == 0, i == 1)
                    p.mm(psP[2][:, 0:128], c_fm[:, qs * 128:(qs + 1) * 128],
                         prevT[:, hh, :], True, True)
                    p.op("vector", "tensor_copy", out=ysl, in_=psP[4][:, 0:128])
                    p.op("vector", "scalar_tensor_tensor", out=ysl,
                         in0=psP[2][:, 0:128], scalar=ea[:, qs, hh:hh + 1], in1=ysl,
                         op0=ALU.mult, op1=ALU.add)
                    p.op("vector", "scalar_tensor_tensor", out=ysl,
                         in0=xtm[:, qs, :], scalar=dp_b[:, hh:hh + 1], in1=ysl,
                         op0=ALU.mult, op1=ALU.add)
            for j in range(2):
                p.op("vector", "tensor_scalar", out=xw1[:, j, :], in0=xtm[:, j, :],
                     scalar1=awk[:, j, hh:hh + 1], scalar2=None, op0=ALU.mult)
            for i in range(2):
                p.mm(psP[3][:, 0:NST], xw1[:, i, :], btm[:, i, :], i == 0, i == 1)
            p.op("vector", "scalar_tensor_tensor", out=prev[:, hh, :],
                 in0=prev[:, hh, :], scalar=cdb[:, hh:hh + 1], in1=psP[3][:, 0:NST],
                 op0=ALU.mult, op1=ALU.add)
        if DBG and ck == own0:
            p.dbg("xnfm", xn_fm)
            p.dbg("xbs", xbs)
            p.dbg("bfm", b_fm)
            p.dbg("cfm", c_fm)
            p.dbg("dts", dts)
            p.dbg("acum", acum)
            p.dbg("awk", awk)
            p.dbg("cdb", cdb)
            p.dbg("prev", prev)
            if own:
                p.dbg("ssb", s_sb)
                p.dbg("ysb", ysb)
                p.dbg("zbuf", zbuf)
        if not own:
            continue
        zf = zbuf.rearrange("p a b -> p (a b)")
        p.op("scalar", "activation", out=zf, in_=zf, func=AF.Silu)
        p.op("vector", "tensor_tensor", out=ysb.rearrange("p a b -> p (a b)"),
             in0=ysb.rearrange("p a b -> p (a b)"), in1=zf, op=ALU.mult)
        for j in range(2):
            p.op("scalar", "activation", out=xsq, in_=ysb[:, j, :], func=AF.Square)
            p.op("vector", "reduce_sum", out=lnb[0], in_=xsq, axis=AXL.X)
            p.op("vector", "tensor_scalar", out=lnb[1], in0=lnb[0],
                 scalar1=1.0 / DIN, scalar2=None, op0=ALU.mult)
            p.op("scalar", "activation", out=lnb[2], in_=lnb[1], func=AF.Sqrt,
                 bias=cm.c(1e-5))
            p.op("vector", "reciprocal", out=lnb[2], in_=lnb[2])
            p.op("vector", "tensor_scalar", out=ynb, in0=ysb[:, j, :],
                 scalar1=lnb[2], scalar2=None, op0=ALU.mult)
            p.op("vector", "tensor_tensor", out=ynb, in0=ynb,
                 in1=nw_b, op=ALU.mult)
            for c in range(16):
                pb = psT0 if c % 2 == 0 else psT1
                p.xpose(pb, ynb[:, c * 128:(c + 1) * 128], cm.ident)
                p.op("scalar", "copy", out=yfm[:, c, j * 128:(j + 1) * 128], in_=pb)
        for c in range(16):
            p.dma("sync", out=woc[c % 2], in_=wout_t[c, :, :], transpose=True)
            for j in range(2):
                for fh in range(2):
                    p.mm(psP[j * 2 + fh], yfm[:, c, j * 128:(j + 1) * 128],
                         woc[c % 2][:, fh * 512:(fh + 1) * 512], c == 0, c == 15)
        for j in range(2):
            trow = oc * 2 + j
            p.dma("sync", out=xres[j], in_=x_own_dram[trow * 128:(trow + 1) * 128, :])
            for fh in range(2):
                p.op("vector", "tensor_tensor", out=xres[j][:, fh * 512:(fh + 1) * 512],
                     in0=xres[j][:, fh * 512:(fh + 1) * 512], in1=psP[j * 2 + fh],
                     op=ALU.add)
            p.dma("sync", out=x_out_dram[trow * 128:(trow + 1) * 128, :], in_=xres[j])

    p.free("xn_g", "zbuf", "xbc", "bpre", "cpre", "xbs", "b_fm", "c_fm",
           "dts", "acum", "nac", "ea", "acr", "awk", "alb", "cdb", "s_sb",
           "g_sb", "xtm", "btm", "xw1", "ysb", "tmp32", "yfm",
           "prev", "prevT", "win_s", "wz_s0", "woc0", "woc1",
           "maskc", "triu",
           "dtb_b", "a_b", "dp_b", "nw_b", "cw_x", "cb_x", "cw_B", "cb_B",
           "cw_C", "cb_C", "small")

    if DBG:
        dbg_xa = p.dram("dbg_xattn", [2048, 1024], F32, "ExternalOutput")
        p.dma("sync", out=dbg_xa[:, :], in_=x_out_dram[:, :])
    xres2 = [p.alloc("xres2", [128, 1024], F32)] * 2
    build_moe(p, cm, xn_own, x_own_dram, x_out_dram, W["Wg"], w1_t, w2_t, w3_t,
              psP, psT0, psT1, psF0, xres2, lnb, xsq)
    p.free("xn_own", "xraw0", "xraw1", "xnt0", "xnt1", "xres2", "scr16")
    for i in range(6):
        p.free(f"ln{i}")


def build_moe(p, cm, xn_own, x_own_dram, x_out_dram, wg_raw, w1_t, w2_t, w3_t,
              psP, psT0, psT1, psF0, xres, lnb, xsq):
    NJ = 22
    jsz = [128] * 21 + [42]
    wg_s = p.alloc("wg_s", [128, NCD, 8], BF16)
    wg_n = p.alloc("wg_n", [8, 1024], BF16)
    glog = p.alloc("glog", [128, NT_OWN, 8], F32)
    gat = p.alloc("gat", [128, NT_OWN, 8], F32)
    p.dma("gpsimd", out=wg_n, in_=wg_raw)
    for c in range(NCD):
        p.xpose(psT0[:, 0:8], wg_n[:, c * 128:(c + 1) * 128], cm.ident[0:8, 0:8])
        p.op("scalar", "copy", out=wg_s[:, c, :], in_=psT0[:, 0:8])
    for t in range(NT_OWN):
        ph = psP[4]
        for c in range(NCD):
            p.mm(ph[:, 0:8], xn_own[:, c, t * 128:(t + 1) * 128], wg_s[:, c, :],
                 c == 0, c == NCD - 1)
        p.op("vector", "tensor_copy", out=glog[:, t, :], in_=ph[:, 0:8])
    m1 = lnb[0]
    m2 = lnb[1]
    rs = lnb[2]
    for t in range(NT_OWN):
        g = glog[:, t, :]
        sel = gat[:, t, :]
        p.op("vector", "reduce_max", out=m1, in_=g, axis=AXL.X)
        p.op("vector", "tensor_scalar", out=sel, in0=g, scalar1=m1, scalar2=None,
             op0=ALU.is_ge)
        p.op("vector", "scalar_tensor_tensor", out=xsq[:, 0:8], in0=sel,
             scalar=NEG, in1=g, op0=ALU.mult, op1=ALU.add)
        p.op("vector", "reduce_max", out=m2, in_=xsq[:, 0:8], axis=AXL.X)
        p.op("vector", "tensor_scalar", out=sel, in0=g, scalar1=m2, scalar2=None,
             op0=ALU.is_ge)
        p.op("vector", "tensor_scalar", out=m1, in0=m1, scalar1=-1.0, scalar2=None,
             op0=ALU.mult)
        p.op("scalar", "activation", out=xsq[:, 0:8], in_=g, func=AF.Exp, bias=m1)
        p.op("vector", "tensor_tensor", out=sel, in0=sel, in1=xsq[:, 0:8],
             op=ALU.mult)
        p.op("vector", "reduce_sum", out=rs, in_=sel, axis=AXL.X)
        p.op("vector", "reciprocal", out=rs, in_=rs)
        p.op("vector", "tensor_scalar", out=sel, in0=sel, scalar1=rs, scalar2=None,
             op0=ALU.mult)
    if 'DBG' in globals() and DBG:
        pass
    try:
        _dbg = DBG
    except NameError:
        _dbg = False
    if _dbg:
        p.dbg("glog", glog)
        p.dbg("gat", gat)
    hid = p.alloc("hid", [128, NJ, 1024], BF16)
    w3s = p.alloc("w3s", [128, NJ, 1024], BF16)
    w1s = [p.alloc(f"w1s{i}", [128, NCD, 256], BF16) for i in range(2)]
    w2s = [p.alloc(f"w2s{i}", [128, NCD, 256], BF16) for i in range(2)]
    sil = p.alloc("sil", [128, 1024], F32)
    for e in range(8):
        p.bump()
        for j in range(NJ):
            p.dma("sync", out=w3s[:, j, :], in_=w3_t[e][j, :, :], transpose=True)
        for half in range(2):
            tok0 = half * 1024
            for jg in range((NJ + 1) // 2):
                wb1, wb2 = w1s[jg % 2], w2s[jg % 2]
                j0 = jg * 2
                nj = min(2, NJ - j0)
                nr = sum(jsz[j0:j0 + nj])
                nrl = (nr + 15) // 16 * 16
                for c in range(NCD):
                    p.dma("sync", out=wb1[:, c, 0:nrl],
                          in_=w1_t[e][c, j0 * 128:j0 * 128 + nrl, :], transpose=True)
                    p.dma("sync", out=wb2[:, c, 0:nrl],
                          in_=w2_t[e][c, j0 * 128:j0 * 128 + nrl, :], transpose=True)
                for ji in range(nj):
                    j = j0 + ji
                    n = jsz[j]
                    for tg in range(2):
                        ph1, ph2 = psP[tg], psP[2 + tg]
                        for c in range(NCD):
                            p.mm(ph1[0:n, :], wb1[:, c, ji * 128:ji * 128 + n],
                                 xn_own[:, c, tok0 + tg * 512:tok0 + (tg + 1) * 512],
                                 c == 0, c == NCD - 1)
                        for c in range(NCD):
                            p.mm(ph2[0:n, :], wb2[:, c, ji * 128:ji * 128 + n],
                                 xn_own[:, c, tok0 + tg * 512:tok0 + (tg + 1) * 512],
                                 c == 0, c == NCD - 1)
                        p.op("scalar", "activation", out=sil[0:n, tg * 512:(tg + 1) * 512],
                             in_=ph1[0:n, :], func=AF.Silu)
                        p.op("vector", "tensor_tensor",
                             out=hid[0:n, j, tg * 512:(tg + 1) * 512],
                             in0=sil[0:n, tg * 512:(tg + 1) * 512], in1=ph2[0:n, :],
                             op=ALU.mult)
            for grp in range(4):
                banks = (psP[0], psP[1], psP[2], psP[3])
                for j in range(NJ):
                    n = jsz[j]
                    for tt in range(2):
                        t = half * 8 + grp * 2 + tt
                        lt = (grp * 2 + tt) * 128
                        for fh in range(2):
                            p.mm(banks[tt * 2 + fh],
                                 hid[0:n, j, lt:lt + 128],
                                 w3s[0:n, j, fh * 512:(fh + 1) * 512],
                                 j == 0, j == NJ - 1)
                for tt in range(2):
                    t = half * 8 + grp * 2 + tt
                    xr = xres[tt]
                    p.dma("sync", out=xr, in_=x_out_dram[t * 128:(t + 1) * 128, :])
                    for fh in range(2):
                        p.op("vector", "scalar_tensor_tensor",
                             out=xr[:, fh * 512:(fh + 1) * 512],
                             in0=banks[tt * 2 + fh], scalar=gat[:, t, e:e + 1],
                             in1=xr[:, fh * 512:(fh + 1) * 512],
                             op0=ALU.mult, op1=ALU.add)
                    p.dma("sync", out=x_out_dram[t * 128:(t + 1) * 128, :], in_=xr)
    p.free("wg_s", "wg_n", "glog", "gat", "hid", "w3s", "w1s0", "w1s1", "w2s0", "w2s1", "sil")



def emit_fp4(p, cm, v_f32, out_bf16, Wd, qsc, eng="vector"):
    G = Wd // 16
    bmax, rec, an, acc1, accB = qsc
    bm, rc = bmax[:, 0:G], rec[:, 0:G]
    a1, aB, anv = acc1[:, 0:Wd], accB[:, 0:Wd], an[:, 0:Wd]
    vg = v_f32.rearrange("p (g i) -> p g i", i=16)
    p.op("vector", "tensor_reduce", out=bm, in_=vg, axis=AXL.X, op=ALU.max,
         apply_absolute_value=True)
    p.op("vector", "tensor_scalar", out=bm, in0=bm, scalar1=1.0 / 6.0,
         scalar2=1e-12, op0=ALU.mult, op1=ALU.max)
    p.op("vector", "reciprocal", out=rc, in_=bm)
    bcr = rc.unsqueeze(2).broadcast_to([128, G, 16])
    p.op(eng, "tensor_tensor", out=anv.rearrange("p (g i) -> p g i", i=16),
         in0=vg, in1=bcr, op=ALU.mult)
    p.op("scalar", "activation", out=a1, in_=anv, func=AF.Abs)
    p.op(eng, "tensor_scalar", out=aB, in0=a1, scalar1=2.5, scalar2=None,
         op0=ALU.is_gt)
    p.op("vector", "scalar_tensor_tensor", out=aB, in0=a1, scalar=3.5, in1=aB,
         op0=ALU.is_gt, op1=ALU.add)
    p.op(eng, "tensor_scalar", out=anv, in0=a1, scalar1=5.0, scalar2=2.0,
         op0=ALU.is_gt, op1=ALU.mult)
    p.op(eng, "tensor_tensor", out=aB, in0=aB, in1=anv, op=ALU.add)
    p.op(eng, "tensor_scalar", out=anv, in0=a1, scalar1=0.25, scalar2=None,
         op0=ALU.is_gt)
    p.op("vector", "scalar_tensor_tensor", out=anv, in0=a1, scalar=0.75, in1=anv,
         op0=ALU.is_gt, op1=ALU.add)
    p.op("vector", "scalar_tensor_tensor", out=anv, in0=a1, scalar=1.25, in1=anv,
         op0=ALU.is_gt, op1=ALU.add)
    p.op("vector", "scalar_tensor_tensor", out=a1, in0=a1, scalar=1.75, in1=anv,
         op0=ALU.is_gt, op1=ALU.add)
    p.op("vector", "scalar_tensor_tensor", out=a1, in0=a1, scalar=0.5, in1=aB,
         op0=ALU.mult, op1=ALU.add)
    p.op("scalar", "activation", out=aB, in_=v_f32, func=AF.Sign)
    p.op(eng, "tensor_tensor", out=a1, in0=a1, in1=aB, op=ALU.mult)
    bcm = bm.unsqueeze(2).broadcast_to([128, G, 16])
    p.op(eng, "tensor_tensor", out=out_bf16.rearrange("p (g i) -> p g i", i=16),
         in0=a1.rearrange("p (g i) -> p g i", i=16), in1=bcm, op=ALU.mult)


def build_block2(p: Prog, cm: Common, h, x_own_dram, x_other_dram, W, x_out_dram):
    own0 = h * T_OWN
    NLAT = 32 * (h + 1)
    KVT = 17

    psP, psT0, psT1, psF0 = std_psums(p)

    wgq_d = p.dram("b2_wgq", [8, 2816, 128], BF16, "Internal")
    wuq_d = p.dram("b2_wuq", [8, 2816, 128], BF16, "Internal")
    wdq_d = p.dram("b2_wdq", [22, 1024, 128], BF16, "Internal")
    xq_d = p.dram("b2_xq", [128, NCD, T_OWN], BF16, "Internal")

    xn_fm = p.alloc("xn_fm", [128, NCD, T_OWN], BF16)
    k_fm2 = p.alloc("k_fm2", [128, 8, KVT * 128], BF16)
    v_tm = p.alloc("v_tm", [128, KVT, 1024], BF16)
    xg_fm = p.alloc("xg_fm", [128, NCD, 128], BF16)
    lnb = [p.alloc(f"ln{i}", [128, 1], F32) for i in range(6)]
    scrA = p.alloc("scrA", [128, 4096], F32)
    xsq = scrA[:, 0:1024]
    xraw = [p.alloc(f"xraw{i}", [128, 1024], F32) for i in range(2)]
    xnf = p.alloc("xnf", [128, 1024], F32)
    xnb = p.alloc("xnb", [128, 1024], BF16)
    xqt = p.alloc("xqt", [128, 1024], BF16)
    xqf = p.alloc("xqf", [128, NCD, 128], BF16)
    poolA = p.alloc("poolA", [128, 16, 32], BF16)
    pooled = [p.alloc(f"pooled{i}", [32, 1024], BF16) for i in range(2)]
    pool_f = p.alloc("pool_f", [128, NCD, 64], BF16)
    qsc = (p.alloc("qsc0", [128, 256], F32), p.alloc("qsc1", [128, 256], F32),
           p.alloc("qsc2", [128, 1024], F32), p.alloc("qsc3", [128, 1024], F32),
           p.alloc("qsc4", [128, 1024], F32))

    p.op("vector", "memset", ap=poolA, constant=1.0 / 64.0)
    for j in range(16):
        for half in range(2):
            p.op("gpsimd", "affine_select", out=poolA[half * 64:(half + 1) * 64, j, :],
                 in_=poolA[half * 64:(half + 1) * 64, j, :],
                 pattern=[[1, 32]], base=-(2 * j + half), channel_multiplier=0,
                 compare_op=ALU.is_equal, fill=0.0)
    lmask = p.alloc("lmask", [128, 2, 256], BF16)
    p.op("vector", "memset", ap=lmask, constant=1.0)
    for v in range(2):
        p.op("gpsimd", "affine_select", out=lmask[:, v, :], in_=lmask[:, v, :],
             pattern=[[1, 256]], base=0, channel_multiplier=-1,
             compare_op=ALU.is_ge, fill=0.0)
        p.op("gpsimd", "affine_select", out=lmask[:, v, :], in_=lmask[:, v, :],
             pattern=[[-1, 256]], base=127, channel_multiplier=1,
             compare_op=ALU.is_ge, fill=0.0)
    p.op("gpsimd", "affine_select", out=lmask[:, 1, :], in_=lmask[:, 1, :],
         pattern=[[1, 256]], base=-128, channel_multiplier=0,
         compare_op=ALU.is_ge, fill=0.0)

    brow = p.alloc("brow", [64, 1024], F32)
    kqb = p.alloc("kqb", [128, 8], F32)
    qqb = p.alloc("qqb", [128, 8], F32)
    rqb = p.alloc("rqb", [128, 2], F32)
    vb_b = p.alloc("vb_b", [128, 1024], F32)
    ob_b = p.alloc("ob_b", [128, 1024], F32)
    rob_b = p.alloc("rob_b", [128, 1024], F32)
    rkvb = p.alloc("rkvb", [1, 512], BF16)
    glw_b = p.alloc("glw_b", [128, 1024], F32)
    grw_b = p.alloc("grw_b", [128, 1024], F32)
    p.dma("sync", out=brow[0:8, 0:128], in_=W["qkv_b"][0:1024].rearrange("(a b) -> a b", b=128))
    p.xpose(psF0[:, 0:8], brow[0:8, 0:128], cm.identf[0:8, 0:8])
    p.op("vector", "tensor_copy", out=qqb, in_=psF0[:, 0:8])
    p.dma("sync", out=brow[0:8, 0:128], in_=W["qkv_b"][1024:2048].rearrange("(a b) -> a b", b=128))
    p.xpose(psF0[:, 0:8], brow[0:8, 0:128], cm.identf[0:8, 0:8])
    p.op("vector", "tensor_copy", out=kqb, in_=psF0[:, 0:8])
    p.dma("sync", out=brow[0:1, :], in_=W["qkv_b"][2048:3072].unsqueeze(0))
    bcast_row(p, cm, psP[0], vb_b, brow[0:1, :], 1024)
    p.dma("sync", out=brow[0:1, :], in_=W["out_b"].unsqueeze(0))
    bcast_row(p, cm, psP[0], ob_b, brow[0:1, :], 1024)
    p.dma("sync", out=brow[0:1, :], in_=W["rout_b"].unsqueeze(0))
    bcast_row(p, cm, psP[0], rob_b, brow[0:1, :], 1024)
    p.dma("sync", out=brow[0:2, 0:128], in_=W["rq_b"].rearrange("(a b) -> a b", b=128))
    p.xpose(psF0[:, 0:2], brow[0:2, 0:128], cm.identf[0:2, 0:2])
    p.op("vector", "tensor_copy", out=rqb, in_=psF0[:, 0:2])
    p.dma("gpsimd", out=rkvb, in_=W["rkv_b"].unsqueeze(0))
    for wsrc, dstb in ((W["gl_w"], glw_b), (W["gr_w"], grw_b)):
        p.dma("sync", out=brow[0:64, 0:1], in_=wsrc)
        p.xpose(psF0[0:1, 0:64], brow[0:64, 0:1], cm.identf[0:64, 0:64])
        p.op("vector", "tensor_copy", out=brow[0:1, 0:64], in_=psF0[0:1, 0:64])
        p.mm(psP[0][:, 0:64], cm.ones_rowf, brow[0:1, 0:64], True, True)
        p.op("vector", "tensor_copy", out=scrA[:, 0:64], in_=psP[0][:, 0:64])
        p.op("vector", "tensor_copy",
             out=dstb.rearrange("p (r d) -> p r d", d=64),
             in_=scrA[:, 0:64].unsqueeze(1).broadcast_to([128, 16, 64]))

    n_ln = 16 * (h + 1)
    for j in range(n_ln):
        own = j >= n_ln - 16
        oj = j - (n_ln - 16)
        buf = xraw[j % 2]
        if own:
            p.dma("sync", out=buf, in_=x_own_dram[oj * 128:(oj + 1) * 128, :])
        else:
            p.dma("sync", out=buf, in_=x_other_dram[j * 128:(j + 1) * 128, :])
        emit_ln(p, cm, buf, xnf, *lnb, xsq)
        p.op("vector", "tensor_copy", out=xnb, in_=xnf)
        bb = 0 if own else 2
        jj = oj if own else j
        for fh in range(2):
            p.mm(psP[bb + fh][0:32, :], poolA[:, jj, :],
                 xnb[:, fh * 512:(fh + 1) * 512], jj == 0, jj == 15)
        if own:
            for c in range(NCD):
                pb = psT0 if c % 2 == 0 else psT1
                p.xpose(pb, xnb[:, c * 128:(c + 1) * 128], cm.ident)
                p.op("scalar", "copy", out=xn_fm[:, c, oj * 128:(oj + 1) * 128], in_=pb)
            emit_fp4(p, cm, xnf, xqt, 1024, qsc, eng="gpsimd" if oj % 2 else "vector")
            for c in range(NCD):
                pb = psT0 if c % 2 == 0 else psT1
                p.xpose(pb, xqt[:, c * 128:(c + 1) * 128], cm.ident)
                p.op("scalar", "copy", out=xqf[:, c, :], in_=pb)
            p.dma("sync", out=xq_d[:, :, oj * 128:(oj + 1) * 128], in_=xqf)
        elif j == 15:
            for c in range(NCD):
                pb = psT0 if c % 2 == 0 else psT1
                p.xpose(pb, xnb[:, c * 128:(c + 1) * 128], cm.ident)
                p.op("scalar", "copy", out=xg_fm[:, c, :], in_=pb)
    npre = NLAT - 32
    for fh in range(2):
        p.op("vector", "tensor_copy", out=pooled[0][:, fh * 512:(fh + 1) * 512],
             in_=psP[0 + fh][0:32, :])
        if h == 1:
            p.op("vector", "tensor_copy", out=pooled[1][:, fh * 512:(fh + 1) * 512],
                 in_=psP[2 + fh][0:32, :])
    for c in range(NCD):
        if h == 1:
            p.xpose(psT0[:, 0:32], pooled[1][:, c * 128:(c + 1) * 128],
                    cm.ident[0:32, 0:32])
            p.op("scalar", "copy", out=pool_f[:, c, 0:32], in_=psT0[:, 0:32])
        p.xpose(psT1[:, 0:32], pooled[0][:, c * 128:(c + 1) * 128],
                cm.ident[0:32, 0:32])
        p.op("scalar", "copy", out=pool_f[:, c, npre:npre + 32], in_=psT1[:, 0:32])
    wrkv = p.alloc("wrkv", [128, NCD, 512], BF16)
    rkv_tm = p.alloc("rkv_tm", [64, 512], BF16)
    rk_fm2 = p.alloc("rk_fm2", [128, 2, 64], BF16)
    p.dma("gpsimd", out=wrkv, in_=W["rkv_w"].rearrange("(c p) f -> p c f", p=128))
    for c in range(NCD):
        p.mm(psP[4][0:NLAT, :], pool_f[:, c, 0:NLAT], wrkv[:, c, :], c == 0, False)
    p.mm(psP[4][0:NLAT, :], cm.ones_row[:, 0:NLAT], rkvb, False, True)
    p.op("vector", "tensor_copy", out=rkv_tm[0:NLAT, :], in_=psP[4][0:NLAT, :])
    for hp in range(2):
        p.xpose(psT0[:, 0:NLAT], rkv_tm[0:NLAT, hp * 128:(hp + 1) * 128],
                cm.ident[0:NLAT, 0:NLAT])
        p.op("scalar", "copy", out=rk_fm2[:, hp, 0:NLAT], in_=psT0[:, 0:NLAT])
    p.free("wrkv", "poolA", "pooled0", "pooled1", "pool_f", "brow",
           "xraw0", "xraw1", "xnf", "xnb", "xqt", "xqf")

    p.bump()
    wkv = p.alloc("wkv", [128, NCD, 1024], BF16)
    p.dma("gpsimd", out=wkv, in_=W["qkv_w"][:, 1024:2048].rearrange("(c p) f -> p c f", p=128))
    if h == 0:
        p.op("vector", "memset", ap=k_fm2[:, :, 0:128], constant=0.0)
        p.op("vector", "memset", ap=v_tm[:, 0, :], constant=0.0)
    for hp in range(8):
        if h == 1:
            p.mm(psP[0][:, 0:128], wkv[:, 0, hp * 128:(hp + 1) * 128], xg_fm[:, 0, :],
                 True, False)
            for c in range(1, NCD):
                p.mm(psP[0][:, 0:128], wkv[:, c, hp * 128:(hp + 1) * 128],
                     xg_fm[:, c, :], False, c == NCD - 1)
            p.op("vector", "tensor_scalar", out=k_fm2[:, hp, 0:128],
                 in0=psP[0][:, 0:128], scalar1=kqb[:, hp:hp + 1], scalar2=None,
                 op0=ALU.add)
        for g in range(4):
            ph = psP[g % 2]
            for c in range(NCD):
                p.mm(ph, wkv[:, c, hp * 128:(hp + 1) * 128],
                     xn_fm[:, c, g * 512:(g + 1) * 512], c == 0, c == NCD - 1)
            p.op("vector", "tensor_scalar", out=k_fm2[:, hp, 128 + g * 512:128 + (g + 1) * 512],
                 in0=ph, scalar1=kqb[:, hp:hp + 1], scalar2=None, op0=ALU.add)
    p.dma("gpsimd", out=wkv, in_=W["qkv_w"][:, 2048:3072].rearrange("(c p) f -> p c f", p=128))
    if h == 1:
        for fh in range(2):
            ph = psP[fh]
            for c in range(NCD):
                p.mm(ph, xg_fm[:, c, :], wkv[:, c, fh * 512:(fh + 1) * 512],
                     c == 0, c == NCD - 1)
            p.op("vector", "tensor_tensor", out=v_tm[:, 0, fh * 512:(fh + 1) * 512],
                 in0=ph, in1=vb_b[:, fh * 512:(fh + 1) * 512], op=ALU.add)
    for oj in range(16):
        for fh in range(2):
            ph = psP[fh]
            for c in range(NCD):
                p.mm(ph, xn_fm[:, c, oj * 128:(oj + 1) * 128],
                     wkv[:, c, fh * 512:(fh + 1) * 512], c == 0, c == NCD - 1)
            p.op("vector", "tensor_tensor", out=v_tm[:, 1 + oj, fh * 512:(fh + 1) * 512],
                 in0=ph, in1=vb_b[:, fh * 512:(fh + 1) * 512], op=ALU.add)
    p.free("wkv", "xg_fm")

    wqp = p.alloc("wqp", [128, NCD, 1024], BF16)
    wrq = p.alloc("wrq", [128, NCD, 256], BF16)
    woc = [p.alloc(f"woc{i}", [128, 1024], BF16) for i in range(2)]
    wro = p.alloc("wro", [128, 2, 1024], BF16)
    p.dma("gpsimd", out=wqp, in_=W["qkv_w"][:, 0:1024].rearrange("(c p) f -> p c f", p=128))
    p.dma("gpsimd", out=wrq, in_=W["rq_w"].rearrange("(c p) f -> p c f", p=128))
    p.dma("gpsimd", out=wro, in_=W["rout_w"].rearrange("(c p) f -> p c f", p=128))
    qf = p.alloc("qf", [128, 128], BF16)
    s_sb = p.alloc("s_sb", [128, 256], F32)
    p_sb = p.alloc("p_sb", [128, 256], BF16)
    pt_sb = p.alloc("pt_sb", [128, 256], BF16)
    loc = p.alloc("loc", [128, 1024], BF16)
    rem = p.alloc("rem", [128, 1024], F32)
    lat = p.alloc("lat", [128, 256], BF16)
    latf = p.alloc("latf", [128, 2, 128], BF16)
    gsc = p.alloc("gsc", [128, 16], F32)
    mixb = scrA[:, 2048:2560].bitcast(BF16)
    mixf = scrA[:, 2560:3072].bitcast(BF16).rearrange("p (c q) -> p c q", q=128)
    lm01 = p.alloc("lm01", [128, 64], BF16)
    xres = scrA[:, 3072:4096]
    mr = lnb

    for j in range(NT_OWN):
        p.bump()
        for hp in range(8):
            ph = psP[0]
            for c in range(NCD):
                p.mm(ph[:, 0:128], wqp[:, c, hp * 128:(hp + 1) * 128],
                     xn_fm[:, c, j * 128:(j + 1) * 128], c == 0, c == NCD - 1)
            p.op("vector", "tensor_scalar", out=qf, in0=ph[:, 0:128],
                 scalar1=qqb[:, hp:hp + 1], scalar2=None, op0=ALU.add)
            for lh in range(2):
                hh = hp * 2 + lh
                base = lh * 64
                p.mm(psP[1][:, 0:256], qf[base:base + 64, :],
                     k_fm2[base:base + 64, hp, j * 128:j * 128 + 256], True, True)
                p.op("vector", "tensor_scalar", out=s_sb, in0=psP[1][:, 0:256],
                     scalar1=0.0, scalar2=NEGSEED, op0=ALU.add, op1=ALU.max,
                     accum_out=mr[0])
                p.op("vector", "tensor_scalar", out=mr[1], in0=mr[0],
                     scalar1=-0.125, scalar2=None, op0=ALU.mult)
                p.op("scalar", "activation", out=p_sb, in_=s_sb, func=AF.Exp,
                     scale=cm.c(0.125), bias=mr[1])
                mv = 1 if (h == 0 and j == 0) else 0
                p.op("vector", "scalar_tensor_tensor", out=p_sb, in0=p_sb,
                     scalar=1.0, in1=lmask[:, mv, :], op0=ALU.mult, op1=ALU.mult,
                     accum_out=mr[2])
                p.op("vector", "tensor_scalar", out=mr[2], in0=mr[2],
                     scalar1=1e-30, scalar2=None, op0=ALU.max)
                p.op("vector", "reciprocal", out=mr[2], in_=mr[2])
                for kt in range(2):
                    pb = psT0 if kt == 0 else psT1
                    p.xpose(pb, p_sb[:, kt * 128:(kt + 1) * 128], cm.ident)
                    p.op("scalar", "copy", out=pt_sb[:, kt * 128:(kt + 1) * 128], in_=pb)
                for kt in range(2):
                    p.mm(psP[2][:, 0:64], pt_sb[:, kt * 128:(kt + 1) * 128],
                         v_tm[:, j + kt, hh * 64:(hh + 1) * 64], kt == 0, kt == 1)
                p.op("vector", "tensor_scalar", out=loc[:, hh * 64:(hh + 1) * 64],
                     in0=psP[2][:, 0:64], scalar1=mr[2], scalar2=None, op0=ALU.mult)
        C0 = own0 // 64 + 2 * j
        p.op("vector", "memset", ap=lm01, constant=1.0)
        for half in range(2):
            p.op("gpsimd", "affine_select", out=lm01[half * 64:(half + 1) * 64, :],
                 in_=lm01[half * 64:(half + 1) * 64, :], pattern=[[-1, 64]],
                 base=C0 + half - 1, channel_multiplier=0, compare_op=ALU.is_ge,
                 fill=0.0)
        for hp in range(2):
            ph = psP[0]
            for c in range(NCD):
                p.mm(ph[:, 0:128], wrq[:, c, hp * 128:(hp + 1) * 128],
                     xn_fm[:, c, j * 128:(j + 1) * 128], c == 0, c == NCD - 1)
            p.op("vector", "tensor_scalar", out=qf, in0=ph[:, 0:128],
                 scalar1=rqb[:, hp:hp + 1], scalar2=None, op0=ALU.add)
            for lh in range(2):
                hh = hp * 2 + lh
                base = lh * 64
                p.mm(psP[1][:, 0:NLAT], qf[base:base + 64, :],
                     rk_fm2[base:base + 64, hp, 0:NLAT], True, True)
                p.op("vector", "tensor_scalar", out=s_sb[:, 0:NLAT],
                     in0=psP[1][:, 0:NLAT], scalar1=0.0, scalar2=NEGSEED,
                     op0=ALU.add, op1=ALU.max, accum_out=mr[0])
                p.op("vector", "tensor_scalar", out=mr[1], in0=mr[0],
                     scalar1=-0.125, scalar2=None, op0=ALU.mult)
                p.op("scalar", "activation", out=p_sb[:, 0:NLAT], in_=s_sb[:, 0:NLAT],
                     func=AF.Exp, scale=cm.c(0.125), bias=mr[1])
                p.op("vector", "scalar_tensor_tensor", out=p_sb[:, 0:NLAT],
                     in0=p_sb[:, 0:NLAT], scalar=1.0, in1=lm01[:, 0:NLAT],
                     op0=ALU.mult, op1=ALU.mult, accum_out=mr[2])
                p.op("vector", "tensor_scalar", out=mr[2], in0=mr[2],
                     scalar1=1e-30, scalar2=None, op0=ALU.max)
                p.op("vector", "reciprocal", out=mr[2], in_=mr[2])
                p.xpose(psT0[0:NLAT, 0:128], p_sb[:, 0:NLAT], cm.ident)
                p.op("scalar", "copy", out=pt_sb[0:NLAT, 0:128], in_=psT0[0:NLAT, 0:128])
                p.mm(psP[2][:, 0:64], pt_sb[0:NLAT, 0:128],
                     rkv_tm[0:NLAT, 256 + hh * 64:256 + (hh + 1) * 64], True, True)
                p.op("vector", "tensor_scalar", out=lat[:, hh * 64:(hh + 1) * 64],
                     in0=psP[2][:, 0:64], scalar1=mr[2], scalar2=None, op0=ALU.mult)
        for lc in range(2):
            pb = psT0 if lc == 0 else psT1
            p.xpose(pb, lat[:, lc * 128:(lc + 1) * 128], cm.ident)
            p.op("scalar", "copy", out=latf[:, lc, :], in_=pb)
        for fh in range(2):
            ph = psP[3 + fh]
            for lc in range(2):
                p.mm(ph, latf[:, lc, :], wro[:, lc, fh * 512:(fh + 1) * 512],
                     lc == 0, lc == 1)
            p.op("vector", "tensor_tensor", out=rem[:, fh * 512:(fh + 1) * 512],
                 in0=ph, in1=rob_b[:, fh * 512:(fh + 1) * 512], op=ALU.add)
        p.op("vector", "tensor_tensor", out=scrA[:, 0:1024], in0=loc, in1=glw_b,
             op=ALU.mult)
        p.op("vector", "tensor_reduce", out=gsc,
             in_=scrA[:, 0:1024].rearrange("p (r d) -> p r d", d=64),
             axis=AXL.X, op=ALU.add)
        p.op("vector", "tensor_tensor", out=scrA[:, 0:1024], in0=rem, in1=grw_b,
             op=ALU.mult)
        p.op("vector", "tensor_reduce", out=scrA[:, 1024:1040],
             in_=scrA[:, 0:1024].rearrange("p (r d) -> p r d", d=64),
             axis=AXL.X, op=ALU.add)
        p.op("vector", "tensor_tensor", out=gsc, in0=gsc, in1=scrA[:, 1024:1040],
             op=ALU.add)
        p.op("scalar", "activation", out=gsc, in_=gsc, func=AF.Sigmoid)
        p.op("vector", "tensor_tensor", out=scrA[:, 0:1024], in0=loc, in1=rem,
             op=ALU.subtract)
        for hh in range(16):
            p.op("vector", "scalar_tensor_tensor", out=mixb[:, hh * 64:(hh + 1) * 64],
                 in0=scrA[:, hh * 64:(hh + 1) * 64], scalar=gsc[:, hh:hh + 1],
                 in1=rem[:, hh * 64:(hh + 1) * 64], op0=ALU.mult, op1=ALU.add)
        for c in range(NCD):
            pb = psT0 if c % 2 == 0 else psT1
            p.xpose(pb, mixb[:, c * 128:(c + 1) * 128], cm.ident)
            p.op("scalar", "copy", out=mixf[:, c, :], in_=pb)
        p.dma("sync", out=xres, in_=x_own_dram[j * 128:(j + 1) * 128, :])
        p.op("vector", "tensor_tensor", out=xres, in0=xres, in1=ob_b, op=ALU.add)
        for c in range(NCD):
            p.dma("gpsimd", out=woc[c % 2],
                  in_=W["out_w"][c * 128:(c + 1) * 128, :])
            for fh in range(2):
                p.mm(psP[3 + fh], mixf[:, c, :], woc[c % 2][:, fh * 512:(fh + 1) * 512],
                     c == 0, c == NCD - 1)
        for fh in range(2):
            p.op("vector", "tensor_tensor", out=xres[:, fh * 512:(fh + 1) * 512],
                 in0=xres[:, fh * 512:(fh + 1) * 512], in1=psP[3 + fh], op=ALU.add)
        p.dma("sync", out=x_out_dram[j * 128:(j + 1) * 128, :], in_=xres)
    p.free("wqp", "wrq", "woc0", "woc1", "wro", "qf", "s_sb", "p_sb", "pt_sb", "loc",
           "rem", "lat", "latf", "gsc", "lm01",
           "k_fm2", "v_tm", "rkv_tm", "rk_fm2", "lmask",
           "kqb", "qqb", "rqb", "vb_b", "rob_b", "rkvb", "glw_b", "grw_b")

    p.bump()
    for i in range(5):
        p.free(f"qsc{i}")
    qsc = (p.alloc("qsc0", [128, 256], F32), p.alloc("qsc1", [128, 256], F32),
           scrA[:, 0:2816], p.alloc("qsc3", [128, 2816], F32),
           p.alloc("qsc4", [128, 2816], F32))
    wtile = [p.alloc(f"wtile{i}", [128, 2816], F32) for i in range(2)]
    wqt = p.alloc("wqt", [128, 2816], BF16)
    for src_w, nt, wd, scratch in ((W["w_gate"], 22, 1024, wgq_d),
                                   (W["w_up"], 22, 1024, wuq_d),
                                   (W["w_down"], 8, 2816, wdq_d)):
        for i in range(nt):
            b = i % 2
            p.dma("gpsimd", out=wtile[b][:, 0:wd], in_=src_w[i * 128:(i + 1) * 128, :])
            emit_fp4(p, cm, wtile[b][:, 0:wd], wqt[:, 0:wd], wd, qsc,
                     eng="gpsimd" if i % 2 else "vector")
            p.dma("gpsimd",
                  out=scratch[:, i * 128:(i + 1) * 128, :].rearrange("c p q -> p c q"),
                  in_=wqt[:, 0:wd].rearrange("p (c q) -> p c q", q=128))
    p.free("wtile0", "wtile1", "wqt", "xn_fm")

    xq_fm = p.alloc("xq_fm", [128, NCD, T_OWN], BF16)
    p.dma("sync", out=xq_fm, in_=xq_d)
    wdq_s = p.alloc("wdq_s", [128, 22, 1024], BF16)
    for jc in range(22):
        p.dma("sync", out=wdq_s[:, jc, :], in_=wdq_d[jc, :, :], transpose=True)
    wgs = [p.alloc(f"wgs{i}", [128, NCD, 512], BF16) for i in range(2)]
    wus = [p.alloc(f"wus{i}", [128, NCD, 512], BF16) for i in range(2)]
    h2 = p.alloc("h2", [128, 2, 2816], F32)
    xrE = scrA[:, 2816:3840]
    hq_tm = p.alloc("hq_tm", [128, 2816], BF16)
    hq_fm = p.alloc("hq_fm", [128, 22, 128], BF16)
    sil = p.alloc("sil", [128, 512], F32)
    fcs = [512, 512, 512, 512, 512, 256]
    for jg in range(8):
        p.bump()
        for fc in range(6):
            f0 = fc * 512
            n = fcs[fc]
            wg, wu = wgs[fc % 2], wus[fc % 2]
            for c in range(NCD):
                p.dma("sync", out=wg[:, c, 0:n], in_=wgq_d[c, f0:f0 + n, :],
                      transpose=True)
                p.dma("sync", out=wu[:, c, 0:n], in_=wuq_d[c, f0:f0 + n, :],
                      transpose=True)
            for jj in range(2):
                j = jg * 2 + jj
                ph1, ph2 = psP[jj * 2], psP[jj * 2 + 1]
                for c in range(NCD):
                    p.mm(ph1[:, 0:n], xq_fm[:, c, j * 128:(j + 1) * 128],
                         wg[:, c, 0:n], c == 0, c == NCD - 1)
                for c in range(NCD):
                    p.mm(ph2[:, 0:n], xq_fm[:, c, j * 128:(j + 1) * 128],
                         wu[:, c, 0:n], c == 0, c == NCD - 1)
                p.op("scalar", "activation", out=sil[:, 0:n], in_=ph1[:, 0:n],
                     func=AF.Silu)
                p.op("vector", "tensor_tensor", out=h2[:, jj, f0:f0 + n],
                     in0=sil[:, 0:n], in1=ph2[:, 0:n], op=ALU.mult)
        for jj in range(2):
            j = jg * 2 + jj
            emit_fp4(p, cm, h2[:, jj, :], hq_tm, 2816, qsc,
                     eng="gpsimd" if jj else "vector")
            for jc in range(22):
                pb = psT0 if jc % 2 == 0 else psT1
                p.xpose(pb, hq_tm[:, jc * 128:(jc + 1) * 128], cm.ident)
                p.op("scalar", "copy", out=hq_fm[:, jc, :], in_=pb)
            for fh in range(2):
                ph = psP[4] if fh == 0 else psP[jj * 2]
                for jc in range(22):
                    p.mm(ph, hq_fm[:, jc, :], wdq_s[:, jc, fh * 512:(fh + 1) * 512],
                         jc == 0, jc == 21)
            xr = xrE
            p.dma("sync", out=xr, in_=x_out_dram[j * 128:(j + 1) * 128, :])
            for fh in range(2):
                ph = psP[4] if fh == 0 else psP[jj * 2]
                p.op("vector", "tensor_tensor", out=xr[:, fh * 512:(fh + 1) * 512],
                     in0=xr[:, fh * 512:(fh + 1) * 512], in1=ph, op=ALU.add)
            p.dma("sync", out=x_out_dram[j * 128:(j + 1) * 128, :], in_=xr)
    p.free("xq_fm", "qsc0", "qsc1", "qsc3", "qsc4", "wdq_s", "wgs0", "wgs1",
           "wus0", "wus1", "h2", "hq_tm", "hq_fm", "sil", "scrA", "ob_b")
    for i in range(6):
        p.free(f"ln{i}")



def build_block3(p: Prog, cm: Common, h, x_own_dram, x_other_dram, W, x_out_dram):
    kv_len = 2048 * (h + 1)
    own0 = h * 2048
    NKV = kv_len // 128

    psP, psT0, psT1, psF0 = std_psums(p)

    xnpre_d = p.dram("b3_xnpre", [128, NCD, 2048], BF16, "Internal")
    p_d = p.dram("b3_p", [128, 16, 256], BF16, "Internal")

    xn_own = p.alloc("xn_own", [128, NCD, T_OWN], BF16)
    cosb = p.alloc("cosb", [128, 512], BF16)
    sinb = p.alloc("sinb", [128, 512], BF16)
    lnb = [p.alloc(f"ln{i}", [128, 1], F32) for i in range(6)]
    scrA = p.alloc("scrA", [128, 4096], F32)
    xsq = scrA[:, 0:1024]
    xraw = [p.alloc(f"xraw{i}", [128, 1024], F32) for i in range(2)]
    xnf = p.alloc("xnf", [128, 1024], F32)
    xnb = p.alloc("xnb", [128, 1024], BF16)
    xn_g = p.alloc("xn_g", [128, NCD, 512], BF16)
    p_tm = p.alloc("p_tm", [128, 256], BF16)
    skb = p.alloc("skb", [128, 256], F32)
    svb = p.alloc("svb", [128, 256], F32)
    wsk1 = p.alloc("wsk1", [128, NCD, 16], BF16)
    wsk2 = p.alloc("wsk2", [16, 256], BF16)
    t1f = p.alloc("t1f", [16, 512], BF16)
    wkt = p.alloc("wkt", [128, 256], F32)
    carry = p.alloc("carry", [1, 256], F32)
    mask01d = p.alloc("mask01d", [128, 128], BF16)
    brow = p.alloc("brow", [32, 1024], F32)

    p.op("vector", "memset", ap=mask01d, constant=1.0)
    p.op("gpsimd", "affine_select", out=mask01d, in_=mask01d, pattern=[[1, 128]],
         base=0, channel_multiplier=-1, compare_op=ALU.is_ge, fill=0.0)
    p.dma("sync", out=brow[0:1, 0:256], in_=W["sk_b2"].unsqueeze(0))
    bcast_row(p, cm, psP[0], skb, brow[0:1, 0:256], 256)
    p.dma("sync", out=brow[0:1, 0:256], in_=W["sv_b2"].unsqueeze(0))
    bcast_row(p, cm, psP[0], svb, brow[0:1, 0:256], 256)


    def ln_tile(src_dram, row, out_fm_slice, stash=None):
        buf = xraw[row % 2]
        p.dma("sync", out=buf, in_=src_dram[row * 128:(row + 1) * 128, :])
        emit_ln(p, cm, buf, xnf, *lnb, xsq)
        p.op("vector", "tensor_copy", out=xnb, in_=xnf)
        for c in range(NCD):
            pb = psT0 if c % 2 == 0 else psT1
            p.xpose(pb, xnb[:, c * 128:(c + 1) * 128], cm.ident)
            p.op("scalar", "copy", out=out_fm_slice[:, c, :], in_=pb)
        if stash is not None:
            p.dma("sync", out=stash, in_=out_fm_slice)

    def rope_fm(dst_fm, src_fm, h4, wdt):
        c0, c1 = 2 * h4, 2 * h4 + 1
        cs = cosb[:, 0:wdt]
        sn = sinb[:, 0:wdt]
        t1 = scrA[:, 0:wdt]
        t2 = scrA[:, 1024:1024 + wdt]
        p.op("vector", "tensor_tensor", out=t1, in0=src_fm[:, c0, :], in1=cs, op=ALU.mult)
        p.op("vector", "tensor_tensor", out=t2, in0=src_fm[:, c1, :], in1=sn, op=ALU.mult)
        p.op("vector", "tensor_tensor", out=dst_fm[:, c0, :], in0=t1, in1=t2,
             op=ALU.subtract)
        p.op("vector", "tensor_tensor", out=t1, in0=src_fm[:, c1, :], in1=cs, op=ALU.mult)
        p.op("vector", "tensor_tensor", out=t2, in0=src_fm[:, c0, :], in1=sn, op=ALU.mult)
        p.op("vector", "tensor_tensor", out=dst_fm[:, c1, :], in0=t1, in1=t2, op=ALU.add)

    def skv_chain(w1ap, w2ap, bias_b, wout_tm, zout, g, xn_src, first_carry,
                  mod4=False):
        ph = psP[2]
        for c in range(NCD):
            p.mm(ph[0:16, :], wsk1[:, c, :], xn_src[:, c, :], c == 0, c == NCD - 1)
        p.op("vector", "tensor_copy", out=t1f, in_=ph[0:16, :])
        for jj in range(4):
            tg = g * 4 + jj
            ph2 = psP[3]
            p.mm(ph2[:, 0:256], t1f[:, jj * 128:(jj + 1) * 128], wsk2, True, True)
            p.op("vector", "tensor_tensor", out=wkt, in0=ph2[:, 0:256], in1=bias_b,
                 op=ALU.add)
            p.op("vector", "tensor_scalar", out=wkt, in0=wkt, scalar1=32.0,
                 scalar2=-32.0, op0=ALU.min, op1=ALU.max)
            p.op("scalar", "activation", out=wkt, in_=wkt, func=AF.Exp)
            p.op("vector", "tensor_copy", out=wout_tm[:, tg % 4 if mod4 else tg, :],
                 in_=wkt)
            p.mm(psP[4][:, 0:256], triu3, wkt, True, False)
            p.mm(psP[4][:, 0:256], cm.ones_rowf, carry, False, True)
            if zout is not None:
                p.op("vector", "reciprocal", out=zout[:, tg - (NKV - 16), :],
                     in_=psP[4][:, 0:256])
            p.mm(psP[2][0:1, 0:256], cm.ones_colf, wkt, True, False)
            p.mm(psP[2][0:1, 0:256], cm.ones_rowf[0:1, 0:1], carry, False, True)
            p.op("scalar", "copy", out=carry, in_=psP[2][0:1, 0:256])

    triu3 = p.alloc("triu3", [128, 128], F32)
    p.op("vector", "tensor_scalar", out=triu3, in0=cm.iot_row,
         scalar1=cm.iot_col, scalar2=None, op0=ALU.is_ge)
    kraw = scrA[:, 2048:4096].bitcast(BF16).rearrange("p (c q) -> p c q", q=512)

    k_fm = p.alloc("k_fm", [128, NCD, kv_len], BF16)
    wk_tm = p.alloc("wk_tm", [128, NKV, 256], BF16)
    zk_own = p.alloc("zk_own", [128, 16, 256], BF16)
    wkv_a = p.alloc("wkv_a", [128, 4, 1024], BF16)
    wkv_b = p.alloc("wkv_b", [128, 4, 1024], BF16)
    p.dma("gpsimd", out=wkv_a, in_=W["Wk"][0:512, :].rearrange("(c p) f -> p c f", p=128))
    p.dma("gpsimd", out=wkv_b, in_=W["Wk"][512:1024, :].rearrange("(c p) f -> p c f", p=128))
    p.dma("gpsimd", out=wsk1, in_=W["sk_w1"].rearrange("(c p) f -> p c f", p=128))
    p.dma("gpsimd", out=wsk2, in_=W["sk_w2"])
    p.op("vector", "memset", ap=carry, constant=0.0)
    for g in range(kv_len // 512):
        own = g >= (kv_len - 2048) // 512
        og = g - (kv_len - 2048) // 512
        if own:
            src = xn_own[:, :, og * 512:(og + 1) * 512]
            for jj in range(4):
                ln_tile(x_own_dram, og * 4 + jj, _fm_slice(xn_own, og * 4 + jj))
        else:
            src = xn_g
            for jj in range(4):
                ln_tile(x_other_dram, g * 4 + jj, _fm_slice(xn_g, jj))
            p.dma("sync", out=xnpre_d[:, :, g * 512:(g + 1) * 512], in_=xn_g)
        for hc in range(NCD):
            ph = psP[hc % 2]
            for c in range(NCD):
                wslc = wkv_a[:, c, :] if c < 4 else wkv_b[:, c - 4, :]
                p.mm(ph, wslc[:, hc * 128:(hc + 1) * 128], src[:, c, :],
                     c == 0, c == NCD - 1)
            p.op("scalar", "copy", out=kraw[:, hc, :], in_=ph)
        p.dma("gpsimd", out=cosb, in_=W["cos_t"][:, g * 512:(g + 1) * 512])
        p.dma("gpsimd", out=sinb, in_=W["sin_t"][:, g * 512:(g + 1) * 512])
        kdst = k_fm[:, :, g * 512:(g + 1) * 512]
        for h4 in range(4):
            rope_fm(kdst, kraw, h4, 512)
        skv_chain(None, None, skb, wk_tm, zk_own if own else None, g, src, g == 0)
    p.free("wkv_a", "wkv_b")

    p.free("brow")
    qraw = p.alloc("qraw", [128, NCD, 128], BF16)
    qfm = p.alloc("qfm", [128, NCD, 128], BF16)
    s_bf = p.alloc("s_bf", [128, 128], BF16)
    lgt = p.alloc("lgt", [128, 256], F32)
    mxg = p.alloc("mxg", [128, 4], F32)
    wq_w = p.alloc("wq_w", [128, NCD, 1024], BF16)
    p.dma("gpsimd", out=wq_w, in_=W["Wq"].rearrange("(c p) f -> p c f", p=128))
    for t in range(NT_OWN):
        p.bump()
        n_st = own0 // 128 + t + 1
        for hc in range(NCD):
            ph = psP[0]
            for c in range(NCD):
                p.mm(ph[:, 0:128], wq_w[:, c, hc * 128:(hc + 1) * 128],
                     xn_own[:, c, t * 128:(t + 1) * 128], c == 0, c == NCD - 1)
            p.op("scalar", "copy", out=qraw[:, hc, :], in_=ph[:, 0:128])
        q0g = own0 + t * 128
        p.dma("gpsimd", out=cosb[:, 0:128], in_=W["cos_t"][:, q0g:q0g + 128])
        p.dma("gpsimd", out=sinb[:, 0:128], in_=W["sin_t"][:, q0g:q0g + 128])
        for h4 in range(4):
            rope_fm(qfm, qraw, h4, 128)
        for h4 in range(4):
            for st in range(n_st):
                for ci in range(2):
                    p.mm(psP[1][:, 0:128], k_fm[:, 2 * h4 + ci, st * 128:(st + 1) * 128],
                         qfm[:, 2 * h4 + ci, :], ci == 0, ci == 1)
                if st == n_st - 1:
                    p.op("vector", "scalar_tensor_tensor", out=s_bf,
                         in0=psP[1][:, 0:128], scalar=1.0, in1=mask01d,
                         op0=ALU.mult, op1=ALU.mult)
                else:
                    p.op("scalar", "copy", out=s_bf, in_=psP[1][:, 0:128])
                p.mm(psP[2][:, h4 * 64:(h4 + 1) * 64], s_bf,
                     wk_tm[:, st, h4 * 64:(h4 + 1) * 64], st == 0, st == n_st - 1)
        p.op("vector", "scalar_tensor_tensor", out=lgt, in0=psP[2][:, 0:256],
             scalar=1.0 / 16.0, in1=zk_own[:, t, :], op0=ALU.mult, op1=ALU.mult)
        p.op("vector", "tensor_reduce", out=mxg,
             in_=lgt.rearrange("p (r d) -> p r d", d=64), axis=AXL.X, op=ALU.max)
        p.op("vector", "tensor_tensor", out=lgt.rearrange("p (r d) -> p r d", d=64),
             in0=lgt.rearrange("p (r d) -> p r d", d=64),
             in1=mxg.unsqueeze(2).broadcast_to([128, 4, 64]), op=ALU.subtract)
        p.op("scalar", "activation", out=lgt, in_=lgt, func=AF.Exp)
        p.op("vector", "tensor_reduce", out=mxg,
             in_=lgt.rearrange("p (r d) -> p r d", d=64), axis=AXL.X, op=ALU.add)
        p.op("vector", "reciprocal", out=mxg, in_=mxg)
        p.op("vector", "tensor_tensor", out=p_tm.rearrange("p (r d) -> p r d", d=64),
             in0=lgt.rearrange("p (r d) -> p r d", d=64),
             in1=mxg.unsqueeze(2).broadcast_to([128, 4, 64]), op=ALU.mult)
        p.dma("sync", out=p_d[:, t, :], in_=p_tm)
    p.free("k_fm", "wk_tm", "zk_own", "wq_w", "qraw", "qfm", "cosb", "sinb")

    p.bump()
    v_tm = p.alloc("v_tm", [128, NKV, 1024], BF16)
    wv_fm2 = p.alloc("wv_fm2", [128, 2, kv_len], BF16)
    zv_own = p.alloc("zv_own", [128, 16, 256], BF16)
    wv_stage = p.alloc("wv_stage", [128, 4, 256], BF16)
    wkv_a = p.alloc("wkv_a2", [128, 4, 1024], BF16)
    wkv_b = p.alloc("wkv_b2", [128, 4, 1024], BF16)
    p.dma("gpsimd", out=wkv_a, in_=W["Wv"][0:512, :].rearrange("(c p) f -> p c f", p=128))
    p.dma("gpsimd", out=wkv_b, in_=W["Wv"][512:1024, :].rearrange("(c p) f -> p c f", p=128))
    p.dma("gpsimd", out=wsk1, in_=W["sv_w1"].rearrange("(c p) f -> p c f", p=128))
    p.dma("gpsimd", out=wsk2, in_=W["sv_w2"])
    p.op("vector", "memset", ap=carry, constant=0.0)
    for g in range(kv_len // 512):
        own = g >= (kv_len - 2048) // 512
        og = g - (kv_len - 2048) // 512
        if own:
            src = xn_own[:, :, og * 512:(og + 1) * 512]
        else:
            src = xn_g
            p.dma("sync", out=xn_g, in_=xnpre_d[:, :, g * 512:(g + 1) * 512])
        for jj in range(4):
            tg = g * 4 + jj
            for fh in range(2):
                ph = psP[fh]
                for c in range(NCD):
                    wslc = wkv_a[:, c, :] if c < 4 else wkv_b[:, c - 4, :]
                    p.mm(ph, src[:, c, jj * 128:(jj + 1) * 128],
                         wslc[:, fh * 512:(fh + 1) * 512], c == 0, c == NCD - 1)
                p.op("vector", "tensor_copy", out=v_tm[:, tg, fh * 512:(fh + 1) * 512],
                     in_=ph)
        skv_chain(None, None, svb, wv_stage, zv_own if own else None, g, src,
                  g == 0, mod4=True)
        for jj in range(4):
            tg = g * 4 + jj
            for hp in range(2):
                pb = psT0 if hp == 0 else psT1
                p.xpose(pb, wv_stage[:, jj, hp * 128:(hp + 1) * 128], cm.ident)
                p.op("scalar", "copy", out=wv_fm2[:, hp, tg * 128:(tg + 1) * 128], in_=pb)
    p.free("wkv_a2", "wkv_b2")

    p.free("xraw0", "xraw1", "xnf", "xnb", "xn_g", "skb", "svb", "wsk1",
           "wsk2", "t1f", "wkt", "wv_stage")
    gnw = p.alloc("gnw", [128, 1024], F32)
    p.dma("sync", out=carry, in_=W["gnorm_w"].unsqueeze(0))
    bcast_row(p, cm, psP[0], gnw[:, 0:256], carry, 256)
    p.op("vector", "tensor_copy",
         out=gnw.rearrange("p (r d) -> p r d", d=256),
         in_=gnw[:, 0:256].unsqueeze(1).broadcast_to([128, 4, 256]))
    wg_w = p.alloc("wg_w", [128, NCD, 1024], BF16)
    wo_wA = p.alloc("wo_wA", [128, 4, 1024], BF16)
    wo_wB = p.alloc("wo_wB", [128, 4, 1024], BF16)
    p.dma("gpsimd", out=wg_w, in_=W["Wg"].rearrange("(c p) f -> p c f", p=128))
    p.dma("gpsimd", out=wo_wA, in_=W["Wo"][0:512, :].rearrange("(c p) f -> p c f", p=128))
    p.dma("gpsimd", out=wo_wB, in_=W["Wo"][512:1024, :].rearrange("(c p) f -> p c f", p=128))
    pt_fm2 = p.alloc("pt_fm2", [128, 2, 128], BF16)
    a_bf = p.alloc("a_bf", [128, 128], BF16)
    osb = p.alloc("osb", [128, 1024], F32)
    gsb = p.alloc("gsb", [128, 1024], F32)
    ptt = p.alloc("ptt", [128, 256], BF16)
    ofin = scrA[:, 1024:1536].bitcast(BF16)
    ofm = scrA[:, 1536:2048].bitcast(BF16).rearrange("p (c q) -> p c q", q=128)
    xres = scrA[:, 2048:3072]
    for t in range(NT_OWN):
        n_st = own0 // 128 + t + 1
        p.dma("sync", out=p_tm, in_=p_d[:, t, :])
        p.op("vector", "tensor_tensor", out=ptt, in0=p_tm,
             in1=zv_own[:, t, :], op=ALU.mult)
        for hp in range(2):
            pb = psT0 if hp == 0 else psT1
            p.xpose(pb, ptt[:, hp * 128:(hp + 1) * 128], cm.ident)
            p.op("scalar", "copy", out=pt_fm2[:, hp, :], in_=pb)
        for h4 in range(4):
            hp, lh = h4 // 2, h4 % 2
            base = lh * 64
            for st in range(n_st):
                p.mm(psP[1][:, 0:128], wv_fm2[base:base + 64, hp, st * 128:(st + 1) * 128],
                     pt_fm2[base:base + 64, hp, :], True, True)
                if st == n_st - 1:
                    p.op("vector", "scalar_tensor_tensor", out=a_bf,
                         in0=psP[1][:, 0:128], scalar=1.0, in1=mask01d,
                         op0=ALU.mult, op1=ALU.mult)
                else:
                    p.op("scalar", "copy", out=a_bf, in_=psP[1][:, 0:128])
                p.mm(psP[2][:, 0:256], a_bf, v_tm[:, st, h4 * 256:(h4 + 1) * 256],
                     st == 0, st == n_st - 1)
            p.op("vector", "tensor_copy", out=osb[:, h4 * 256:(h4 + 1) * 256],
                 in_=psP[2][:, 0:256])
        for fh in range(2):
            ph = psP[3 + fh]
            for c in range(NCD):
                p.mm(ph, xn_own[:, c, t * 128:(t + 1) * 128],
                     wg_w[:, c, fh * 512:(fh + 1) * 512], c == 0, c == NCD - 1)
            p.op("vector", "tensor_copy", out=gsb[:, fh * 512:(fh + 1) * 512], in_=ph)
        p.op("scalar", "activation", out=scrA[:, 0:1024], in_=osb, func=AF.Square)
        p.op("vector", "tensor_reduce", out=mxg,
             in_=scrA[:, 0:1024].rearrange("p (r d) -> p r d", d=256),
             axis=AXL.X, op=ALU.add)
        p.op("vector", "tensor_scalar", out=mxg, in0=mxg, scalar1=1.0 / 256.0,
             scalar2=None, op0=ALU.mult)
        p.op("scalar", "activation", out=mxg, in_=mxg, func=AF.Sqrt, bias=cm.c(1e-5))
        p.op("vector", "reciprocal", out=mxg, in_=mxg)
        p.op("vector", "tensor_tensor", out=osb.rearrange("p (r d) -> p r d", d=256),
             in0=osb.rearrange("p (r d) -> p r d", d=256),
             in1=mxg.unsqueeze(2).broadcast_to([128, 4, 256]), op=ALU.mult)
        p.op("vector", "tensor_tensor", out=osb, in0=osb, in1=gnw, op=ALU.mult)
        p.op("scalar", "activation", out=scrA[:, 0:1024], in_=gsb, func=AF.Sigmoid)
        p.op("vector", "tensor_tensor", out=gsb, in0=gsb, in1=scrA[:, 0:1024],
             op=ALU.mult)
        p.op("vector", "tensor_tensor", out=ofin, in0=osb, in1=gsb, op=ALU.mult)
        for c in range(NCD):
            pb = psT0 if c % 2 == 0 else psT1
            p.xpose(pb, ofin[:, c * 128:(c + 1) * 128], cm.ident)
            p.op("scalar", "copy", out=ofm[:, c, :], in_=pb)
        p.dma("sync", out=xres, in_=x_own_dram[t * 128:(t + 1) * 128, :])
        for fh in range(2):
            ph = psP[3 + fh]
            for c in range(NCD):
                wo_c = wo_wA[:, c, :] if c < 4 else wo_wB[:, c - 4, :]
                p.mm(ph, ofm[:, c, :], wo_c[:, fh * 512:(fh + 1) * 512],
                     c == 0, c == NCD - 1)
            p.op("vector", "tensor_tensor", out=xres[:, fh * 512:(fh + 1) * 512],
                 in0=xres[:, fh * 512:(fh + 1) * 512], in1=ph, op=ALU.add)
        p.dma("sync", out=x_out_dram[t * 128:(t + 1) * 128, :], in_=xres)
    p.free("v_tm", "wv_fm2", "zv_own", "wg_w", "wo_wA", "wo_wB", "pt_fm2",
           "a_bf", "osb", "gsb", "ptt", "p_tm", "mxg",
           "s_bf", "lgt", "gnw", "carry", "mask01d", "triu3")

    brow = p.alloc("brow", [32, 1024], F32)
    build_b3_ffn(p, cm, xn_own, x_own_dram, x_out_dram, W, psP, psT0, psT1,
                 psF0, lnb, xraw, brow, scrA)
    p.free("xn_own", "scrA", "brow")
    for i in range(6):
        p.free(f"ln{i}")


def _fm_slice(fm, tile):
    return fm[:, :, tile * 128:(tile + 1) * 128]


def build_b3_ffn(p, cm, xn_own, x_own_dram, x_out_dram, W, psP, psT0, psT1,
                 psF0, lnb, xraw, brow, scrA):
    wgt = p.alloc("wgt", [128, NCD, 2], BF16)
    sel = p.alloc("sel", [128, 16, 2], F32)
    p.dma("sync", out=brow[0:2, :], in_=W["gate_w"])
    for c in range(NCD):
        p.xpose(psF0[:, 0:2], brow[0:2, c * 128:(c + 1) * 128], cm.identf[0:2, 0:2])
        p.op("vector", "tensor_copy", out=wgt[:, c, :], in_=psF0[:, 0:2])
    for t in range(NT_OWN):
        ph = psP[4]
        for c in range(NCD):
            p.mm(ph[:, 0:2], xn_own[:, c, t * 128:(t + 1) * 128], wgt[:, c, :],
                 c == 0, c == NCD - 1)
        p.op("vector", "tensor_copy", out=sel[:, t, :], in_=ph[:, 0:2])
        p.op("vector", "tensor_tensor", out=lnb[0], in0=sel[:, t, 0:1],
             in1=sel[:, t, 1:2], op=ALU.subtract)
        p.op("vector", "tensor_scalar", out=sel[:, t, 0:1], in0=lnb[0], scalar1=0.0,
             scalar2=None, op0=ALU.is_ge)
        p.op("vector", "tensor_scalar", out=sel[:, t, 1:2], in0=sel[:, t, 0:1],
             scalar1=-1.0, scalar2=1.0, op0=ALU.mult, op1=ALU.add)
    cst = [p.alloc(f"cst{e}", [128, 1024], F32) for e in range(2)]
    rb1 = p.alloc("rb1", [128, 32], F32)
    w2s = [p.alloc(f"w2s{i}", [128, 1024], BF16) for i in range(2)]
    for e in range(2):
        p.dma("sync", out=brow[0:4, :], in_=W["b1"][e].rearrange("(a b) -> a b", b=1024))
        p.op("vector", "tensor_scalar", out=brow[0:4, :], in0=brow[0:4, :],
             scalar1=0.0, scalar2=None, op0=ALU.max)
        for cc in range(8):
            p.xpose(psF0[:, 0:4], brow[0:4, cc * 128:(cc + 1) * 128],
                    cm.identf[0:4, 0:4])
            p.op("vector", "tensor_copy", out=rb1[:, cc * 4:(cc + 1) * 4],
                 in_=psF0[:, 0:4])
        rb1b = p.alloc("rb1b", [128, 32], BF16)
        p.op("vector", "tensor_copy", out=rb1b, in_=rb1)
        for fh in range(2):
            ph = psP[fh]
            first = True
            for r in range(4):
                for cc in range(8):
                    jf = r * 8 + cc
                    p.dma("gpsimd", out=w2s[jf % 2],
                          in_=W["w2"][e, jf * 128:(jf + 1) * 128, :])
                    p.mm(ph[0:1, :], rb1b[:, cc * 4 + r:cc * 4 + r + 1],
                         w2s[jf % 2][:, fh * 512:(fh + 1) * 512], first, jf == 31)
                    first = False
            p.op("vector", "tensor_copy", out=cst[e][0:1, fh * 512:(fh + 1) * 512],
                 in_=ph[0:1, :])
        p.dma("sync", out=brow[0:1, :], in_=W["b2"][e].unsqueeze(0))
        p.op("vector", "tensor_tensor", out=cst[e][0:1, :], in0=cst[e][0:1, :],
             in1=brow[0:1, :], op=ALU.add)
        p.op("vector", "tensor_copy", out=brow.bitcast(BF16)[0:1, 0:1024],
             in_=cst[e][0:1, :])
        for fh in range(2):
            p.mm(psP[0], cm.ones_row,
                 brow.bitcast(BF16)[0:1, fh * 512:(fh + 1) * 512], True, True)
            p.op("vector", "tensor_copy", out=cst[e][:, fh * 512:(fh + 1) * 512],
                 in_=psP[0])
    p.free("rb1", "rb1b")

    hid = p.alloc("hid", [128, 32, 1024], BF16)
    w1s = [p.alloc(f"w1s{i}", [128, NCD, 512], BF16) for i in range(2)]
    b1t = p.alloc("b1t", [128, 32], F32)
    b1r = p.alloc("b1r", [32, 128], F32)

    def mlp_pass(w1ap, b1ap, w2ap, b2row_or_none, epilogue):
        p.bump()
        p.dma("sync", out=b1r, in_=b1ap.rearrange("(j p) -> j p", p=128))
        p.xpose(psF0[:, 0:32], b1r, cm.identf[0:32, 0:32])
        p.op("vector", "tensor_copy", out=b1t, in_=psF0[:, 0:32])
        for half in range(2):
            p.bump()
            tok0 = half * 1024
            for jg in range(8):
                wb = w1s[jg % 2]
                for c in range(NCD):
                    p.dma("gpsimd", out=wb[:, c, :],
                          in_=w1ap[c * 128:(c + 1) * 128, jg * 512:(jg + 1) * 512])
                for j4 in range(4):
                    j = jg * 4 + j4
                    for tg in range(2):
                        ph2 = psP[tg]
                        for c in range(NCD):
                            p.mm(ph2, wb[:, c, j4 * 128:(j4 + 1) * 128],
                                 xn_own[:, c, tok0 + tg * 512:tok0 + (tg + 1) * 512],
                                 c == 0, c == NCD - 1)
                        p.op("scalar", "activation", out=hid[:, j, tg * 512:(tg + 1) * 512],
                             in_=ph2, func=AF.Relu, bias=b1t[:, j:j + 1])
            for grp in range(4):
                for j in range(32):
                    p.dma("gpsimd", out=w2s[j % 2],
                          in_=w2ap[j * 128:(j + 1) * 128, :])
                    for tt in range(2):
                        lt = (grp * 2 + tt) * 128
                        for fh in range(2):
                            p.mm(psP[tt * 2 + fh], hid[:, j, lt:lt + 128],
                                 w2s[j % 2][:, fh * 512:(fh + 1) * 512], j == 0, j == 31)
                for tt in range(2):
                    t = half * 8 + grp * 2 + tt
                    epilogue(t, (psP[tt * 2], psP[tt * 2 + 1]))

    xr = scrA[:, 0:1024]

    def epi_shared(t, banks):
        p.dma("sync", out=xr, in_=x_out_dram[t * 128:(t + 1) * 128, :])
        p.op("vector", "tensor_tensor", out=xr, in0=xr, in1=sb2b, op=ALU.add)
        for fh in range(2):
            p.op("vector", "tensor_tensor", out=xr[:, fh * 512:(fh + 1) * 512],
                 in0=xr[:, fh * 512:(fh + 1) * 512], in1=banks[fh], op=ALU.add)
        p.dma("sync", out=x_out_dram[t * 128:(t + 1) * 128, :], in_=xr)

    def mk_epi_expert(e):
        def epi(t, banks):
            p.dma("sync", out=xr, in_=x_out_dram[t * 128:(t + 1) * 128, :])
            s_ap = sel[:, t, e:e + 1]
            o_ap = sel[:, t, 1 - e:2 - e]
            for fh in range(2):
                p.op("vector", "scalar_tensor_tensor",
                     out=xr[:, fh * 512:(fh + 1) * 512], in0=banks[fh], scalar=s_ap,
                     in1=xr[:, fh * 512:(fh + 1) * 512], op0=ALU.mult, op1=ALU.add)
                p.op("vector", "scalar_tensor_tensor",
                     out=xr[:, fh * 512:(fh + 1) * 512],
                     in0=cst[e][:, fh * 512:(fh + 1) * 512], scalar=o_ap,
                     in1=xr[:, fh * 512:(fh + 1) * 512], op0=ALU.mult, op1=ALU.add)
            p.dma("sync", out=x_out_dram[t * 128:(t + 1) * 128, :], in_=xr)
        return epi

    sb2b = p.alloc("sb2b", [128, 1024], F32)
    p.dma("sync", out=brow[0:1, :], in_=W["sb2"].unsqueeze(0))
    bcast_row(p, cm, psP[4], sb2b, brow[0:1, :], 1024)
    mlp_pass(W["sw1"], W["sb1"], W["sw2"], None, epi_shared)
    for e in range(2):
        mlp_pass(W["w1"][e], W["b1"][e], W["w2"][e], None, mk_epi_expert(e))
    p.free("wgt", "sel", "cst0", "cst1", "w2s0", "w2s1", "hid", "w1s0", "w1s1",
           "b1t", "b1r", "sb2b")



def _shard_x(x_full):
    shards = []
    for core in range(8):
        b, h = core // 2, core % 2
        own = np.ascontiguousarray(x_full[b, h * 2048:(h + 1) * 2048, :])
        other = np.ascontiguousarray(x_full[b, 0:2048, :])
        shards.append((own, other))
    return shards


def _run_block0(x_full, inputs):
    outs = []
    progs = {}
    for h in (0, 1):
        p = Prog()
        cm = Common(p)
        x_own = p.dram("x_own", [2048, 1024], F32, "ExternalInput")
        x_other = p.dram("x_other", [2048, 1024], F32, "ExternalInput")
        W = {
            "q": p.dram("wq_d", [1024, 64], F32, "ExternalInput"),
            "k": p.dram("wk_d", [1024, 64], F32, "ExternalInput"),
            "v": p.dram("wv_d", [1024, 1024], F32, "ExternalInput"),
            "fw1": p.dram("fw1_d", [4096, 1024], F32, "ExternalInput"),
            "b1": p.dram("fb1_d", [4096], F32, "ExternalInput"),
            "fw2": p.dram("fw2_d", [1024, 4096], F32, "ExternalInput"),
            "b2": p.dram("fb2_d", [1024], F32, "ExternalInput"),
        }
        x_out = p.dram("x_new", [2048, 1024], F32, "ExternalOutput")
        build_block0(p, cm, h, x_own, x_other, W, x_out)
        progs[h] = p.build()
    shards = _shard_x(x_full)
    res_by_core = {}
    for h in (0, 1):
        cores = [c for c in range(8) if c % 2 == h]
        in_maps = []
        for c in cores:
            own, other = shards[c]
            in_maps.append({
                "x_own": own, "x_other": other,
                "wq_d": inputs["L0_attn_Wq"], "wk_d": inputs["L0_attn_Wk"],
                "wv_d": inputs["L0_attn_Wv"],
                "fw1_d": inputs["L0_ffn_W1"], "fb1_d": inputs["L0_ffn_b1"],
                "fw2_d": inputs["L0_ffn_W2"], "fb2_d": inputs["L0_ffn_b2"],
            })
        res = run_bass_kernel_spmd(progs[h], in_maps, core_ids=cores)
        for c, r in zip(cores, res.results):
            res_by_core[c] = r["x_new"]
    x_new = np.empty_like(x_full)
    for core in range(8):
        b, hh = core // 2, core % 2
        x_new[b, hh * 2048:(hh + 1) * 2048, :] = res_by_core[core]
    return x_new


def _run_block1(x_full, inputs):
    progs = {}
    for h in (0, 1):
        p = Prog()
        cm = Common(p)
        x_own = p.dram("x_own", [2048, 1024], F32, "ExternalInput")
        x_other = p.dram("x_other", [2048, 1024], F32, "ExternalInput")
        W = {
            "in_proj": p.dram("w_inp", [4240, 1024], F32, "ExternalInput"),
            "conv_w": p.dram("w_cw", [2176, 4], F32, "ExternalInput"),
            "conv_b": p.dram("w_cb", [2176], F32, "ExternalInput"),
            "dt_bias": p.dram("w_dtb", [16], F32, "ExternalInput"),
            "A_log": p.dram("w_al", [16], F32, "ExternalInput"),
            "Dp": p.dram("w_dp", [16], F32, "ExternalInput"),
            "norm_w": p.dram("w_nw", [2048], F32, "ExternalInput"),
            "out_proj": p.dram("w_outp", [1024, 2048], F32, "ExternalInput"),
            "Wg": p.dram("w_wg", [8, 1024], F32, "ExternalInput"),
            "W1": p.dram("w_w1", [8, 2730, 1024], F32, "ExternalInput"),
            "W2": p.dram("w_w2", [8, 2730, 1024], F32, "ExternalInput"),
            "W3": p.dram("w_w3", [8, 1024, 2730], F32, "ExternalInput"),
        }
        x_out = p.dram("x_new", [2048, 1024], F32, "ExternalOutput")
        build_block1(p, cm, h, x_own, x_other, W, x_out)
        progs[h] = p.build()
    shards = _shard_x(x_full)
    res_by_core = {}
    for h in (0, 1):
        cores = [c for c in range(8) if c % 2 == h]
        in_maps = []
        for c in cores:
            own, other = shards[c]
            in_maps.append({
                "x_own": own, "x_other": other,
                "w_inp": inputs["L1_attn_in_proj_w"],
                "w_cw": inputs["L1_attn_conv_w"], "w_cb": inputs["L1_attn_conv_b"],
                "w_dtb": inputs["L1_attn_dt_bias"], "w_al": inputs["L1_attn_A_log"],
                "w_dp": inputs["L1_attn_Dp"], "w_nw": inputs["L1_attn_norm_w"],
                "w_outp": inputs["L1_attn_out_proj_w"],
                "w_wg": inputs["L1_ffn_Wg"], "w_w1": inputs["L1_ffn_W1"],
                "w_w2": inputs["L1_ffn_W2"], "w_w3": inputs["L1_ffn_W3"],
            })
        res = run_bass_kernel_spmd(progs[h], in_maps, core_ids=cores)
        for c, r in zip(cores, res.results):
            res_by_core[c] = r["x_new"]
    x_new = np.empty_like(x_full)
    for core in range(8):
        b, hh = core // 2, core % 2
        x_new[b, hh * 2048:(hh + 1) * 2048, :] = res_by_core[core]
    return x_new


def _run_block2(x_full, inputs):
    progs = {}
    for h in (0, 1):
        p = Prog()
        cm = Common(p)
        x_own = p.dram("x_own", [2048, 1024], F32, "ExternalInput")
        x_other = p.dram("x_other", [2048, 1024], F32, "ExternalInput")
        W = {
            "qkv_w": p.dram("w_qkv", [1024, 3072], F32, "ExternalInput"),
            "qkv_b": p.dram("w_qkvb", [3072], F32, "ExternalInput"),
            "rq_w": p.dram("w_rq", [1024, 256], F32, "ExternalInput"),
            "rq_b": p.dram("w_rqb", [256], F32, "ExternalInput"),
            "rkv_w": p.dram("w_rkv", [1024, 512], F32, "ExternalInput"),
            "rkv_b": p.dram("w_rkvb", [512], F32, "ExternalInput"),
            "rout_w": p.dram("w_ro", [256, 1024], F32, "ExternalInput"),
            "rout_b": p.dram("w_rob", [1024], F32, "ExternalInput"),
            "out_w": p.dram("w_ow", [1024, 1024], F32, "ExternalInput"),
            "out_b": p.dram("w_ob", [1024], F32, "ExternalInput"),
            "gl_w": p.dram("w_gl", [64, 1], F32, "ExternalInput"),
            "gr_w": p.dram("w_gr", [64, 1], F32, "ExternalInput"),
            "w_gate": p.dram("w_wg", [2816, 1024], F32, "ExternalInput"),
            "w_up": p.dram("w_wu", [2816, 1024], F32, "ExternalInput"),
            "w_down": p.dram("w_wd", [1024, 2816], F32, "ExternalInput"),
        }
        x_out = p.dram("x_new", [2048, 1024], F32, "ExternalOutput")
        build_block2(p, cm, h, x_own, x_other, W, x_out)
        progs[h] = p.build()
    shards = _shard_x(x_full)
    res_by_core = {}
    for h in (0, 1):
        cores = [c for c in range(8) if c % 2 == h]
        in_maps = []
        for c in cores:
            own, other = shards[c]
            in_maps.append({
                "x_own": own, "x_other": other,
                "w_qkv": inputs["L2_attn_qkv_w"], "w_qkvb": inputs["L2_attn_qkv_b"],
                "w_rq": inputs["L2_attn_rq_w"], "w_rqb": inputs["L2_attn_rq_b"],
                "w_rkv": inputs["L2_attn_rkv_w"], "w_rkvb": inputs["L2_attn_rkv_b"],
                "w_ro": inputs["L2_attn_rout_w"], "w_rob": inputs["L2_attn_rout_b"],
                "w_ow": inputs["L2_attn_out_w"], "w_ob": inputs["L2_attn_out_b"],
                "w_gl": inputs["L2_attn_gl_w"], "w_gr": inputs["L2_attn_gr_w"],
                "w_wg": inputs["L2_ffn_w_gate"], "w_wu": inputs["L2_ffn_w_up"],
                "w_wd": inputs["L2_ffn_w_down"],
            })
        res = run_bass_kernel_spmd(progs[h], in_maps, core_ids=cores)
        for c, r in zip(cores, res.results):
            res_by_core[c] = r["x_new"]
    x_new = np.empty_like(x_full)
    for core in range(8):
        b, hh = core // 2, core % 2
        x_new[b, hh * 2048:(hh + 1) * 2048, :] = res_by_core[core]
    return x_new


def _rope_tables():
    half = 128
    inv_freq = (1.0 / (10000.0 ** (np.arange(half, dtype=np.float32) / half))).astype(np.float32)
    ang = (np.arange(4096, dtype=np.float32)[:, None] * inv_freq[None, :]).astype(np.float32)
    cos_t = np.ascontiguousarray(np.cos(ang).T.astype(np.float32))
    sin_t = np.ascontiguousarray(np.sin(ang).T.astype(np.float32))
    return cos_t, sin_t


def _run_block3(x_full, inputs):
    cos_t, sin_t = _rope_tables()
    progs = {}
    for h in (0, 1):
        p = Prog()
        cm = Common(p)
        x_own = p.dram("x_own", [2048, 1024], F32, "ExternalInput")
        x_other = p.dram("x_other", [2048, 1024], F32, "ExternalInput")
        W = {
            "Wq": p.dram("w_q", [1024, 1024], F32, "ExternalInput"),
            "Wk": p.dram("w_k", [1024, 1024], F32, "ExternalInput"),
            "Wv": p.dram("w_v", [1024, 1024], F32, "ExternalInput"),
            "Wg": p.dram("w_g", [1024, 1024], F32, "ExternalInput"),
            "Wo": p.dram("w_o", [1024, 1024], F32, "ExternalInput"),
            "sk_w1": p.dram("w_sk1", [1024, 16], F32, "ExternalInput"),
            "sk_w2": p.dram("w_sk2", [16, 256], F32, "ExternalInput"),
            "sk_b2": p.dram("w_skb", [256], F32, "ExternalInput"),
            "sv_w1": p.dram("w_sv1", [1024, 16], F32, "ExternalInput"),
            "sv_w2": p.dram("w_sv2", [16, 256], F32, "ExternalInput"),
            "sv_b2": p.dram("w_svb", [256], F32, "ExternalInput"),
            "gnorm_w": p.dram("w_gn", [256], F32, "ExternalInput"),
            "cos_t": p.dram("w_cos", [128, 4096], F32, "ExternalInput"),
            "sin_t": p.dram("w_sin", [128, 4096], F32, "ExternalInput"),
            "gate_w": p.dram("w_gw", [2, 1024], F32, "ExternalInput"),
            "w1": p.dram("w_w1", [2, 1024, 4096], F32, "ExternalInput"),
            "b1": p.dram("w_b1", [2, 4096], F32, "ExternalInput"),
            "w2": p.dram("w_w2", [2, 4096, 1024], F32, "ExternalInput"),
            "b2": p.dram("w_b2", [2, 1024], F32, "ExternalInput"),
            "sw1": p.dram("w_sw1", [1024, 4096], F32, "ExternalInput"),
            "sb1": p.dram("w_sb1", [4096], F32, "ExternalInput"),
            "sw2": p.dram("w_sw2", [4096, 1024], F32, "ExternalInput"),
            "sb2": p.dram("w_sb2", [1024], F32, "ExternalInput"),
        }
        x_out = p.dram("x_new", [2048, 1024], F32, "ExternalOutput")
        build_block3(p, cm, h, x_own, x_other, W, x_out)
        progs[h] = p.build()
    shards = _shard_x(x_full)
    res_by_core = {}
    for h in (0, 1):
        cores = [c for c in range(8) if c % 2 == h]
        in_maps = []
        for c in cores:
            own, other = shards[c]
            in_maps.append({
                "x_own": own, "x_other": other,
                "w_q": inputs["L3_attn_Wq"], "w_k": inputs["L3_attn_Wk"],
                "w_v": inputs["L3_attn_Wv"], "w_g": inputs["L3_attn_Wg"],
                "w_o": inputs["L3_attn_Wo"],
                "w_sk1": inputs["L3_attn_sk_w1"], "w_sk2": inputs["L3_attn_sk_w2"],
                "w_skb": inputs["L3_attn_sk_b2"],
                "w_sv1": inputs["L3_attn_sv_w1"], "w_sv2": inputs["L3_attn_sv_w2"],
                "w_svb": inputs["L3_attn_sv_b2"],
                "w_gn": inputs["L3_attn_gnorm_w"],
                "w_cos": cos_t, "w_sin": sin_t,
                "w_gw": inputs["L3_ffn_gate_w"], "w_w1": inputs["L3_ffn_w1"],
                "w_b1": inputs["L3_ffn_b1"], "w_w2": inputs["L3_ffn_w2"],
                "w_b2": inputs["L3_ffn_b2"],
                "w_sw1": inputs["L3_ffn_sw1"], "w_sb1": inputs["L3_ffn_sb1"],
                "w_sw2": inputs["L3_ffn_sw2"], "w_sb2": inputs["L3_ffn_sb2"],
            })
        res = run_bass_kernel_spmd(progs[h], in_maps, core_ids=cores)
        for c, r in zip(cores, res.results):
            res_by_core[c] = r["x_new"]
    x_new = np.empty_like(x_full)
    for core in range(8):
        b, hh = core // 2, core % 2
        x_new[b, hh * 2048:(hh + 1) * 2048, :] = res_by_core[core]
    return x_new


WSHAPES = {
    "L0_attn_Wq": [1024, 64], "L0_attn_Wk": [1024, 64], "L0_attn_Wv": [1024, 1024],
    "L0_ffn_W1": [4096, 1024], "L0_ffn_b1": [4096], "L0_ffn_W2": [1024, 4096],
    "L0_ffn_b2": [1024],
    "L1_attn_in_proj_w": [4240, 1024], "L1_attn_conv_w": [2176, 4],
    "L1_attn_conv_b": [2176], "L1_attn_dt_bias": [16], "L1_attn_A_log": [16],
    "L1_attn_Dp": [16], "L1_attn_norm_w": [2048], "L1_attn_out_proj_w": [1024, 2048],
    "L1_ffn_Wg": [8, 1024], "L1_ffn_W1": [8, 2730, 1024],
    "L1_ffn_W2": [8, 2730, 1024], "L1_ffn_W3": [8, 1024, 2730],
    "L2_attn_qkv_w": [1024, 3072], "L2_attn_qkv_b": [3072],
    "L2_attn_rq_w": [1024, 256], "L2_attn_rq_b": [256],
    "L2_attn_rkv_w": [1024, 512], "L2_attn_rkv_b": [512],
    "L2_attn_rout_w": [256, 1024], "L2_attn_rout_b": [1024],
    "L2_attn_out_w": [1024, 1024], "L2_attn_out_b": [1024],
    "L2_attn_gl_w": [64, 1], "L2_attn_gr_w": [64, 1],
    "L2_ffn_w_gate": [2816, 1024], "L2_ffn_w_up": [2816, 1024],
    "L2_ffn_w_down": [1024, 2816],
    "L3_attn_Wq": [1024, 1024], "L3_attn_Wk": [1024, 1024],
    "L3_attn_Wv": [1024, 1024], "L3_attn_Wg": [1024, 1024],
    "L3_attn_Wo": [1024, 1024],
    "L3_attn_sk_w1": [1024, 16], "L3_attn_sk_w2": [16, 256], "L3_attn_sk_b2": [256],
    "L3_attn_sv_w1": [1024, 16], "L3_attn_sv_w2": [16, 256], "L3_attn_sv_b2": [256],
    "L3_attn_gnorm_w": [256],
    "L3_ffn_gate_w": [2, 1024], "L3_ffn_w1": [2, 1024, 4096], "L3_ffn_b1": [2, 4096],
    "L3_ffn_w2": [2, 4096, 1024], "L3_ffn_b2": [2, 1024],
    "L3_ffn_sw1": [1024, 4096], "L3_ffn_sb1": [4096], "L3_ffn_sw2": [4096, 1024],
    "L3_ffn_sb2": [1024],
}

RG_PAIRS = [[0, 1], [2, 3], [4, 5], [6, 7]]


def _build_fused():
    p = Prog()
    x_full = p.dram("x_full", [4096, 1024], F32, "ExternalInput")
    x_new = p.dram("x_new", [2048, 1024], F32, "ExternalOutput")
    wd = {k: p.dram(k, s, F32, "ExternalInput") for k, s in WSHAPES.items()}
    wd["cos_t"] = p.dram("cos_t", [128, 4096], F32, "ExternalInput")
    wd["sin_t"] = p.dram("sin_t", [128, 4096], F32, "ExternalInput")
    cc_in = [p.dram(f"cc_in{k}", [2048, 1024], F32, "Internal") for k in range(3)]
    gath = [p.dram(f"gath{k}", [4096, 1024], F32, "Internal") for k in range(3)]

    W0 = {"q": wd["L0_attn_Wq"], "k": wd["L0_attn_Wk"], "v": wd["L0_attn_Wv"],
          "fw1": wd["L0_ffn_W1"], "b1": wd["L0_ffn_b1"],
          "fw2": wd["L0_ffn_W2"], "b2": wd["L0_ffn_b2"]}
    W1 = {"in_proj": wd["L1_attn_in_proj_w"], "conv_w": wd["L1_attn_conv_w"],
          "conv_b": wd["L1_attn_conv_b"], "dt_bias": wd["L1_attn_dt_bias"],
          "A_log": wd["L1_attn_A_log"], "Dp": wd["L1_attn_Dp"],
          "norm_w": wd["L1_attn_norm_w"], "out_proj": wd["L1_attn_out_proj_w"],
          "Wg": wd["L1_ffn_Wg"], "W1": wd["L1_ffn_W1"], "W2": wd["L1_ffn_W2"],
          "W3": wd["L1_ffn_W3"]}
    W2 = {"qkv_w": wd["L2_attn_qkv_w"], "qkv_b": wd["L2_attn_qkv_b"],
          "rq_w": wd["L2_attn_rq_w"], "rq_b": wd["L2_attn_rq_b"],
          "rkv_w": wd["L2_attn_rkv_w"], "rkv_b": wd["L2_attn_rkv_b"],
          "rout_w": wd["L2_attn_rout_w"], "rout_b": wd["L2_attn_rout_b"],
          "out_w": wd["L2_attn_out_w"], "out_b": wd["L2_attn_out_b"],
          "gl_w": wd["L2_attn_gl_w"], "gr_w": wd["L2_attn_gr_w"],
          "w_gate": wd["L2_ffn_w_gate"], "w_up": wd["L2_ffn_w_up"],
          "w_down": wd["L2_ffn_w_down"]}
    W3 = {"Wq": wd["L3_attn_Wq"], "Wk": wd["L3_attn_Wk"], "Wv": wd["L3_attn_Wv"],
          "Wg": wd["L3_attn_Wg"], "Wo": wd["L3_attn_Wo"],
          "sk_w1": wd["L3_attn_sk_w1"], "sk_w2": wd["L3_attn_sk_w2"],
          "sk_b2": wd["L3_attn_sk_b2"], "sv_w1": wd["L3_attn_sv_w1"],
          "sv_w2": wd["L3_attn_sv_w2"], "sv_b2": wd["L3_attn_sv_b2"],
          "gnorm_w": wd["L3_attn_gnorm_w"], "cos_t": wd["cos_t"],
          "sin_t": wd["sin_t"],
          "gate_w": wd["L3_ffn_gate_w"], "w1": wd["L3_ffn_w1"],
          "b1": wd["L3_ffn_b1"], "w2": wd["L3_ffn_w2"], "b2": wd["L3_ffn_b2"],
          "sw1": wd["L3_ffn_sw1"], "sb1": wd["L3_ffn_sb1"],
          "sw2": wd["L3_ffn_sw2"], "sb2": wd["L3_ffn_sb2"]}

    for h in (0, 1):
        p.begin_variant(h)
        cm = Common(p)
        own = slice(h * 2048, (h + 1) * 2048)
        build_block0(p, cm, h, x_full[own, :], x_full[0:2048, :], W0, cc_in[0])
        p.op("gpsimd", "collective_compute", kind="AllGather", op=ALU.bypass,
             replica_groups=RG_PAIRS, ins=[cc_in[0][:, :]], outs=[gath[0][:, :]])
        build_block1(p, cm, h, gath[0][own, :], gath[0][0:2048, :], W1, cc_in[1])
        p.op("gpsimd", "collective_compute", kind="AllGather", op=ALU.bypass,
             replica_groups=RG_PAIRS, ins=[cc_in[1][:, :]], outs=[gath[1][:, :]])
        build_block2(p, cm, h, gath[1][own, :], gath[1][0:2048, :], W2, cc_in[2])
        p.op("gpsimd", "collective_compute", kind="AllGather", op=ALU.bypass,
             replica_groups=RG_PAIRS, ins=[cc_in[2][:, :]], outs=[gath[2][:, :]])
        build_block3(p, cm, h, gath[2][own, :], gath[2][0:2048, :], W3, x_new)
    return p.build()


def _build_block_2var(bidx):
    p = Prog()
    x_own = p.dram("x_own", [2048, 1024], F32, "ExternalInput")
    x_other = p.dram("x_other", [2048, 1024], F32, "ExternalInput")
    x_new = p.dram("x_new", [2048, 1024], F32, "ExternalOutput")
    wd = {k: p.dram(k, s, F32, "ExternalInput") for k, s in WSHAPES.items()}
    wd["cos_t"] = p.dram("cos_t", [128, 4096], F32, "ExternalInput")
    wd["sin_t"] = p.dram("sin_t", [128, 4096], F32, "ExternalInput")
    W0 = {"q": wd["L0_attn_Wq"], "k": wd["L0_attn_Wk"], "v": wd["L0_attn_Wv"],
          "fw1": wd["L0_ffn_W1"], "b1": wd["L0_ffn_b1"],
          "fw2": wd["L0_ffn_W2"], "b2": wd["L0_ffn_b2"]}
    W1 = {"in_proj": wd["L1_attn_in_proj_w"], "conv_w": wd["L1_attn_conv_w"],
          "conv_b": wd["L1_attn_conv_b"], "dt_bias": wd["L1_attn_dt_bias"],
          "A_log": wd["L1_attn_A_log"], "Dp": wd["L1_attn_Dp"],
          "norm_w": wd["L1_attn_norm_w"], "out_proj": wd["L1_attn_out_proj_w"],
          "Wg": wd["L1_ffn_Wg"], "W1": wd["L1_ffn_W1"], "W2": wd["L1_ffn_W2"],
          "W3": wd["L1_ffn_W3"]}
    W2 = {"qkv_w": wd["L2_attn_qkv_w"], "qkv_b": wd["L2_attn_qkv_b"],
          "rq_w": wd["L2_attn_rq_w"], "rq_b": wd["L2_attn_rq_b"],
          "rkv_w": wd["L2_attn_rkv_w"], "rkv_b": wd["L2_attn_rkv_b"],
          "rout_w": wd["L2_attn_rout_w"], "rout_b": wd["L2_attn_rout_b"],
          "out_w": wd["L2_attn_out_w"], "out_b": wd["L2_attn_out_b"],
          "gl_w": wd["L2_attn_gl_w"], "gr_w": wd["L2_attn_gr_w"],
          "w_gate": wd["L2_ffn_w_gate"], "w_up": wd["L2_ffn_w_up"],
          "w_down": wd["L2_ffn_w_down"]}
    W3 = {"Wq": wd["L3_attn_Wq"], "Wk": wd["L3_attn_Wk"], "Wv": wd["L3_attn_Wv"],
          "Wg": wd["L3_attn_Wg"], "Wo": wd["L3_attn_Wo"],
          "sk_w1": wd["L3_attn_sk_w1"], "sk_w2": wd["L3_attn_sk_w2"],
          "sk_b2": wd["L3_attn_sk_b2"], "sv_w1": wd["L3_attn_sv_w1"],
          "sv_w2": wd["L3_attn_sv_w2"], "sv_b2": wd["L3_attn_sv_b2"],
          "gnorm_w": wd["L3_attn_gnorm_w"], "cos_t": wd["cos_t"],
          "sin_t": wd["sin_t"],
          "gate_w": wd["L3_ffn_gate_w"], "w1": wd["L3_ffn_w1"],
          "b1": wd["L3_ffn_b1"], "w2": wd["L3_ffn_w2"], "b2": wd["L3_ffn_b2"],
          "sw1": wd["L3_ffn_sw1"], "sb1": wd["L3_ffn_sb1"],
          "sw2": wd["L3_ffn_sw2"], "sb2": wd["L3_ffn_sb2"]}
    builders = [
        lambda p, cm, h: build_block0(p, cm, h, x_own, x_other, W0, x_new),
        lambda p, cm, h: build_block1(p, cm, h, x_own, x_other, W1, x_new),
        lambda p, cm, h: build_block2(p, cm, h, x_own, x_other, W2, x_new),
        lambda p, cm, h: build_block3(p, cm, h, x_own, x_other, W3, x_new),
    ]
    for h in (0, 1):
        p.begin_variant(h)
        cm = Common(p)
        builders[bidx](p, cm, h)
    return p.build()


def kernel(**inputs):
    cos_t, sin_t = _rope_tables()
    x = np.ascontiguousarray(inputs["L0_attn_X"].astype(np.float32))
    for bidx in range(4):
        nc = _build_block_2var(bidx)
        in_maps = []
        for core in range(8):
            b, h = core // 2, core % 2
            im = {k: inputs[k] for k in WSHAPES}
            im["x_own"] = np.ascontiguousarray(x[b, h * 2048:(h + 1) * 2048])
            im["x_other"] = np.ascontiguousarray(x[b, 0:2048])
            im["cos_t"] = cos_t
            im["sin_t"] = sin_t
            in_maps.append(im)
        res = run_bass_kernel_spmd(nc, in_maps, core_ids=list(range(8)))
        xn = np.empty_like(x)
        for core in range(8):
            b, h = core // 2, core % 2
            xn[b, h * 2048:(h + 1) * 2048, :] = res.results[core]["x_new"]
        x = xn
    return x


def _kernel_fused_wip(**inputs):
    cos_t, sin_t = _rope_tables()
    nc = _build_fused()
    x = np.ascontiguousarray(inputs["L0_attn_X"].astype(np.float32))
    in_maps = []
    for core in range(8):
        b = core // 2
        im = {k: inputs[k] for k in WSHAPES}
        im["x_full"] = np.ascontiguousarray(x[b])
        im["cos_t"] = cos_t
        im["sin_t"] = sin_t
        in_maps.append(im)
    res = run_bass_kernel_spmd(nc, in_maps, core_ids=list(range(8)))
    out = np.empty((4, 4096, 1024), dtype=np.float32)
    for core in range(8):
        b, h = core // 2, core % 2
        out[b, h * 2048:(h + 1) * 2048, :] = res.results[core]["x_new"]
    return out


def _unused_kernel_tail():
    pass


# version 05 (final)
# speedup vs baseline: 3.3971x; 1.6407x over previous
import contextlib
import math
import numpy as np
import ml_dtypes
import concourse.bass as bass
import concourse.mybir as mybir
from concourse.bass_utils import run_bass_kernel_spmd

F32 = mybir.dt.float32
BF16 = mybir.dt.bfloat16
AF = mybir.ActivationFunctionType
ALU = mybir.AluOpType
AXL = mybir.AxisListType

NEG = -1.0e30
NEGSEED = -3.0e38
DBG = False
D = 1024
T_OWN = 2048
NT_OWN = 16
NCD = 8

BF = np.dtype(ml_dtypes.bfloat16)


def _span(ap: bass.AP):
    t = ap.tensor
    spc = getattr(ap.space, "value", ap.space)
    if spc in ("sbuf", "psum", "SBUF", "PSUM"):
        dims = ap.ap
        p0 = ap.base_partition
        if callable(p0):
            p0 = p0()
        pn = dims[0][1]
        esz = 2 if ap.dtype == BF16 else 4
        off = ap.offset * esz if isinstance(ap.offset, int) else 0
        lo = hi = 0
        for step, cnt in dims[1:]:
            d = step * (cnt - 1) * esz
            if d >= 0:
                hi += d
            else:
                lo += d
        return (p0, p0 + pn, off + lo, off + hi + esz)
    else:
        return (0, 128, 0, 1 << 62)


class Prog:
    ENGS = ("tensor", "vector", "scalar", "gpsimd", "sync")

    def __init__(self, arena_kb=192):
        self.nc = bass.Bass("TRN2", target_bir_lowering=False)
        self.es = contextlib.ExitStack()
        self.ops = []
        self.acc = {}
        self.psum_names = set()
        self.dram_names = set()
        self.dram_aps = {}
        self.psum_aps = {}
        self.ext_sems = {}
        self.var = None
        self.seg = 0
        self.common = []
        self._fin = False
        self.arena = self.es.enter_context(
            self.nc.sbuf_tensor("ARENA", [128, arena_kb * 256], F32))
        self.arena_bf = self.arena[:].bitcast(BF16)
        self._arena_bytes = arena_kb * 1024
        self._freelist = [(0, arena_kb * 1024)]
        self._allocs = {}

    def alloc(self, name, shape, dtype, align=64):
        esz = 2 if dtype == BF16 else 4
        n = 1
        for s in shape[1:]:
            n *= s
        nbytes = n * esz
        nbytes_al = (nbytes + align - 1) // align * align
        for i, (off, sz) in enumerate(self._freelist):
            offa = (off + align - 1) // align * align
            if offa + nbytes_al <= off + sz:
                newfree = []
                if offa > off:
                    newfree.append((off, offa - off))
                rest = (off + sz) - (offa + nbytes_al)
                if rest > 0:
                    newfree.append((offa + nbytes_al, rest))
                self._freelist[i:i + 1] = newfree
                self._allocs[name] = (offa, nbytes_al, shape, dtype)
                return self._view(offa, shape, dtype)
        allocs = sorted((v[0], v[1], k) for k, v in self._allocs.items())
        raise MemoryError(f"arena OOM for {name}: {nbytes_al}B/partition; "
                          f"free={self._freelist}; allocs={allocs}")

    def _view(self, off_bytes, shape, dtype):
        esz = 2 if dtype == BF16 else 4
        base = self.arena_bf if dtype == BF16 else self.arena[:]
        n = 1
        for s in shape[1:]:
            n *= s
        v = base[0:shape[0], off_bytes // esz: off_bytes // esz + n]
        if len(shape) > 2:
            pat = "p (" + " ".join(f"a{i}" for i in range(len(shape) - 1)) + ") -> p " + \
                  " ".join(f"a{i}" for i in range(len(shape) - 1))
            kw = {f"a{i}": shape[i + 1] for i in range(1, len(shape) - 1)}
            v = v.rearrange(pat, **kw)
        return v

    def free(self, *names):
        for name in names:
            off, sz, _, _ = self._allocs.pop(name)
            self._freelist.append((off, sz))
        self._freelist.sort()
        merged = []
        for off, sz in self._freelist:
            if merged and merged[-1][0] + merged[-1][1] == off:
                merged[-1] = (merged[-1][0], merged[-1][1] + sz)
            else:
                merged.append((off, sz))
        self._freelist = [tuple(x) for x in merged]

    def dram(self, name, shape, dtype, kind):
        if name in self.dram_aps:
            return self.dram_aps[name]
        ap = self.nc.dram_tensor(name, list(shape), dtype, kind=kind).ap()
        self.dram_names.add(name)
        self.dram_aps[name] = ap
        return ap

    def sbuf(self, name, shape, dtype):
        if not hasattr(self, "_sbufs"):
            self._sbufs = {}
        if name not in self._sbufs:
            self._sbufs[name] = self.es.enter_context(
                self.nc.sbuf_tensor(name, list(shape), dtype))[:]
        return self._sbufs[name]

    def psum(self, name, shape, dtype):
        if name in self.psum_aps:
            return self.psum_aps[name]
        ap = self.es.enter_context(self.nc.psum_tensor(name, list(shape), dtype))[:]
        self.psum_names.add(name)
        self.psum_aps[name] = ap
        return ap

    def ext_sem(self, name):
        if name not in self.ext_sems:
            self.ext_sems[name] = self.es.enter_context(self.nc.semaphore("x_" + name))
        return self.ext_sems[name]

    def begin_variant(self, v):
        self.var = v
        self.acc = {}
        self._freelist = [(0, len(self._allocs) * 0 + self._arena_bytes)]
        self._allocs = {}

    def mark_sem_write(self, tname, semname, value):
        self.ext_sem(semname)
        self.acc[tname] = {-1: [[0, 128, 0, 1 << 62, ("sem", semname, value),
                                 True, True]]}

    def bump(self):
        self.seg += 1

    def add_common(self, seg, fn):
        self.common.append((seg, fn))

    def op(self, eng, meth, *, deps=(), xreads=(), **kwargs):
        reads, writes = [], []
        for k, v in kwargs.items():
            aps = []
            _collect_aps(v, aps)
            for ap in aps:
                nm = ap.tensor.name
                sp = _span(ap)
                if nm in self.psum_names:
                    sp = (0, 128, 0, 2048)
                is_w = k in ("out", "out_ap", "accum_out", "ap", "outs")
                (writes if is_w else reads).append((nm, sp))
        for nm in xreads:
            reads.append((nm, (0, 128, 0, 1 << 62)))
        idx = len(self.ops)
        depset = set(deps)
        semdeps = set()
        BK = 8192

        def buckets_of(sp):
            if sp[3] - sp[2] > (1 << 40):
                return None
            return range(sp[2] // BK, (sp[3] - 1) // BK + 1)

        def scan(nm, sp, writes_only):
            ent = self.acc.get(nm)
            if ent is None:
                return
            br = buckets_of(sp)
            cand = []
            if br is None:
                for lst in ent.values():
                    cand.append(lst)
            else:
                g = ent.get(-1)
                if g:
                    cand.append(g)
                for bi in br:
                    lst = ent.get(bi)
                    if lst:
                        cand.append(lst)
            seen = set()
            for lst in cand:
                for rec in lst:
                    if not rec[6] or id(rec) in seen:
                        continue
                    seen.add(id(rec))
                    if (rec[5] or not writes_only is False) and False:
                        pass
                    if writes_only and not rec[5]:
                        continue
                    if _ovl(rec, sp):
                        (semdeps if isinstance(rec[4], tuple) else depset).add(rec[4])

        for nm, sp in reads:
            scan(nm, sp, True)
        for nm, sp in writes:
            scan(nm, sp, False)
        is_dma = (meth.startswith("dma") or meth == "indirect_dma_start"
                  or meth == "collective_compute")
        deps2 = []
        for d in sorted(depset):
            dop = self.ops[d]
            if eng == "tensor" and dop["eng"] == "tensor" and not dop["dma"]:
                continue
            deps2.append(d)
            dop["inc"] = True
        semkey = None
        if is_dma:
            semkey = writes[0][0] if writes else "_misc"
            if semkey == "ARENA":
                b0 = writes[0][1][2]
                for anm, (aoff, asz, _, _) in self._allocs.items():
                    if aoff <= b0 < aoff + asz:
                        semkey = "A_" + anm
                        break
        rec = dict(eng=eng, meth=meth, kwargs=kwargs, deps=deps2, dma=is_dma,
                   inc=False, w=writes, sem=None, val=None, semkey=semkey,
                   var=self.var, seg=self.seg, semdeps=sorted(semdeps))
        self.ops.append(rec)

        def add_record(nm, sp, dep, is_w):
            ent = self.acc.setdefault(nm, {})
            r = [sp[0], sp[1], sp[2], sp[3], dep, is_w, True]
            br = buckets_of(sp)
            if br is None:
                ent.setdefault(-1, []).append(r)
            else:
                for bi in br:
                    ent.setdefault(bi, []).append(r)

        for nm, sp in writes:
            ent = self.acc.get(nm)
            if ent is not None:
                br = buckets_of(sp)
                lsts = ent.values() if br is None else                     [ent[bi] for bi in br if bi in ent]
                for lst in lsts:
                    for r in lst:
                        if r[6] and _cover(sp, r):
                            r[6] = False
                    lst[:] = [r for r in lst if r[6]]
            add_record(nm, sp, idx, True)
        for nm, sp in reads:
            add_record(nm, sp, idx, False)
        return idx

    def dbg(self, name, ap):
        shp = [d[1] for d in ap.ap]
        d = self.dram("dbg_" + name, shp, ap.dtype, "ExternalOutput")
        self.dma("sync", out=d[tuple(slice(None) for _ in shp)], in_=ap)

    def mm(self, out, lhsT, rhs, start, stop, **kw):
        return self.op("tensor", "matmul", out=out, lhsT=lhsT, rhs=rhs,
                       start=start, stop=stop, **kw)

    def dma(self, eng, out, in_, **kw):
        return self.op(eng, "dma_start", out=out, in_=in_, **kw)

    def xpose(self, out_psum, in_sbuf, ident):
        return self.op("tensor", "transpose", out=out_psum, in_=in_sbuf,
                       identity=ident)

    def build(self, verbose=False):
        assert not self._fin
        self._fin = True
        nc = self.nc
        eng_sems, dma_sems = {}, {}
        for o in self.ops:
            if o["dma"]:
                key = o["semkey"]
                if key not in dma_sems:
                    dma_sems[key] = self.es.enter_context(
                        nc.semaphore(f"d{len(dma_sems)}_{key}"[:28]))
                o["sem"] = ("d", key)
            elif o["inc"]:
                e = (o["eng"], o["seg"] // 1000)
                if e not in eng_sems:
                    eng_sems[e] = self.es.enter_context(
                        nc.semaphore(f"p_{e[0]}_{e[1]}"))
                o["sem"] = ("e", e)
        nsem = len(eng_sems) + len(dma_sems) + len(self.ext_sems)
        if nsem > 250:
            raise RuntimeError(f"{nsem} semaphores")
        vals = {}
        for o in self.ops:
            if o["sem"]:
                k = (o["var"], o["sem"])
                vals[k] = vals.get(k, 0) + (16 if o["dma"] else 1)
                o["val"] = vals[k]
        if vals:
            mx = max(vals.items(), key=lambda kv: kv[1])
            print(f"[prog] max sem count: {mx[0]} = {mx[1]}")
            for kk, vv in vals.items():
                if vv > 60000:
                    raise RuntimeError(f"sem overflow risk: {kk} = {vv}")
        self.es.enter_context(nc.allow_low_precision(
            reason="bf16 operands/outputs are deliberate (fp32 accumulation in PSUM)"))
        blk = self.es.enter_context(nc.Block())
        allops = self.ops
        variants = sorted({o["var"] for o in self.ops})
        multivar = variants != [None]
        import os as _os
        coal = int(_os.environ.get("SEGCOAL", "1000000"))
        for o in self.ops:
            o["seg"] = o["seg"] // coal
        self.common = [(s // coal, f) for s, f in self.common]
        segs = sorted({o["seg"] for o in self.ops})

        def sem_of(o):
            kind, key = o["sem"]
            return (dma_sems if kind == "d" else eng_sems)[key]

        by = {}
        for o in self.ops:
            by.setdefault((o["seg"], o["var"], o["eng"]), []).append(o)
        fin = {}
        for o in self.ops:
            if o["dma"] and o["w"] and o["w"][0][0] in self.dram_names:
                fin.setdefault(o["var"], {})[o["sem"][1]] = o["val"]

        def replay(h, oplist, waited):
            for o in oplist:
                if o["dma"] and o["val"] > 16:
                    s = sem_of(o)
                    if waited.get(id(s), -1) < o["val"] - 16:
                        h.wait_ge(s, o["val"] - 16)
                        waited[id(s)] = o["val"] - 16
                for sd in o["semdeps"]:
                    _, semname, v = sd
                    s = self.ext_sems[semname]
                    if waited.get(id(s), -1) < v:
                        h.wait_ge(s, v)
                        waited[id(s)] = v
                for d in o["deps"]:
                    dop = allops[d]
                    s, v = sem_of(dop), dop["val"]
                    if waited.get(id(s), -1) < v:
                        h.wait_ge(s, v)
                        waited[id(s)] = v
                try:
                    ins = getattr(h, o["meth"])(**o["kwargs"])
                except Exception:
                    import sys
                    print("FAILING OP:", o["eng"], o["meth"],
                          {k: (tuple(v.shape) if isinstance(v, bass.AP) else v)
                           for k, v in o["kwargs"].items()}, file=sys.stderr)
                    raise
                if o["dma"]:
                    ins.then_inc(sem_of(o), 16)
                elif o["inc"]:
                    ins.then_inc(sem_of(o), 1)

        common_by_seg = {}
        for seg, fn in self.common:
            common_by_seg.setdefault(seg, []).append(fn)

        def make_fn(eng):
            def f(h):
                waited = {v: {} for v in variants}
                if multivar:
                    r = h.alloc_register("hvar")
                    h.reg_mov(r, h.partition_id())
                    h.reg_mod(r, r, 2)
                for seg in segs:
                    if multivar:
                        o0 = by.get((seg, 0, eng), [])
                        o1 = by.get((seg, 1, eng), [])
                        if o0 or o1:
                            with h.If_cmp(r, 0, "IS_EQ"):
                                replay(h, o0, waited[0])
                            with h.Else():
                                replay(h, o1, waited[1])
                            h.end_ifs()
                    else:
                        replay(h, by.get((seg, None, eng), []), waited[None])
                    if eng == "gpsimd":
                        for fn2 in common_by_seg.get(seg, []):
                            fn2(h)
                if eng == "sync":
                    if multivar:
                        with h.If_cmp(r, 0, "IS_EQ"):
                            for key, val in fin.get(0, {}).items():
                                h.wait_ge(dma_sems[key], val)
                        with h.Else():
                            for key, val in fin.get(1, {}).items():
                                h.wait_ge(dma_sems[key], val)
                        h.end_ifs()
                    else:
                        for key, val in fin.get(None, {}).items():
                            h.wait_ge(dma_sems[key], val)
            return f

        engs_used = sorted({o["eng"] for o in self.ops})
        if "gpsimd" not in engs_used and self.common:
            engs_used.append("gpsimd")
        if "sync" not in engs_used:
            engs_used.append("sync")
        for e in self.ENGS:
            if e in engs_used:
                getattr(blk, e)(make_fn(e))
        self.es.close()
        if verbose:
            print(f"[prog] {len(self.ops)} ops, {nsem} sems")
        return nc


def _collect_aps(v, acc):
    if isinstance(v, bass.AP):
        acc.append(v)
    elif isinstance(v, (list, tuple)):
        for e in v:
            _collect_aps(e, acc)


def _ovl(rec, sp):
    return rec[0] < sp[1] and sp[0] < rec[1] and rec[2] < sp[3] and sp[2] < rec[3]


def _cover(sp, rec):
    return sp[0] <= rec[0] and rec[1] <= sp[1] and sp[2] <= rec[2] and rec[3] <= sp[3]



def std_psums(p: Prog):
    psP = [p.psum(f"ps{i}", [128, 512], F32) for i in range(6)]
    psT0 = p.psum("psT0", [128, 128], BF16)
    psT1 = p.psum("psT1", [128, 128], BF16)
    psF0 = psP[5][:, 0:128]
    return psP, psT0, psT1, psF0


class Common:
    def __init__(self, p: Prog):
        self.p = p
        self.ident = p.sbuf("c_ident", [128, 128], BF16)
        self.identf = p.sbuf("c_identf", [128, 128], F32)
        iot_row = p.sbuf("c_iotrow", [128, 128], F32)
        iot_col = p.sbuf("c_iotcol", [128, 1], F32)
        self.iot_row = iot_row
        self.iot_col = iot_col
        self.ones_row = p.sbuf("c_onesrow", [1, 128], BF16)
        self.ones_rowf = p.sbuf("c_onesrowf", [1, 128], F32)
        self.ones_colf = p.sbuf("c_onescolf", [128, 1], F32)
        p.op("gpsimd", "iota", out=iot_row, pattern=[[1, 128]], base=0,
             channel_multiplier=0, allow_small_or_imprecise_dtypes=True)
        p.op("gpsimd", "iota", out=iot_col, pattern=[[0, 1]], base=0,
             channel_multiplier=1, allow_small_or_imprecise_dtypes=True)
        p.op("vector", "tensor_scalar", out=self.ident, in0=iot_row,
             scalar1=iot_col, scalar2=None, op0=ALU.is_equal)
        p.op("vector", "tensor_scalar", out=self.identf, in0=iot_row,
             scalar1=iot_col, scalar2=None, op0=ALU.is_equal)
        p.op("vector", "memset", ap=self.ones_row, constant=1.0)
        p.op("vector", "memset", ap=self.ones_rowf, constant=1.0)
        p.op("vector", "memset", ap=self.ones_colf, constant=1.0)
        self._consts = {}

    def c(self, val):
        if val not in self._consts:
            t = self.p.sbuf(f"c_k{len(self._consts)}", [128, 1], F32)
            self.p.op("vector", "memset", ap=t, constant=float(val))
            self._consts[val] = t
        return self._consts[val]


def emit_ln(p, cm, x_f32, xn_out, s1, s2, mean, var, rstd, nmr, xsq, n=1024):
    p.op("scalar", "activation", out=xsq, in_=x_f32, func=AF.Square)
    p.op("vector", "reduce_sum", out=s1, in_=x_f32, axis=AXL.X)
    p.op("vector", "reduce_sum", out=s2, in_=xsq, axis=AXL.X)
    p.op("vector", "tensor_scalar", out=mean, in0=s1, scalar1=1.0 / n,
         scalar2=None, op0=ALU.mult)
    p.op("vector", "tensor_scalar", out=s2, in0=s2, scalar1=1.0 / n,
         scalar2=None, op0=ALU.mult)
    p.op("vector", "scalar_tensor_tensor", out=var, in0=mean, scalar=mean,
         in1=s2, op0=ALU.mult, op1=ALU.subtract)
    p.op("vector", "tensor_scalar", out=var, in0=var, scalar1=-1.0,
         scalar2=None, op0=ALU.mult)
    p.op("scalar", "activation", out=rstd, in_=var, func=AF.Sqrt, bias=cm.c(1e-5))
    p.op("vector", "reciprocal", out=rstd, in_=rstd)
    p.op("vector", "tensor_scalar", out=nmr, in0=mean, scalar1=rstd,
         scalar2=-1.0, op0=ALU.mult, op1=ALU.mult)
    p.op("vector", "tensor_scalar", out=xn_out, in0=x_f32,
         scalar1=rstd, scalar2=nmr, op0=ALU.mult, op1=ALU.add)



def build_block0(p: Prog, cm: Common, h, x_own_dram, x_other_dram, W,
                 x_out_dram):
    nc_kv = 4 * (h + 1)
    kv_len = 512 * nc_kv
    nt_kv = kv_len // 128

    psP, psT0, psT1, _psF0 = std_psums(p)
    psS, psA, psB, psC, psD, psE = psP

    w1t_d = p.dram("b0_w1t", [8, 4096, 128], BF16, "Internal")
    w2t_d = p.dram("b0_w2t", [32, 1024, 128], BF16, "Internal")

    xn_fm = p.alloc("xn_fm", [128, NCD, T_OWN], BF16)
    lnb = [p.alloc(f"ln{i}", [128, 1], F32) for i in range(6)]
    xsq = p.alloc("xsq", [128, 1024], F32)

    wtile = [p.alloc(f"wtile{i}", [128, 4096], F32) for i in range(2)]
    ttile = [p.alloc(f"ttile{i}", [128, 4096], BF16) for i in range(2)]
    tmpt = p.alloc("tmpt", [128, 4096], F32)
    absa = p.alloc("absa", [128, 2], F32)
    scl = p.alloc("scl", [128, 2], F32)

    for mi, (src, nt, wd, scratch) in enumerate(
            ((W["fw1"], 32, 1024, w1t_d), (W["fw2"], 8, 4096, w2t_d))):
        for i in range(nt):
            b = i % 2
            p.dma("gpsimd", out=wtile[b][:, 0:wd], in_=src[i * 128:(i + 1) * 128, :])
            p.op("scalar", "activation", out=tmpt[:, 0:wd], in_=wtile[b][:, 0:wd],
                 func=AF.Abs, accum_out=absa[:, 1:2])
            if i == 0:
                p.op("vector", "tensor_copy", out=absa[:, 0:1], in_=absa[:, 1:2])
            else:
                p.op("vector", "tensor_tensor", out=absa[:, 0:1],
                     in0=absa[:, 0:1], in1=absa[:, 1:2], op=ALU.add)
        p.mm(psS[0:1, 0:1], cm.ones_colf, absa[:, 0:1], True, True)
        p.op("scalar", "activation", out=scl[0:1, 1:2], in_=psS[0:1, 0:1],
             func=AF.Copy, scale=cm.c(1.0 / (4096.0 * 1024.0))[0:1, :])
        p.op("vector", "tensor_scalar", out=scl[0:1, 1:2], in0=scl[0:1, 1:2],
             scalar1=1e-8, scalar2=None, op0=ALU.max)
        p.op("vector", "reciprocal", out=scl[0:1, 1:2], in_=scl[0:1, 1:2])
        p.mm(psA[:, 0:1], cm.ones_rowf, scl[0:1, 1:2], True, True)
        p.op("vector", "tensor_copy", out=scl[:, 0:1], in_=psA[:, 0:1])
        for i in range(nt):
            b = i % 2
            p.dma("gpsimd", out=wtile[b][:, 0:wd], in_=src[i * 128:(i + 1) * 128, :])
            p.op("vector", "tensor_scalar", out=tmpt[:, 0:wd], in0=wtile[b][:, 0:wd],
                 scalar1=scl[:, 0:1], scalar2=0.5, op0=ALU.mult, op1=ALU.is_gt)
            p.op("gpsimd", "tensor_scalar", out=wtile[b][:, 0:wd], in0=wtile[b][:, 0:wd],
                 scalar1=scl[:, 0:1], scalar2=-0.5, op0=ALU.mult, op1=ALU.is_lt)
            p.op("vector", "tensor_tensor", out=ttile[b][:, 0:wd], in0=tmpt[:, 0:wd],
                 in1=wtile[b][:, 0:wd], op=ALU.subtract)
            p.dma("gpsimd",
                  out=scratch[:, i * 128:(i + 1) * 128, :].rearrange("c p q -> p c q"),
                  in_=ttile[b][:, 0:wd].rearrange("p (c q) -> p c q", q=128))
    p.free("wtile0", "wtile1", "ttile0", "ttile1", "tmpt", "absa", "scl")

    p.bump()
    wq = p.alloc("wq", [128, NCD, 64], BF16)
    wk = p.alloc("wk", [128, NCD, 64], BF16)
    wv = p.alloc("wv", [128, NCD, 1024], BF16)
    q_fm = p.alloc("q_fm", [64, T_OWN], BF16)
    k_fm = p.alloc("k_fm", [64, kv_len], BF16)
    v_tm = p.alloc("v_tm", [128, nt_kv, 1024], BF16)
    xg_fm = p.alloc("xg_fm", [128, NCD, 512], BF16)
    xraw = [p.alloc(f"xraw{i}", [128, 1024], F32) for i in range(2)]
    xn_t = [p.alloc(f"xn_t{i}", [128, 1024], BF16) for i in range(2)]

    p.dma("gpsimd", out=wq, in_=W["q"].rearrange("(c p) f -> p c f", p=128))
    p.dma("gpsimd", out=wk, in_=W["k"].rearrange("(c p) f -> p c f", p=128))
    p.dma("gpsimd", out=wv, in_=W["v"].rearrange("(c p) f -> p c f", p=128))

    for g in range(nc_kv):
        own = g >= nc_kv - 4
        og = g - (nc_kv - 4)
        dst = xn_fm[:, :, og * 512:(og + 1) * 512] if own else xg_fm
        for j in range(4):
            t = g * 4 + j
            buf = xraw[j % 2]
            src = x_own_dram if own else x_other_dram
            row = (og * 4 + j) if own else t
            p.dma("sync", out=buf, in_=src[row * 128:(row + 1) * 128, :])
            xnt = xn_t[j % 2]
            emit_ln(p, cm, buf, xnt, *lnb, xsq)
            for c in range(NCD):
                pb = psT0 if c % 2 == 0 else psT1
                p.xpose(pb, xnt[:, c * 128:(c + 1) * 128], cm.ident)
                p.op("scalar", "copy", out=dst[:, c, j * 128:(j + 1) * 128], in_=pb)
        for c in range(NCD):
            p.mm(psS[0:64, :], wk[:, c, :], dst[:, c, :], c == 0, c == NCD - 1)
        p.op("scalar", "copy", out=k_fm[:, g * 512:(g + 1) * 512], in_=psS[0:64, :])
        if own:
            for c in range(NCD):
                p.mm(psA[0:64, :], wq[:, c, :], dst[:, c, :], c == 0, c == NCD - 1)
            p.op("scalar", "copy", out=q_fm[:, og * 512:(og + 1) * 512], in_=psA[0:64, :])
        for j in range(4):
            for fh in range(2):
                po = psB if fh == 0 else psC
                for c in range(NCD):
                    p.mm(po, dst[:, c, j * 128:(j + 1) * 128],
                         wv[:, c, fh * 512:(fh + 1) * 512], c == 0, c == NCD - 1)
                p.op("vector", "tensor_copy",
                     out=v_tm[:, g * 4 + j, fh * 512:(fh + 1) * 512], in_=po)
    p.free("wq", "wk", "wv", "xg_fm", "xraw0", "xraw1", "xn_t0", "xn_t1")

    s_row = p.alloc("s_row", [128, kv_len], F32)
    p_row = p.alloc("p_row", [128, kv_len], BF16)
    pt_buf = p.alloc("pt_buf", [128, nt_kv * 128], BF16)
    mr = p.alloc("mr", [128, 4], F32)
    rinv = p.alloc("rinv", [128, 1], F32)
    xres = [p.alloc(f"xres{i}", [128, 1024], F32) for i in range(2)]

    for t in range(NT_OWN):
        p.bump()
        q0g = h * T_OWN + t * 128
        n_ch = q0g // 512 + 1
        n_kt = q0g // 128 + 1
        p.dma("sync", out=xres[t % 2], in_=x_own_dram[t * 128:(t + 1) * 128, :])
        for kc in range(n_ch):
            p.mm(psS, q_fm[:, t * 128:(t + 1) * 128],
                 k_fm[:, kc * 512:(kc + 1) * 512], True, True)
            sl = s_row[:, kc * 512:(kc + 1) * 512]
            if kc == n_ch - 1:
                p.op("scalar", "copy", out=sl, in_=psS)
                p.op("gpsimd", "affine_select", out=sl, in_=sl,
                     pattern=[[-1, 512]], base=q0g - kc * 512,
                     channel_multiplier=1, compare_op=ALU.is_ge, fill=NEG)
                src_ts = sl
            else:
                src_ts = psS
            seed = NEGSEED if kc == 0 else mr[:, (kc - 1) % 2:(kc - 1) % 2 + 1]
            p.op("vector", "tensor_scalar", out=sl, in0=src_ts, scalar1=0.0,
                 scalar2=seed, op0=ALU.add, op1=ALU.max,
                 accum_out=mr[:, kc % 2:kc % 2 + 1])
        mfin = mr[:, (n_ch - 1) % 2:(n_ch - 1) % 2 + 1]
        p.op("vector", "tensor_scalar", out=mr[:, 2:3], in0=mfin,
             scalar1=-0.125, scalar2=None, op0=ALU.mult)
        p.op("scalar", "activation", out=p_row[:, 0:n_ch * 512],
             in_=s_row[:, 0:n_ch * 512], func=AF.Exp, scale=cm.c(0.125),
             bias=mr[:, 2:3], accum_out=mr[:, 3:4])
        p.op("vector", "reciprocal", out=rinv, in_=mr[:, 3:4])
        for kt in range(n_kt):
            pb = psT0 if kt % 2 == 0 else psT1
            p.xpose(pb, p_row[:, kt * 128:(kt + 1) * 128], cm.ident)
            p.op("scalar", "copy", out=pt_buf[:, kt * 128:(kt + 1) * 128], in_=pb)
        for fh in range(2):
            po = psA if fh == 0 else psB
            for kt in range(n_kt):
                p.mm(po, pt_buf[:, kt * 128:(kt + 1) * 128],
                     v_tm[:, kt, fh * 512:(fh + 1) * 512], kt == 0, kt == n_kt - 1)
            p.op("vector", "scalar_tensor_tensor",
                 out=xres[t % 2][:, fh * 512:(fh + 1) * 512], in0=po,
                 scalar=rinv, in1=xres[t % 2][:, fh * 512:(fh + 1) * 512],
                 op0=ALU.mult, op1=ALU.add)
        p.dma("sync", out=x_out_dram[t * 128:(t + 1) * 128, :], in_=xres[t % 2])
    p.free("s_row", "p_row", "pt_buf", "mr", "rinv", "xres0", "xres1",
           "q_fm", "k_fm", "v_tm")

    b1s = p.alloc("b1s", [32, 128], F32)
    b1t = p.alloc("b1t", [128, 32], F32)
    b2r = p.alloc("b2r", [1, 1024], BF16)
    b2b = p.alloc("b2b", [128, 1024], F32)
    p.dma("sync", out=b1s, in_=W["b1"].rearrange("(j p) -> j p", p=128))
    p.xpose(psS[:, 0:32], b1s, cm.identf[0:32, 0:32])
    p.op("vector", "tensor_copy", out=b1t, in_=psS[:, 0:32])
    p.dma("gpsimd", out=b2r, in_=W["b2"].unsqueeze(0))
    for fh in range(2):
        p.mm(psA, cm.ones_row, b2r[:, fh * 512:(fh + 1) * 512], True, True)
        p.op("vector", "tensor_copy", out=b2b[:, fh * 512:(fh + 1) * 512], in_=psA)

    w1ts = [p.alloc(f"w1ts{i}", [128, NCD, 512], BF16) for i in range(2)]
    w2ts = [p.alloc(f"w2ts{i}", [128, 1024], BF16) for i in range(2)]
    h_fm = p.alloc("h_fm", [128, 32, 256], BF16)
    xres2 = [p.alloc(f"xres2{i}", [128, 1024], F32) for i in range(2)]

    for g in range(8):
        p.bump()
        tok0 = g * 256
        for jg in range(8):
            wb = w1ts[jg % 2]
            for c in range(NCD):
                p.dma("sync", out=wb[:, c, :],
                      in_=w1t_d[c, jg * 512:(jg + 1) * 512, :], transpose=True)
            for j4 in range(4):
                j = jg * 4 + j4
                ph = psS if j % 2 == 0 else psA
                for c in range(NCD):
                    p.mm(ph[:, 0:256], wb[:, c, j4 * 128:(j4 + 1) * 128],
                         xn_fm[:, c, tok0:tok0 + 256], c == 0, c == NCD - 1)
                p.op("scalar", "activation", out=h_fm[:, j, :], in_=ph[:, 0:256],
                     func=AF.Gelu, bias=b1t[:, j:j + 1])
        banks = (psB, psC, psD, psE)
        for j in range(32):
            wb2 = w2ts[j % 2]
            p.dma("sync", out=wb2, in_=w2t_d[j, :, :], transpose=True)
            for tt in range(2):
                for fh in range(2):
                    p.mm(banks[tt * 2 + fh],
                         h_fm[:, j, tt * 128:(tt + 1) * 128],
                         wb2[:, fh * 512:(fh + 1) * 512], j == 0, j == 31)
        for tt in range(2):
            t = g * 2 + tt
            xr = xres2[tt]
            p.dma("sync", out=xr, in_=x_out_dram[t * 128:(t + 1) * 128, :])
            p.op("vector", "tensor_tensor", out=xr, in0=xr, in1=b2b, op=ALU.add)
            for fh in range(2):
                p.op("vector", "tensor_tensor",
                     out=xr[:, fh * 512:(fh + 1) * 512],
                     in0=xr[:, fh * 512:(fh + 1) * 512],
                     in1=banks[tt * 2 + fh], op=ALU.add)
            p.dma("sync", out=x_out_dram[t * 128:(t + 1) * 128, :], in_=xr)
    p.free("b1s", "b1t", "b2r", "b2b", "w1ts0", "w1ts1", "w2ts0", "w2ts1",
           "h_fm", "xres20", "xres21", "xn_fm")
    for nm in ("ln0", "ln1", "ln2", "ln3", "ln4", "ln5", "xsq"):
        p.free(nm)



def make_wt_scratch(p: Prog, name, w_dram, O, I, row_step=512):
    C = (I + 127) // 128
    O_pad = (O + 15) // 16 * 16
    scr = p.dram(name, [C, O_pad, 128], BF16, "Internal")
    i_full = (I // 128) * 128
    for r0 in range(0, O, row_step):
        nr = min(row_step, O - r0)
        if i_full > 0:
            p.dma("gpsimd",
                  out=scr[0:i_full // 128, r0:r0 + nr, :].rearrange("c p q -> p c q"),
                  in_=w_dram[r0:r0 + nr, 0:i_full].rearrange("p (c q) -> p c q", q=128))
        if I > i_full:
            p.dma("gpsimd",
                  out=scr[C - 1, r0:r0 + nr, 0:I - i_full],
                  in_=w_dram[r0:r0 + nr, i_full:I])
    return scr


def bcast_row(p, cm, psum, out_sb, row_ap, width):
    for o in range(0, width, 512):
        w = min(512, width - o)
        p.mm(psum[:, 0:w], cm.ones_rowf, row_ap[:, o:o + w], True, True)
        p.op("vector", "tensor_copy", out=out_sb[:, o:o + w], in_=psum[:, 0:w])


def emit_softplus(p, cm, x, tmp):
    p.op("scalar", "activation", out=tmp, in_=x, func=AF.Abs)
    p.op("scalar", "activation", out=tmp, in_=tmp, func=AF.Exp, scale=cm.c(-1.0))
    p.op("scalar", "activation", out=tmp, in_=tmp, func=AF.Ln, bias=cm.c(1.0))
    p.op("vector", "tensor_scalar", out=x, in0=x, scalar1=0.0, scalar2=None,
         op0=ALU.max)
    p.op("vector", "tensor_tensor", out=x, in0=x, in1=tmp, op=ALU.add)



NH = 16
HP = 128
NST = 64
CH = 256
DIN = 2048
DPROJ = 4240
FFM = 2730


def build_block1(p: Prog, cm: Common, h, x_own_dram, x_other_dram, W, x_out_dram):
    n_chunks = 8 * (h + 1)
    own0 = n_chunks - 8

    psP, psT0, psT1, psF0 = std_psums(p)

    win_t = make_wt_scratch(p, "b1_wint", W["in_proj"], DPROJ, 1024)
    wout_t = make_wt_scratch(p, "b1_woutt", W["out_proj"], 1024, DIN)
    w1_t = [make_wt_scratch(p, f"b1_w1t{e}", W["W1"][e], FFM, 1024) for e in range(8)]
    w2_t = [make_wt_scratch(p, f"b1_w2t{e}", W["W2"][e], FFM, 1024) for e in range(8)]
    w3_t = [make_wt_scratch(p, f"b1_w3t{e}", W["W3"][e], 1024, FFM) for e in range(8)]

    small = p.alloc("small", [1, 2096], F32)
    dtb_b = p.alloc("dtb_b", [128, 16], F32)
    a_b = p.alloc("a_b", [128, 16], F32)
    dp_b = p.alloc("dp_b", [128, 16], F32)
    nw_b = p.alloc("nw_b", [128, 2048], BF16)
    cw_x = p.alloc("cw_x", [128, 16, 4], F32)
    cb_x = p.alloc("cb_x", [128, 16], F32)
    cw_B = p.alloc("cw_B", [64, 4], F32)
    cb_B = p.alloc("cb_B", [64, 1], F32)
    cw_C = p.alloc("cw_C", [64, 4], F32)
    cb_C = p.alloc("cb_C", [64, 1], F32)
    p.dma("sync", out=small[:, 0:16], in_=W["dt_bias"].unsqueeze(0))
    p.dma("sync", out=small[:, 16:32], in_=W["A_log"].unsqueeze(0))
    p.dma("sync", out=small[:, 32:48], in_=W["Dp"].unsqueeze(0))
    p.dma("sync", out=small[:, 48:2096], in_=W["norm_w"].unsqueeze(0))
    p.dma("sync", out=cw_x,
          in_=W["conv_w"][0:2048, :].rearrange("(c p) w -> p c w", p=128))
    p.dma("sync", out=cb_x, in_=W["conv_b"][0:2048].rearrange("(c p) -> p c", p=128),
          allow_slow_non_contiguous=True)
    p.dma("sync", out=cw_B, in_=W["conv_w"][2048:2112, :])
    p.dma("sync", out=cb_B, in_=W["conv_b"][2048:2112].unsqueeze(1))
    p.dma("sync", out=cw_C, in_=W["conv_w"][2112:2176, :])
    p.dma("sync", out=cb_C, in_=W["conv_b"][2112:2176].unsqueeze(1))
    p.op("scalar", "activation", out=small[:, 16:32], in_=small[:, 16:32], func=AF.Exp)
    p.op("vector", "tensor_scalar", out=small[:, 16:32], in0=small[:, 16:32],
         scalar1=-1.0, scalar2=None, op0=ALU.mult)
    bcast_row(p, cm, psP[0], dtb_b, small[:, 0:16], 16)
    bcast_row(p, cm, psP[0], a_b, small[:, 16:32], 16)
    bcast_row(p, cm, psP[0], dp_b, small[:, 32:48], 16)
    nw_f = p.alloc("nw_f", [128, 2048], F32)
    bcast_row(p, cm, psP[0], nw_f, small[:, 48:2096], 2048)
    p.op("vector", "tensor_copy", out=nw_b, in_=nw_f)
    p.free("nw_f")

    maskc = p.alloc("maskc", [128, 2, 256], F32)
    p.op("vector", "memset", ap=maskc, constant=0.0)
    for i in range(2):
        p.op("gpsimd", "affine_select", out=maskc[:, i, :], in_=maskc[:, i, :],
             pattern=[[1, 256]], base=-i * 128, channel_multiplier=-1,
             compare_op=ALU.is_ge, fill=NEG)
    triu = p.alloc("triu", [128, 128], F32)
    p.op("vector", "tensor_scalar", out=triu, in0=cm.iot_row,
         scalar1=cm.iot_col, scalar2=None, op0=ALU.is_ge)

    xn_own = p.alloc("xn_own", [128, NCD, T_OWN], BF16)
    prev = p.alloc("prev", [128, NH, NST], F32)
    prevT = p.alloc("prevT", [64, NH, HP], BF16)
    p.op("vector", "memset", ap=prev, constant=0.0)
    win_s = p.alloc("win_s", [128, NCD, 2192], BF16)
    for c in range(NCD):
        p.dma("sync", out=win_s[:, c, 0:2048], in_=win_t[c, 2048:4096, :],
              transpose=True)
        p.dma("sync", out=win_s[:, c, 2048:2192], in_=win_t[c, 4096:4240, :],
              transpose=True)
    wz_s = [p.alloc("wz_s0", [128, NCD, 512], BF16)] * 2
    woc = [p.alloc(f"woc{i}", [128, 1024], BF16) for i in range(2)]

    xn_g = p.alloc("xn_g", [128, NCD, CH], BF16)
    xraw = [p.alloc(f"xraw{i}", [128, 1024], F32) for i in range(2)]
    xnt = [p.alloc(f"xnt{i}", [128, 1024], BF16) for i in range(2)]
    lnb = [p.alloc(f"ln{i}", [128, 1], F32) for i in range(6)]
    scr16 = p.alloc("scr16", [128, 4096], F32)
    xsq = scr16[:, 0:2048]
    zbuf = p.alloc("zbuf", [128, 2, DIN], BF16)
    xbc = p.alloc("xbc", [128, 16, CH + 4], BF16)
    bpre = p.alloc("bpre", [64, CH + 4], BF16)
    cpre = p.alloc("cpre", [64, CH + 4], BF16)
    xbs = p.alloc("xbs", [128, 16, CH], BF16)
    b_fm = p.alloc("b_fm", [64, CH], BF16)
    c_fm = p.alloc("c_fm", [64, CH], BF16)
    dts = p.alloc("dts", [128, 2, NH], F32)
    acum = p.alloc("acum", [128, 2, NH], F32)
    nac = p.alloc("nac", [128, 2, NH], F32)
    ea = p.alloc("ea", [128, 2, NH], F32)
    acr = p.alloc("acr", [1, CH], F32)
    awk = p.alloc("awk", [128, 2, NH], F32)
    alb = p.alloc("alb", [128, 16], F32)
    cdb = p.alloc("cdb", [128, 16], F32)
    s_sb = p.alloc("s_sb", [128, 2, CH], F32)
    g_sb = p.alloc("g_sb", [128, 2, CH], BF16)
    xtm = p.alloc("xtm", [128, 2, HP], BF16)
    btm = p.alloc("btm", [128, 2, NST], BF16)
    xw1 = p.alloc("xw1", [128, 2, HP], BF16)
    ysb = p.alloc("ysb", [128, 2, DIN], F32)
    tmp32 = p.alloc("tmp32", [128, 2, NH], F32)
    carry = None
    yfm = p.alloc("yfm", [128, 16, CH], BF16)
    ynb = None
    xres = [scr16[:, 0:1024]] * 2
    ynb = scr16[:, 2048:3072].bitcast(BF16)
    e_sb = scr16[:, 3072:3328]
    ge = scr16[:, 3328:3456].bitcast(BF16)
    carry = scr16[0:1, 3968:4000]

    p.op("vector", "memset", ap=xbc[:, :, 0:4], constant=0.0)
    p.op("vector", "memset", ap=bpre[:, 0:4], constant=0.0)
    p.op("vector", "memset", ap=cpre[:, 0:4], constant=0.0)

    XOF = 2048

    for ck in range(n_chunks):
        own = ck >= own0
        oc = ck - own0
        if own:
            p.bump()
        xn_fm = xn_own[:, :, oc * CH:(oc + 1) * CH] if own else xn_g[:, :, :]
        for j in range(2):
            tglob = ck * 2 + j
            buf = xraw[j % 2]
            if own:
                p.dma("sync", out=buf,
                      in_=x_own_dram[(oc * 2 + j) * 128:(oc * 2 + j + 1) * 128, :])
            else:
                p.dma("sync", out=buf, in_=x_other_dram[tglob * 128:(tglob + 1) * 128, :])
            emit_ln(p, cm, buf, xnt[j % 2], *lnb, xsq[:, 0:1024])
            for c in range(NCD):
                pb = psT0 if c % 2 == 0 else psT1
                p.xpose(pb, xnt[j % 2][:, c * 128:(c + 1) * 128], cm.ident)
                p.op("scalar", "copy", out=xn_fm[:, c, j * 128:(j + 1) * 128], in_=pb)
        if ck > 0:
            p.op("vector", "tensor_copy", out=xbc[:, :, 0:4], in_=xbc[:, :, CH:CH + 4])
            p.op("vector", "tensor_copy", out=bpre[:, 0:4], in_=bpre[:, CH:CH + 4])
            p.op("vector", "tensor_copy", out=cpre[:, 0:4], in_=cpre[:, CH:CH + 4])
        for fo in range(16):
            ph = psP[fo % 2]
            for c in range(NCD):
                p.mm(ph[:, 0:CH], win_s[:, c, fo * 128:(fo + 1) * 128],
                     xn_fm[:, c, :], c == 0, c == NCD - 1)
            p.op("vector", "tensor_copy", out=xbc[:, fo, 4:CH + 4], in_=ph[:, 0:CH])
        for dst, col0 in ((bpre, 2048), (cpre, 2112)):
            ph = psP[2]
            for c in range(NCD):
                p.mm(ph[0:64, 0:CH], win_s[:, c, col0:col0 + 64],
                     xn_fm[:, c, :], c == 0, c == NCD - 1)
            p.op("vector", "tensor_copy", out=dst[:, 4:CH + 4], in_=ph[0:64, 0:CH])
        if own:
            for f4 in range(4):
                wz = wz_s[f4 % 2]
                for c in range(NCD):
                    p.dma("sync", out=wz[:, c, :],
                          in_=win_t[c, f4 * 512:(f4 + 1) * 512, :], transpose=True)
                for j in range(2):
                    ph = psP[3 + j]
                    for c in range(NCD):
                        p.mm(ph, xn_fm[:, c, j * 128:(j + 1) * 128],
                             wz[:, c, :], c == 0, c == NCD - 1)
                    p.op("vector", "tensor_copy", out=zbuf[:, j, f4 * 512:(f4 + 1) * 512],
                         in_=ph)
        for j in range(2):
            ph = psP[2]
            for c in range(NCD):
                p.mm(ph[:, 0:16], xn_fm[:, c, j * 128:(j + 1) * 128],
                     win_s[:, c, 2176:2192], c == 0, c == NCD - 1)
            p.op("vector", "tensor_copy", out=dts[:, j, :], in_=ph[:, 0:16])
        conv_out = scr16.rearrange("p (c t) -> p c t", t=CH)
        for fo in range(16):
            co = conv_out[:, fo, :]
            p.op("vector", "tensor_scalar", out=co, in0=xbc[:, fo, 1:CH + 1],
                 scalar1=cw_x[:, fo, 0:1], scalar2=cb_x[:, fo:fo + 1],
                 op0=ALU.mult, op1=ALU.add)
            for w in range(1, 4):
                p.op("vector", "scalar_tensor_tensor", out=co,
                     in0=xbc[:, fo, 1 + w:CH + 1 + w], scalar=cw_x[:, fo, w:w + 1],
                     in1=co, op0=ALU.mult, op1=ALU.add)
            p.op("scalar", "activation", out=xbs[:, fo, :], in_=co, func=AF.Silu)
        for pre, cw, cb, dstf in ((bpre, cw_B, cb_B, b_fm), (cpre, cw_C, cb_C, c_fm)):
            co = conv_out[0:64, 0, :]
            p.op("vector", "tensor_scalar", out=co, in0=pre[:, 1:CH + 1],
                 scalar1=cw[:, 0:1], scalar2=cb[:, 0:1], op0=ALU.mult, op1=ALU.add)
            for w in range(1, 4):
                p.op("vector", "scalar_tensor_tensor", out=co, in0=pre[:, 1 + w:CH + 1 + w],
                     scalar=cw[:, w:w + 1], in1=co, op0=ALU.mult, op1=ALU.add)
            p.op("scalar", "activation", out=dstf, in_=co, func=AF.Silu)
        dflat = dts.rearrange("p a b -> p (a b)")
        p.op("vector", "tensor_tensor", out=dts[:, 0, :], in0=dts[:, 0, :],
             in1=dtb_b, op=ALU.add)
        p.op("vector", "tensor_tensor", out=dts[:, 1, :], in0=dts[:, 1, :],
             in1=dtb_b, op=ALU.add)
        emit_softplus(p, cm, dflat, tmp32.rearrange("p a b -> p (a b)"))
        for j in range(2):
            p.op("vector", "tensor_tensor", out=tmp32[:, j, :], in0=dts[:, j, :],
                 in1=a_b, op=ALU.mult)
        p.mm(psF0[:, 0:16], triu, tmp32[:, 0, :], True, True)
        p.op("vector", "tensor_copy", out=acum[:, 0, :], in_=psF0[:, 0:16])
        p.mm(psP[2][0:1, 0:16], cm.ones_colf, tmp32[:, 0, :], True, True)
        p.op("scalar", "copy", out=carry[:, 0:16], in_=psP[2][0:1, 0:16])
        p.mm(psF0[:, 0:16], triu, tmp32[:, 1, :], True, False)
        p.mm(psF0[:, 0:16], cm.ones_rowf, carry[:, 0:16], False, True)
        p.op("vector", "tensor_copy", out=acum[:, 1, :], in_=psF0[:, 0:16])
        p.mm(psP[2][0:1, 0:16], cm.ones_colf, tmp32[:, 0, :], True, False)
        p.mm(psP[2][0:1, 0:16], cm.ones_colf, tmp32[:, 1, :], False, True)
        p.op("scalar", "copy", out=carry[:, 16:32], in_=psP[2][0:1, 0:16])
        p.mm(psP[2][:, 0:16], cm.ones_rowf, carry[:, 16:32], True, True)
        p.op("vector", "tensor_copy", out=alb, in_=psP[2][:, 0:16])
        p.op("scalar", "activation", out=cdb, in_=alb, func=AF.Exp)
        p.op("vector", "tensor_scalar", out=nac.rearrange("p a b -> p (a b)"),
             in0=acum.rearrange("p a b -> p (a b)"), scalar1=-1.0, scalar2=None,
             op0=ALU.mult)
        p.op("scalar", "activation", out=ea.rearrange("p a b -> p (a b)"),
             in_=acum.rearrange("p a b -> p (a b)"), func=AF.Exp)
        for j in range(2):
            p.op("vector", "tensor_tensor", out=awk[:, j, :], in0=alb,
                 in1=acum[:, j, :], op=ALU.subtract)
        p.op("scalar", "activation", out=awk.rearrange("p a b -> p (a b)"),
             in_=awk.rearrange("p a b -> p (a b)"), func=AF.Exp)
        for j in range(2):
            p.op("vector", "tensor_tensor", out=awk[:, j, :], in0=awk[:, j, :],
                 in1=dts[:, j, :], op=ALU.mult)
        for i in range(2):
            pb = psT0 if i == 0 else psT1
            p.xpose(pb[:, 0:64], b_fm[:, i * 128:(i + 1) * 128], cm.ident[0:64, 0:64])
            p.op("scalar", "copy", out=btm[:, i, :], in_=pb[:, 0:64])
        if own:
            for i in range(2):
                p.mm(psP[2][:, 0:CH], b_fm[:, i * 128:(i + 1) * 128], c_fm,
                     True, True)
                p.op("vector", "tensor_copy", out=s_sb[:, i, :], in_=psP[2][:, 0:CH])
            for hh in range(NH):
                p.op("tensor", "transpose", out=psF0[0:64, 0:128], in_=prev[:, hh, :],
                     identity=cm.identf)
                p.op("vector", "tensor_copy", out=prevT[:, hh, :], in_=psF0[0:64, 0:128])
        for hh in range(NH):
            for j in range(2):
                pb = psT0 if j == 0 else psT1
                p.xpose(pb, xbs[:, hh, j * 128:(j + 1) * 128], cm.ident)
                p.op("scalar", "copy", out=xtm[:, j, :], in_=pb)
            if own:
                for j in range(2):
                    p.op("tensor", "transpose", out=psF0[0:1, 0:128],
                         in_=acum[:, j, hh:hh + 1], identity=cm.identf)
                    p.op("scalar", "copy", out=acr[:, j * 128:(j + 1) * 128],
                         in_=psF0[0:1, 0:128])
                for i in range(2):
                    p.mm(psP[3][:, 0:CH], cm.ones_rowf, acr, True, True)
                    p.op("vector", "tensor_tensor", out=e_sb, in0=psP[3][:, 0:CH],
                         in1=maskc[:, i, :], op=ALU.add)
                    p.op("scalar", "activation", out=ge, in_=e_sb, func=AF.Exp,
                         bias=nac[:, i, hh:hh + 1])
                    p.op("vector", "tensor_tensor", out=g_sb[:, i, :], in0=s_sb[:, i, :],
                         in1=ge, op=ALU.mult)
                for j in range(2):
                    p.op("vector", "tensor_scalar", out=xw1[:, j, :], in0=xtm[:, j, :],
                         scalar1=dts[:, j, hh:hh + 1], scalar2=None, op0=ALU.mult)
                for qs in range(2):
                    ysl = ysb[:, qs, hh * 128:(hh + 1) * 128]
                    for i in range(2):
                        p.mm(psP[4][:, 0:128], g_sb[:, i, qs * 128:(qs + 1) * 128],
                             xw1[:, i, :], i == 0, i == 1)
                    p.mm(psP[2][:, 0:128], c_fm[:, qs * 128:(qs + 1) * 128],
                         prevT[:, hh, :], True, True)
                    p.op("vector", "tensor_copy", out=ysl, in_=psP[4][:, 0:128])
                    p.op("vector", "scalar_tensor_tensor", out=ysl,
                         in0=psP[2][:, 0:128], scalar=ea[:, qs, hh:hh + 1], in1=ysl,
                         op0=ALU.mult, op1=ALU.add)
                    p.op("vector", "scalar_tensor_tensor", out=ysl,
                         in0=xtm[:, qs, :], scalar=dp_b[:, hh:hh + 1], in1=ysl,
                         op0=ALU.mult, op1=ALU.add)
            for j in range(2):
                p.op("vector", "tensor_scalar", out=xw1[:, j, :], in0=xtm[:, j, :],
                     scalar1=awk[:, j, hh:hh + 1], scalar2=None, op0=ALU.mult)
            for i in range(2):
                p.mm(psP[3][:, 0:NST], xw1[:, i, :], btm[:, i, :], i == 0, i == 1)
            p.op("vector", "scalar_tensor_tensor", out=prev[:, hh, :],
                 in0=prev[:, hh, :], scalar=cdb[:, hh:hh + 1], in1=psP[3][:, 0:NST],
                 op0=ALU.mult, op1=ALU.add)
        if DBG and ck == own0:
            p.dbg("xnfm", xn_fm)
            p.dbg("xbs", xbs)
            p.dbg("bfm", b_fm)
            p.dbg("cfm", c_fm)
            p.dbg("dts", dts)
            p.dbg("acum", acum)
            p.dbg("awk", awk)
            p.dbg("cdb", cdb)
            p.dbg("prev", prev)
            if own:
                p.dbg("ssb", s_sb)
                p.dbg("ysb", ysb)
                p.dbg("zbuf", zbuf)
        if not own:
            continue
        zf = zbuf.rearrange("p a b -> p (a b)")
        p.op("scalar", "activation", out=zf, in_=zf, func=AF.Silu)
        p.op("vector", "tensor_tensor", out=ysb.rearrange("p a b -> p (a b)"),
             in0=ysb.rearrange("p a b -> p (a b)"), in1=zf, op=ALU.mult)
        for j in range(2):
            p.op("scalar", "activation", out=xsq, in_=ysb[:, j, :], func=AF.Square)
            p.op("vector", "reduce_sum", out=lnb[0], in_=xsq, axis=AXL.X)
            p.op("vector", "tensor_scalar", out=lnb[1], in0=lnb[0],
                 scalar1=1.0 / DIN, scalar2=None, op0=ALU.mult)
            p.op("scalar", "activation", out=lnb[2], in_=lnb[1], func=AF.Sqrt,
                 bias=cm.c(1e-5))
            p.op("vector", "reciprocal", out=lnb[2], in_=lnb[2])
            p.op("vector", "tensor_scalar", out=ynb, in0=ysb[:, j, :],
                 scalar1=lnb[2], scalar2=None, op0=ALU.mult)
            p.op("vector", "tensor_tensor", out=ynb, in0=ynb,
                 in1=nw_b, op=ALU.mult)
            for c in range(16):
                pb = psT0 if c % 2 == 0 else psT1
                p.xpose(pb, ynb[:, c * 128:(c + 1) * 128], cm.ident)
                p.op("scalar", "copy", out=yfm[:, c, j * 128:(j + 1) * 128], in_=pb)
        for c in range(16):
            p.dma("sync", out=woc[c % 2], in_=wout_t[c, :, :], transpose=True)
            for j in range(2):
                for fh in range(2):
                    p.mm(psP[j * 2 + fh], yfm[:, c, j * 128:(j + 1) * 128],
                         woc[c % 2][:, fh * 512:(fh + 1) * 512], c == 0, c == 15)
        for j in range(2):
            trow = oc * 2 + j
            p.dma("sync", out=xres[j], in_=x_own_dram[trow * 128:(trow + 1) * 128, :])
            for fh in range(2):
                p.op("vector", "tensor_tensor", out=xres[j][:, fh * 512:(fh + 1) * 512],
                     in0=xres[j][:, fh * 512:(fh + 1) * 512], in1=psP[j * 2 + fh],
                     op=ALU.add)
            p.dma("sync", out=x_out_dram[trow * 128:(trow + 1) * 128, :], in_=xres[j])

    p.free("xn_g", "zbuf", "xbc", "bpre", "cpre", "xbs", "b_fm", "c_fm",
           "dts", "acum", "nac", "ea", "acr", "awk", "alb", "cdb", "s_sb",
           "g_sb", "xtm", "btm", "xw1", "ysb", "tmp32", "yfm",
           "prev", "prevT", "win_s", "wz_s0", "woc0", "woc1",
           "maskc", "triu",
           "dtb_b", "a_b", "dp_b", "nw_b", "cw_x", "cb_x", "cw_B", "cb_B",
           "cw_C", "cb_C", "small")

    if DBG:
        dbg_xa = p.dram("dbg_xattn", [2048, 1024], F32, "ExternalOutput")
        p.dma("sync", out=dbg_xa[:, :], in_=x_out_dram[:, :])
    xres2 = [p.alloc("xres2", [128, 1024], F32)] * 2
    build_moe(p, cm, xn_own, x_own_dram, x_out_dram, W["Wg"], w1_t, w2_t, w3_t,
              psP, psT0, psT1, psF0, xres2, lnb, xsq)
    p.free("xn_own", "xraw0", "xraw1", "xnt0", "xnt1", "xres2", "scr16")
    for i in range(6):
        p.free(f"ln{i}")


def build_moe(p, cm, xn_own, x_own_dram, x_out_dram, wg_raw, w1_t, w2_t, w3_t,
              psP, psT0, psT1, psF0, xres, lnb, xsq):
    NJ = 22
    jsz = [128] * 21 + [42]
    wg_s = p.alloc("wg_s", [128, NCD, 8], BF16)
    wg_n = p.alloc("wg_n", [8, 1024], BF16)
    glog = p.alloc("glog", [128, NT_OWN, 8], F32)
    gat = p.alloc("gat", [128, NT_OWN, 8], F32)
    p.dma("gpsimd", out=wg_n, in_=wg_raw)
    for c in range(NCD):
        p.xpose(psT0[:, 0:8], wg_n[:, c * 128:(c + 1) * 128], cm.ident[0:8, 0:8])
        p.op("scalar", "copy", out=wg_s[:, c, :], in_=psT0[:, 0:8])
    for t in range(NT_OWN):
        ph = psP[4]
        for c in range(NCD):
            p.mm(ph[:, 0:8], xn_own[:, c, t * 128:(t + 1) * 128], wg_s[:, c, :],
                 c == 0, c == NCD - 1)
        p.op("vector", "tensor_copy", out=glog[:, t, :], in_=ph[:, 0:8])
    m1 = lnb[0]
    m2 = lnb[1]
    rs = lnb[2]
    for t in range(NT_OWN):
        g = glog[:, t, :]
        sel = gat[:, t, :]
        p.op("vector", "reduce_max", out=m1, in_=g, axis=AXL.X)
        p.op("vector", "tensor_scalar", out=sel, in0=g, scalar1=m1, scalar2=None,
             op0=ALU.is_ge)
        p.op("vector", "scalar_tensor_tensor", out=xsq[:, 0:8], in0=sel,
             scalar=NEG, in1=g, op0=ALU.mult, op1=ALU.add)
        p.op("vector", "reduce_max", out=m2, in_=xsq[:, 0:8], axis=AXL.X)
        p.op("vector", "tensor_scalar", out=sel, in0=g, scalar1=m2, scalar2=None,
             op0=ALU.is_ge)
        p.op("vector", "tensor_scalar", out=m1, in0=m1, scalar1=-1.0, scalar2=None,
             op0=ALU.mult)
        p.op("scalar", "activation", out=xsq[:, 0:8], in_=g, func=AF.Exp, bias=m1)
        p.op("vector", "tensor_tensor", out=sel, in0=sel, in1=xsq[:, 0:8],
             op=ALU.mult)
        p.op("vector", "reduce_sum", out=rs, in_=sel, axis=AXL.X)
        p.op("vector", "reciprocal", out=rs, in_=rs)
        p.op("vector", "tensor_scalar", out=sel, in0=sel, scalar1=rs, scalar2=None,
             op0=ALU.mult)
    if 'DBG' in globals() and DBG:
        pass
    try:
        _dbg = DBG
    except NameError:
        _dbg = False
    if _dbg:
        p.dbg("glog", glog)
        p.dbg("gat", gat)
    hid = p.alloc("hid", [128, NJ, 1024], BF16)
    w3s = p.alloc("w3s", [128, NJ, 1024], BF16)
    w1s = [p.alloc(f"w1s{i}", [128, NCD, 256], BF16) for i in range(2)]
    w2s = [p.alloc(f"w2s{i}", [128, NCD, 256], BF16) for i in range(2)]
    sil = p.alloc("sil", [128, 1024], F32)
    for e in range(8):
        p.bump()
        for j in range(NJ):
            p.dma("sync", out=w3s[:, j, :], in_=w3_t[e][j, :, :], transpose=True)
        for half in range(2):
            tok0 = half * 1024
            for jg in range((NJ + 1) // 2):
                wb1, wb2 = w1s[jg % 2], w2s[jg % 2]
                j0 = jg * 2
                nj = min(2, NJ - j0)
                nr = sum(jsz[j0:j0 + nj])
                nrl = (nr + 15) // 16 * 16
                for c in range(NCD):
                    p.dma("sync", out=wb1[:, c, 0:nrl],
                          in_=w1_t[e][c, j0 * 128:j0 * 128 + nrl, :], transpose=True)
                    p.dma("sync", out=wb2[:, c, 0:nrl],
                          in_=w2_t[e][c, j0 * 128:j0 * 128 + nrl, :], transpose=True)
                for ji in range(nj):
                    j = j0 + ji
                    n = jsz[j]
                    for tg in range(2):
                        ph1, ph2 = psP[tg], psP[2 + tg]
                        for c in range(NCD):
                            p.mm(ph1[0:n, :], wb1[:, c, ji * 128:ji * 128 + n],
                                 xn_own[:, c, tok0 + tg * 512:tok0 + (tg + 1) * 512],
                                 c == 0, c == NCD - 1)
                        for c in range(NCD):
                            p.mm(ph2[0:n, :], wb2[:, c, ji * 128:ji * 128 + n],
                                 xn_own[:, c, tok0 + tg * 512:tok0 + (tg + 1) * 512],
                                 c == 0, c == NCD - 1)
                        p.op("scalar", "activation", out=sil[0:n, tg * 512:(tg + 1) * 512],
                             in_=ph1[0:n, :], func=AF.Silu)
                        p.op("vector", "tensor_tensor",
                             out=hid[0:n, j, tg * 512:(tg + 1) * 512],
                             in0=sil[0:n, tg * 512:(tg + 1) * 512], in1=ph2[0:n, :],
                             op=ALU.mult)
            for grp in range(4):
                banks = (psP[0], psP[1], psP[2], psP[3])
                for j in range(NJ):
                    n = jsz[j]
                    for tt in range(2):
                        t = half * 8 + grp * 2 + tt
                        lt = (grp * 2 + tt) * 128
                        for fh in range(2):
                            p.mm(banks[tt * 2 + fh],
                                 hid[0:n, j, lt:lt + 128],
                                 w3s[0:n, j, fh * 512:(fh + 1) * 512],
                                 j == 0, j == NJ - 1)
                for tt in range(2):
                    t = half * 8 + grp * 2 + tt
                    xr = xres[tt]
                    p.dma("sync", out=xr, in_=x_out_dram[t * 128:(t + 1) * 128, :])
                    for fh in range(2):
                        p.op("vector", "scalar_tensor_tensor",
                             out=xr[:, fh * 512:(fh + 1) * 512],
                             in0=banks[tt * 2 + fh], scalar=gat[:, t, e:e + 1],
                             in1=xr[:, fh * 512:(fh + 1) * 512],
                             op0=ALU.mult, op1=ALU.add)
                    p.dma("sync", out=x_out_dram[t * 128:(t + 1) * 128, :], in_=xr)
    p.free("wg_s", "wg_n", "glog", "gat", "hid", "w3s", "w1s0", "w1s1", "w2s0", "w2s1", "sil")



def emit_fp4(p, cm, v_f32, out_bf16, Wd, qsc, eng="vector"):
    G = Wd // 16
    bmax, rec, an, acc1, accB = qsc
    bm, rc = bmax[:, 0:G], rec[:, 0:G]
    a1, aB, anv = acc1[:, 0:Wd], accB[:, 0:Wd], an[:, 0:Wd]
    vg = v_f32.rearrange("p (g i) -> p g i", i=16)
    p.op("vector", "tensor_reduce", out=bm, in_=vg, axis=AXL.X, op=ALU.max,
         apply_absolute_value=True)
    p.op("vector", "tensor_scalar", out=bm, in0=bm, scalar1=1.0 / 6.0,
         scalar2=1e-12, op0=ALU.mult, op1=ALU.max)
    p.op("vector", "reciprocal", out=rc, in_=bm)
    bcr = rc.unsqueeze(2).broadcast_to([128, G, 16])
    p.op(eng, "tensor_tensor", out=anv.rearrange("p (g i) -> p g i", i=16),
         in0=vg, in1=bcr, op=ALU.mult)
    p.op("scalar", "activation", out=a1, in_=anv, func=AF.Abs)
    p.op(eng, "tensor_scalar", out=aB, in0=a1, scalar1=2.5, scalar2=None,
         op0=ALU.is_gt)
    p.op("vector", "scalar_tensor_tensor", out=aB, in0=a1, scalar=3.5, in1=aB,
         op0=ALU.is_gt, op1=ALU.add)
    p.op(eng, "tensor_scalar", out=anv, in0=a1, scalar1=5.0, scalar2=2.0,
         op0=ALU.is_gt, op1=ALU.mult)
    p.op(eng, "tensor_tensor", out=aB, in0=aB, in1=anv, op=ALU.add)
    p.op(eng, "tensor_scalar", out=anv, in0=a1, scalar1=0.25, scalar2=None,
         op0=ALU.is_gt)
    p.op("vector", "scalar_tensor_tensor", out=anv, in0=a1, scalar=0.75, in1=anv,
         op0=ALU.is_gt, op1=ALU.add)
    p.op("vector", "scalar_tensor_tensor", out=anv, in0=a1, scalar=1.25, in1=anv,
         op0=ALU.is_gt, op1=ALU.add)
    p.op("vector", "scalar_tensor_tensor", out=a1, in0=a1, scalar=1.75, in1=anv,
         op0=ALU.is_gt, op1=ALU.add)
    p.op("vector", "scalar_tensor_tensor", out=a1, in0=a1, scalar=0.5, in1=aB,
         op0=ALU.mult, op1=ALU.add)
    p.op("scalar", "activation", out=aB, in_=v_f32, func=AF.Sign)
    p.op(eng, "tensor_tensor", out=a1, in0=a1, in1=aB, op=ALU.mult)
    bcm = bm.unsqueeze(2).broadcast_to([128, G, 16])
    p.op(eng, "tensor_tensor", out=out_bf16.rearrange("p (g i) -> p g i", i=16),
         in0=a1.rearrange("p (g i) -> p g i", i=16), in1=bcm, op=ALU.mult)


def build_block2(p: Prog, cm: Common, h, x_own_dram, x_other_dram, W, x_out_dram):
    own0 = h * T_OWN
    NLAT = 32 * (h + 1)
    KVT = 17

    psP, psT0, psT1, psF0 = std_psums(p)

    wgq_d = p.dram("b2_wgq", [8, 2816, 128], BF16, "Internal")
    wuq_d = p.dram("b2_wuq", [8, 2816, 128], BF16, "Internal")
    wdq_d = p.dram("b2_wdq", [22, 1024, 128], BF16, "Internal")
    xq_d = p.dram("b2_xq", [128, NCD, T_OWN], BF16, "Internal")

    xn_fm = p.alloc("xn_fm", [128, NCD, T_OWN], BF16)
    k_fm2 = p.alloc("k_fm2", [128, 8, KVT * 128], BF16)
    v_tm = p.alloc("v_tm", [128, KVT, 1024], BF16)
    xg_fm = p.alloc("xg_fm", [128, NCD, 128], BF16)
    lnb = [p.alloc(f"ln{i}", [128, 1], F32) for i in range(6)]
    scrA = p.alloc("scrA", [128, 4096], F32)
    xsq = scrA[:, 0:1024]
    xraw = [p.alloc(f"xraw{i}", [128, 1024], F32) for i in range(2)]
    xnf = p.alloc("xnf", [128, 1024], F32)
    xnb = p.alloc("xnb", [128, 1024], BF16)
    xqt = p.alloc("xqt", [128, 1024], BF16)
    xqf = p.alloc("xqf", [128, NCD, 128], BF16)
    poolA = p.alloc("poolA", [128, 16, 32], BF16)
    pooled = [p.alloc(f"pooled{i}", [32, 1024], BF16) for i in range(2)]
    pool_f = p.alloc("pool_f", [128, NCD, 64], BF16)
    qsc = (p.alloc("qsc0", [128, 256], F32), p.alloc("qsc1", [128, 256], F32),
           p.alloc("qsc2", [128, 1024], F32), p.alloc("qsc3", [128, 1024], F32),
           p.alloc("qsc4", [128, 1024], F32))

    p.op("vector", "memset", ap=poolA, constant=1.0 / 64.0)
    for j in range(16):
        for half in range(2):
            p.op("gpsimd", "affine_select", out=poolA[half * 64:(half + 1) * 64, j, :],
                 in_=poolA[half * 64:(half + 1) * 64, j, :],
                 pattern=[[1, 32]], base=-(2 * j + half), channel_multiplier=0,
                 compare_op=ALU.is_equal, fill=0.0)
    lmask = p.alloc("lmask", [128, 2, 256], BF16)
    p.op("vector", "memset", ap=lmask, constant=1.0)
    for v in range(2):
        p.op("gpsimd", "affine_select", out=lmask[:, v, :], in_=lmask[:, v, :],
             pattern=[[1, 256]], base=0, channel_multiplier=-1,
             compare_op=ALU.is_ge, fill=0.0)
        p.op("gpsimd", "affine_select", out=lmask[:, v, :], in_=lmask[:, v, :],
             pattern=[[-1, 256]], base=127, channel_multiplier=1,
             compare_op=ALU.is_ge, fill=0.0)
    p.op("gpsimd", "affine_select", out=lmask[:, 1, :], in_=lmask[:, 1, :],
         pattern=[[1, 256]], base=-128, channel_multiplier=0,
         compare_op=ALU.is_ge, fill=0.0)

    brow = p.alloc("brow", [64, 1024], F32)
    kqb = p.alloc("kqb", [128, 8], F32)
    qqb = p.alloc("qqb", [128, 8], F32)
    rqb = p.alloc("rqb", [128, 2], F32)
    vb_b = p.alloc("vb_b", [128, 1024], F32)
    ob_b = p.alloc("ob_b", [128, 1024], F32)
    rob_b = p.alloc("rob_b", [128, 1024], F32)
    rkvb = p.alloc("rkvb", [1, 512], BF16)
    glw_b = p.alloc("glw_b", [128, 1024], F32)
    grw_b = p.alloc("grw_b", [128, 1024], F32)
    p.dma("sync", out=brow[0:8, 0:128], in_=W["qkv_b"][0:1024].rearrange("(a b) -> a b", b=128))
    p.xpose(psF0[:, 0:8], brow[0:8, 0:128], cm.identf[0:8, 0:8])
    p.op("vector", "tensor_copy", out=qqb, in_=psF0[:, 0:8])
    p.dma("sync", out=brow[0:8, 0:128], in_=W["qkv_b"][1024:2048].rearrange("(a b) -> a b", b=128))
    p.xpose(psF0[:, 0:8], brow[0:8, 0:128], cm.identf[0:8, 0:8])
    p.op("vector", "tensor_copy", out=kqb, in_=psF0[:, 0:8])
    p.dma("sync", out=brow[0:1, :], in_=W["qkv_b"][2048:3072].unsqueeze(0))
    bcast_row(p, cm, psP[0], vb_b, brow[0:1, :], 1024)
    p.dma("sync", out=brow[0:1, :], in_=W["out_b"].unsqueeze(0))
    bcast_row(p, cm, psP[0], ob_b, brow[0:1, :], 1024)
    p.dma("sync", out=brow[0:1, :], in_=W["rout_b"].unsqueeze(0))
    bcast_row(p, cm, psP[0], rob_b, brow[0:1, :], 1024)
    p.dma("sync", out=brow[0:2, 0:128], in_=W["rq_b"].rearrange("(a b) -> a b", b=128))
    p.xpose(psF0[:, 0:2], brow[0:2, 0:128], cm.identf[0:2, 0:2])
    p.op("vector", "tensor_copy", out=rqb, in_=psF0[:, 0:2])
    p.dma("gpsimd", out=rkvb, in_=W["rkv_b"].unsqueeze(0))
    for wsrc, dstb in ((W["gl_w"], glw_b), (W["gr_w"], grw_b)):
        p.dma("sync", out=brow[0:64, 0:1], in_=wsrc)
        p.xpose(psF0[0:1, 0:64], brow[0:64, 0:1], cm.identf[0:64, 0:64])
        p.op("vector", "tensor_copy", out=brow[0:1, 0:64], in_=psF0[0:1, 0:64])
        p.mm(psP[0][:, 0:64], cm.ones_rowf, brow[0:1, 0:64], True, True)
        p.op("vector", "tensor_copy", out=scrA[:, 0:64], in_=psP[0][:, 0:64])
        p.op("vector", "tensor_copy",
             out=dstb.rearrange("p (r d) -> p r d", d=64),
             in_=scrA[:, 0:64].unsqueeze(1).broadcast_to([128, 16, 64]))

    n_ln = 16 * (h + 1)
    for j in range(n_ln):
        own = j >= n_ln - 16
        oj = j - (n_ln - 16)
        buf = xraw[j % 2]
        if own:
            p.dma("sync", out=buf, in_=x_own_dram[oj * 128:(oj + 1) * 128, :])
        else:
            p.dma("sync", out=buf, in_=x_other_dram[j * 128:(j + 1) * 128, :])
        emit_ln(p, cm, buf, xnf, *lnb, xsq)
        p.op("vector", "tensor_copy", out=xnb, in_=xnf)
        bb = 0 if own else 2
        jj = oj if own else j
        for fh in range(2):
            p.mm(psP[bb + fh][0:32, :], poolA[:, jj, :],
                 xnb[:, fh * 512:(fh + 1) * 512], jj == 0, jj == 15)
        if own:
            for c in range(NCD):
                pb = psT0 if c % 2 == 0 else psT1
                p.xpose(pb, xnb[:, c * 128:(c + 1) * 128], cm.ident)
                p.op("scalar", "copy", out=xn_fm[:, c, oj * 128:(oj + 1) * 128], in_=pb)
            emit_fp4(p, cm, xnf, xqt, 1024, qsc, eng="gpsimd" if oj % 2 else "vector")
            for c in range(NCD):
                pb = psT0 if c % 2 == 0 else psT1
                p.xpose(pb, xqt[:, c * 128:(c + 1) * 128], cm.ident)
                p.op("scalar", "copy", out=xqf[:, c, :], in_=pb)
            p.dma("sync", out=xq_d[:, :, oj * 128:(oj + 1) * 128], in_=xqf)
        elif j == 15:
            for c in range(NCD):
                pb = psT0 if c % 2 == 0 else psT1
                p.xpose(pb, xnb[:, c * 128:(c + 1) * 128], cm.ident)
                p.op("scalar", "copy", out=xg_fm[:, c, :], in_=pb)
    npre = NLAT - 32
    for fh in range(2):
        p.op("vector", "tensor_copy", out=pooled[0][:, fh * 512:(fh + 1) * 512],
             in_=psP[0 + fh][0:32, :])
        if h == 1:
            p.op("vector", "tensor_copy", out=pooled[1][:, fh * 512:(fh + 1) * 512],
                 in_=psP[2 + fh][0:32, :])
    for c in range(NCD):
        if h == 1:
            p.xpose(psT0[:, 0:32], pooled[1][:, c * 128:(c + 1) * 128],
                    cm.ident[0:32, 0:32])
            p.op("scalar", "copy", out=pool_f[:, c, 0:32], in_=psT0[:, 0:32])
        p.xpose(psT1[:, 0:32], pooled[0][:, c * 128:(c + 1) * 128],
                cm.ident[0:32, 0:32])
        p.op("scalar", "copy", out=pool_f[:, c, npre:npre + 32], in_=psT1[:, 0:32])
    wrkv = p.alloc("wrkv", [128, NCD, 512], BF16)
    rkv_tm = p.alloc("rkv_tm", [64, 512], BF16)
    rk_fm2 = p.alloc("rk_fm2", [128, 2, 64], BF16)
    p.dma("gpsimd", out=wrkv, in_=W["rkv_w"].rearrange("(c p) f -> p c f", p=128))
    for c in range(NCD):
        p.mm(psP[4][0:NLAT, :], pool_f[:, c, 0:NLAT], wrkv[:, c, :], c == 0, False)
    p.mm(psP[4][0:NLAT, :], cm.ones_row[:, 0:NLAT], rkvb, False, True)
    p.op("vector", "tensor_copy", out=rkv_tm[0:NLAT, :], in_=psP[4][0:NLAT, :])
    for hp in range(2):
        p.xpose(psT0[:, 0:NLAT], rkv_tm[0:NLAT, hp * 128:(hp + 1) * 128],
                cm.ident[0:NLAT, 0:NLAT])
        p.op("scalar", "copy", out=rk_fm2[:, hp, 0:NLAT], in_=psT0[:, 0:NLAT])
    p.free("wrkv", "poolA", "pooled0", "pooled1", "pool_f", "brow",
           "xraw0", "xraw1", "xnf", "xnb", "xqt", "xqf")

    p.bump()
    wkv = p.alloc("wkv", [128, NCD, 1024], BF16)
    p.dma("gpsimd", out=wkv, in_=W["qkv_w"][:, 1024:2048].rearrange("(c p) f -> p c f", p=128))
    if h == 0:
        p.op("vector", "memset", ap=k_fm2[:, :, 0:128], constant=0.0)
        p.op("vector", "memset", ap=v_tm[:, 0, :], constant=0.0)
    for hp in range(8):
        if h == 1:
            p.mm(psP[0][:, 0:128], wkv[:, 0, hp * 128:(hp + 1) * 128], xg_fm[:, 0, :],
                 True, False)
            for c in range(1, NCD):
                p.mm(psP[0][:, 0:128], wkv[:, c, hp * 128:(hp + 1) * 128],
                     xg_fm[:, c, :], False, c == NCD - 1)
            p.op("vector", "tensor_scalar", out=k_fm2[:, hp, 0:128],
                 in0=psP[0][:, 0:128], scalar1=kqb[:, hp:hp + 1], scalar2=None,
                 op0=ALU.add)
        for g in range(4):
            ph = psP[g % 2]
            for c in range(NCD):
                p.mm(ph, wkv[:, c, hp * 128:(hp + 1) * 128],
                     xn_fm[:, c, g * 512:(g + 1) * 512], c == 0, c == NCD - 1)
            p.op("vector", "tensor_scalar", out=k_fm2[:, hp, 128 + g * 512:128 + (g + 1) * 512],
                 in0=ph, scalar1=kqb[:, hp:hp + 1], scalar2=None, op0=ALU.add)
    p.dma("gpsimd", out=wkv, in_=W["qkv_w"][:, 2048:3072].rearrange("(c p) f -> p c f", p=128))
    if h == 1:
        for fh in range(2):
            ph = psP[fh]
            for c in range(NCD):
                p.mm(ph, xg_fm[:, c, :], wkv[:, c, fh * 512:(fh + 1) * 512],
                     c == 0, c == NCD - 1)
            p.op("vector", "tensor_tensor", out=v_tm[:, 0, fh * 512:(fh + 1) * 512],
                 in0=ph, in1=vb_b[:, fh * 512:(fh + 1) * 512], op=ALU.add)
    for oj in range(16):
        for fh in range(2):
            ph = psP[fh]
            for c in range(NCD):
                p.mm(ph, xn_fm[:, c, oj * 128:(oj + 1) * 128],
                     wkv[:, c, fh * 512:(fh + 1) * 512], c == 0, c == NCD - 1)
            p.op("vector", "tensor_tensor", out=v_tm[:, 1 + oj, fh * 512:(fh + 1) * 512],
                 in0=ph, in1=vb_b[:, fh * 512:(fh + 1) * 512], op=ALU.add)
    p.free("wkv", "xg_fm")

    wqp = p.alloc("wqp", [128, NCD, 1024], BF16)
    wrq = p.alloc("wrq", [128, NCD, 256], BF16)
    woc = [p.alloc(f"woc{i}", [128, 1024], BF16) for i in range(2)]
    wro = p.alloc("wro", [128, 2, 1024], BF16)
    p.dma("gpsimd", out=wqp, in_=W["qkv_w"][:, 0:1024].rearrange("(c p) f -> p c f", p=128))
    p.dma("gpsimd", out=wrq, in_=W["rq_w"].rearrange("(c p) f -> p c f", p=128))
    p.dma("gpsimd", out=wro, in_=W["rout_w"].rearrange("(c p) f -> p c f", p=128))
    qf = p.alloc("qf", [128, 128], BF16)
    s_sb = p.alloc("s_sb", [128, 256], F32)
    p_sb = p.alloc("p_sb", [128, 256], BF16)
    pt_sb = p.alloc("pt_sb", [128, 256], BF16)
    loc = p.alloc("loc", [128, 1024], BF16)
    rem = p.alloc("rem", [128, 1024], F32)
    lat = p.alloc("lat", [128, 256], BF16)
    latf = p.alloc("latf", [128, 2, 128], BF16)
    gsc = p.alloc("gsc", [128, 16], F32)
    mixb = scrA[:, 2048:2560].bitcast(BF16)
    mixf = scrA[:, 2560:3072].bitcast(BF16).rearrange("p (c q) -> p c q", q=128)
    lm01 = p.alloc("lm01", [128, 64], BF16)
    xres = scrA[:, 3072:4096]
    mr = lnb

    for j in range(NT_OWN):
        p.bump()
        for hp in range(8):
            ph = psP[0]
            for c in range(NCD):
                p.mm(ph[:, 0:128], wqp[:, c, hp * 128:(hp + 1) * 128],
                     xn_fm[:, c, j * 128:(j + 1) * 128], c == 0, c == NCD - 1)
            p.op("vector", "tensor_scalar", out=qf, in0=ph[:, 0:128],
                 scalar1=qqb[:, hp:hp + 1], scalar2=None, op0=ALU.add)
            for lh in range(2):
                hh = hp * 2 + lh
                base = lh * 64
                p.mm(psP[1][:, 0:256], qf[base:base + 64, :],
                     k_fm2[base:base + 64, hp, j * 128:j * 128 + 256], True, True)
                p.op("vector", "tensor_scalar", out=s_sb, in0=psP[1][:, 0:256],
                     scalar1=0.0, scalar2=NEGSEED, op0=ALU.add, op1=ALU.max,
                     accum_out=mr[0])
                p.op("vector", "tensor_scalar", out=mr[1], in0=mr[0],
                     scalar1=-0.125, scalar2=None, op0=ALU.mult)
                p.op("scalar", "activation", out=p_sb, in_=s_sb, func=AF.Exp,
                     scale=cm.c(0.125), bias=mr[1])
                mv = 1 if (h == 0 and j == 0) else 0
                p.op("vector", "scalar_tensor_tensor", out=p_sb, in0=p_sb,
                     scalar=1.0, in1=lmask[:, mv, :], op0=ALU.mult, op1=ALU.mult,
                     accum_out=mr[2])
                p.op("vector", "tensor_scalar", out=mr[2], in0=mr[2],
                     scalar1=1e-30, scalar2=None, op0=ALU.max)
                p.op("vector", "reciprocal", out=mr[2], in_=mr[2])
                for kt in range(2):
                    pb = psT0 if kt == 0 else psT1
                    p.xpose(pb, p_sb[:, kt * 128:(kt + 1) * 128], cm.ident)
                    p.op("scalar", "copy", out=pt_sb[:, kt * 128:(kt + 1) * 128], in_=pb)
                for kt in range(2):
                    p.mm(psP[2][:, 0:64], pt_sb[:, kt * 128:(kt + 1) * 128],
                         v_tm[:, j + kt, hh * 64:(hh + 1) * 64], kt == 0, kt == 1)
                p.op("vector", "tensor_scalar", out=loc[:, hh * 64:(hh + 1) * 64],
                     in0=psP[2][:, 0:64], scalar1=mr[2], scalar2=None, op0=ALU.mult)
        C0 = own0 // 64 + 2 * j
        p.op("vector", "memset", ap=lm01, constant=1.0)
        for half in range(2):
            p.op("gpsimd", "affine_select", out=lm01[half * 64:(half + 1) * 64, :],
                 in_=lm01[half * 64:(half + 1) * 64, :], pattern=[[-1, 64]],
                 base=C0 + half - 1, channel_multiplier=0, compare_op=ALU.is_ge,
                 fill=0.0)
        for hp in range(2):
            ph = psP[0]
            for c in range(NCD):
                p.mm(ph[:, 0:128], wrq[:, c, hp * 128:(hp + 1) * 128],
                     xn_fm[:, c, j * 128:(j + 1) * 128], c == 0, c == NCD - 1)
            p.op("vector", "tensor_scalar", out=qf, in0=ph[:, 0:128],
                 scalar1=rqb[:, hp:hp + 1], scalar2=None, op0=ALU.add)
            for lh in range(2):
                hh = hp * 2 + lh
                base = lh * 64
                p.mm(psP[1][:, 0:NLAT], qf[base:base + 64, :],
                     rk_fm2[base:base + 64, hp, 0:NLAT], True, True)
                p.op("vector", "tensor_scalar", out=s_sb[:, 0:NLAT],
                     in0=psP[1][:, 0:NLAT], scalar1=0.0, scalar2=NEGSEED,
                     op0=ALU.add, op1=ALU.max, accum_out=mr[0])
                p.op("vector", "tensor_scalar", out=mr[1], in0=mr[0],
                     scalar1=-0.125, scalar2=None, op0=ALU.mult)
                p.op("scalar", "activation", out=p_sb[:, 0:NLAT], in_=s_sb[:, 0:NLAT],
                     func=AF.Exp, scale=cm.c(0.125), bias=mr[1])
                p.op("vector", "scalar_tensor_tensor", out=p_sb[:, 0:NLAT],
                     in0=p_sb[:, 0:NLAT], scalar=1.0, in1=lm01[:, 0:NLAT],
                     op0=ALU.mult, op1=ALU.mult, accum_out=mr[2])
                p.op("vector", "tensor_scalar", out=mr[2], in0=mr[2],
                     scalar1=1e-30, scalar2=None, op0=ALU.max)
                p.op("vector", "reciprocal", out=mr[2], in_=mr[2])
                p.xpose(psT0[0:NLAT, 0:128], p_sb[:, 0:NLAT], cm.ident)
                p.op("scalar", "copy", out=pt_sb[0:NLAT, 0:128], in_=psT0[0:NLAT, 0:128])
                p.mm(psP[2][:, 0:64], pt_sb[0:NLAT, 0:128],
                     rkv_tm[0:NLAT, 256 + hh * 64:256 + (hh + 1) * 64], True, True)
                p.op("vector", "tensor_scalar", out=lat[:, hh * 64:(hh + 1) * 64],
                     in0=psP[2][:, 0:64], scalar1=mr[2], scalar2=None, op0=ALU.mult)
        for lc in range(2):
            pb = psT0 if lc == 0 else psT1
            p.xpose(pb, lat[:, lc * 128:(lc + 1) * 128], cm.ident)
            p.op("scalar", "copy", out=latf[:, lc, :], in_=pb)
        for fh in range(2):
            ph = psP[3 + fh]
            for lc in range(2):
                p.mm(ph, latf[:, lc, :], wro[:, lc, fh * 512:(fh + 1) * 512],
                     lc == 0, lc == 1)
            p.op("vector", "tensor_tensor", out=rem[:, fh * 512:(fh + 1) * 512],
                 in0=ph, in1=rob_b[:, fh * 512:(fh + 1) * 512], op=ALU.add)
        p.op("vector", "tensor_tensor", out=scrA[:, 0:1024], in0=loc, in1=glw_b,
             op=ALU.mult)
        p.op("vector", "tensor_reduce", out=gsc,
             in_=scrA[:, 0:1024].rearrange("p (r d) -> p r d", d=64),
             axis=AXL.X, op=ALU.add)
        p.op("vector", "tensor_tensor", out=scrA[:, 0:1024], in0=rem, in1=grw_b,
             op=ALU.mult)
        p.op("vector", "tensor_reduce", out=scrA[:, 1024:1040],
             in_=scrA[:, 0:1024].rearrange("p (r d) -> p r d", d=64),
             axis=AXL.X, op=ALU.add)
        p.op("vector", "tensor_tensor", out=gsc, in0=gsc, in1=scrA[:, 1024:1040],
             op=ALU.add)
        p.op("scalar", "activation", out=gsc, in_=gsc, func=AF.Sigmoid)
        p.op("vector", "tensor_tensor", out=scrA[:, 0:1024], in0=loc, in1=rem,
             op=ALU.subtract)
        for hh in range(16):
            p.op("vector", "scalar_tensor_tensor", out=mixb[:, hh * 64:(hh + 1) * 64],
                 in0=scrA[:, hh * 64:(hh + 1) * 64], scalar=gsc[:, hh:hh + 1],
                 in1=rem[:, hh * 64:(hh + 1) * 64], op0=ALU.mult, op1=ALU.add)
        for c in range(NCD):
            pb = psT0 if c % 2 == 0 else psT1
            p.xpose(pb, mixb[:, c * 128:(c + 1) * 128], cm.ident)
            p.op("scalar", "copy", out=mixf[:, c, :], in_=pb)
        p.dma("sync", out=xres, in_=x_own_dram[j * 128:(j + 1) * 128, :])
        p.op("vector", "tensor_tensor", out=xres, in0=xres, in1=ob_b, op=ALU.add)
        for c in range(NCD):
            p.dma("gpsimd", out=woc[c % 2],
                  in_=W["out_w"][c * 128:(c + 1) * 128, :])
            for fh in range(2):
                p.mm(psP[3 + fh], mixf[:, c, :], woc[c % 2][:, fh * 512:(fh + 1) * 512],
                     c == 0, c == NCD - 1)
        for fh in range(2):
            p.op("vector", "tensor_tensor", out=xres[:, fh * 512:(fh + 1) * 512],
                 in0=xres[:, fh * 512:(fh + 1) * 512], in1=psP[3 + fh], op=ALU.add)
        p.dma("sync", out=x_out_dram[j * 128:(j + 1) * 128, :], in_=xres)
    p.free("wqp", "wrq", "woc0", "woc1", "wro", "qf", "s_sb", "p_sb", "pt_sb", "loc",
           "rem", "lat", "latf", "gsc", "lm01",
           "k_fm2", "v_tm", "rkv_tm", "rk_fm2", "lmask",
           "kqb", "qqb", "rqb", "vb_b", "rob_b", "rkvb", "glw_b", "grw_b")

    p.bump()
    for i in range(5):
        p.free(f"qsc{i}")
    qsc = (p.alloc("qsc0", [128, 256], F32), p.alloc("qsc1", [128, 256], F32),
           scrA[:, 0:2816], p.alloc("qsc3", [128, 2816], F32),
           p.alloc("qsc4", [128, 2816], F32))
    wtile = [p.alloc(f"wtile{i}", [128, 2816], F32) for i in range(2)]
    wqt = p.alloc("wqt", [128, 2816], BF16)
    for src_w, nt, wd, scratch in ((W["w_gate"], 22, 1024, wgq_d),
                                   (W["w_up"], 22, 1024, wuq_d),
                                   (W["w_down"], 8, 2816, wdq_d)):
        for i in range(nt):
            b = i % 2
            p.dma("gpsimd", out=wtile[b][:, 0:wd], in_=src_w[i * 128:(i + 1) * 128, :])
            emit_fp4(p, cm, wtile[b][:, 0:wd], wqt[:, 0:wd], wd, qsc,
                     eng="gpsimd" if i % 2 else "vector")
            p.dma("gpsimd",
                  out=scratch[:, i * 128:(i + 1) * 128, :].rearrange("c p q -> p c q"),
                  in_=wqt[:, 0:wd].rearrange("p (c q) -> p c q", q=128))
    p.free("wtile0", "wtile1", "wqt", "xn_fm")

    xq_fm = p.alloc("xq_fm", [128, NCD, T_OWN], BF16)
    p.dma("sync", out=xq_fm, in_=xq_d)
    wdq_s = p.alloc("wdq_s", [128, 22, 1024], BF16)
    for jc in range(22):
        p.dma("sync", out=wdq_s[:, jc, :], in_=wdq_d[jc, :, :], transpose=True)
    wgs = [p.alloc(f"wgs{i}", [128, NCD, 512], BF16) for i in range(2)]
    wus = [p.alloc(f"wus{i}", [128, NCD, 512], BF16) for i in range(2)]
    h2 = p.alloc("h2", [128, 2, 2816], F32)
    xrE = scrA[:, 2816:3840]
    hq_tm = p.alloc("hq_tm", [128, 2816], BF16)
    hq_fm = p.alloc("hq_fm", [128, 22, 128], BF16)
    sil = p.alloc("sil", [128, 512], F32)
    fcs = [512, 512, 512, 512, 512, 256]
    for jg in range(8):
        p.bump()
        for fc in range(6):
            f0 = fc * 512
            n = fcs[fc]
            wg, wu = wgs[fc % 2], wus[fc % 2]
            for c in range(NCD):
                p.dma("sync", out=wg[:, c, 0:n], in_=wgq_d[c, f0:f0 + n, :],
                      transpose=True)
                p.dma("sync", out=wu[:, c, 0:n], in_=wuq_d[c, f0:f0 + n, :],
                      transpose=True)
            for jj in range(2):
                j = jg * 2 + jj
                ph1, ph2 = psP[jj * 2], psP[jj * 2 + 1]
                for c in range(NCD):
                    p.mm(ph1[:, 0:n], xq_fm[:, c, j * 128:(j + 1) * 128],
                         wg[:, c, 0:n], c == 0, c == NCD - 1)
                for c in range(NCD):
                    p.mm(ph2[:, 0:n], xq_fm[:, c, j * 128:(j + 1) * 128],
                         wu[:, c, 0:n], c == 0, c == NCD - 1)
                p.op("scalar", "activation", out=sil[:, 0:n], in_=ph1[:, 0:n],
                     func=AF.Silu)
                p.op("vector", "tensor_tensor", out=h2[:, jj, f0:f0 + n],
                     in0=sil[:, 0:n], in1=ph2[:, 0:n], op=ALU.mult)
        for jj in range(2):
            j = jg * 2 + jj
            emit_fp4(p, cm, h2[:, jj, :], hq_tm, 2816, qsc,
                     eng="gpsimd" if jj else "vector")
            for jc in range(22):
                pb = psT0 if jc % 2 == 0 else psT1
                p.xpose(pb, hq_tm[:, jc * 128:(jc + 1) * 128], cm.ident)
                p.op("scalar", "copy", out=hq_fm[:, jc, :], in_=pb)
            for fh in range(2):
                ph = psP[4] if fh == 0 else psP[jj * 2]
                for jc in range(22):
                    p.mm(ph, hq_fm[:, jc, :], wdq_s[:, jc, fh * 512:(fh + 1) * 512],
                         jc == 0, jc == 21)
            xr = xrE
            p.dma("sync", out=xr, in_=x_out_dram[j * 128:(j + 1) * 128, :])
            for fh in range(2):
                ph = psP[4] if fh == 0 else psP[jj * 2]
                p.op("vector", "tensor_tensor", out=xr[:, fh * 512:(fh + 1) * 512],
                     in0=xr[:, fh * 512:(fh + 1) * 512], in1=ph, op=ALU.add)
            p.dma("sync", out=x_out_dram[j * 128:(j + 1) * 128, :], in_=xr)
    p.free("xq_fm", "qsc0", "qsc1", "qsc3", "qsc4", "wdq_s", "wgs0", "wgs1",
           "wus0", "wus1", "h2", "hq_tm", "hq_fm", "sil", "scrA", "ob_b")
    for i in range(6):
        p.free(f"ln{i}")



def build_block3(p: Prog, cm: Common, h, x_own_dram, x_other_dram, W, x_out_dram):
    kv_len = 2048 * (h + 1)
    own0 = h * 2048
    NKV = kv_len // 128

    psP, psT0, psT1, psF0 = std_psums(p)

    xnpre_d = p.dram("b3_xnpre", [128, NCD, 2048], BF16, "Internal")
    p_d = p.dram("b3_p", [128, 16, 256], BF16, "Internal")

    xn_own = p.alloc("xn_own", [128, NCD, T_OWN], BF16)
    cosb = p.alloc("cosb", [128, 512], BF16)
    sinb = p.alloc("sinb", [128, 512], BF16)
    lnb = [p.alloc(f"ln{i}", [128, 1], F32) for i in range(6)]
    scrA = p.alloc("scrA", [128, 4096], F32)
    xsq = scrA[:, 0:1024]
    xraw = [p.alloc(f"xraw{i}", [128, 1024], F32) for i in range(2)]
    xnf = p.alloc("xnf", [128, 1024], F32)
    xnb = p.alloc("xnb", [128, 1024], BF16)
    xn_g = p.alloc("xn_g", [128, NCD, 512], BF16)
    p_tm = p.alloc("p_tm", [128, 256], BF16)
    skb = p.alloc("skb", [128, 256], F32)
    svb = p.alloc("svb", [128, 256], F32)
    wsk1 = p.alloc("wsk1", [128, NCD, 16], BF16)
    wsk2 = p.alloc("wsk2", [16, 256], BF16)
    t1f = p.alloc("t1f", [16, 512], BF16)
    wkt = p.alloc("wkt", [128, 256], F32)
    carry = p.alloc("carry", [1, 256], F32)
    mask01d = p.alloc("mask01d", [128, 128], BF16)
    brow = p.alloc("brow", [32, 1024], F32)

    p.op("vector", "memset", ap=mask01d, constant=1.0)
    p.op("gpsimd", "affine_select", out=mask01d, in_=mask01d, pattern=[[1, 128]],
         base=0, channel_multiplier=-1, compare_op=ALU.is_ge, fill=0.0)
    p.dma("sync", out=brow[0:1, 0:256], in_=W["sk_b2"].unsqueeze(0))
    bcast_row(p, cm, psP[0], skb, brow[0:1, 0:256], 256)
    p.dma("sync", out=brow[0:1, 0:256], in_=W["sv_b2"].unsqueeze(0))
    bcast_row(p, cm, psP[0], svb, brow[0:1, 0:256], 256)


    def ln_tile(src_dram, row, out_fm_slice, stash=None):
        buf = xraw[row % 2]
        p.dma("sync", out=buf, in_=src_dram[row * 128:(row + 1) * 128, :])
        emit_ln(p, cm, buf, xnf, *lnb, xsq)
        p.op("vector", "tensor_copy", out=xnb, in_=xnf)
        for c in range(NCD):
            pb = psT0 if c % 2 == 0 else psT1
            p.xpose(pb, xnb[:, c * 128:(c + 1) * 128], cm.ident)
            p.op("scalar", "copy", out=out_fm_slice[:, c, :], in_=pb)
        if stash is not None:
            p.dma("sync", out=stash, in_=out_fm_slice)

    def rope_fm(dst_fm, src_fm, h4, wdt):
        c0, c1 = 2 * h4, 2 * h4 + 1
        cs = cosb[:, 0:wdt]
        sn = sinb[:, 0:wdt]
        t1 = scrA[:, 0:wdt]
        t2 = scrA[:, 1024:1024 + wdt]
        p.op("vector", "tensor_tensor", out=t1, in0=src_fm[:, c0, :], in1=cs, op=ALU.mult)
        p.op("vector", "tensor_tensor", out=t2, in0=src_fm[:, c1, :], in1=sn, op=ALU.mult)
        p.op("vector", "tensor_tensor", out=dst_fm[:, c0, :], in0=t1, in1=t2,
             op=ALU.subtract)
        p.op("vector", "tensor_tensor", out=t1, in0=src_fm[:, c1, :], in1=cs, op=ALU.mult)
        p.op("vector", "tensor_tensor", out=t2, in0=src_fm[:, c0, :], in1=sn, op=ALU.mult)
        p.op("vector", "tensor_tensor", out=dst_fm[:, c1, :], in0=t1, in1=t2, op=ALU.add)

    def skv_chain(w1ap, w2ap, bias_b, wout_tm, zout, g, xn_src, first_carry,
                  mod4=False):
        ph = psP[2]
        for c in range(NCD):
            p.mm(ph[0:16, :], wsk1[:, c, :], xn_src[:, c, :], c == 0, c == NCD - 1)
        p.op("vector", "tensor_copy", out=t1f, in_=ph[0:16, :])
        for jj in range(4):
            tg = g * 4 + jj
            ph2 = psP[3]
            p.mm(ph2[:, 0:256], t1f[:, jj * 128:(jj + 1) * 128], wsk2, True, True)
            p.op("vector", "tensor_tensor", out=wkt, in0=ph2[:, 0:256], in1=bias_b,
                 op=ALU.add)
            p.op("vector", "tensor_scalar", out=wkt, in0=wkt, scalar1=32.0,
                 scalar2=-32.0, op0=ALU.min, op1=ALU.max)
            p.op("scalar", "activation", out=wkt, in_=wkt, func=AF.Exp)
            p.op("vector", "tensor_copy", out=wout_tm[:, tg % 4 if mod4 else tg, :],
                 in_=wkt)
            p.mm(psP[4][:, 0:256], triu3, wkt, True, False)
            p.mm(psP[4][:, 0:256], cm.ones_rowf, carry, False, True)
            if zout is not None:
                p.op("vector", "reciprocal", out=zout[:, tg - (NKV - 16), :],
                     in_=psP[4][:, 0:256])
            p.mm(psP[2][0:1, 0:256], cm.ones_colf, wkt, True, False)
            p.mm(psP[2][0:1, 0:256], cm.ones_rowf[0:1, 0:1], carry, False, True)
            p.op("scalar", "copy", out=carry, in_=psP[2][0:1, 0:256])

    triu3 = p.alloc("triu3", [128, 128], F32)
    p.op("vector", "tensor_scalar", out=triu3, in0=cm.iot_row,
         scalar1=cm.iot_col, scalar2=None, op0=ALU.is_ge)
    kraw = scrA[:, 2048:4096].bitcast(BF16).rearrange("p (c q) -> p c q", q=512)

    k_fm = p.alloc("k_fm", [128, NCD, kv_len], BF16)
    wk_tm = p.alloc("wk_tm", [128, NKV, 256], BF16)
    zk_own = p.alloc("zk_own", [128, 16, 256], BF16)
    wkv_a = p.alloc("wkv_a", [128, 4, 1024], BF16)
    wkv_b = p.alloc("wkv_b", [128, 4, 1024], BF16)
    p.dma("gpsimd", out=wkv_a, in_=W["Wk"][0:512, :].rearrange("(c p) f -> p c f", p=128))
    p.dma("gpsimd", out=wkv_b, in_=W["Wk"][512:1024, :].rearrange("(c p) f -> p c f", p=128))
    p.dma("gpsimd", out=wsk1, in_=W["sk_w1"].rearrange("(c p) f -> p c f", p=128))
    p.dma("gpsimd", out=wsk2, in_=W["sk_w2"])
    p.op("vector", "memset", ap=carry, constant=0.0)
    for g in range(kv_len // 512):
        own = g >= (kv_len - 2048) // 512
        og = g - (kv_len - 2048) // 512
        if own:
            src = xn_own[:, :, og * 512:(og + 1) * 512]
            for jj in range(4):
                ln_tile(x_own_dram, og * 4 + jj, _fm_slice(xn_own, og * 4 + jj))
        else:
            src = xn_g
            for jj in range(4):
                ln_tile(x_other_dram, g * 4 + jj, _fm_slice(xn_g, jj))
            p.dma("sync", out=xnpre_d[:, :, g * 512:(g + 1) * 512], in_=xn_g)
        for hc in range(NCD):
            ph = psP[hc % 2]
            for c in range(NCD):
                wslc = wkv_a[:, c, :] if c < 4 else wkv_b[:, c - 4, :]
                p.mm(ph, wslc[:, hc * 128:(hc + 1) * 128], src[:, c, :],
                     c == 0, c == NCD - 1)
            p.op("scalar", "copy", out=kraw[:, hc, :], in_=ph)
        p.dma("gpsimd", out=cosb, in_=W["cos_t"][:, g * 512:(g + 1) * 512])
        p.dma("gpsimd", out=sinb, in_=W["sin_t"][:, g * 512:(g + 1) * 512])
        kdst = k_fm[:, :, g * 512:(g + 1) * 512]
        for h4 in range(4):
            rope_fm(kdst, kraw, h4, 512)
        skv_chain(None, None, skb, wk_tm, zk_own if own else None, g, src, g == 0)
    p.free("wkv_a", "wkv_b")

    p.free("brow")
    qraw = p.alloc("qraw", [128, NCD, 128], BF16)
    qfm = p.alloc("qfm", [128, NCD, 128], BF16)
    s_bf = p.alloc("s_bf", [128, 128], BF16)
    lgt = p.alloc("lgt", [128, 256], F32)
    mxg = p.alloc("mxg", [128, 4], F32)
    wq_w = p.alloc("wq_w", [128, NCD, 1024], BF16)
    p.dma("gpsimd", out=wq_w, in_=W["Wq"].rearrange("(c p) f -> p c f", p=128))
    for t in range(NT_OWN):
        p.bump()
        n_st = own0 // 128 + t + 1
        for hc in range(NCD):
            ph = psP[0]
            for c in range(NCD):
                p.mm(ph[:, 0:128], wq_w[:, c, hc * 128:(hc + 1) * 128],
                     xn_own[:, c, t * 128:(t + 1) * 128], c == 0, c == NCD - 1)
            p.op("scalar", "copy", out=qraw[:, hc, :], in_=ph[:, 0:128])
        q0g = own0 + t * 128
        p.dma("gpsimd", out=cosb[:, 0:128], in_=W["cos_t"][:, q0g:q0g + 128])
        p.dma("gpsimd", out=sinb[:, 0:128], in_=W["sin_t"][:, q0g:q0g + 128])
        for h4 in range(4):
            rope_fm(qfm, qraw, h4, 128)
        for h4 in range(4):
            for st in range(n_st):
                for ci in range(2):
                    p.mm(psP[1][:, 0:128], k_fm[:, 2 * h4 + ci, st * 128:(st + 1) * 128],
                         qfm[:, 2 * h4 + ci, :], ci == 0, ci == 1)
                if st == n_st - 1:
                    p.op("vector", "scalar_tensor_tensor", out=s_bf,
                         in0=psP[1][:, 0:128], scalar=1.0, in1=mask01d,
                         op0=ALU.mult, op1=ALU.mult)
                else:
                    p.op("scalar", "copy", out=s_bf, in_=psP[1][:, 0:128])
                p.mm(psP[2][:, h4 * 64:(h4 + 1) * 64], s_bf,
                     wk_tm[:, st, h4 * 64:(h4 + 1) * 64], st == 0, st == n_st - 1)
        p.op("vector", "scalar_tensor_tensor", out=lgt, in0=psP[2][:, 0:256],
             scalar=1.0 / 16.0, in1=zk_own[:, t, :], op0=ALU.mult, op1=ALU.mult)
        p.op("vector", "tensor_reduce", out=mxg,
             in_=lgt.rearrange("p (r d) -> p r d", d=64), axis=AXL.X, op=ALU.max)
        p.op("vector", "tensor_tensor", out=lgt.rearrange("p (r d) -> p r d", d=64),
             in0=lgt.rearrange("p (r d) -> p r d", d=64),
             in1=mxg.unsqueeze(2).broadcast_to([128, 4, 64]), op=ALU.subtract)
        p.op("scalar", "activation", out=lgt, in_=lgt, func=AF.Exp)
        p.op("vector", "tensor_reduce", out=mxg,
             in_=lgt.rearrange("p (r d) -> p r d", d=64), axis=AXL.X, op=ALU.add)
        p.op("vector", "reciprocal", out=mxg, in_=mxg)
        p.op("vector", "tensor_tensor", out=p_tm.rearrange("p (r d) -> p r d", d=64),
             in0=lgt.rearrange("p (r d) -> p r d", d=64),
             in1=mxg.unsqueeze(2).broadcast_to([128, 4, 64]), op=ALU.mult)
        p.dma("sync", out=p_d[:, t, :], in_=p_tm)
    p.free("k_fm", "wk_tm", "zk_own", "wq_w", "qraw", "qfm", "cosb", "sinb")

    p.bump()
    v_tm = p.alloc("v_tm", [128, NKV, 1024], BF16)
    wv_fm2 = p.alloc("wv_fm2", [128, 2, kv_len], BF16)
    zv_own = p.alloc("zv_own", [128, 16, 256], BF16)
    wv_stage = p.alloc("wv_stage", [128, 4, 256], BF16)
    wkv_a = p.alloc("wkv_a2", [128, 4, 1024], BF16)
    wkv_b = p.alloc("wkv_b2", [128, 4, 1024], BF16)
    p.dma("gpsimd", out=wkv_a, in_=W["Wv"][0:512, :].rearrange("(c p) f -> p c f", p=128))
    p.dma("gpsimd", out=wkv_b, in_=W["Wv"][512:1024, :].rearrange("(c p) f -> p c f", p=128))
    p.dma("gpsimd", out=wsk1, in_=W["sv_w1"].rearrange("(c p) f -> p c f", p=128))
    p.dma("gpsimd", out=wsk2, in_=W["sv_w2"])
    p.op("vector", "memset", ap=carry, constant=0.0)
    for g in range(kv_len // 512):
        own = g >= (kv_len - 2048) // 512
        og = g - (kv_len - 2048) // 512
        if own:
            src = xn_own[:, :, og * 512:(og + 1) * 512]
        else:
            src = xn_g
            p.dma("sync", out=xn_g, in_=xnpre_d[:, :, g * 512:(g + 1) * 512])
        for jj in range(4):
            tg = g * 4 + jj
            for fh in range(2):
                ph = psP[fh]
                for c in range(NCD):
                    wslc = wkv_a[:, c, :] if c < 4 else wkv_b[:, c - 4, :]
                    p.mm(ph, src[:, c, jj * 128:(jj + 1) * 128],
                         wslc[:, fh * 512:(fh + 1) * 512], c == 0, c == NCD - 1)
                p.op("vector", "tensor_copy", out=v_tm[:, tg, fh * 512:(fh + 1) * 512],
                     in_=ph)
        skv_chain(None, None, svb, wv_stage, zv_own if own else None, g, src,
                  g == 0, mod4=True)
        for jj in range(4):
            tg = g * 4 + jj
            for hp in range(2):
                pb = psT0 if hp == 0 else psT1
                p.xpose(pb, wv_stage[:, jj, hp * 128:(hp + 1) * 128], cm.ident)
                p.op("scalar", "copy", out=wv_fm2[:, hp, tg * 128:(tg + 1) * 128], in_=pb)
    p.free("wkv_a2", "wkv_b2")

    p.free("xraw0", "xraw1", "xnf", "xnb", "xn_g", "skb", "svb", "wsk1",
           "wsk2", "t1f", "wkt", "wv_stage")
    gnw = p.alloc("gnw", [128, 1024], F32)
    p.dma("sync", out=carry, in_=W["gnorm_w"].unsqueeze(0))
    bcast_row(p, cm, psP[0], gnw[:, 0:256], carry, 256)
    p.op("vector", "tensor_copy",
         out=gnw.rearrange("p (r d) -> p r d", d=256),
         in_=gnw[:, 0:256].unsqueeze(1).broadcast_to([128, 4, 256]))
    wg_w = p.alloc("wg_w", [128, NCD, 1024], BF16)
    wo_wA = p.alloc("wo_wA", [128, 4, 1024], BF16)
    wo_wB = p.alloc("wo_wB", [128, 4, 1024], BF16)
    p.dma("gpsimd", out=wg_w, in_=W["Wg"].rearrange("(c p) f -> p c f", p=128))
    p.dma("gpsimd", out=wo_wA, in_=W["Wo"][0:512, :].rearrange("(c p) f -> p c f", p=128))
    p.dma("gpsimd", out=wo_wB, in_=W["Wo"][512:1024, :].rearrange("(c p) f -> p c f", p=128))
    pt_fm2 = p.alloc("pt_fm2", [128, 2, 128], BF16)
    a_bf = p.alloc("a_bf", [128, 128], BF16)
    osb = p.alloc("osb", [128, 1024], F32)
    gsb = p.alloc("gsb", [128, 1024], F32)
    ptt = p.alloc("ptt", [128, 256], BF16)
    ofin = scrA[:, 1024:1536].bitcast(BF16)
    ofm = scrA[:, 1536:2048].bitcast(BF16).rearrange("p (c q) -> p c q", q=128)
    xres = scrA[:, 2048:3072]
    for t in range(NT_OWN):
        n_st = own0 // 128 + t + 1
        p.dma("sync", out=p_tm, in_=p_d[:, t, :])
        p.op("vector", "tensor_tensor", out=ptt, in0=p_tm,
             in1=zv_own[:, t, :], op=ALU.mult)
        for hp in range(2):
            pb = psT0 if hp == 0 else psT1
            p.xpose(pb, ptt[:, hp * 128:(hp + 1) * 128], cm.ident)
            p.op("scalar", "copy", out=pt_fm2[:, hp, :], in_=pb)
        for h4 in range(4):
            hp, lh = h4 // 2, h4 % 2
            base = lh * 64
            for st in range(n_st):
                p.mm(psP[1][:, 0:128], wv_fm2[base:base + 64, hp, st * 128:(st + 1) * 128],
                     pt_fm2[base:base + 64, hp, :], True, True)
                if st == n_st - 1:
                    p.op("vector", "scalar_tensor_tensor", out=a_bf,
                         in0=psP[1][:, 0:128], scalar=1.0, in1=mask01d,
                         op0=ALU.mult, op1=ALU.mult)
                else:
                    p.op("scalar", "copy", out=a_bf, in_=psP[1][:, 0:128])
                p.mm(psP[2][:, 0:256], a_bf, v_tm[:, st, h4 * 256:(h4 + 1) * 256],
                     st == 0, st == n_st - 1)
            p.op("vector", "tensor_copy", out=osb[:, h4 * 256:(h4 + 1) * 256],
                 in_=psP[2][:, 0:256])
        for fh in range(2):
            ph = psP[3 + fh]
            for c in range(NCD):
                p.mm(ph, xn_own[:, c, t * 128:(t + 1) * 128],
                     wg_w[:, c, fh * 512:(fh + 1) * 512], c == 0, c == NCD - 1)
            p.op("vector", "tensor_copy", out=gsb[:, fh * 512:(fh + 1) * 512], in_=ph)
        p.op("scalar", "activation", out=scrA[:, 0:1024], in_=osb, func=AF.Square)
        p.op("vector", "tensor_reduce", out=mxg,
             in_=scrA[:, 0:1024].rearrange("p (r d) -> p r d", d=256),
             axis=AXL.X, op=ALU.add)
        p.op("vector", "tensor_scalar", out=mxg, in0=mxg, scalar1=1.0 / 256.0,
             scalar2=None, op0=ALU.mult)
        p.op("scalar", "activation", out=mxg, in_=mxg, func=AF.Sqrt, bias=cm.c(1e-5))
        p.op("vector", "reciprocal", out=mxg, in_=mxg)
        p.op("vector", "tensor_tensor", out=osb.rearrange("p (r d) -> p r d", d=256),
             in0=osb.rearrange("p (r d) -> p r d", d=256),
             in1=mxg.unsqueeze(2).broadcast_to([128, 4, 256]), op=ALU.mult)
        p.op("vector", "tensor_tensor", out=osb, in0=osb, in1=gnw, op=ALU.mult)
        p.op("scalar", "activation", out=scrA[:, 0:1024], in_=gsb, func=AF.Sigmoid)
        p.op("vector", "tensor_tensor", out=gsb, in0=gsb, in1=scrA[:, 0:1024],
             op=ALU.mult)
        p.op("vector", "tensor_tensor", out=ofin, in0=osb, in1=gsb, op=ALU.mult)
        for c in range(NCD):
            pb = psT0 if c % 2 == 0 else psT1
            p.xpose(pb, ofin[:, c * 128:(c + 1) * 128], cm.ident)
            p.op("scalar", "copy", out=ofm[:, c, :], in_=pb)
        p.dma("sync", out=xres, in_=x_own_dram[t * 128:(t + 1) * 128, :])
        for fh in range(2):
            ph = psP[3 + fh]
            for c in range(NCD):
                wo_c = wo_wA[:, c, :] if c < 4 else wo_wB[:, c - 4, :]
                p.mm(ph, ofm[:, c, :], wo_c[:, fh * 512:(fh + 1) * 512],
                     c == 0, c == NCD - 1)
            p.op("vector", "tensor_tensor", out=xres[:, fh * 512:(fh + 1) * 512],
                 in0=xres[:, fh * 512:(fh + 1) * 512], in1=ph, op=ALU.add)
        p.dma("sync", out=x_out_dram[t * 128:(t + 1) * 128, :], in_=xres)
    p.free("v_tm", "wv_fm2", "zv_own", "wg_w", "wo_wA", "wo_wB", "pt_fm2",
           "a_bf", "osb", "gsb", "ptt", "p_tm", "mxg",
           "s_bf", "lgt", "gnw", "carry", "mask01d", "triu3")

    brow = p.alloc("brow", [32, 1024], F32)
    build_b3_ffn(p, cm, xn_own, x_own_dram, x_out_dram, W, psP, psT0, psT1,
                 psF0, lnb, xraw, brow, scrA)
    p.free("xn_own", "scrA", "brow")
    for i in range(6):
        p.free(f"ln{i}")


def _fm_slice(fm, tile):
    return fm[:, :, tile * 128:(tile + 1) * 128]


def build_b3_ffn(p, cm, xn_own, x_own_dram, x_out_dram, W, psP, psT0, psT1,
                 psF0, lnb, xraw, brow, scrA):
    wgt = p.alloc("wgt", [128, NCD, 2], BF16)
    sel = p.alloc("sel", [128, 16, 2], F32)
    p.dma("sync", out=brow[0:2, :], in_=W["gate_w"])
    for c in range(NCD):
        p.xpose(psF0[:, 0:2], brow[0:2, c * 128:(c + 1) * 128], cm.identf[0:2, 0:2])
        p.op("vector", "tensor_copy", out=wgt[:, c, :], in_=psF0[:, 0:2])
    for t in range(NT_OWN):
        ph = psP[4]
        for c in range(NCD):
            p.mm(ph[:, 0:2], xn_own[:, c, t * 128:(t + 1) * 128], wgt[:, c, :],
                 c == 0, c == NCD - 1)
        p.op("vector", "tensor_copy", out=sel[:, t, :], in_=ph[:, 0:2])
        p.op("vector", "tensor_tensor", out=lnb[0], in0=sel[:, t, 0:1],
             in1=sel[:, t, 1:2], op=ALU.subtract)
        p.op("vector", "tensor_scalar", out=sel[:, t, 0:1], in0=lnb[0], scalar1=0.0,
             scalar2=None, op0=ALU.is_ge)
        p.op("vector", "tensor_scalar", out=sel[:, t, 1:2], in0=sel[:, t, 0:1],
             scalar1=-1.0, scalar2=1.0, op0=ALU.mult, op1=ALU.add)
    cst = [p.alloc(f"cst{e}", [128, 1024], F32) for e in range(2)]
    rb1 = p.alloc("rb1", [128, 32], F32)
    w2s = [p.alloc(f"w2s{i}", [128, 1024], BF16) for i in range(2)]
    for e in range(2):
        p.dma("sync", out=brow[0:4, :], in_=W["b1"][e].rearrange("(a b) -> a b", b=1024))
        p.op("vector", "tensor_scalar", out=brow[0:4, :], in0=brow[0:4, :],
             scalar1=0.0, scalar2=None, op0=ALU.max)
        for cc in range(8):
            p.xpose(psF0[:, 0:4], brow[0:4, cc * 128:(cc + 1) * 128],
                    cm.identf[0:4, 0:4])
            p.op("vector", "tensor_copy", out=rb1[:, cc * 4:(cc + 1) * 4],
                 in_=psF0[:, 0:4])
        rb1b = p.alloc("rb1b", [128, 32], BF16)
        p.op("vector", "tensor_copy", out=rb1b, in_=rb1)
        for fh in range(2):
            ph = psP[fh]
            first = True
            for r in range(4):
                for cc in range(8):
                    jf = r * 8 + cc
                    p.dma("gpsimd", out=w2s[jf % 2],
                          in_=W["w2"][e, jf * 128:(jf + 1) * 128, :])
                    p.mm(ph[0:1, :], rb1b[:, cc * 4 + r:cc * 4 + r + 1],
                         w2s[jf % 2][:, fh * 512:(fh + 1) * 512], first, jf == 31)
                    first = False
            p.op("vector", "tensor_copy", out=cst[e][0:1, fh * 512:(fh + 1) * 512],
                 in_=ph[0:1, :])
        p.dma("sync", out=brow[0:1, :], in_=W["b2"][e].unsqueeze(0))
        p.op("vector", "tensor_tensor", out=cst[e][0:1, :], in0=cst[e][0:1, :],
             in1=brow[0:1, :], op=ALU.add)
        p.op("vector", "tensor_copy", out=brow.bitcast(BF16)[0:1, 0:1024],
             in_=cst[e][0:1, :])
        for fh in range(2):
            p.mm(psP[0], cm.ones_row,
                 brow.bitcast(BF16)[0:1, fh * 512:(fh + 1) * 512], True, True)
            p.op("vector", "tensor_copy", out=cst[e][:, fh * 512:(fh + 1) * 512],
                 in_=psP[0])
    p.free("rb1", "rb1b")

    hid = p.alloc("hid", [128, 32, 1024], BF16)
    w1s = [p.alloc(f"w1s{i}", [128, NCD, 512], BF16) for i in range(2)]
    b1t = p.alloc("b1t", [128, 32], F32)
    b1r = p.alloc("b1r", [32, 128], F32)

    def mlp_pass(w1ap, b1ap, w2ap, b2row_or_none, epilogue):
        p.bump()
        p.dma("sync", out=b1r, in_=b1ap.rearrange("(j p) -> j p", p=128))
        p.xpose(psF0[:, 0:32], b1r, cm.identf[0:32, 0:32])
        p.op("vector", "tensor_copy", out=b1t, in_=psF0[:, 0:32])
        for half in range(2):
            p.bump()
            tok0 = half * 1024
            for jg in range(8):
                wb = w1s[jg % 2]
                for c in range(NCD):
                    p.dma("gpsimd", out=wb[:, c, :],
                          in_=w1ap[c * 128:(c + 1) * 128, jg * 512:(jg + 1) * 512])
                for j4 in range(4):
                    j = jg * 4 + j4
                    for tg in range(2):
                        ph2 = psP[tg]
                        for c in range(NCD):
                            p.mm(ph2, wb[:, c, j4 * 128:(j4 + 1) * 128],
                                 xn_own[:, c, tok0 + tg * 512:tok0 + (tg + 1) * 512],
                                 c == 0, c == NCD - 1)
                        p.op("scalar", "activation", out=hid[:, j, tg * 512:(tg + 1) * 512],
                             in_=ph2, func=AF.Relu, bias=b1t[:, j:j + 1])
            for grp in range(4):
                for j in range(32):
                    p.dma("gpsimd", out=w2s[j % 2],
                          in_=w2ap[j * 128:(j + 1) * 128, :])
                    for tt in range(2):
                        lt = (grp * 2 + tt) * 128
                        for fh in range(2):
                            p.mm(psP[tt * 2 + fh], hid[:, j, lt:lt + 128],
                                 w2s[j % 2][:, fh * 512:(fh + 1) * 512], j == 0, j == 31)
                for tt in range(2):
                    t = half * 8 + grp * 2 + tt
                    epilogue(t, (psP[tt * 2], psP[tt * 2 + 1]))

    xr = scrA[:, 0:1024]

    def epi_shared(t, banks):
        p.dma("sync", out=xr, in_=x_out_dram[t * 128:(t + 1) * 128, :])
        p.op("vector", "tensor_tensor", out=xr, in0=xr, in1=sb2b, op=ALU.add)
        for fh in range(2):
            p.op("vector", "tensor_tensor", out=xr[:, fh * 512:(fh + 1) * 512],
                 in0=xr[:, fh * 512:(fh + 1) * 512], in1=banks[fh], op=ALU.add)
        p.dma("sync", out=x_out_dram[t * 128:(t + 1) * 128, :], in_=xr)

    def mk_epi_expert(e):
        def epi(t, banks):
            p.dma("sync", out=xr, in_=x_out_dram[t * 128:(t + 1) * 128, :])
            s_ap = sel[:, t, e:e + 1]
            o_ap = sel[:, t, 1 - e:2 - e]
            for fh in range(2):
                p.op("vector", "scalar_tensor_tensor",
                     out=xr[:, fh * 512:(fh + 1) * 512], in0=banks[fh], scalar=s_ap,
                     in1=xr[:, fh * 512:(fh + 1) * 512], op0=ALU.mult, op1=ALU.add)
                p.op("vector", "scalar_tensor_tensor",
                     out=xr[:, fh * 512:(fh + 1) * 512],
                     in0=cst[e][:, fh * 512:(fh + 1) * 512], scalar=o_ap,
                     in1=xr[:, fh * 512:(fh + 1) * 512], op0=ALU.mult, op1=ALU.add)
            p.dma("sync", out=x_out_dram[t * 128:(t + 1) * 128, :], in_=xr)
        return epi

    sb2b = p.alloc("sb2b", [128, 1024], F32)
    p.dma("sync", out=brow[0:1, :], in_=W["sb2"].unsqueeze(0))
    bcast_row(p, cm, psP[4], sb2b, brow[0:1, :], 1024)
    mlp_pass(W["sw1"], W["sb1"], W["sw2"], None, epi_shared)
    for e in range(2):
        mlp_pass(W["w1"][e], W["b1"][e], W["w2"][e], None, mk_epi_expert(e))
    p.free("wgt", "sel", "cst0", "cst1", "w2s0", "w2s1", "hid", "w1s0", "w1s1",
           "b1t", "b1r", "sb2b")



def _shard_x(x_full):
    shards = []
    for core in range(8):
        b, h = core // 2, core % 2
        own = np.ascontiguousarray(x_full[b, h * 2048:(h + 1) * 2048, :])
        other = np.ascontiguousarray(x_full[b, 0:2048, :])
        shards.append((own, other))
    return shards


def _run_block0(x_full, inputs):
    outs = []
    progs = {}
    for h in (0, 1):
        p = Prog()
        cm = Common(p)
        x_own = p.dram("x_own", [2048, 1024], F32, "ExternalInput")
        x_other = p.dram("x_other", [2048, 1024], F32, "ExternalInput")
        W = {
            "q": p.dram("wq_d", [1024, 64], F32, "ExternalInput"),
            "k": p.dram("wk_d", [1024, 64], F32, "ExternalInput"),
            "v": p.dram("wv_d", [1024, 1024], F32, "ExternalInput"),
            "fw1": p.dram("fw1_d", [4096, 1024], F32, "ExternalInput"),
            "b1": p.dram("fb1_d", [4096], F32, "ExternalInput"),
            "fw2": p.dram("fw2_d", [1024, 4096], F32, "ExternalInput"),
            "b2": p.dram("fb2_d", [1024], F32, "ExternalInput"),
        }
        x_out = p.dram("x_new", [2048, 1024], F32, "ExternalOutput")
        build_block0(p, cm, h, x_own, x_other, W, x_out)
        progs[h] = p.build()
    shards = _shard_x(x_full)
    res_by_core = {}
    for h in (0, 1):
        cores = [c for c in range(8) if c % 2 == h]
        in_maps = []
        for c in cores:
            own, other = shards[c]
            in_maps.append({
                "x_own": own, "x_other": other,
                "wq_d": inputs["L0_attn_Wq"], "wk_d": inputs["L0_attn_Wk"],
                "wv_d": inputs["L0_attn_Wv"],
                "fw1_d": inputs["L0_ffn_W1"], "fb1_d": inputs["L0_ffn_b1"],
                "fw2_d": inputs["L0_ffn_W2"], "fb2_d": inputs["L0_ffn_b2"],
            })
        res = run_bass_kernel_spmd(progs[h], in_maps, core_ids=cores)
        for c, r in zip(cores, res.results):
            res_by_core[c] = r["x_new"]
    x_new = np.empty_like(x_full)
    for core in range(8):
        b, hh = core // 2, core % 2
        x_new[b, hh * 2048:(hh + 1) * 2048, :] = res_by_core[core]
    return x_new


def _run_block1(x_full, inputs):
    progs = {}
    for h in (0, 1):
        p = Prog()
        cm = Common(p)
        x_own = p.dram("x_own", [2048, 1024], F32, "ExternalInput")
        x_other = p.dram("x_other", [2048, 1024], F32, "ExternalInput")
        W = {
            "in_proj": p.dram("w_inp", [4240, 1024], F32, "ExternalInput"),
            "conv_w": p.dram("w_cw", [2176, 4], F32, "ExternalInput"),
            "conv_b": p.dram("w_cb", [2176], F32, "ExternalInput"),
            "dt_bias": p.dram("w_dtb", [16], F32, "ExternalInput"),
            "A_log": p.dram("w_al", [16], F32, "ExternalInput"),
            "Dp": p.dram("w_dp", [16], F32, "ExternalInput"),
            "norm_w": p.dram("w_nw", [2048], F32, "ExternalInput"),
            "out_proj": p.dram("w_outp", [1024, 2048], F32, "ExternalInput"),
            "Wg": p.dram("w_wg", [8, 1024], F32, "ExternalInput"),
            "W1": p.dram("w_w1", [8, 2730, 1024], F32, "ExternalInput"),
            "W2": p.dram("w_w2", [8, 2730, 1024], F32, "ExternalInput"),
            "W3": p.dram("w_w3", [8, 1024, 2730], F32, "ExternalInput"),
        }
        x_out = p.dram("x_new", [2048, 1024], F32, "ExternalOutput")
        build_block1(p, cm, h, x_own, x_other, W, x_out)
        progs[h] = p.build()
    shards = _shard_x(x_full)
    res_by_core = {}
    for h in (0, 1):
        cores = [c for c in range(8) if c % 2 == h]
        in_maps = []
        for c in cores:
            own, other = shards[c]
            in_maps.append({
                "x_own": own, "x_other": other,
                "w_inp": inputs["L1_attn_in_proj_w"],
                "w_cw": inputs["L1_attn_conv_w"], "w_cb": inputs["L1_attn_conv_b"],
                "w_dtb": inputs["L1_attn_dt_bias"], "w_al": inputs["L1_attn_A_log"],
                "w_dp": inputs["L1_attn_Dp"], "w_nw": inputs["L1_attn_norm_w"],
                "w_outp": inputs["L1_attn_out_proj_w"],
                "w_wg": inputs["L1_ffn_Wg"], "w_w1": inputs["L1_ffn_W1"],
                "w_w2": inputs["L1_ffn_W2"], "w_w3": inputs["L1_ffn_W3"],
            })
        res = run_bass_kernel_spmd(progs[h], in_maps, core_ids=cores)
        for c, r in zip(cores, res.results):
            res_by_core[c] = r["x_new"]
    x_new = np.empty_like(x_full)
    for core in range(8):
        b, hh = core // 2, core % 2
        x_new[b, hh * 2048:(hh + 1) * 2048, :] = res_by_core[core]
    return x_new


def _run_block2(x_full, inputs):
    progs = {}
    for h in (0, 1):
        p = Prog()
        cm = Common(p)
        x_own = p.dram("x_own", [2048, 1024], F32, "ExternalInput")
        x_other = p.dram("x_other", [2048, 1024], F32, "ExternalInput")
        W = {
            "qkv_w": p.dram("w_qkv", [1024, 3072], F32, "ExternalInput"),
            "qkv_b": p.dram("w_qkvb", [3072], F32, "ExternalInput"),
            "rq_w": p.dram("w_rq", [1024, 256], F32, "ExternalInput"),
            "rq_b": p.dram("w_rqb", [256], F32, "ExternalInput"),
            "rkv_w": p.dram("w_rkv", [1024, 512], F32, "ExternalInput"),
            "rkv_b": p.dram("w_rkvb", [512], F32, "ExternalInput"),
            "rout_w": p.dram("w_ro", [256, 1024], F32, "ExternalInput"),
            "rout_b": p.dram("w_rob", [1024], F32, "ExternalInput"),
            "out_w": p.dram("w_ow", [1024, 1024], F32, "ExternalInput"),
            "out_b": p.dram("w_ob", [1024], F32, "ExternalInput"),
            "gl_w": p.dram("w_gl", [64, 1], F32, "ExternalInput"),
            "gr_w": p.dram("w_gr", [64, 1], F32, "ExternalInput"),
            "w_gate": p.dram("w_wg", [2816, 1024], F32, "ExternalInput"),
            "w_up": p.dram("w_wu", [2816, 1024], F32, "ExternalInput"),
            "w_down": p.dram("w_wd", [1024, 2816], F32, "ExternalInput"),
        }
        x_out = p.dram("x_new", [2048, 1024], F32, "ExternalOutput")
        build_block2(p, cm, h, x_own, x_other, W, x_out)
        progs[h] = p.build()
    shards = _shard_x(x_full)
    res_by_core = {}
    for h in (0, 1):
        cores = [c for c in range(8) if c % 2 == h]
        in_maps = []
        for c in cores:
            own, other = shards[c]
            in_maps.append({
                "x_own": own, "x_other": other,
                "w_qkv": inputs["L2_attn_qkv_w"], "w_qkvb": inputs["L2_attn_qkv_b"],
                "w_rq": inputs["L2_attn_rq_w"], "w_rqb": inputs["L2_attn_rq_b"],
                "w_rkv": inputs["L2_attn_rkv_w"], "w_rkvb": inputs["L2_attn_rkv_b"],
                "w_ro": inputs["L2_attn_rout_w"], "w_rob": inputs["L2_attn_rout_b"],
                "w_ow": inputs["L2_attn_out_w"], "w_ob": inputs["L2_attn_out_b"],
                "w_gl": inputs["L2_attn_gl_w"], "w_gr": inputs["L2_attn_gr_w"],
                "w_wg": inputs["L2_ffn_w_gate"], "w_wu": inputs["L2_ffn_w_up"],
                "w_wd": inputs["L2_ffn_w_down"],
            })
        res = run_bass_kernel_spmd(progs[h], in_maps, core_ids=cores)
        for c, r in zip(cores, res.results):
            res_by_core[c] = r["x_new"]
    x_new = np.empty_like(x_full)
    for core in range(8):
        b, hh = core // 2, core % 2
        x_new[b, hh * 2048:(hh + 1) * 2048, :] = res_by_core[core]
    return x_new


def _rope_tables():
    half = 128
    inv_freq = (1.0 / (10000.0 ** (np.arange(half, dtype=np.float32) / half))).astype(np.float32)
    ang = (np.arange(4096, dtype=np.float32)[:, None] * inv_freq[None, :]).astype(np.float32)
    cos_t = np.ascontiguousarray(np.cos(ang).T.astype(np.float32))
    sin_t = np.ascontiguousarray(np.sin(ang).T.astype(np.float32))
    return cos_t, sin_t


def _run_block3(x_full, inputs):
    cos_t, sin_t = _rope_tables()
    progs = {}
    for h in (0, 1):
        p = Prog()
        cm = Common(p)
        x_own = p.dram("x_own", [2048, 1024], F32, "ExternalInput")
        x_other = p.dram("x_other", [2048, 1024], F32, "ExternalInput")
        W = {
            "Wq": p.dram("w_q", [1024, 1024], F32, "ExternalInput"),
            "Wk": p.dram("w_k", [1024, 1024], F32, "ExternalInput"),
            "Wv": p.dram("w_v", [1024, 1024], F32, "ExternalInput"),
            "Wg": p.dram("w_g", [1024, 1024], F32, "ExternalInput"),
            "Wo": p.dram("w_o", [1024, 1024], F32, "ExternalInput"),
            "sk_w1": p.dram("w_sk1", [1024, 16], F32, "ExternalInput"),
            "sk_w2": p.dram("w_sk2", [16, 256], F32, "ExternalInput"),
            "sk_b2": p.dram("w_skb", [256], F32, "ExternalInput"),
            "sv_w1": p.dram("w_sv1", [1024, 16], F32, "ExternalInput"),
            "sv_w2": p.dram("w_sv2", [16, 256], F32, "ExternalInput"),
            "sv_b2": p.dram("w_svb", [256], F32, "ExternalInput"),
            "gnorm_w": p.dram("w_gn", [256], F32, "ExternalInput"),
            "cos_t": p.dram("w_cos", [128, 4096], F32, "ExternalInput"),
            "sin_t": p.dram("w_sin", [128, 4096], F32, "ExternalInput"),
            "gate_w": p.dram("w_gw", [2, 1024], F32, "ExternalInput"),
            "w1": p.dram("w_w1", [2, 1024, 4096], F32, "ExternalInput"),
            "b1": p.dram("w_b1", [2, 4096], F32, "ExternalInput"),
            "w2": p.dram("w_w2", [2, 4096, 1024], F32, "ExternalInput"),
            "b2": p.dram("w_b2", [2, 1024], F32, "ExternalInput"),
            "sw1": p.dram("w_sw1", [1024, 4096], F32, "ExternalInput"),
            "sb1": p.dram("w_sb1", [4096], F32, "ExternalInput"),
            "sw2": p.dram("w_sw2", [4096, 1024], F32, "ExternalInput"),
            "sb2": p.dram("w_sb2", [1024], F32, "ExternalInput"),
        }
        x_out = p.dram("x_new", [2048, 1024], F32, "ExternalOutput")
        build_block3(p, cm, h, x_own, x_other, W, x_out)
        progs[h] = p.build()
    shards = _shard_x(x_full)
    res_by_core = {}
    for h in (0, 1):
        cores = [c for c in range(8) if c % 2 == h]
        in_maps = []
        for c in cores:
            own, other = shards[c]
            in_maps.append({
                "x_own": own, "x_other": other,
                "w_q": inputs["L3_attn_Wq"], "w_k": inputs["L3_attn_Wk"],
                "w_v": inputs["L3_attn_Wv"], "w_g": inputs["L3_attn_Wg"],
                "w_o": inputs["L3_attn_Wo"],
                "w_sk1": inputs["L3_attn_sk_w1"], "w_sk2": inputs["L3_attn_sk_w2"],
                "w_skb": inputs["L3_attn_sk_b2"],
                "w_sv1": inputs["L3_attn_sv_w1"], "w_sv2": inputs["L3_attn_sv_w2"],
                "w_svb": inputs["L3_attn_sv_b2"],
                "w_gn": inputs["L3_attn_gnorm_w"],
                "w_cos": cos_t, "w_sin": sin_t,
                "w_gw": inputs["L3_ffn_gate_w"], "w_w1": inputs["L3_ffn_w1"],
                "w_b1": inputs["L3_ffn_b1"], "w_w2": inputs["L3_ffn_w2"],
                "w_b2": inputs["L3_ffn_b2"],
                "w_sw1": inputs["L3_ffn_sw1"], "w_sb1": inputs["L3_ffn_sb1"],
                "w_sw2": inputs["L3_ffn_sw2"], "w_sb2": inputs["L3_ffn_sb2"],
            })
        res = run_bass_kernel_spmd(progs[h], in_maps, core_ids=cores)
        for c, r in zip(cores, res.results):
            res_by_core[c] = r["x_new"]
    x_new = np.empty_like(x_full)
    for core in range(8):
        b, hh = core // 2, core % 2
        x_new[b, hh * 2048:(hh + 1) * 2048, :] = res_by_core[core]
    return x_new


WSHAPES = {
    "L0_attn_Wq": [1024, 64], "L0_attn_Wk": [1024, 64], "L0_attn_Wv": [1024, 1024],
    "L0_ffn_W1": [4096, 1024], "L0_ffn_b1": [4096], "L0_ffn_W2": [1024, 4096],
    "L0_ffn_b2": [1024],
    "L1_attn_in_proj_w": [4240, 1024], "L1_attn_conv_w": [2176, 4],
    "L1_attn_conv_b": [2176], "L1_attn_dt_bias": [16], "L1_attn_A_log": [16],
    "L1_attn_Dp": [16], "L1_attn_norm_w": [2048], "L1_attn_out_proj_w": [1024, 2048],
    "L1_ffn_Wg": [8, 1024], "L1_ffn_W1": [8, 2730, 1024],
    "L1_ffn_W2": [8, 2730, 1024], "L1_ffn_W3": [8, 1024, 2730],
    "L2_attn_qkv_w": [1024, 3072], "L2_attn_qkv_b": [3072],
    "L2_attn_rq_w": [1024, 256], "L2_attn_rq_b": [256],
    "L2_attn_rkv_w": [1024, 512], "L2_attn_rkv_b": [512],
    "L2_attn_rout_w": [256, 1024], "L2_attn_rout_b": [1024],
    "L2_attn_out_w": [1024, 1024], "L2_attn_out_b": [1024],
    "L2_attn_gl_w": [64, 1], "L2_attn_gr_w": [64, 1],
    "L2_ffn_w_gate": [2816, 1024], "L2_ffn_w_up": [2816, 1024],
    "L2_ffn_w_down": [1024, 2816],
    "L3_attn_Wq": [1024, 1024], "L3_attn_Wk": [1024, 1024],
    "L3_attn_Wv": [1024, 1024], "L3_attn_Wg": [1024, 1024],
    "L3_attn_Wo": [1024, 1024],
    "L3_attn_sk_w1": [1024, 16], "L3_attn_sk_w2": [16, 256], "L3_attn_sk_b2": [256],
    "L3_attn_sv_w1": [1024, 16], "L3_attn_sv_w2": [16, 256], "L3_attn_sv_b2": [256],
    "L3_attn_gnorm_w": [256],
    "L3_ffn_gate_w": [2, 1024], "L3_ffn_w1": [2, 1024, 4096], "L3_ffn_b1": [2, 4096],
    "L3_ffn_w2": [2, 4096, 1024], "L3_ffn_b2": [2, 1024],
    "L3_ffn_sw1": [1024, 4096], "L3_ffn_sb1": [4096], "L3_ffn_sw2": [4096, 1024],
    "L3_ffn_sb2": [1024],
}

RG_PAIRS = [[0, 1], [2, 3], [4, 5], [6, 7]]


def _build_fused():
    p = Prog()
    x_full = p.dram("x_full", [4096, 1024], F32, "ExternalInput")
    x_new = p.dram("x_new", [2048, 1024], F32, "ExternalOutput")
    wd = {k: p.dram(k, s, F32, "ExternalInput") for k, s in WSHAPES.items()}
    wd["cos_t"] = p.dram("cos_t", [128, 4096], F32, "ExternalInput")
    wd["sin_t"] = p.dram("sin_t", [128, 4096], F32, "ExternalInput")
    cc_in = [p.dram(f"cc_in{k}", [2048, 1024], F32, "Internal") for k in range(3)]
    gath = [p.dram(f"gath{k}", [4096, 1024], F32, "Internal") for k in range(3)]

    W0 = {"q": wd["L0_attn_Wq"], "k": wd["L0_attn_Wk"], "v": wd["L0_attn_Wv"],
          "fw1": wd["L0_ffn_W1"], "b1": wd["L0_ffn_b1"],
          "fw2": wd["L0_ffn_W2"], "b2": wd["L0_ffn_b2"]}
    W1 = {"in_proj": wd["L1_attn_in_proj_w"], "conv_w": wd["L1_attn_conv_w"],
          "conv_b": wd["L1_attn_conv_b"], "dt_bias": wd["L1_attn_dt_bias"],
          "A_log": wd["L1_attn_A_log"], "Dp": wd["L1_attn_Dp"],
          "norm_w": wd["L1_attn_norm_w"], "out_proj": wd["L1_attn_out_proj_w"],
          "Wg": wd["L1_ffn_Wg"], "W1": wd["L1_ffn_W1"], "W2": wd["L1_ffn_W2"],
          "W3": wd["L1_ffn_W3"]}
    W2 = {"qkv_w": wd["L2_attn_qkv_w"], "qkv_b": wd["L2_attn_qkv_b"],
          "rq_w": wd["L2_attn_rq_w"], "rq_b": wd["L2_attn_rq_b"],
          "rkv_w": wd["L2_attn_rkv_w"], "rkv_b": wd["L2_attn_rkv_b"],
          "rout_w": wd["L2_attn_rout_w"], "rout_b": wd["L2_attn_rout_b"],
          "out_w": wd["L2_attn_out_w"], "out_b": wd["L2_attn_out_b"],
          "gl_w": wd["L2_attn_gl_w"], "gr_w": wd["L2_attn_gr_w"],
          "w_gate": wd["L2_ffn_w_gate"], "w_up": wd["L2_ffn_w_up"],
          "w_down": wd["L2_ffn_w_down"]}
    W3 = {"Wq": wd["L3_attn_Wq"], "Wk": wd["L3_attn_Wk"], "Wv": wd["L3_attn_Wv"],
          "Wg": wd["L3_attn_Wg"], "Wo": wd["L3_attn_Wo"],
          "sk_w1": wd["L3_attn_sk_w1"], "sk_w2": wd["L3_attn_sk_w2"],
          "sk_b2": wd["L3_attn_sk_b2"], "sv_w1": wd["L3_attn_sv_w1"],
          "sv_w2": wd["L3_attn_sv_w2"], "sv_b2": wd["L3_attn_sv_b2"],
          "gnorm_w": wd["L3_attn_gnorm_w"], "cos_t": wd["cos_t"],
          "sin_t": wd["sin_t"],
          "gate_w": wd["L3_ffn_gate_w"], "w1": wd["L3_ffn_w1"],
          "b1": wd["L3_ffn_b1"], "w2": wd["L3_ffn_w2"], "b2": wd["L3_ffn_b2"],
          "sw1": wd["L3_ffn_sw1"], "sb1": wd["L3_ffn_sb1"],
          "sw2": wd["L3_ffn_sw2"], "sb2": wd["L3_ffn_sb2"]}

    for h in (0, 1):
        p.begin_variant(h)
        cm = Common(p)
        own = slice(h * 2048, (h + 1) * 2048)
        build_block0(p, cm, h, x_full[own, :], x_full[0:2048, :], W0, cc_in[0])
        p.op("gpsimd", "collective_compute", kind="AllGather", op=ALU.bypass,
             replica_groups=RG_PAIRS, ins=[cc_in[0][:, :]], outs=[gath[0][:, :]])
        build_block1(p, cm, h, gath[0][own, :], gath[0][0:2048, :], W1, cc_in[1])
        p.op("gpsimd", "collective_compute", kind="AllGather", op=ALU.bypass,
             replica_groups=RG_PAIRS, ins=[cc_in[1][:, :]], outs=[gath[1][:, :]])
        build_block2(p, cm, h, gath[1][own, :], gath[1][0:2048, :], W2, cc_in[2])
        p.op("gpsimd", "collective_compute", kind="AllGather", op=ALU.bypass,
             replica_groups=RG_PAIRS, ins=[cc_in[2][:, :]], outs=[gath[2][:, :]])
        build_block3(p, cm, h, gath[2][own, :], gath[2][0:2048, :], W3, x_new)
    return p.build()


def _build_block_2var(bidx):
    p = Prog()
    x_own = p.dram("x_own", [2048, 1024], F32, "ExternalInput")
    x_other = p.dram("x_other", [2048, 1024], F32, "ExternalInput")
    x_new = p.dram("x_new", [2048, 1024], F32, "ExternalOutput")
    wd = {k: p.dram(k, s, F32, "ExternalInput") for k, s in WSHAPES.items()}
    wd["cos_t"] = p.dram("cos_t", [128, 4096], F32, "ExternalInput")
    wd["sin_t"] = p.dram("sin_t", [128, 4096], F32, "ExternalInput")
    W0 = {"q": wd["L0_attn_Wq"], "k": wd["L0_attn_Wk"], "v": wd["L0_attn_Wv"],
          "fw1": wd["L0_ffn_W1"], "b1": wd["L0_ffn_b1"],
          "fw2": wd["L0_ffn_W2"], "b2": wd["L0_ffn_b2"]}
    W1 = {"in_proj": wd["L1_attn_in_proj_w"], "conv_w": wd["L1_attn_conv_w"],
          "conv_b": wd["L1_attn_conv_b"], "dt_bias": wd["L1_attn_dt_bias"],
          "A_log": wd["L1_attn_A_log"], "Dp": wd["L1_attn_Dp"],
          "norm_w": wd["L1_attn_norm_w"], "out_proj": wd["L1_attn_out_proj_w"],
          "Wg": wd["L1_ffn_Wg"], "W1": wd["L1_ffn_W1"], "W2": wd["L1_ffn_W2"],
          "W3": wd["L1_ffn_W3"]}
    W2 = {"qkv_w": wd["L2_attn_qkv_w"], "qkv_b": wd["L2_attn_qkv_b"],
          "rq_w": wd["L2_attn_rq_w"], "rq_b": wd["L2_attn_rq_b"],
          "rkv_w": wd["L2_attn_rkv_w"], "rkv_b": wd["L2_attn_rkv_b"],
          "rout_w": wd["L2_attn_rout_w"], "rout_b": wd["L2_attn_rout_b"],
          "out_w": wd["L2_attn_out_w"], "out_b": wd["L2_attn_out_b"],
          "gl_w": wd["L2_attn_gl_w"], "gr_w": wd["L2_attn_gr_w"],
          "w_gate": wd["L2_ffn_w_gate"], "w_up": wd["L2_ffn_w_up"],
          "w_down": wd["L2_ffn_w_down"]}
    W3 = {"Wq": wd["L3_attn_Wq"], "Wk": wd["L3_attn_Wk"], "Wv": wd["L3_attn_Wv"],
          "Wg": wd["L3_attn_Wg"], "Wo": wd["L3_attn_Wo"],
          "sk_w1": wd["L3_attn_sk_w1"], "sk_w2": wd["L3_attn_sk_w2"],
          "sk_b2": wd["L3_attn_sk_b2"], "sv_w1": wd["L3_attn_sv_w1"],
          "sv_w2": wd["L3_attn_sv_w2"], "sv_b2": wd["L3_attn_sv_b2"],
          "gnorm_w": wd["L3_attn_gnorm_w"], "cos_t": wd["cos_t"],
          "sin_t": wd["sin_t"],
          "gate_w": wd["L3_ffn_gate_w"], "w1": wd["L3_ffn_w1"],
          "b1": wd["L3_ffn_b1"], "w2": wd["L3_ffn_w2"], "b2": wd["L3_ffn_b2"],
          "sw1": wd["L3_ffn_sw1"], "sb1": wd["L3_ffn_sb1"],
          "sw2": wd["L3_ffn_sw2"], "sb2": wd["L3_ffn_sb2"]}
    builders = [
        lambda p, cm, h: build_block0(p, cm, h, x_own, x_other, W0, x_new),
        lambda p, cm, h: build_block1(p, cm, h, x_own, x_other, W1, x_new),
        lambda p, cm, h: build_block2(p, cm, h, x_own, x_other, W2, x_new),
        lambda p, cm, h: build_block3(p, cm, h, x_own, x_other, W3, x_new),
    ]
    for h in (0, 1):
        p.begin_variant(h)
        cm = Common(p)
        builders[bidx](p, cm, h)
    return p.build()


def kernel(**inputs):
    cos_t, sin_t = _rope_tables()
    x = np.ascontiguousarray(inputs["L0_attn_X"].astype(np.float32))
    for bidx in range(4):
        nc = _build_block_2var(bidx)
        in_maps = []
        for core in range(8):
            b, h = core // 2, core % 2
            im = {k: inputs[k] for k in WSHAPES}
            im["x_own"] = np.ascontiguousarray(x[b, h * 2048:(h + 1) * 2048])
            im["x_other"] = np.ascontiguousarray(x[b, 0:2048])
            im["cos_t"] = cos_t
            im["sin_t"] = sin_t
            in_maps.append(im)
        res = run_bass_kernel_spmd(nc, in_maps, core_ids=list(range(8)))
        xn = np.empty_like(x)
        for core in range(8):
            b, h = core // 2, core % 2
            xn[b, h * 2048:(h + 1) * 2048, :] = res.results[core]["x_new"]
        x = xn
    return x


def _kernel_fused_wip(**inputs):
    cos_t, sin_t = _rope_tables()
    nc = _build_fused()
    x = np.ascontiguousarray(inputs["L0_attn_X"].astype(np.float32))
    in_maps = []
    for core in range(8):
        b = core // 2
        im = {k: inputs[k] for k in WSHAPES}
        im["x_full"] = np.ascontiguousarray(x[b])
        im["cos_t"] = cos_t
        im["sin_t"] = sin_t
        in_maps.append(im)
    res = run_bass_kernel_spmd(nc, in_maps, core_ids=list(range(8)))
    out = np.empty((4, 4096, 1024), dtype=np.float32)
    for core in range(8):
        b, h = core // 2, core % 2
        out[b, h * 2048:(h + 1) * 2048, :] = res.results[core]["x_new"]
    return out


def _unused_kernel_tail():
    pass
